# version 09
# baseline (speedup 1.0000x reference)
import math
import sys as _sys
import numpy as np
import concourse.bass as bass
import concourse.mybir as mybir
from concourse.bass_utils import run_bass_kernel_spmd

F32 = mybir.dt.float32
BF16 = mybir.dt.bfloat16
AF = mybir.ActivationFunctionType
ALU = mybir.AluOpType
AX = mybir.AxisListType


STREAMS = ("pe", "act", "dve", "pool", "sp")
ENG_ATTR = {"pe": "tensor", "act": "scalar", "dve": "vector",
            "pool": "gpsimd", "sp": "sync"}
DMA_QUEUES = {"sp": "q_sp", "act": "q_act", "pool": "q_pool"}


def _overlap(a, b):
    return a[0] < b[1] and b[0] < a[1] and a[2] < b[3] and b[2] < a[3]


class Region:
    __slots__ = ("t", "box")

    def __init__(self, t, p0, p1, f0, f1):
        self.t = t
        self.box = (p0, p1, f0, f1)


def R(t, p0=0, p1=128, f0=0, f1=None):
    if f1 is None:
        f1 = 1 << 40
    return Region(t, p0, p1, f0, f1)


class Prog:
    def __init__(self, name="mk"):
        self.nc = bass.Bass("TRN2", target_bir_lowering=False)
        self.ops = {s: [] for s in STREAMS}
        self.meta = {s: [] for s in STREAMS}
        self.sems = {}
        for s in STREAMS:
            self.sems[s] = self.nc.alloc_semaphore(name=f"c_{s}")
        for q in DMA_QUEUES.values():
            self.sems[q] = self.nc.alloc_semaphore(name=q)
        self.cnt = {k: 0 for k in self.sems}
        self.vc = {s: {} for s in STREAMS}
        self.vc_shared = {s: False for s in STREAMS}
        self.ckpt = {k: {} for k in self.sems}
        self.recs = {}
        self.n_waits = 0
        self.n_ops = 0
        self._const_cols = {}
        self._const_pool = None
        self._const_used = 0
        self._const_row_pool = None
        self._const_row_used = 0
        self._const_fenced = False

    def fence_consts(self, extra_regions=()):
        regs = list(extra_regions)
        if self._const_pool is not None:
            regs.append(R(self._const_pool))
        if self._const_row_pool is not None:
            regs.append(Region(self._const_row_pool, 0, 1, 0, 512))
        for s in STREAMS:
            self.emit(s, lambda e: e.nop(), reads=regs)
        self._const_fenced = True

    def sbuf(self, name, shape, dtype):
        return self.nc.alloc_sbuf_tensor(name, shape, dtype)

    def psum(self, name, shape, dtype=F32):
        return self.nc.alloc_psum_tensor(name, shape, dtype)

    def dram(self, name, shape, dtype, kind="Internal"):
        return self.nc.dram_tensor(name, shape, dtype, kind=kind)

    def const_col(self, val):
        if val in self._const_cols:
            return self._const_cols[val]
        assert not self._const_fenced, f"const_col({val}) after fence"
        if self._const_pool is None:
            self._const_pool = self.sbuf("const_pool", [128, 64], F32)
        i = self._const_used
        assert i < 64, "const pool exhausted"
        self._const_used += 1
        ap = self._const_pool.ap()[:, i:i + 1]
        self.emit("dve", lambda e, ap=ap, v=float(val): e.memset(ap, v),
                  writes=[Region(self._const_pool, 0, 128, i, i + 1)])
        self._const_cols[val] = ap
        return ap

    def const_row(self, val):
        key = ("row", val)
        if key in self._const_cols:
            return self._const_cols[key]
        assert not self._const_fenced, "const_row after fence"
        if self._const_row_pool is None:
            self._const_row_pool = self.sbuf("const_rows", [1, 512], F32)
        i = self._const_row_used
        assert i < 4
        self._const_row_used += 1
        ap = self._const_row_pool.ap()[0:1, i * 128:(i + 1) * 128]
        self.emit("dve", lambda e, ap=ap, v=float(val): e.memset(ap, v),
                  writes=[Region(self._const_row_pool, 0, 1, i * 128, (i + 1) * 128)])
        self._const_cols[key] = ap
        return ap

    def _freeze_vc(self, s):
        self.vc_shared[s] = True
        return self.vc[s]

    def _own_vc(self, s):
        if self.vc_shared[s]:
            self.vc[s] = dict(self.vc[s])
            self.vc_shared[s] = False
        return self.vc[s]

    def _need_edges(self, s, regions, kinds):
        edges = {}
        for reg, kind in zip(regions, kinds):
            rec = self.recs.get(id(reg.t))
            if rec is None:
                continue
            confl = rec["w"] if kind == "r" else rec["w"] + rec["r"]
            for box, sem, val in confl:
                if _overlap(box, reg.box):
                    if edges.get(sem, 0) < val:
                        edges[sem] = val
        vc = self.vc[s]
        out = {m: v for m, v in edges.items() if vc.get(m, 0) < v}
        return out

    def _apply_edges(self, s, edges):
        if not edges:
            return
        vc = self._own_vc(s)
        for m, v in edges.items():
            ck = self.ckpt[m].get(v)
            if ck:
                for m2, v2 in ck.items():
                    if vc.get(m2, 0) < v2:
                        vc[m2] = v2
            if vc.get(m, 0) < v:
                vc[m] = v
        self.n_waits += len(edges)

    def _record(self, regions, kinds, sem, val):
        for reg, kind in zip(regions, kinds):
            rec = self.recs.setdefault(id(reg.t), {"w": [], "r": []})
            if kind == "w":
                box = reg.box
                rec["w"] = [e for e in rec["w"] if not _covers(box, e[0])]
                rec["r"] = [e for e in rec["r"] if not _covers(box, e[0])]
                rec["w"].append((box, sem, val))
            else:
                rec["r"].append((reg.box, sem, val))

    def emit(self, s, fn, reads=(), writes=(), dma=False):
        self.meta[s].append(_sys._getframe(1).f_lineno)
        regions = list(reads) + list(writes)
        kinds = ["r"] * len(reads) + ["w"] * len(writes)
        edges = self._need_edges(s, regions, kinds)
        if dma:
            q = DMA_QUEUES[s]
            prev = self.cnt[q]
            if prev and self.vc[s].get(q, 0) < prev:
                edges[q] = max(edges.get(q, 0), prev)
            sem, amt = q, 16
        else:
            sem, amt = s, 1
        self._apply_edges(s, edges)
        self.cnt[sem] += amt
        val = self.cnt[sem]
        self.ckpt[sem][val] = self._freeze_vc(s)
        self._record(regions, kinds, sem, val)
        semobj = {m: self.sems[m] for m in edges}
        mys = self.sems[sem]
        wl = sorted(edges.items())

        def op(eng, fn=fn, wl=wl, semobj=semobj, mys=mys, amt=amt):
            for m, v in wl:
                eng.wait_ge(semobj[m], v)
            inst = fn(eng)
            inst.then_inc(mys, amt)
        self.ops[s].append(op)
        self.n_ops += 1
        return val

    def barrier_val(self, stream_or_sem):
        return self.cnt[stream_or_sem]

    def finalize(self, output_dmas=None):
        nc = self.nc
        final = {q: self.cnt[q] for q in DMA_QUEUES.values() if self.cnt[q]}

        def fin(eng, final=final):
            for m, v in final.items():
                eng.wait_ge(self.sems[m], v)
        self.ops["sp"].append(fin)
        with nc.Block() as block:
            for s in STREAMS:
                getattr(block, ENG_ATTR[s])(_runner(self.ops[s]))
        return nc


def _covers(a, b):
    return a[0] <= b[0] and a[1] >= b[1] and a[2] <= b[2] and a[3] >= b[3]


def _runner(ops):
    def _(eng):
        for f in ops:
            f(eng)
    return _



MAGIC = float(3 << 22)
D = 1024
ND = D // 128


class Arena:

    def __init__(self, t, n):
        self.t = t
        self.n = n
        self.off = 0
        self.marks = []

    def alloc(self, n):
        assert self.off + n <= self.n, f"arena overflow {self.off}+{n}>{self.n}"
        o = self.off
        self.off += n
        return o

    def save(self):
        self.marks.append(self.off)

    def restore(self):
        self.off = self.marks.pop()

    def reset(self):
        self.off = 0
        self.marks = []


class View:
    __slots__ = ("t", "o", "n")

    def __init__(self, t, o, n):
        self.t, self.o, self.n = t, o, n

    def ap(self, f0=0, f1=None, p0=0, p1=128):
        if f1 is None:
            f1 = self.n
        return self.t.ap()[p0:p1, self.o + f0:self.o + f1]

    def reg(self, f0=0, f1=None, p0=0, p1=128):
        if f1 is None:
            f1 = self.n
        return Region(self.t, p0, p1, self.o + f0, self.o + f1)


class BView:
    __slots__ = ("t", "o", "n")

    def __init__(self, t, o, n):
        self.t, self.o, self.n = t, o, n

    def ap(self, f0=0, f1=None, p0=0, p1=128):
        if f1 is None:
            f1 = self.n
        return self.t.ap().bitcast(BF16)[p0:p1, self.o + f0:self.o + f1]

    def reg(self, f0=0, f1=None, p0=0, p1=128):
        if f1 is None:
            f1 = self.n
        return Region(self.t, p0, p1, (self.o + f0) // 2,
                      (self.o + f1 + 1) // 2)


class MK:

    def __init__(self, p: Prog, T_elem=2048, n_elem=2):
        self.p = p
        self.T = T_elem
        self.NE = n_elem
        self.NT = T_elem * n_elem
        self.TT = self.NT // 128
        self.TTE = T_elem // 128
        nc = p.nc
        self.x = p.sbuf("x_res", [128, self.TT * D], BF16)
        self.ar16_t = p.sbuf("ar16", [128, 34304], BF16)
        self.ar32_t = p.sbuf("ar32", [128, 17408], F32)
        self.small_t = p.sbuf("small", [128, 960], F32)
        self.ident = p.sbuf("ident", [128, 128], BF16)
        self.id32 = p.sbuf("id32m", [128, 128], F32)
        self.ones_bf = p.sbuf("ones_bf", [128, 128], BF16)
        self.ar16 = Arena(self.ar16_t, 34304)
        self.ar32 = Arena(self.ar32_t, 17408)
        self.small = Arena(self.small_t, 960)
        self.ps = [p.psum(f"ps{i}", [128, 512], F32) for i in range(8)]

    def v16(self, n):
        return View(self.ar16_t, self.ar16.alloc(n), n)

    def v16_in32(self, n):
        assert n % 2 == 0
        o = self.ar32.alloc(n // 2)
        return BView(self.ar32_t, o * 2, n)

    def v32(self, n):
        return View(self.ar32_t, self.ar32.alloc(n), n)

    def vsm(self, n):
        return View(self.small_t, self.small.alloc(n), n)

    CONST_VALS = (1.0, -0.5, 1e-5, 0.125, -1.0, 0.5, 127.0,
                  1.0 / 127.0, 1.0 / 1024.0, 0.3535533905932738, 1e-6)

    def load_const_tiles(self, identd, onesd=None):
        p = self.p
        p.emit("pool", lambda e: e.dma_start(out=self.ident.ap()[:, :],
                                             in_=identd[:, :]),
               writes=[R(self.ident)], dma=True)
        p.emit("sp", lambda e: e.dma_start(out=self.id32.ap()[:, :],
                                           in_=identd[:, :]),
               writes=[R(self.id32)], dma=True)
        p.emit("dve", lambda e: e.memset(self.ones_bf.ap()[:, :], 1.0),
               writes=[R(self.ones_bf)])
        for v in self.CONST_VALS:
            p.const_col(v)
        p.const_row(1.0)
        p.fence_consts([R(self.ident), R(self.id32), R(self.ones_bf)])

    def transpose8(self, src_fn, dst_ap, dst_reg, reads, pb=0, n=8):
        p = self.p
        bank = self.ps[pb]
        bap = bank.ap().bitcast(BF16)

        def tp(t):
            last = None
            for i in range(n):
                last = t.transpose(bap[:, i * 128:(i + 1) * 128],
                                   src_fn(i), self.ident.ap()[:, :])
            return last
        p.emit("pe", tp, reads=list(reads) + [R(self.ident)], writes=[R(bank)])
        p.emit("dve", lambda e: e.tensor_copy(dst_ap, bap[:, 0:n * 128]),
               reads=[R(bank)], writes=[dst_reg])

    def page_x(self, xp, direction, order=None):
        p = self.p
        tiles = list(order) if order is not None else list(range(self.TT))
        for tt in tiles:
            fs = (tt * D, (tt + 1) * D)
            if direction == "out":
                p.emit("sp", lambda e, fs=fs: e.dma_start(
                    out=xp.ap()[:, fs[0]:fs[1]],
                    in_=self.x.ap()[:, fs[0]:fs[1]]),
                    reads=[Region(self.x, 0, 128, *fs)],
                    writes=[Region(xp, 0, 128, *fs)], dma=True)
            else:
                p.emit("sp", lambda e, fs=fs: e.dma_start(
                    out=self.x.ap()[:, fs[0]:fs[1]],
                    in_=xp.ap()[:, fs[0]:fs[1]]),
                    reads=[Region(xp, 0, 128, *fs)],
                    writes=[Region(self.x, 0, 128, *fs)], dma=True)

    def ln_stats(self, tt_tiles, rstd_v, negm_v, scale_extra=None):
        p = self.p
        x = self.x
        n = len(tt_tiles)
        self.ar16.save(); self.small.save()
        sq = self.vsm(n)
        sm = negm_v
        var = self.vsm(n)
        tmps = [self.v16(1024) for _ in range(2)]
        for j, tt in enumerate(tt_tiles):
            xa = x.ap()[:, tt * D:(tt + 1) * D]
            xr = Region(x, 0, 128, tt * D, (tt + 1) * D)
            tmp16 = tmps[j % 2]
            p.emit("act", lambda e, xa=xa, tmp16=tmp16, j=j: e.activation(
                tmp16.ap(), xa, AF.Square, accum_out=sq.ap(j, j + 1)),
                reads=[xr], writes=[tmp16.reg(), sq.reg(j, j + 1)])
            p.emit("dve", lambda e, xa=xa, j=j: e.reduce_sum(
                sm.ap(j, j + 1), xa, axis=AX.X, negate=True),
                reads=[xr], writes=[sm.reg(j, j + 1)])
        p.emit("dve", lambda e: e.tensor_single_scalar(
            sm.ap(0, n), sm.ap(0, n), 1.0 / D, ALU.mult),
            reads=[sm.reg(0, n)], writes=[sm.reg(0, n)])
        p.emit("dve", lambda e: e.tensor_tensor(
            var.ap(0, n), sm.ap(0, n), sm.ap(0, n), ALU.mult),
            reads=[sm.reg(0, n)], writes=[var.reg(0, n)])
        p.emit("dve", lambda e: e.scalar_tensor_tensor(
            var.ap(0, n), sq.ap(0, n), 1.0 / D, var.ap(0, n),
            ALU.mult, ALU.subtract),
            reads=[sq.reg(0, n), var.reg(0, n)], writes=[var.reg(0, n)])
        c_eps = self.p.const_col(1e-5)
        c_half = self.p.const_col(-0.5)
        p.emit("act", lambda e: e.activation(
            var.ap(0, n), var.ap(0, n), AF.Ln, bias=c_eps),
            reads=[var.reg(0, n)], writes=[var.reg(0, n)])
        p.emit("act", lambda e: e.activation(
            rstd_v.ap(0, n), var.ap(0, n), AF.Exp, scale=c_half),
            reads=[var.reg(0, n)], writes=[rstd_v.reg(0, n)])
        self.ar16.restore(); self.small.restore()

    def ln_apply(self, tt, negm_v, rstd_v, out_ap, out_reg, j=None):
        if j is None:
            j = tt
        x = self.x
        xa = x.ap()[:, tt * D:(tt + 1) * D]
        xr = Region(x, 0, 128, tt * D, (tt + 1) * D)
        self.p.emit("dve", lambda e: e.tensor_scalar(
            out_ap, xa, negm_v.ap(j, j + 1), rstd_v.ap(j, j + 1),
            ALU.add, ALU.mult),
            reads=[xr, negm_v.reg(j, j + 1), rstd_v.reg(j, j + 1)],
            writes=[out_reg])




def l0_ffn_prepare(mk, w1T, w2T):
    p = mk.p
    T1d = p.dram("T1d", [1024, 4096], BF16)
    T2d = p.dram("T2d", [4096, 1024], BF16)
    pers = p.sbuf("l0_pers", [128, 2], F32)

    def one(wi, wd, Td, rows, cols):
        mk.ar32.save(); mk.ar16.save(); mk.small.save()
        tiles = [mk.v32(cols) for _ in range(2)]
        junk = mk.v16(cols)
        acc = mk.vsm(1)
        ssum = mk.vsm(1)
        nt = rows // 128
        p.emit("dve", lambda e: e.memset(acc.ap(), 0.0), writes=[acc.reg()])

        def pass1(r):
            tile = tiles[r % 2]
            p.emit("sp", lambda e: e.dma_start(
                out=tile.ap(), in_=wd[r * 128:(r + 1) * 128, :]),
                writes=[tile.reg()], dma=True)
            p.emit("act", lambda e: e.activation(
                junk.ap(), tile.ap(), AF.Abs, accum_out=ssum.ap()),
                reads=[tile.reg()], writes=[junk.reg(), ssum.reg()])
            p.emit("dve", lambda e: e.tensor_tensor(
                acc.ap(), acc.ap(), ssum.ap(), ALU.add),
                reads=[acc.reg(), ssum.reg()], writes=[acc.reg()])
        for r in range(nt):
            pass1(r)
        ps = mk.ps[0]
        ones_c = p.const_col(1.0)
        ones_r = p.const_row(1.0)
        p.emit("pe", lambda e: e.matmul(
            ps.ap()[0:1, 0:1], acc.ap(), ones_c,
            start=True, stop=True),
            reads=[acc.reg()], writes=[R(ps)])
        s_sc = mk.vsm(1)
        rc = float(rows * cols)
        p.emit("dve", lambda e: e.tensor_scalar(
            s_sc.ap(p0=0, p1=1), ps.ap()[0:1, 0:1], 1.0 / rc, 1e-5,
            ALU.mult, ALU.max),
            reads=[R(ps)], writes=[s_sc.reg(p0=0, p1=1)])
        rinv = mk.vsm(1)
        p.emit("dve", lambda e: e.reciprocal(
            rinv.ap(p0=0, p1=1), s_sc.ap(p0=0, p1=1)),
            reads=[s_sc.reg(p0=0, p1=1)], writes=[rinv.reg(p0=0, p1=1)])
        rcol = mk.vsm(1)
        scol = View(pers, wi, 1)

        def bcast(src, dst):
            p.emit("pe", lambda e: e.matmul(
                ps.ap()[:, 0:1], ones_r, src.ap(p0=0, p1=1),
                start=True, stop=True),
                reads=[src.reg(p0=0, p1=1)], writes=[R(ps)])
            p.emit("dve", lambda e: e.tensor_copy(dst.ap(), ps.ap()[:, 0:1]),
                   reads=[R(ps)], writes=[dst.reg()])
        bcast(rinv, rcol)
        bcast(s_sc, scol)

        def pass2(r):
            tile = tiles[r % 2]
            p.emit("sp", lambda e: e.dma_start(
                out=tile.ap(), in_=wd[r * 128:(r + 1) * 128, :]),
                writes=[tile.reg()], dma=True)
            p.emit("dve", lambda e: e.tensor_scalar(
                tile.ap(), tile.ap(), rcol.ap(), MAGIC, ALU.mult, ALU.add),
                reads=[tile.reg(), rcol.reg()], writes=[tile.reg()])
            p.emit("dve", lambda e: e.tensor_single_scalar(
                tile.ap(), tile.ap(), -MAGIC, ALU.add),
                reads=[tile.reg()], writes=[tile.reg()])
            p.emit("dve", lambda e: e.tensor_scalar(
                junk.ap(), tile.ap(), 1.0, -1.0, ALU.min, ALU.max),
                reads=[tile.reg()], writes=[junk.reg()])
            p.emit("sp", lambda e: e.dma_start(
                out=Td.ap()[r * 128:(r + 1) * 128, :], in_=junk.ap()),
                reads=[junk.reg()], writes=[Region(Td, 0, 128, r, r + 1)],
                dma=True)
        for r in range(nt):
            pass2(r)
        mk.ar32.restore(); mk.ar16.restore(); mk.small.restore()
        return scol

    s1 = one(0, w1T, T1d, 1024, 4096)
    s2 = one(1, w2T, T2d, 4096, 1024)
    return T1d, T2d, [s1, s2]


def l0_ffn_apply(mk, T1d, T2d, scols):
    p = mk.p
    mk.ar16.reset(); mk.ar32.reset(); mk.small.reset()
    s1c, s2c = scols
    NTT = mk.TT
    negm = mk.vsm(NTT)
    rstd = mk.vsm(NTT)
    mk.ln_stats(list(range(NTT)), rstd, negm)
    xqT = mk.v16(4096)
    hqT = [mk.v16(4096) for _ in range(4)]
    w1r = [mk.v16(4096) for _ in range(2)]
    w2r = [mk.v16(2048) for _ in range(2)]
    hbuf = mk.v32(16384)

    def do_group(g):
        tts = [g * 4 + i for i in range(4)]
        mk.small.save()
        deq1 = mk.vsm(4)
        deq2 = mk.vsm(4)

        def xquant(i, tt):
            mk.ar32.save(); mk.ar16.save(); mk.small.save()
            t = mk.v32(1024)
            xq = mk.v16(1024)
            mxn = mk.vsm(1)
            scq = mk.vsm(1)
            xa = mk.x.ap()[:, tt * D:(tt + 1) * D]
            xr = Region(mk.x, 0, 128, tt * D, (tt + 1) * D)
            p.emit("dve", lambda e: e.tensor_scalar(
                t.ap(), xa, negm.ap(tt, tt + 1), None, ALU.add),
                reads=[xr, negm.reg(tt, tt + 1)], writes=[t.reg()])
            p.emit("dve", lambda e: e.reduce_max(
                mxn.ap(), t.ap(), axis=AX.X, apply_absolute_value=True),
                reads=[t.reg()], writes=[mxn.reg()])
            p.emit("dve", lambda e: e.tensor_scalar(
                mxn.ap(), mxn.ap(), rstd.ap(tt, tt + 1), 1e-5,
                ALU.mult, ALU.max),
                reads=[mxn.reg(), rstd.reg(tt, tt + 1)], writes=[mxn.reg()])
            p.emit("dve", lambda e: e.reciprocal(
                scq.ap(), mxn.ap()), reads=[mxn.reg()], writes=[scq.reg()])
            p.emit("dve", lambda e: e.tensor_scalar(
                scq.ap(), scq.ap(), rstd.ap(tt, tt + 1), 127.0,
                ALU.mult, ALU.mult),
                reads=[scq.reg(), rstd.reg(tt, tt + 1)], writes=[scq.reg()])
            p.emit("dve", lambda e: e.tensor_scalar(
                deq1.ap(i, i + 1), mxn.ap(), s1c.ap(), 1.0 / 127.0,
                ALU.mult, ALU.mult),
                reads=[mxn.reg(), s1c.reg()], writes=[deq1.reg(i, i + 1)])
            p.emit("dve", lambda e: e.tensor_scalar(
                t.ap(), t.ap(), scq.ap(), MAGIC, ALU.mult, ALU.add),
                reads=[t.reg(), scq.reg()], writes=[t.reg()])
            p.emit("dve", lambda e: e.tensor_single_scalar(
                xq.ap(), t.ap(), -MAGIC, ALU.add),
                reads=[t.reg()], writes=[xq.reg()])
            dst = mk.ar16_t.ap()[:, xqT.o:xqT.o + 4096].rearrange(
                "p (c t) -> p c t", c=8)[:, :, i * 128:(i + 1) * 128]
            mk.transpose8(lambda dc: xq.ap(dc * 128, (dc + 1) * 128),
                          dst, xqT.reg(), [xq.reg()], pb=i % 2)
            mk.ar32.restore(); mk.ar16.restore(); mk.small.restore()
        for i, tt in enumerate(tts):
            xquant(i, tt)

        def hblock(hb):
            wv = w1r[hb % 2]
            p.emit("pool", lambda e: e.dma_start(
                out=wv.ap().rearrange("p (c h) -> p c h", c=8),
                in_=T1d.ap().rearrange("(c p) h -> p c h", p=128)
                [:, :, hb * 512:(hb + 1) * 512]),
                reads=[R(T1d)], writes=[wv.reg()], dma=True)

            def one_tt(i):
                bank = mk.ps[4 + i % 2]

                def mmh(t):
                    last = None
                    for dc in range(8):
                        last = t.matmul(
                            bank.ap()[:, :],
                            xqT.ap(dc * 512 + i * 128, dc * 512 + (i + 1) * 128),
                            wv.ap(dc * 512, (dc + 1) * 512),
                            start=(dc == 0), stop=(dc == 7))
                    return last
                p.emit("pe", mmh, reads=[xqT.reg(), wv.reg()], writes=[R(bank)])
                hs = (i * 4096 + hb * 512, i * 4096 + (hb + 1) * 512)
                p.emit("act", lambda e: e.activation(
                    hbuf.ap(*hs), bank.ap()[:, :], AF.Gelu,
                    scale=deq1.ap(i, i + 1)),
                    reads=[R(bank), deq1.reg(i, i + 1)], writes=[hbuf.reg(*hs)])
            for i in range(4):
                one_tt(i)
        for hb in range(8):
            hblock(hb)

        def hquant(i, tt):
            mk.small.save()
            hs = (i * 4096, (i + 1) * 4096)
            mxh = mk.vsm(1)
            sch = mk.vsm(1)
            hv = View(mk.ar16_t, xqT.o, 4096)
            p.emit("dve", lambda e: e.reduce_max(
                mxh.ap(), hbuf.ap(*hs), axis=AX.X, apply_absolute_value=True),
                reads=[hbuf.reg(*hs)], writes=[mxh.reg()])
            p.emit("dve", lambda e: e.tensor_single_scalar(
                mxh.ap(), mxh.ap(), 1e-5, ALU.max),
                reads=[mxh.reg()], writes=[mxh.reg()])
            p.emit("dve", lambda e: e.reciprocal(
                sch.ap(), mxh.ap()), reads=[mxh.reg()], writes=[sch.reg()])
            p.emit("dve", lambda e: e.tensor_single_scalar(
                sch.ap(), sch.ap(), 127.0, ALU.mult),
                reads=[sch.reg()], writes=[sch.reg()])
            p.emit("dve", lambda e: e.tensor_scalar(
                deq2.ap(i, i + 1), mxh.ap(), s2c.ap(), 0.5 / 127.0,
                ALU.mult, ALU.mult),
                reads=[mxh.reg(), s2c.reg()], writes=[deq2.reg(i, i + 1)])
            p.emit("dve", lambda e: e.tensor_scalar(
                hbuf.ap(*hs), hbuf.ap(*hs), sch.ap(), MAGIC, ALU.mult, ALU.add),
                reads=[hbuf.reg(*hs), sch.reg()], writes=[hbuf.reg(*hs)])
            p.emit("dve", lambda e: e.tensor_single_scalar(
                hv.ap(), hbuf.ap(*hs), -MAGIC, ALU.add),
                reads=[hbuf.reg(*hs)], writes=[hv.reg()])
            tv = hqT[i]

            def tp4(b4):
                dst = mk.ar16_t.ap()[:, tv.o + b4 * 1024:tv.o + (b4 + 1) * 1024]
                mk.transpose8(
                    lambda k: hv.ap((b4 * 8 + k) * 128, (b4 * 8 + k + 1) * 128),
                    dst,
                    Region(tv.t, 0, 128, tv.o + b4 * 1024, tv.o + (b4 + 1) * 1024),
                    [hv.reg()], pb=4 + b4 % 2)
            for b4 in range(4):
                tp4(b4)
            mk.small.restore()
        for i, tt in enumerate(tts):
            hquant(i, tt)

        def yblock(hb):
            wv = w2r[hb % 2]
            p.emit("pool", lambda e: e.dma_start(
                out=wv.ap().rearrange("p (c h) -> p c h", c=2),
                in_=T2d.ap().rearrange("(c p) o -> p c o", p=128)
                [:, hb * 2:(hb + 1) * 2, :]),
                reads=[R(T2d)], writes=[wv.reg()], dma=True)

            def one(i, ob):
                tv = hqT[i]
                bank = mk.ps[i * 2 + ob]

                def mmy(t):
                    last = None
                    for hc in range(2):
                        last = t.matmul(
                            bank.ap()[:, :],
                            tv.ap((hb * 2 + hc) * 128, (hb * 2 + hc + 1) * 128),
                            wv.ap(hc * 1024 + ob * 512, hc * 1024 + (ob + 1) * 512),
                            start=(hb == 0 and hc == 0),
                            stop=(hb == 15 and hc == 1))
                    return last
                p.emit("pe", mmy, reads=[tv.reg(), wv.reg()], writes=[R(bank)])
            for i in range(4):
                for ob in range(2):
                    one(i, ob)
        for hb in range(16):
            yblock(hb)

        def evac(i, tt, ob):
            bank = mk.ps[i * 2 + ob]
            xs = (tt * D + ob * 512, tt * D + (ob + 1) * 512)
            xa = mk.x.ap()[:, xs[0]:xs[1]]
            xreg = Region(mk.x, 0, 128, *xs)
            p.emit("dve", lambda e: e.scalar_tensor_tensor(
                xa, bank.ap()[:, :], deq2.ap(i, i + 1), xa,
                ALU.mult, ALU.add),
                reads=[R(bank), deq2.reg(i, i + 1), xreg], writes=[xreg])
        for i, tt in enumerate(tts):
            for ob in range(2):
                evac(i, tt, ob)
        mk.small.restore()
    for g in range(NTT // 4):
        do_group(g)


def _fp4_qabs_chain(p, y_v, q_v, engs=("dve",)):
    k = [0]

    def E():
        k[0] += 1
        return engs[k[0] % len(engs)]
    acc_a = q_v
    p.emit(E(), lambda e: e.tensor_single_scalar(
        acc_a.ap(), y_v.ap(), 0.25, ALU.is_gt),
        reads=[y_v.reg()], writes=[acc_a.reg()])
    for m in (0.75, 1.25, 1.75):
        p.emit(E(), lambda e, m=m: e.scalar_tensor_tensor(
            acc_a.ap(), y_v.ap(), m, acc_a.ap(), ALU.is_gt, ALU.add),
            reads=[y_v.reg(), acc_a.reg()], writes=[acc_a.reg()])
    p.emit(E(), lambda e: e.tensor_single_scalar(
        acc_a.ap(), acc_a.ap(), 0.5, ALU.mult),
        reads=[acc_a.reg()], writes=[acc_a.reg()])
    for m in (2.5, 3.5):
        p.emit(E(), lambda e, m=m: e.scalar_tensor_tensor(
            acc_a.ap(), y_v.ap(), m, acc_a.ap(), ALU.is_gt, ALU.add),
            reads=[y_v.reg(), acc_a.reg()], writes=[acc_a.reg()])
    p.emit(E(), lambda e: e.tensor_single_scalar(
        y_v.ap(), y_v.ap(), 5.0, ALU.is_gt),
        reads=[y_v.reg()], writes=[y_v.reg()])
    p.emit(E(), lambda e: e.scalar_tensor_tensor(
        q_v.ap(), y_v.ap(), 2.0, acc_a.ap(), ALU.mult, ALU.add),
        reads=[y_v.reg(), acc_a.reg()], writes=[q_v.reg()])


def l3_ffn_prepare(mk, nat, trans, E8d, ident32d):
    p = mk.p
    E8 = p.sbuf("E8s", [8, 128], F32)
    id32 = p.sbuf("id32", [128, 128], F32)
    p.emit("sp", lambda e: e.dma_start(out=E8.ap()[:, :], in_=E8d[:, :]),
           writes=[R(E8)], dma=True)
    p.emit("sp", lambda e: e.dma_start(out=id32.ap()[:, :], in_=ident32d[:, :]),
           writes=[R(id32)], dma=True)
    out = {}
    shapes = {"wg": (2816, 1024), "wu": (2816, 1024), "wd": (1024, 2816)}

    def one_matrix(nm):
        N, Kd = shapes[nm]
        wq = p.dram(f"l3q_{nm}", [Kd, N], BF16)
        out[nm] = wq
        nb = Kd // 16
        scT_d = p.dram(f"l3sT_{nm}", [nb, N], F32)
        ivT_d = p.dram(f"l3iT_{nm}", [nb, N], F32)
        mk.ar32.save(); mk.small.save()
        tile = mk.v32(Kd)
        sc = mk.v32(nb)
        iv = mk.v32(nb)
        stg = mk.v32(128)

        def phaseA(r):
            p.emit("sp", lambda e: e.dma_start(
                out=tile.ap(), in_=nat[nm][r * 128:(r + 1) * 128, :]),
                writes=[tile.reg()], dma=True)
            p.emit("dve", lambda e: e.tensor_reduce(
                sc.ap(), tile.ap().rearrange("p (b k) -> p b k", k=16),
                axis=AX.X, op=ALU.max, apply_absolute_value=True),
                reads=[tile.reg()], writes=[sc.reg()])
            p.emit("dve", lambda e: e.tensor_scalar(
                sc.ap(), sc.ap(), 1.0 / 6.0, 1e-12, ALU.mult, ALU.max),
                reads=[sc.reg()], writes=[sc.reg()])
            p.emit("dve", lambda e: e.reciprocal(
                iv.ap(), sc.ap()), reads=[sc.reg()], writes=[iv.reg()])
            for src, dst_d in ((sc, scT_d), (iv, ivT_d)):
                nsp = 0
                while nsp < nb:
                    w = min(128, nb - nsp)
                    bank = mk.ps[0]
                    p.emit("pe", lambda e, src=src, nsp=nsp, w=w: e.matmul(
                        bank.ap()[0:w, 0:128],
                        src.ap(nsp, nsp + w), id32.ap()[:, :],
                        start=True, stop=True),
                        reads=[src.reg(nsp, nsp + w), R(id32)],
                        writes=[R(bank)])
                    p.emit("dve", lambda e, w=w: e.tensor_copy(
                        stg.ap(p1=w), bank.ap()[0:w, 0:128]),
                        reads=[R(bank)], writes=[stg.reg()])
                    p.emit("sp", lambda e, dst_d=dst_d, nsp=nsp, w=w, r=r:
                           e.dma_start(
                        out=dst_d.ap()[nsp:nsp + w, r * 128:(r + 1) * 128],
                        in_=stg.ap(p1=w)),
                        reads=[stg.reg()],
                        writes=[Region(dst_d, nsp, nsp + w, r, r + 1)],
                        dma=True)
                    nsp += w
        for r in range(N // 128):
            phaseA(r)
        mk.ar32.restore(); mk.small.restore()
        mk.ar32.save(); mk.ar16.save()
        wt = mk.v32(N)
        yv = mk.v32(N)
        qv = mk.v32(N)
        s8 = mk.v32(N)
        i8 = mk.v32(N)
        sg = mk.v16(N)
        qo = mk.v16(N)

        def phaseB(kc):
            p.emit("sp", lambda e: e.dma_start(
                out=wt.ap(), in_=trans[nm][kc * 128:(kc + 1) * 128, :]),
                writes=[wt.reg()], dma=True)
            p.emit("sp", lambda e: e.dma_start(
                out=s8.ap(p1=8), in_=scT_d.ap()[kc * 8:(kc + 1) * 8, :]),
                reads=[R(scT_d)], writes=[s8.reg()], dma=True)
            p.emit("sp", lambda e: e.dma_start(
                out=i8.ap(p1=8), in_=ivT_d.ap()[kc * 8:(kc + 1) * 8, :]),
                reads=[R(ivT_d)], writes=[i8.reg()], dma=True)
            ns = 0
            sb = mk.ps[6]
            ib = mk.ps[7]
            while ns < N:
                w = min(512, N - ns)
                p.emit("pe", lambda e, ns=ns, w=w: e.matmul(
                    sb.ap()[:, 0:w], E8.ap()[:, :],
                    s8.ap(ns, ns + w, p1=8),
                    start=True, stop=True),
                    reads=[s8.reg(), R(E8)], writes=[R(sb)])
                p.emit("pe", lambda e, ns=ns, w=w: e.matmul(
                    ib.ap()[:, 0:w], E8.ap()[:, :],
                    i8.ap(ns, ns + w, p1=8),
                    start=True, stop=True),
                    reads=[i8.reg(), R(E8)], writes=[R(ib)])
                p.emit("dve", lambda e, ns=ns, w=w: e.tensor_tensor(
                    yv.ap(ns, ns + w), wt.ap(ns, ns + w), ib.ap()[:, 0:w],
                    ALU.mult),
                    reads=[wt.reg(ns, ns + w), R(ib)],
                    writes=[yv.reg(ns, ns + w)])
                p.emit("dve", lambda e, ns=ns, w=w: e.scalar_tensor_tensor(
                    yv.ap(ns, ns + w), yv.ap(ns, ns + w), -1.0,
                    yv.ap(ns, ns + w), ALU.mult, ALU.max),
                    reads=[yv.reg(ns, ns + w)], writes=[yv.reg(ns, ns + w)])
                _fp4_qabs_chain(p, View(yv.t, yv.o + ns, w),
                                View(qv.t, qv.o + ns, w))
                p.emit("act", lambda e, ns=ns, w=w: e.sign(
                    sg.ap(ns, ns + w), wt.ap(ns, ns + w)),
                    reads=[wt.reg(ns, ns + w)], writes=[sg.reg(ns, ns + w)])
                p.emit("dve", lambda e, ns=ns, w=w: e.tensor_tensor(
                    qv.ap(ns, ns + w), qv.ap(ns, ns + w), sb.ap()[:, 0:w],
                    ALU.mult),
                    reads=[qv.reg(ns, ns + w), R(sb)],
                    writes=[qv.reg(ns, ns + w)])
                p.emit("pool", lambda e, ns=ns, w=w: e.tensor_tensor(
                    qo.ap(ns, ns + w), qv.ap(ns, ns + w), sg.ap(ns, ns + w),
                    ALU.mult),
                    reads=[qv.reg(ns, ns + w), sg.reg(ns, ns + w)],
                    writes=[qo.reg(ns, ns + w)])
                ns += w
            p.emit("sp", lambda e: e.dma_start(
                out=wq.ap()[kc * 128:(kc + 1) * 128, :], in_=qo.ap()),
                reads=[qo.reg()], writes=[Region(wq, 0, 128, kc, kc + 1)],
                dma=True)
        for kc in range(Kd // 128):
            phaseB(kc)
        mk.ar32.restore(); mk.ar16.restore()

    for nm in ("wg", "wu", "wd"):
        one_matrix(nm)
    return out


def l3_ffn_apply(mk, wqd):
    p = mk.p
    mk.ar16.reset(); mk.ar32.reset(); mk.small.reset()
    NTT = mk.TT
    HN = 2816
    NHC = HN // 128
    negm = mk.vsm(NTT)
    rstd = mk.vsm(NTT)
    mk.ln_stats(list(range(NTT)), rstd, negm)
    xqT = mk.v16(4096)
    hqT = [mk.v16(NHC * 128) for _ in range(4)]
    wgr = [mk.v16(2048) for _ in range(2)]
    wur = [mk.v16(2048) for _ in range(2)]
    wdr = [mk.v16(2048) for _ in range(2)]
    hbuf = mk.v32(4 * HN)

    def do_group(g):
        tts = [g * 4 + i for i in range(4)]

        def xquant(i, tt):
            mk.ar32.save(); mk.ar16.save(); mk.small.save()
            t = mk.v32(1024)
            yv = mk.v32(1024)
            qv = mk.v32(1024)
            bs = mk.v32(64)
            iv = mk.v32(64)
            sg16 = mk.v16(1024)
            xq = mk.v16(1024)
            mk.ln_apply(tt, negm, rstd, t.ap(), t.reg())
            p.emit("dve", lambda e: e.tensor_reduce(
                bs.ap(), t.ap().rearrange("p (b k) -> p b k", k=16),
                axis=AX.X, op=ALU.max, apply_absolute_value=True),
                reads=[t.reg()], writes=[bs.reg()])
            p.emit("dve", lambda e: e.tensor_scalar(
                bs.ap(), bs.ap(), 1.0 / 6.0, 1e-12, ALU.mult, ALU.max),
                reads=[bs.reg()], writes=[bs.reg()])
            p.emit("dve", lambda e: e.reciprocal(
                iv.ap(), bs.ap()), reads=[bs.reg()], writes=[iv.reg()])
            ivb = iv.ap().rearrange("p (b o) -> p b o", o=1)\
                .broadcast_to((128, 64, 16))
            bsb = bs.ap().rearrange("p (b o) -> p b o", o=1)\
                .broadcast_to((128, 64, 16))
            tv3 = t.ap().rearrange("p (b k) -> p b k", k=16)
            yv3 = yv.ap().rearrange("p (b k) -> p b k", k=16)
            p.emit("dve", lambda e: e.tensor_tensor(
                yv3, tv3, ivb, ALU.mult),
                reads=[t.reg(), iv.reg()], writes=[yv.reg()])
            p.emit("dve", lambda e: e.scalar_tensor_tensor(
                yv.ap(), yv.ap(), -1.0, yv.ap(), ALU.mult, ALU.max),
                reads=[yv.reg()], writes=[yv.reg()])
            _fp4_qabs_chain(p, yv, qv)
            p.emit("act", lambda e: e.sign(
                sg16.ap(), t.ap()), reads=[t.reg()], writes=[sg16.reg()])
            qv3 = qv.ap().rearrange("p (b k) -> p b k", k=16)
            p.emit("dve", lambda e: e.tensor_tensor(
                qv3, qv3, bsb, ALU.mult),
                reads=[qv.reg(), bs.reg()], writes=[qv.reg()])
            p.emit("pool", lambda e: e.tensor_tensor(
                xq.ap(), qv.ap(), sg16.ap(), ALU.mult),
                reads=[qv.reg(), sg16.reg()], writes=[xq.reg()])
            dst = mk.ar16_t.ap()[:, xqT.o:xqT.o + 4096].rearrange(
                "p (c t) -> p c t", c=8)[:, :, i * 128:(i + 1) * 128]
            mk.transpose8(lambda dc: xq.ap(dc * 128, (dc + 1) * 128),
                          dst, xqT.reg(), [xq.reg()], pb=i % 2)
            mk.ar32.restore(); mk.ar16.restore(); mk.small.restore()
        for i, tt in enumerate(tts):
            xquant(i, tt)

        def hblock(hb, w):
            wgv, wuv = wgr[hb % 2], wur[hb % 2]

            def ld(wv, src):
                p.emit("pool", lambda e: e.dma_start(
                    out=wv.ap(0, 8 * w).rearrange("p (c h) -> p c h", c=8),
                    in_=src.ap().rearrange("(c p) h -> p c h", p=128)
                    [:, :, hb * 256:hb * 256 + w]),
                    reads=[R(src)], writes=[wv.reg()], dma=True)
            ld(wgv, wqd["wg"])
            ld(wuv, wqd["wu"])

            def one_tt(i):
                bg, bu = mk.ps[4], mk.ps[5]

                def mmg(t, wv=wgv, bank=bg):
                    last = None
                    for dc in range(8):
                        last = t.matmul(
                            bank.ap()[:, 0:w],
                            xqT.ap(dc * 512 + i * 128, dc * 512 + (i + 1) * 128),
                            wv.ap(dc * w, (dc + 1) * w),
                            start=(dc == 0), stop=(dc == 7))
                    return last
                p.emit("pe", mmg, reads=[xqT.reg(), wgv.reg()], writes=[R(bg)])
                p.emit("pe", lambda t: mmg(t, wuv, bu),
                       reads=[xqT.reg(), wuv.reg()], writes=[R(bu)])
                mk.ar16.save()
                sv = mk.v16(w)
                p.emit("act", lambda e: e.activation(
                    sv.ap(), bg.ap()[:, 0:w], AF.Silu),
                    reads=[R(bg)], writes=[sv.reg()])
                hs = (i * HN + hb * 256, i * HN + hb * 256 + w)
                p.emit("dve", lambda e: e.tensor_tensor(
                    hbuf.ap(*hs), sv.ap(), bu.ap()[:, 0:w], ALU.mult),
                    reads=[sv.reg(), R(bu)], writes=[hbuf.reg(*hs)])
                mk.ar16.restore()
            for i in range(4):
                one_tt(i)
        for hb in range(11):
            hblock(hb, 256)

        def hquant(i, tt):
            mk.ar32.save(); mk.ar16.save(); mk.small.save()
            hs = (i * HN, (i + 1) * HN)
            NB = HN // 16
            yv = mk.v32(HN)
            qv = mk.v32(HN)
            bs = mk.v32(NB)
            iv = mk.v32(NB)
            sg16 = mk.v16(HN)
            hq = mk.v16(HN)
            p.emit("dve", lambda e: e.tensor_reduce(
                bs.ap(), hbuf.ap(*hs).rearrange("p (b k) -> p b k", k=16),
                axis=AX.X, op=ALU.max, apply_absolute_value=True),
                reads=[hbuf.reg(*hs)], writes=[bs.reg()])
            p.emit("dve", lambda e: e.tensor_scalar(
                bs.ap(), bs.ap(), 1.0 / 6.0, 1e-12, ALU.mult, ALU.max),
                reads=[bs.reg()], writes=[bs.reg()])
            p.emit("dve", lambda e: e.reciprocal(
                iv.ap(), bs.ap()), reads=[bs.reg()], writes=[iv.reg()])
            ivb = iv.ap().rearrange("p (b o) -> p b o", o=1)\
                .broadcast_to((128, NB, 16))
            bsb = bs.ap().rearrange("p (b o) -> p b o", o=1)\
                .broadcast_to((128, NB, 16))
            hv3 = hbuf.ap(*hs).rearrange("p (b k) -> p b k", k=16)
            yv3 = yv.ap().rearrange("p (b k) -> p b k", k=16)
            p.emit("dve", lambda e: e.tensor_tensor(
                yv3, hv3, ivb, ALU.mult),
                reads=[hbuf.reg(*hs), iv.reg()], writes=[yv.reg()])
            p.emit("dve", lambda e: e.scalar_tensor_tensor(
                yv.ap(), yv.ap(), -1.0, yv.ap(), ALU.mult, ALU.max),
                reads=[yv.reg()], writes=[yv.reg()])
            _fp4_qabs_chain(p, yv, qv)
            p.emit("act", lambda e: e.sign(
                sg16.ap(), hbuf.ap(*hs)),
                reads=[hbuf.reg(*hs)], writes=[sg16.reg()])
            qv3 = qv.ap().rearrange("p (b k) -> p b k", k=16)
            p.emit("dve", lambda e: e.tensor_tensor(
                qv3, qv3, bsb, ALU.mult),
                reads=[qv.reg(), bs.reg()], writes=[qv.reg()])
            p.emit("pool", lambda e: e.tensor_tensor(
                hq.ap(), qv.ap(), sg16.ap(), ALU.mult),
                reads=[qv.reg(), sg16.reg()], writes=[hq.reg()])
            tv = hqT[i]

            def tpb(b8, n):
                dst = mk.ar16_t.ap()[:, tv.o + b8 * 1024:tv.o + b8 * 1024 + n * 128]
                mk.transpose8(
                    lambda k2: hq.ap((b8 * 8 + k2) * 128, (b8 * 8 + k2 + 1) * 128),
                    dst,
                    Region(tv.t, 0, 128, tv.o + b8 * 1024,
                           tv.o + b8 * 1024 + n * 128),
                    [hq.reg()], pb=4 + b8 % 2, n=n)
            tpb(0, 8); tpb(1, 8); tpb(2, 6)
            mk.ar32.restore(); mk.ar16.restore(); mk.small.restore()
        for i, tt in enumerate(tts):
            hquant(i, tt)

        def yblock(hb):
            wv = wdr[hb % 2]
            p.emit("pool", lambda e: e.dma_start(
                out=wv.ap().rearrange("p (c h) -> p c h", c=2),
                in_=wqd["wd"].ap().rearrange("(c p) o -> p c o", p=128)
                [:, hb * 2:(hb + 1) * 2, :]),
                reads=[R(wqd["wd"])], writes=[wv.reg()], dma=True)

            def one(i, ob):
                tv = hqT[i]
                bank = mk.ps[i * 2 + ob]

                def mmy(t):
                    last = None
                    for hc in range(2):
                        last = t.matmul(
                            bank.ap()[:, :],
                            tv.ap((hb * 2 + hc) * 128, (hb * 2 + hc + 1) * 128),
                            wv.ap(hc * 1024 + ob * 512, hc * 1024 + (ob + 1) * 512),
                            start=(hb == 0 and hc == 0),
                            stop=(hb == 10 and hc == 1))
                    return last
                p.emit("pe", mmy, reads=[tv.reg(), wv.reg()], writes=[R(bank)])
            for i in range(4):
                for ob in range(2):
                    one(i, ob)
        for hb in range(11):
            yblock(hb)

        def evac(i, tt, ob):
            bank = mk.ps[i * 2 + ob]
            xs = (tt * D + ob * 512, tt * D + (ob + 1) * 512)
            xa = mk.x.ap()[:, xs[0]:xs[1]]
            xreg = Region(mk.x, 0, 128, *xs)
            p.emit("dve", lambda e: e.scalar_tensor_tensor(
                xa, bank.ap()[:, :], 0.5, xa, ALU.mult, ALU.add),
                reads=[R(bank), xreg], writes=[xreg])
        for i, tt in enumerate(tts):
            for ob in range(2):
                evac(i, tt, ob)
    for g in range(NTT // 4):
        do_group(g)


def attn_apply(mk, cfg, elem):
    p = mk.p
    mk.ar16.reset(); mk.ar32.reset(); mk.small.reset()
    TTE = mk.TTE
    T = mk.T
    tt0 = elem * TTE
    x = mk.x
    bq_cols = mk.vsm(24)
    p.emit("sp", lambda e: e.dma_start(out=bq_cols.ap(), in_=cfg["bqkv_cols"]),
           writes=[bq_cols.reg()], dma=True)
    pb_bc = mk.v16(1024)
    onesr = p.const_row(1.0)
    mk.ar32.save()
    brow = mk.v32(1024)
    p.emit("sp", lambda e: e.dma_start(
        out=brow.ap(0, 1024, p1=1), in_=cfg["bproj_row"][0:1, :]),
        writes=[brow.reg()], dma=True)

    def mkbc(src_off, dst):
        def one(ob):
            bank = mk.ps[0]
            p.emit("pe", lambda e: e.matmul(
                bank.ap()[:, :], onesr,
                brow.ap(src_off + ob * 512, src_off + (ob + 1) * 512, p1=1),
                start=True, stop=True),
                reads=[brow.reg()], writes=[R(bank)])
            p.emit("dve", lambda e: e.tensor_copy(
                dst.ap(ob * 512, (ob + 1) * 512), bank.ap()[:, :]),
                reads=[R(bank)], writes=[dst.reg(ob * 512, (ob + 1) * 512)])
        one(0); one(1)
    mkbc(0, pb_bc)
    mk.ar32.restore()
    y32 = mk.v32(TTE * 1024)
    lora = cfg.get("lora")
    if lora:
        lt = mk.v16(160)
        lofs = {}
        off = 0
        for nm, rows, cols in (("a_q", 64, 16), ("b_q", 16, 64),
                               ("a_v", 64, 16), ("b_v", 16, 64)):
            lofs[nm] = (off, rows, cols)
            p.emit("pool", lambda e, nm=nm, off=off, rows=rows, cols=cols:
                   e.dma_start(out=lt.ap(off, off + cols, p1=rows),
                               in_=lora[nm][:, :]),
                   writes=[lt.reg()], dma=True)
            if rows == 64:
                p.emit("pool", lambda e, nm=nm, off=off, cols=cols:
                       e.dma_start(out=lt.ap(off, off + cols, p0=64, p1=128),
                                   in_=lora[nm][:, :]),
                       writes=[lt.reg()], dma=True)
            off += cols

        def lap(nm, pb=0):
            o, rows, cols = lofs[nm]
            if rows == 64 and pb:
                return lt.ap(o, o + cols, p0=64, p1=128), lt.reg()
            return lt.ap(o, o + cols, p1=rows), lt.reg()
    rope = cfg.get("rope")
    xp = cfg["xpage"]
    mk.page_x(xp, "out", order=[t for t in
                                list(range(26, mk.TT)) + list(range(26))
                                if t < mk.TT])
    negm = mk.vsm(TTE)
    rstd = mk.vsm(TTE)
    mk.ln_stats(list(range(tt0, tt0 + TTE)), rstd, negm)
    xnT = mk.v16(8 * T)

    xnring = [mk.v16_in32(1024) for _ in range(2)]

    def mkxnT(j):
        tt = tt0 + j
        xn = xnring[j % 2]
        mk.ln_apply(tt, negm, rstd, xn.ap(), xn.reg(), j=j)
        dst = mk.ar16_t.ap()[:, xnT.o:xnT.o + 8 * T].rearrange(
            "p (c t) -> p c t", c=8)[:, :, j * 128:(j + 1) * 128]
        mk.transpose8(lambda dc: xn.ap(dc * 128, (dc + 1) * 128),
                      dst, xnT.reg(), [xn.reg()], pb=j % 2)
    for j in range(TTE):
        mkxnT(j)
    need = 8 * T + 4096 + 4096 + 2048 + 4096 + 1024 + 1024
    if mk.TT * D >= need:
        bt = x
    else:
        bt = p.sbuf(f"attn_borrow{elem}_{p.cnt['pe']}", [128, need], BF16)
    KB = View(bt, 0, 8 * T)
    o_ = 8 * T
    QB = View(bt, o_, 4096)
    AFM = View(bt, o_ + 4096, 4096)
    PRB = [View(bt, o_ + 8192 + i * 512, 512) for i in range(4)]
    WRG = [View(bt, o_ + 10240 + i * 2048, 2048) for i in range(2)]
    VSQ = View(bt, o_ + 14336, 1024)
    if rope is not None:
        rot = View(bt, o_ + 15360, 1024)
        p.emit("pool", lambda e: e.dma_start(
            out=rot.ap().rearrange("p (c k) -> p c k", c=8),
            in_=rope[:, :, :].rearrange("c p k -> p c k")),
            writes=[rot.reg()], dma=True)
    VB = mk.v16(TTE * 1040)
    kn2 = mk.vsm(8)
    WSEC = {"q": 0, "k": 1024, "v": 2048}

    def ldw_qkv(wv, sec, oc, width=128):
        c0 = WSEC[sec] + oc * 128
        p.emit("pool", lambda e: e.dma_start(
            out=wv.ap(0, 8 * width).rearrange("p (c o) -> p c o", c=8),
            in_=cfg["wqkv"].rearrange("(c p) o -> p c o", p=128)
            [:, :, c0:c0 + width]),
            writes=[wv.reg()], dma=True)

    def kprod(oc):
        wv = WRG[oc % 2]
        ldw_qkv(wv, "k", oc)

        def one(ts4):
            bank = mk.ps[4 + ts4 % 2]

            def mmk(t):
                last = None
                for dc in range(8):
                    last = t.matmul(
                        bank.ap()[:, :],
                        wv.ap(dc * 128, (dc + 1) * 128),
                        xnT.ap(dc * T + ts4 * 512, dc * T + (ts4 + 1) * 512),
                        start=(dc == 0), stop=(dc == 7))
                return last
            p.emit("pe", mmk, reads=[wv.reg(), xnT.reg()], writes=[R(bank)])
            ks = (oc * T + ts4 * 512, oc * T + (ts4 + 1) * 512)
            if rope is None:
                p.emit("act", lambda e: e.activation(
                    KB.ap(*ks), bank.ap()[:, :], AF.Identity,
                    bias=bq_cols.ap(8 + oc, 9 + oc)),
                    reads=[R(bank), bq_cols.reg()], writes=[KB.reg(*ks)])
            else:
                ktmp = View(bt, VSQ.o + 512, 512)
                p.emit("act", lambda e: e.activation(
                    ktmp.ap(), bank.ap()[:, :], AF.Identity,
                    bias=bq_cols.ap(8 + oc, 9 + oc)),
                    reads=[R(bank), bq_cols.reg()], writes=[ktmp.reg()])
                bank2 = mk.ps[6]
                p.emit("pe", lambda e: e.matmul(
                    bank2.ap()[:, :], rot.ap(oc * 128, (oc + 1) * 128),
                    ktmp.ap(), start=True, stop=True),
                    reads=[rot.reg(), ktmp.reg()], writes=[R(bank2)])
                p.emit("dve", lambda e: e.tensor_copy(
                    KB.ap(*ks), bank2.ap()[:, :]),
                    reads=[R(bank2)], writes=[KB.reg(*ks)])
            p.emit("act", lambda e: e.activation(
                VSQ.ap(0, 512), KB.ap(*ks), AF.Square),
                reads=[KB.reg(*ks)], writes=[VSQ.reg(0, 512)])
            nb = mk.ps[7]
            p.emit("pe", lambda e: e.matmul(
                nb.ap()[0:1, :], mk.ones_bf.ap()[:, 0:1], VSQ.ap(0, 512),
                start=True, stop=True),
                reads=[VSQ.reg(0, 512), R(mk.ones_bf)], writes=[R(nb)])
            mk.small.save()
            mx = mk.vsm(1)
            p.emit("dve", lambda e: e.reduce_max(
                mx.ap(p1=1), nb.ap()[0:1, :], axis=AX.X),
                reads=[R(nb)], writes=[mx.reg(p1=1)])
            if ts4 == 0:
                p.emit("dve", lambda e: e.tensor_copy(
                    kn2.ap(oc, oc + 1, p1=1), mx.ap(p1=1)),
                    reads=[mx.reg(p1=1)], writes=[kn2.reg(oc, oc + 1)])
            else:
                p.emit("dve", lambda e: e.tensor_tensor(
                    kn2.ap(oc, oc + 1, p1=1), kn2.ap(oc, oc + 1, p1=1),
                    mx.ap(p1=1), ALU.max),
                    reads=[kn2.reg(oc, oc + 1), mx.reg(p1=1)],
                    writes=[kn2.reg(oc, oc + 1)])
            mk.small.restore()
        for ts4 in range(TTE // 4):
            one(ts4)
    for oc in range(8):
        kprod(oc)

    def vprod(oc):
        wv = WRG[oc % 2]
        ldw_qkv(wv, "v", oc)
        vfm = View(bt, o_, T)

        def one(ts4):
            bank = mk.ps[4 + ts4 % 2]

            def mmv(t):
                last = None
                for dc in range(8):
                    last = t.matmul(
                        bank.ap()[:, :],
                        wv.ap(dc * 128, (dc + 1) * 128),
                        xnT.ap(dc * T + ts4 * 512, dc * T + (ts4 + 1) * 512),
                        start=(dc == 0), stop=(dc == 7))
                return last
            p.emit("pe", mmv, reads=[wv.reg(), xnT.reg()], writes=[R(bank)])
            vs = (ts4 * 512, (ts4 + 1) * 512)
            p.emit("act", lambda e: e.activation(
                vfm.ap(*vs), bank.ap()[:, :], AF.Identity,
                bias=bq_cols.ap(16 + oc, 17 + oc)),
                reads=[R(bank), bq_cols.reg()], writes=[vfm.reg(*vs)])
        for ts4 in range(TTE // 4):
            one(ts4)
        if lora:
            def lhead(hh):
                pb = (hh) * 64

                def onet(ts4):
                    vs = (ts4 * 512, (ts4 + 1) * 512)
                    b1 = mk.ps[6]
                    ava, avr = lap("a_v", pb)
                    bva, bvr = lap("b_v")
                    p.emit("pe", lambda e: e.matmul(
                        b1.ap()[0:16, :], ava,
                        vfm.ap(*vs, p0=pb, p1=pb + 64),
                        start=True, stop=True),
                        reads=[avr, vfm.reg(*vs)], writes=[R(b1)])
                    tmp = View(bt, VSQ.o + 512, 512)
                    p.emit("dve", lambda e: e.tensor_copy(
                        tmp.ap(p1=16), b1.ap()[0:16, :]),
                        reads=[R(b1)], writes=[tmp.reg()])
                    b2 = mk.ps[7]
                    p.emit("pe", lambda e: e.matmul(
                        b2.ap()[0:64, :], bva, tmp.ap(p1=16),
                        start=True, stop=True),
                        reads=[bvr, tmp.reg()], writes=[R(b2)])
                    p.emit("dve", lambda e: e.scalar_tensor_tensor(
                        vfm.ap(*vs, p0=pb, p1=pb + 64), b2.ap()[0:64, :], 2.0,
                        vfm.ap(*vs, p0=pb, p1=pb + 64), ALU.mult, ALU.add),
                        reads=[R(b2), vfm.reg(*vs)], writes=[vfm.reg(*vs)])
                for ts4 in range(TTE // 4):
                    onet(ts4)
            lhead(0); lhead(1)
        def tpv(j):
            bank = mk.ps[4 + j % 2]
            bap = bank.ap().bitcast(BF16)
            p.emit("pe", lambda e: e.transpose(
                bap[:, 0:128], vfm.ap(j * 128, (j + 1) * 128),
                mk.ident.ap()[:, :]),
                reads=[vfm.reg(j * 128, (j + 1) * 128), R(mk.ident)],
                writes=[R(bank)])
            dst = mk.ar16_t.ap()[:, VB.o + j * 1040 + (2 * oc) * 65:
                                 VB.o + j * 1040 + (2 * oc) * 65 + 130]\
                .rearrange("p (h c) -> p h c", c=65)[:, :, 0:64]
            dreg = Region(VB.t, 0, 128, VB.o + j * 1040 + 2 * oc * 65,
                          VB.o + j * 1040 + 2 * oc * 65 + 130)
            p.emit("dve", lambda e: e.tensor_copy(
                dst, bap[:, 0:128].rearrange("p (h c) -> p h c", c=64)),
                reads=[R(bank)], writes=[dreg])
        for j in range(TTE):
            tpv(j)
    for oc in range(8):
        vprod(oc)
    def vones(j):
        dst = mk.ar16_t.ap()[:, VB.o + j * 1040:VB.o + (j + 1) * 1040]\
            .rearrange("p (h c) -> p h c", c=65)[:, :, 64:65]
        p.emit("dve", lambda e: e.memset(dst, 1.0),
               writes=[Region(VB.t, 0, 128, VB.o + j * 1040,
                              VB.o + (j + 1) * 1040)])
    for j in range(TTE):
        vones(j)

    nqt = TTE // 4

    import os as _os
    _dbg = _os.environ.get("ATTN_DBG")

    def qtile(qt):
        def qprod(oc):
            wv = WRG[oc % 2]
            ldw_qkv(wv, "q", oc)
            bank = mk.ps[4 + oc % 2]

            def mmq(t):
                last = None
                for dc in range(8):
                    last = t.matmul(
                        bank.ap()[:, :],
                        wv.ap(dc * 128, (dc + 1) * 128),
                        xnT.ap(dc * T + qt * 512, dc * T + (qt + 1) * 512),
                        start=(dc == 0), stop=(dc == 7))
                return last
            p.emit("pe", mmq, reads=[wv.reg(), xnT.reg()], writes=[R(bank)])
            qs = (oc * 512, (oc + 1) * 512)
            if rope is None:
                p.emit("act", lambda e: e.activation(
                    QB.ap(*qs), bank.ap()[:, :], AF.Identity,
                    bias=bq_cols.ap(oc, oc + 1)),
                    reads=[R(bank), bq_cols.reg()], writes=[QB.reg(*qs)])
            else:
                qtmp = View(bt, VSQ.o + 512, 512)
                p.emit("act", lambda e: e.activation(
                    qtmp.ap(), bank.ap()[:, :], AF.Identity,
                    bias=bq_cols.ap(oc, oc + 1)),
                    reads=[R(bank), bq_cols.reg()], writes=[qtmp.reg()])
                bank2 = mk.ps[6]
                p.emit("pe", lambda e: e.matmul(
                    bank2.ap()[:, :], rot.ap(oc * 128, (oc + 1) * 128),
                    qtmp.ap(), start=True, stop=True),
                    reads=[rot.reg(), qtmp.reg()], writes=[R(bank2)])
                p.emit("dve", lambda e: e.tensor_copy(
                    QB.ap(*qs), bank2.ap()[:, :]),
                    reads=[R(bank2)], writes=[QB.reg(*qs)])
        for oc in range(8):
            qprod(oc)
        if lora:
            def lheadq(h):
                oc, pb = h // 2, (h % 2) * 64
                qs = (oc * 512, (oc + 1) * 512)
                b1 = mk.ps[6]
                aqa, aqr = lap("a_q", pb)
                bqa, bqr = lap("b_q")
                p.emit("pe", lambda e: e.matmul(
                    b1.ap()[0:16, :], aqa, QB.ap(*qs, p0=pb, p1=pb + 64),
                    start=True, stop=True),
                    reads=[aqr, QB.reg(*qs)], writes=[R(b1)])
                tmp = View(bt, VSQ.o + 512, 512)
                p.emit("dve", lambda e: e.tensor_copy(
                    tmp.ap(p1=16), b1.ap()[0:16, :]),
                    reads=[R(b1)], writes=[tmp.reg()])
                b2 = mk.ps[7]
                p.emit("pe", lambda e: e.matmul(
                    b2.ap()[0:64, :], bqa, tmp.ap(p1=16),
                    start=True, stop=True),
                    reads=[bqr, tmp.reg()], writes=[R(b2)])
                p.emit("dve", lambda e: e.scalar_tensor_tensor(
                    QB.ap(*qs, p0=pb, p1=pb + 64), b2.ap()[0:64, :], 2.0,
                    QB.ap(*qs, p0=pb, p1=pb + 64), ALU.mult, ALU.add),
                    reads=[R(b2), QB.reg(*qs)], writes=[QB.reg(*qs)])
            for h in range(16):
                lheadq(h)
        mk.small.save()
        nbias = mk.vsm(8)

        def mkbound(oc):
            qs = (oc * 512, (oc + 1) * 512)
            p.emit("act", lambda e: e.activation(
                VSQ.ap(0, 512), QB.ap(*qs), AF.Square),
                reads=[QB.reg(*qs)], writes=[VSQ.reg(0, 512)])
            nb = mk.ps[7]
            p.emit("pe", lambda e: e.matmul(
                nb.ap()[0:1, :], mk.ones_bf.ap()[:, 0:1], VSQ.ap(0, 512),
                start=True, stop=True),
                reads=[VSQ.reg(0, 512), R(mk.ones_bf)], writes=[R(nb)])
            mk.small.save()
            qn2 = mk.vsm(1)
            p.emit("dve", lambda e: e.reduce_max(
                qn2.ap(p1=1), nb.ap()[0:1, :], axis=AX.X),
                reads=[R(nb)], writes=[qn2.reg(p1=1)])
            p.emit("dve", lambda e: e.tensor_tensor(
                qn2.ap(p1=1), qn2.ap(p1=1), kn2.ap(oc, oc + 1, p1=1),
                ALU.mult),
                reads=[qn2.reg(p1=1), kn2.reg(oc, oc + 1)],
                writes=[qn2.reg(p1=1)])
            p.emit("act", lambda e: e.activation(
                qn2.ap(p1=1), qn2.ap(p1=1), AF.Ln),
                reads=[qn2.reg(p1=1)], writes=[qn2.reg(p1=1)])
            p.emit("act", lambda e: e.activation(
                qn2.ap(p1=1), qn2.ap(p1=1), AF.Exp,
                scale=p.const_col(0.5)[0:1, :]),
                reads=[qn2.reg(p1=1)], writes=[qn2.reg(p1=1)])
            p.emit("dve", lambda e: e.tensor_scalar(
                qn2.ap(p1=1), qn2.ap(p1=1), -0.125, -1.0, ALU.mult, ALU.add),
                reads=[qn2.reg(p1=1)], writes=[qn2.reg(p1=1)])
            bb = mk.ps[7]
            p.emit("pe", lambda e: e.matmul(
                bb.ap()[:, 0:1], onesr, qn2.ap(p1=1), start=True, stop=True),
                reads=[qn2.reg(p1=1)], writes=[R(bb)])
            p.emit("dve", lambda e: e.tensor_copy(
                nbias.ap(oc, oc + 1), bb.ap()[:, 0:1]),
                reads=[R(bb)], writes=[nbias.reg(oc, oc + 1)])
            mk.small.restore()
        for oc in range(8):
            mkbound(oc)
        if _dbg and qt == 0:
            dk = p.dram(f"dbg_k{elem}", [128, 8 * T], F32, kind="ExternalOutput")
            dq = p.dram(f"dbg_q{elem}", [128, 4096], F32, kind="ExternalOutput")
            dn = p.dram(f"dbg_nb{elem}", [128, 8], F32, kind="ExternalOutput")
            p.emit("pool", lambda e: e.dma_start(out=dk.ap()[:, :], in_=KB.ap()),
                   reads=[KB.reg()], dma=True)
            p.emit("pool", lambda e: e.dma_start(out=dq.ap()[:, :], in_=QB.ap()),
                   reads=[QB.reg()], dma=True)
            p.emit("pool", lambda e: e.dma_start(out=dn.ap()[:, :], in_=nbias.ap(0, 8)),
                   reads=[nbias.reg(0, 8)], dma=True)
        c8 = p.const_col(0.125)

        def head(h):
            oc, pb = h // 2, (h % 2) * 64
            av = mk.ps[h % 2]

            def kc_step(kc):
                sb = mk.ps[2 + kc % 2]
                p.emit("pe", lambda e: e.matmul(
                    sb.ap()[:, :],
                    KB.ap(oc * T + kc * 128, oc * T + (kc + 1) * 128,
                          p0=pb, p1=pb + 64),
                    QB.ap(oc * 512, (oc + 1) * 512, p0=pb, p1=pb + 64),
                    start=True, stop=True),
                    reads=[KB.reg(), QB.reg()], writes=[R(sb)])
                pr = PRB[kc % 4]
                p.emit("act", lambda e: e.activation(
                    pr.ap(), sb.ap()[:, :], AF.Exp, scale=c8,
                    bias=nbias.ap(oc, oc + 1)),
                    reads=[R(sb), nbias.reg(oc, oc + 1)], writes=[pr.reg()])
                p.emit("pe", lambda e: e.matmul(
                    av.ap()[0:65, :],
                    mk.ar16_t.ap()[:, VB.o + kc * 1040 + h * 65:
                                   VB.o + kc * 1040 + (h + 1) * 65],
                    pr.ap(), start=(kc == 0), stop=(kc == TTE - 1)),
                    reads=[Region(VB.t, 0, 128, VB.o + kc * 1040,
                                  VB.o + (kc + 1) * 1040), pr.reg()],
                    writes=[R(av)])
            for kc in range(TTE):
                kc_step(kc)
            zrow = View(mk.ar32_t, xnring[0].o // 2, 512)
            ibc = View(mk.ar32_t, xnring[0].o // 2 + 512, 512)
            p.emit("dve", lambda e: e.tensor_copy(
                zrow.ap(p1=1), av.ap()[64:65, :]),
                reads=[R(av)], writes=[zrow.reg()])
            p.emit("act", lambda e: e.activation(
                zrow.ap(p1=1), zrow.ap(p1=1), AF.Ln),
                reads=[zrow.reg()], writes=[zrow.reg()])
            p.emit("act", lambda e: e.activation(
                zrow.ap(p1=1), zrow.ap(p1=1), AF.Exp,
                scale=p.const_col(-1.0)[0:1, :]),
                reads=[zrow.reg()], writes=[zrow.reg()])
            bb = mk.ps[6 + h % 2]
            p.emit("pe", lambda e: e.matmul(
                bb.ap()[0:64, :], onesr[0:1, 0:64], zrow.ap(p1=1),
                start=True, stop=True),
                reads=[zrow.reg()], writes=[R(bb)])
            p.emit("dve", lambda e: e.tensor_copy(
                ibc.ap(p1=64), bb.ap()[0:64, :]),
                reads=[R(bb)], writes=[ibc.reg()])
            afs = (oc * 512, (oc + 1) * 512)
            p.emit("dve", lambda e: e.tensor_tensor(
                AFM.ap(*afs, p0=pb, p1=pb + 64), av.ap()[0:64, :],
                ibc.ap(p1=64), ALU.mult),
                reads=[R(av), ibc.reg()], writes=[AFM.reg(*afs)])
        for h in range(16):
            head(h)
        def proj(ob):
            wv = QB
            p.emit("pool", lambda e: e.dma_start(
                out=wv.ap(0, 8 * 512).rearrange("p (c o) -> p c o", c=8),
                in_=cfg["wproj"].rearrange("(c p) o -> p c o", p=128)
                [:, :, ob * 512:(ob + 1) * 512]),
                writes=[wv.reg()], dma=True)

            def one(j):
                bank = mk.ps[4 + j % 2]

                def mmp(t):
                    last = None
                    for dc in range(8):
                        last = t.matmul(
                            bank.ap()[:, :],
                            AFM.ap(dc * 512 + j * 128, dc * 512 + (j + 1) * 128),
                            wv.ap(dc * 512, (dc + 1) * 512),
                            start=(dc == 0), stop=(dc == 7))
                    return last
                p.emit("pe", mmp, reads=[AFM.reg(), wv.reg()], writes=[R(bank)])
                ys = ((qt * 4 + j) * D + ob * 512,
                      (qt * 4 + j) * D + (ob + 1) * 512)
                p.emit("dve", lambda e: e.tensor_copy(
                    y32.ap(*ys), bank.ap()[:, :]),
                    reads=[R(bank)], writes=[y32.reg(*ys)])
            for j in range(4):
                one(j)
        if _dbg and qt == 0:
            da = p.dram(f"dbg_afm{elem}", [128, 4096], F32, kind="ExternalOutput")
            p.emit("pool", lambda e: e.dma_start(out=da.ap()[:, :], in_=AFM.ap()),
                   reads=[AFM.reg()], dma=True)
        proj(0); proj(1)
        mk.small.restore()
    for qt in range(nqt):
        qtile(qt)
    mk.page_x(xp, "in")

    def finadd(j):
        tt = tt0 + j
        xs = (tt * D, (tt + 1) * D)
        xa = x.ap()[:, xs[0]:xs[1]]
        xreg = Region(x, 0, 128, *xs)
        ys = (j * D, (j + 1) * D)
        p.emit("dve", lambda e: e.tensor_tensor(
            y32.ap(*ys), y32.ap(*ys), pb_bc.ap(), ALU.add),
            reads=[y32.reg(*ys), pb_bc.reg()], writes=[y32.reg(*ys)])
        p.emit("dve", lambda e: e.tensor_tensor(
            xa, xa, y32.ap(*ys), ALU.add),
            reads=[xreg, y32.reg(*ys)], writes=[xreg])
    for j in range(TTE):
        finadd(j)


def moe_apply(mk, cfg, xpage):
    p = mk.p
    kind = cfg["kind"]
    HN = 4096 if kind == "l1" else 2816
    NHC = HN // 128
    halves = [(0, 8), (8, 8), (16, 8), (24, 8)] if kind == "l1" else \
             [(0, 11), (11, 11)]
    for elem in range(mk.NE):
        _moe_tp(mk, cfg, xpage, elem, HN, NHC, halves)


def _moe_tp(mk, cfg, xpage, elem, HN, NHC, halves):
    p = mk.p
    kind = cfg["kind"]
    mk.ar16.reset(); mk.ar32.reset(); mk.small.reset()
    TTE = mk.TTE
    T = mk.T
    tt0 = elem * TTE
    x = mk.x
    onesr = p.const_row(1.0)
    xp = xpage
    _hm = max(n for _, n in halves)
    _st = min(_hm * T // 1024, mk.TT)
    mk.page_x(xp, "out", order=list(range(_st, mk.TT)) + list(range(_st)))
    negm = mk.vsm(TTE)
    rstd = mk.vsm(TTE)
    mk.ln_stats(list(range(tt0, tt0 + TTE)), rstd, negm)
    xnT = mk.v16(8 * T)
    sel8 = mk.v16(1024)
    p.emit("pool", lambda e: e.dma_start(
        out=sel8.ap(p1=8), in_=cfg["sel8"][:, :]),
        writes=[sel8.reg()], dma=True)
    if kind == "l2":
        lnw_bc = mk.v16(1024)
        mk.ar32.save()
        lrow = mk.v32(1024)
        p.emit("sp", lambda e: e.dma_start(
            out=lrow.ap(p1=1), in_=cfg["lnw_row"][0:1, :]),
            writes=[lrow.reg()], dma=True)

        def bcob(ob):
            bank = mk.ps[0]
            p.emit("pe", lambda e: e.matmul(
                bank.ap()[:, :], onesr, lrow.ap(ob * 512, (ob + 1) * 512, p1=1),
                start=True, stop=True),
                reads=[lrow.reg()], writes=[R(bank)])
            p.emit("dve", lambda e: e.tensor_copy(
                lnw_bc.ap(ob * 512, (ob + 1) * 512), bank.ap()[:, :]),
                reads=[R(bank)], writes=[lnw_bc.reg(ob * 512, (ob + 1) * 512)])
        bcob(0); bcob(1)
        mk.ar32.restore()
        rstd2 = mk.vsm(TTE)

    xnring = [mk.v16_in32(1024) for _ in range(2)]

    def mkxnT(j):
        tt = tt0 + j
        mk.ar16.save(); mk.ar32.save()
        xn = xnring[j % 2]
        mk.ln_apply(tt, negm, rstd, xn.ap(), xn.reg(), j=j)
        if kind == "l2":
            xsq = mk.v32(2)
            jk = mk.v16(1024)
            p.emit("act", lambda e: e.activation(
                jk.ap(), xn.ap(), AF.Square, accum_out=xsq.ap(0, 1)),
                reads=[xn.reg()], writes=[jk.reg(), xsq.reg(0, 1)])
            p.emit("act", lambda e: e.activation(
                xsq.ap(1, 2), xsq.ap(0, 1), AF.Ln,
                bias=p.const_col(1e-5), scale=p.const_col(1.0 / D)),
                reads=[xsq.reg(0, 1)], writes=[xsq.reg(1, 2)])
            p.emit("act", lambda e, j=j: e.activation(
                rstd2.ap(j, j + 1), xsq.ap(1, 2), AF.Exp,
                scale=p.const_col(-0.5)),
                reads=[xsq.reg(1, 2)], writes=[rstd2.reg(j, j + 1)])
            p.emit("dve", lambda e, j=j: e.tensor_scalar(
                xn.ap(), xn.ap(), rstd2.ap(j, j + 1), None, ALU.mult),
                reads=[xn.reg(), rstd2.reg(j, j + 1)], writes=[xn.reg()])
            p.emit("dve", lambda e: e.tensor_tensor(
                xn.ap(), xn.ap(), lnw_bc.ap(), ALU.mult),
                reads=[xn.reg(), lnw_bc.reg()], writes=[xn.reg()])
        dst = mk.ar16_t.ap()[:, xnT.o:xnT.o + 8 * T].rearrange(
            "p (c t) -> p c t", c=8)[:, :, j * 128:(j + 1) * 128]
        mk.transpose8(lambda dc: xn.ap(dc * 128, (dc + 1) * 128),
                      dst, xnT.reg(), [xn.reg()], pb=j % 2)
        mk.ar16.restore(); mk.ar32.restore()
    for j in range(TTE):
        mkxnT(j)
    gT = mk.v16(T)

    def gates(j):
        mk.small.save(); mk.ar32.save()
        gbuf = mk.v32(8)
        bank = mk.ps[0]

        def mmg(t):
            last = None
            for dc in range(8):
                last = t.matmul(
                    bank.ap()[0:128, 0:8],
                    xnT.ap(dc * T + j * 128, dc * T + (j + 1) * 128),
                    wgst.ap(dc * 8, (dc + 1) * 8),
                    start=(dc == 0), stop=(dc == 7))
            return last
        p.emit("pe", mmg, reads=[xnT.reg(), wgst.reg()], writes=[R(bank)])
        lg = gbuf
        p.emit("dve", lambda e: e.tensor_copy(lg.ap(), bank.ap()[:, 0:8]),
               reads=[R(bank)], writes=[lg.reg()])
        m1 = mk.vsm(1); m2 = mk.vsm(1); g1 = mk.vsm(1); g2 = mk.vsm(1)
        msk = mk.v32(8)
        p.emit("dve", lambda e: e.reduce_max(
            m1.ap(), lg.ap(), axis=AX.X), reads=[lg.reg()], writes=[m1.reg()])
        p.emit("dve", lambda e: e.tensor_scalar(
            msk.ap(), lg.ap(), m1.ap(), None, ALU.is_ge),
            reads=[lg.reg(), m1.reg()], writes=[msk.reg()])
        lm = mk.v32(8)
        p.emit("dve", lambda e: e.scalar_tensor_tensor(
            lm.ap(), msk.ap(), -1e30, lg.ap(), ALU.mult, ALU.add),
            reads=[msk.reg(), lg.reg()], writes=[lm.reg()])
        p.emit("dve", lambda e: e.reduce_max(
            m2.ap(), lm.ap(), axis=AX.X), reads=[lm.reg()], writes=[m2.reg()])
        if kind == "l1":
            p.emit("dve", lambda e: e.tensor_tensor(
                g1.ap(), m2.ap(), m1.ap(), ALU.subtract),
                reads=[m1.reg(), m2.reg()], writes=[g1.reg()])
            p.emit("act", lambda e: e.activation(
                g1.ap(), g1.ap(), AF.Exp), reads=[g1.reg()], writes=[g1.reg()])
            p.emit("dve", lambda e: e.tensor_single_scalar(
                g1.ap(), g1.ap(), 1.0, ALU.add),
                reads=[g1.reg()], writes=[g1.reg()])
            p.emit("dve", lambda e: e.reciprocal(
                g1.ap(), g1.ap()), reads=[g1.reg()], writes=[g1.reg()])
            p.emit("dve", lambda e: e.scalar_tensor_tensor(
                g2.ap(), g1.ap(), -1.0, p.const_col(1.0), ALU.mult, ALU.add),
                reads=[g1.reg()], writes=[g2.reg()])
        else:
            Z = mk.vsm(1)
            ex = mk.v32(8)
            nm1 = mk.vsm(1)
            p.emit("dve", lambda e: e.tensor_single_scalar(
                nm1.ap(), m1.ap(), -1.0, ALU.mult),
                reads=[m1.reg()], writes=[nm1.reg()])
            p.emit("act", lambda e: e.activation(
                ex.ap(), lg.ap(), AF.Exp, bias=nm1.ap(), accum_out=Z.ap()),
                reads=[lg.reg(), nm1.reg()], writes=[ex.reg(), Z.reg()])
            p.emit("dve", lambda e: e.reciprocal(
                g1.ap(), Z.ap()), reads=[Z.reg()], writes=[g1.reg()])
            p.emit("dve", lambda e: e.tensor_tensor(
                g2.ap(), m2.ap(), m1.ap(), ALU.subtract),
                reads=[m1.reg(), m2.reg()], writes=[g2.reg()])
            p.emit("act", lambda e: e.activation(
                g2.ap(), g2.ap(), AF.Exp), reads=[g2.reg()], writes=[g2.reg()])
            p.emit("dve", lambda e: e.tensor_tensor(
                g2.ap(), g2.ap(), g1.ap(), ALU.mult),
                reads=[g2.reg(), g1.reg()], writes=[g2.reg()])
        msk2 = mk.v32(8)
        p.emit("dve", lambda e: e.tensor_scalar(
            msk2.ap(), lm.ap(), m2.ap(), None, ALU.is_ge),
            reads=[lm.reg(), m2.reg()], writes=[msk2.reg()])
        p.emit("dve", lambda e: e.tensor_scalar(
            msk.ap(), msk.ap(), g1.ap(), None, ALU.mult),
            reads=[msk.reg(), g1.reg()], writes=[msk.reg()])
        p.emit("dve", lambda e: e.scalar_tensor_tensor(
            gbuf.ap(), msk2.ap(), g2.ap(), msk.ap(), ALU.mult, ALU.add),
            reads=[msk2.reg(), g2.reg(), msk.reg()], writes=[gbuf.reg()])
        bank2 = mk.ps[1]
        p.emit("pe", lambda e: e.matmul(
            bank2.ap()[0:8, 0:128], gbuf.ap(), mk.id32.ap()[:, :],
            start=True, stop=True),
            reads=[gbuf.reg(), R(mk.id32)], writes=[R(bank2)])
        p.emit("dve", lambda e, j=j: e.tensor_copy(
            gT.ap(j * 128, (j + 1) * 128, p1=8), bank2.ap()[0:8, 0:128]),
            reads=[R(bank2)], writes=[gT.reg()])
        mk.small.restore(); mk.ar32.restore()
    wgst = mk.v16(64)
    p.emit("pool", lambda e: e.dma_start(
        out=wgst.ap().rearrange("p (c o) -> p c o", c=8),
        in_=cfg["w_gT" if kind == "l1" else "router"]
        .rearrange("(c p) o -> p c o", p=128)),
        writes=[wgst.reg()], dma=True)
    for j in range(TTE):
        gates(j)
    HHM = max(n for _, n in halves)
    WB = 8 if kind == "l1" else 2
    S = 8 * WB * 128
    nw = 1 if kind == "l1" else 2
    bneed = HHM * T + 2 * nw * S
    bt = x if mk.TT * D >= bneed else \
        p.sbuf(f"moe_borrow_{kind}_{elem}", [128, bneed], BF16)
    hbuf = View(bt, 0, HHM * T)
    gbc = mk.v16(2048)
    wslots = [View(bt, HHM * T + i * nw * S, S) for i in range(2)]
    w3slots = [View(bt, HHM * T + i * nw * S + S, S) for i in range(2)] \
        if kind == "l2" else None
    sched = []
    for ei_ in range(8):
        for h0_, nhc_ in halves:
            for b0_ in range(0, nhc_, WB):
                sched.append((ei_, h0_ + b0_, min(WB, nhc_ - b0_)))

    def ldblk(i):
        if i >= len(sched):
            return
        ei2, c0, nb2 = sched[i]
        pairs = [("w1" if kind == "l2" else "w_in", wslots[i % 2], "pool")]
        if kind == "l2":
            pairs.append(("w3", w3slots[i % 2], "pool"))
        for nm2, wv, qs in pairs:
            p.emit(qs, lambda e, nm2=nm2, wv=wv, ei2=ei2, c0=c0, nb2=nb2:
                   e.dma_start(
                out=wv.ap(0, S).rearrange(
                    "p (c o) -> p c o", c=8)[:, :, 0:nb2 * 128],
                in_=cfg[nm2][ei2].rearrange("(c p) o -> p c o", p=128)
                [:, :, c0 * 128:(c0 + nb2) * 128]),
                writes=[Region(wv.t, 0, 128, wv.o, wv.o + S)], dma=True)
    blkc = [0]
    ldblk(0)
    w2h = mk.v16(HHM * 1024)
    y32 = mk.v32(TTE * 1024)
    p.emit("dve", lambda e: e.memset(y32.ap(), 0.0), writes=[y32.reg()])
    ysc = [View(mk.ar32_t, xnring[0].o // 2 + i * 512, 512) for i in range(2)]

    def expert(ei):
        def gb(ts4):
            bank = mk.ps[0]
            p.emit("pe", lambda e: e.matmul(
                bank.ap()[:, :], sel8.ap(ei * 128, (ei + 1) * 128, p1=8),
                gT.ap(ts4 * 512, (ts4 + 1) * 512, p1=8),
                start=True, stop=True),
                reads=[sel8.reg(), gT.reg()], writes=[R(bank)])
            p.emit("dve", lambda e: e.tensor_copy(
                gbc.ap(ts4 * 512, (ts4 + 1) * 512), bank.ap()[:, :]),
                reads=[R(bank)],
                writes=[Region(gbc.t, 0, 128, gbc.o + ts4 * 512,
                               gbc.o + (ts4 + 1) * 512)])
        for ts4 in range(T // 512):
            gb(ts4)

        def half(h0, nhc):
            p.emit("pool", lambda e: e.dma_start(
                out=w2h.ap(0, nhc * 1024).rearrange("p (c o) -> p c o", c=nhc),
                in_=cfg["w2" if kind == "l2" else "w_out"][ei]
                .rearrange("(c p) o -> p c o", p=128)[:, h0:h0 + nhc, :]),
                reads=[R(cfg["w2" if kind == "l2" else "w_out"].tensor)]
                if False else [],
                writes=[Region(w2h.t, 0, 128, w2h.o, w2h.o + nhc * 1024)],
                dma=True)

            def hchunk(hci):
                hc = h0 + hci
                boff = (hci % WB) * 128
                bw = WB * 128
                if hci % WB == 0:
                    i = blkc[0]
                    assert sched[i] == (ei, h0 + hci, min(WB, nhc - hci))
                    ldblk(i + 1)
                    blkc[0] += 1
                i = blkc[0] - 1
                wv = wslots[i % 2]
                wv3 = w3slots[i % 2] if kind == "l2" else None

                def ts_tile(ts4):
                    bank = mk.ps[2 + ts4 % 2]

                    def mmh(t, wvv, bk):
                        last = None
                        for dc in range(8):
                            last = t.matmul(
                                bk.ap()[:, :],
                                wvv.ap(dc * bw + boff, dc * bw + boff + 128),
                                xnT.ap(dc * T + ts4 * 512,
                                       dc * T + (ts4 + 1) * 512),
                                start=(dc == 0), stop=(dc == 7))
                        return last
                    p.emit("pe", lambda t: mmh(t, wv, bank),
                           reads=[Region(wv.t, 0, 128, wv.o, wv.o + 8 * bw),
                                  xnT.reg()],
                           writes=[R(bank)])
                    hs = (hci * T + ts4 * 512, hci * T + (ts4 + 1) * 512)
                    hreg = Region(hbuf.t, 0, 128, hbuf.o + hs[0],
                                  hbuf.o + hs[1])
                    gbr = Region(gbc.t, 0, 128, gbc.o + ts4 * 512,
                                 gbc.o + (ts4 + 1) * 512)
                    if kind == "l1":
                        p.emit("dve", lambda e: e.tensor_tensor(
                            hbuf.ap(*hs), bank.ap()[:, :],
                            gbc.ap(ts4 * 512, (ts4 + 1) * 512), ALU.mult),
                            reads=[R(bank), gbr], writes=[hreg])
                    else:
                        bank3 = mk.ps[4 + ts4 % 2]
                        p.emit("pe", lambda t: mmh(t, wv3, bank3),
                               reads=[Region(wv3.t, 0, 128, wv3.o,
                                             wv3.o + 8 * bw), xnT.reg()],
                               writes=[R(bank3)])
                        sv = ysc[ts4 % 2]
                        p.emit("act", lambda e: e.activation(
                            sv.ap(), bank.ap()[:, :], AF.Silu),
                            reads=[R(bank)], writes=[sv.reg()])
                        p.emit("dve", lambda e: e.tensor_tensor(
                            sv.ap(), sv.ap(), bank3.ap()[:, :], ALU.mult),
                            reads=[sv.reg(), R(bank3)], writes=[sv.reg()])
                        p.emit("dve", lambda e: e.tensor_tensor(
                            hbuf.ap(*hs), sv.ap(),
                            gbc.ap(ts4 * 512, (ts4 + 1) * 512), ALU.mult),
                            reads=[sv.reg(), gbr], writes=[hreg])
                for ts4 in range(T // 512):
                    ts_tile(ts4)
            for hci in range(nhc):
                hchunk(hci)
            def ymm(ts4):
                def one(j2, ob):
                    bank = mk.ps[j2 * 2 + ob]

                    def mm(t):
                        last = None
                        for hci in range(nhc):
                            last = t.matmul(
                                bank.ap()[:, :],
                                hbuf.ap(hci * T + ts4 * 512 + j2 * 128,
                                        hci * T + ts4 * 512 + (j2 + 1) * 128),
                                w2h.ap(hci * 1024 + ob * 512,
                                       hci * 1024 + (ob + 1) * 512),
                                start=(hci == 0), stop=(hci == nhc - 1))
                        return last
                    p.emit("pe", mm,
                           reads=[Region(hbuf.t, 0, 128, hbuf.o,
                                         hbuf.o + nhc * T),
                                  Region(w2h.t, 0, 128, w2h.o,
                                         w2h.o + nhc * 1024)],
                           writes=[R(bank)])
                    ys = ((ts4 * 4 + j2) * 1024 + ob * 512,
                          (ts4 * 4 + j2) * 1024 + (ob + 1) * 512)
                    p.emit("dve", lambda e: e.tensor_tensor(
                        y32.ap(*ys), y32.ap(*ys), bank.ap()[:, :], ALU.add),
                        reads=[y32.reg(*ys), R(bank)], writes=[y32.reg(*ys)])
                for j2 in range(4):
                    for ob in range(2):
                        one(j2, ob)
            for ts4 in range(T // 512):
                ymm(ts4)
        for h0, nhc in halves:
            half(h0, nhc)
    for ei in range(8):
        expert(ei)
    mk.page_x(xp, "in")

    def finadd(j):
        tt = tt0 + j
        xs = (tt * D, (tt + 1) * D)
        xa = x.ap()[:, xs[0]:xs[1]]
        xreg = Region(x, 0, 128, *xs)
        ys = (j * D, (j + 1) * D)
        p.emit("dve", lambda e: e.scalar_tensor_tensor(
            xa, y32.ap(*ys), 0.5, xa, ALU.mult, ALU.add),
            reads=[y32.reg(*ys), xreg], writes=[xreg])
    for j in range(TTE):
        finadd(j)


def l1_attn_apply(mk, cfg, elem):
    p = mk.p
    mk.ar16.reset(); mk.ar32.reset(); mk.small.reset()
    xp = cfg["xpage"]
    mk.page_x(xp, "out")
    TTE = mk.TTE
    T = mk.T
    NB = T // 64
    tt0 = elem * TTE
    x = mk.x
    onesr = p.const_row(1.0)
    sc4 = 0.3535533905932738
    negm = mk.vsm(TTE)
    rstd = mk.vsm(TTE)
    mk.ln_stats(list(range(tt0, tt0 + TTE)), rstd, negm)
    xnT = mk.v16(8 * T)

    xnring = [mk.v16_in32(1024) for _ in range(2)]

    def mkxnT(j):
        tt = tt0 + j
        xn = xnring[j % 2]
        mk.ln_apply(tt, negm, rstd, xn.ap(), xn.reg(), j=j)
        dst = mk.ar16_t.ap()[:, xnT.o:xnT.o + 8 * T].rearrange(
            "p (c t) -> p c t", c=8)[:, :, j * 128:(j + 1) * 128]
        mk.transpose8(lambda dc: xn.ap(dc * 128, (dc + 1) * 128),
                      dst, xnT.reg(), [xn.reg()], pb=j % 2)
    for j in range(TTE):
        mkxnT(j)
    bo_bc = mk.v16(1024)
    mk.ar32.save()
    brow = mk.v32(1024)
    p.emit("sp", lambda e: e.dma_start(
        out=brow.ap(p1=1), in_=cfg["bo_row"][0:1, :]),
        writes=[brow.reg()], dma=True)

    def bcob(ob):
        bank = mk.ps[0]
        p.emit("pe", lambda e: e.matmul(
            bank.ap()[:, :], onesr, brow.ap(ob * 512, (ob + 1) * 512, p1=1),
            start=True, stop=True), reads=[brow.reg()], writes=[R(bank)])
        p.emit("dve", lambda e: e.tensor_copy(
            bo_bc.ap(ob * 512, (ob + 1) * 512), bank.ap()[:, :]),
            reads=[R(bank)], writes=[bo_bc.reg(ob * 512, (ob + 1) * 512)])
    bcob(0); bcob(1)
    mk.ar32.restore()
    VT = mk.v16(TTE * 1040)
    ctxn = mk.v16(2 * 65)
    need = 2 * 8 * T
    bt = x if mk.TT * D >= need else p.sbuf(f"l1a_borrow{elem}",
                                            [128, need], BF16)
    QF = View(bt, 0, 8 * T)
    KT = View(bt, 8 * T, 8 * T)
    WSRC = {"q": "wqT", "k": "wkT", "v": "wvT"}

    def ldw(wv, nm, ob):
        p.emit("pool", lambda e: e.dma_start(
            out=wv.ap(0, 8 * 512).rearrange("p (c o) -> p c o", c=8),
            in_=cfg[nm].rearrange("(c p) o -> p c o", p=128)
            [:, :, ob * 512:(ob + 1) * 512]),
            writes=[wv.reg(0, 4096)], dma=True)
    wring = [mk.v16_in32(1024) for _ in range(2)]

    def qprod(oc):
        wv = wring[oc % 2]
        p.emit("pool", lambda e: e.dma_start(
            out=wv.ap(0, 1024).rearrange("p (c o) -> p c o", c=8),
            in_=cfg["wqT"].rearrange("(c p) o -> p c o", p=128)
            [:, :, oc * 128:(oc + 1) * 128]),
            writes=[wv.reg(0, 1024)], dma=True)

        def one(ts4):
            bank = mk.ps[4 + ts4 % 2]

            def mm(t):
                last = None
                for dc in range(8):
                    last = t.matmul(
                        bank.ap()[:, :], wv.ap(dc * 128, (dc + 1) * 128),
                        xnT.ap(dc * T + ts4 * 512, dc * T + (ts4 + 1) * 512),
                        start=(dc == 0), stop=(dc == 7))
                return last
            p.emit("pe", mm,
                   reads=[wv.reg(0, 1024), xnT.reg()],
                   writes=[R(bank)])
            qs = (oc * T + ts4 * 512, oc * T + (ts4 + 1) * 512)
            p.emit("act", lambda e: e.activation(
                QF.ap(*qs), bank.ap()[:, :], AF.Exp,
                scale=p.const_col(sc4)),
                reads=[R(bank)], writes=[QF.reg(*qs)])
        for ts4 in range(T // 512):
            one(ts4)
    for oc in range(8):
        qprod(oc)
    kvw = [mk.v16_in32(4096) for _ in range(2)]

    def kv_all(sec, wnm):
        ldw(kvw[0], wnm, 0)
        ldw(kvw[1], wnm, 1)
        for j in range(TTE):
            for ob in range(2):
                wv = kvw[ob]
                bank = mk.ps[4 + (j * 2 + ob) % 2]

                def mm(t, j=j, ob=ob, wv=wv, bank=bank):
                    last = None
                    for dc in range(8):
                        last = t.matmul(
                            bank.ap()[:, :],
                            xnT.ap(dc * T + j * 128, dc * T + (j + 1) * 128),
                            wv.ap(dc * 512, (dc + 1) * 512),
                            start=(dc == 0), stop=(dc == 7))
                    return last
                p.emit("pe", mm,
                       reads=[wv.reg(0, 4096), xnT.reg()],
                       writes=[R(bank)])
                if sec == "k":
                    ks = (j * 1024 + ob * 512, j * 1024 + (ob + 1) * 512)
                    kreg = Region(KT.t, 0, 128, KT.o + ks[0], KT.o + ks[1])
                    mk.ar32.save()
                    t1 = mk.v32(512)
                    t2 = mk.v32(512)
                    p.emit("dve", lambda e, bank=bank, t1=t1: e.tensor_scalar(
                        t1.ap(), bank.ap()[:, :], sc4, 0.0, ALU.mult, ALU.min),
                        reads=[R(bank)], writes=[t1.reg()])
                    p.emit("act", lambda e, t1=t1: e.activation(
                        t1.ap(), t1.ap(), AF.Exp),
                        reads=[t1.reg()], writes=[t1.reg()])
                    p.emit("dve", lambda e, bank=bank, t2=t2: e.tensor_scalar(
                        t2.ap(), bank.ap()[:, :], sc4, 0.0, ALU.mult, ALU.max),
                        reads=[R(bank)], writes=[t2.reg()])
                    p.emit("dve", lambda e, ks=ks, t1=t1, t2=t2, kreg=kreg:
                           e.tensor_tensor(
                        KT.t.ap()[:, KT.o + ks[0]:KT.o + ks[1]], t1.ap(),
                        t2.ap(), ALU.add),
                        reads=[t1.reg(), t2.reg()], writes=[kreg])
                    mk.ar32.restore()
                else:
                    dst = mk.ar16_t.ap()[:, VT.o + j * 1040 + ob * 520:
                                         VT.o + j * 1040 + ob * 520 + 520]\
                        .rearrange("p (h c) -> p h c", c=65)[:, :, 0:64]
                    dreg = Region(VT.t, 0, 128, VT.o + j * 1040 + ob * 520,
                                  VT.o + j * 1040 + (ob + 1) * 520)
                    p.emit("dve", lambda e, bank=bank, dst=dst, dreg=dreg:
                           e.tensor_copy(
                        dst, bank.ap()[:, :].rearrange("p (h c) -> p h c",
                                                       c=64)),
                        reads=[R(bank)], writes=[dreg])
    kv_all("k", "wkT")
    kv_all("v", "wvT")

    def vones(j):
        dst = mk.ar16_t.ap()[:, VT.o + j * 1040:VT.o + (j + 1) * 1040]\
            .rearrange("p (h c) -> p h c", c=65)[:, :, 64:65]
        p.emit("dve", lambda e: e.memset(dst, 1.0),
               writes=[Region(VT.t, 0, 128, VT.o + j * 1040,
                              VT.o + (j + 1) * 1040)])
    for j in range(TTE):
        vones(j)
    ctx = mk.v32(8 * 65)
    p.emit("dve", lambda e: e.memset(ctx.ap(), 0.0), writes=[ctx.reg()])
    afm = View(mk.ar16_t, xnT.o, 8 * T)
    zrowbuf = mk.v32(T)
    p.emit("dve", lambda e: e.memset(ctxn.ap(64, 65), 1.0),
           writes=[Region(ctxn.t, 0, 128, ctxn.o + 64, ctxn.o + 65)])
    p.emit("dve", lambda e: e.memset(ctxn.ap(129, 130), 1.0),
           writes=[Region(ctxn.t, 0, 128, ctxn.o + 129, ctxn.o + 130)])

    def head(h):
        oc, pb = h // 2, (h % 2) * 64
        cs = (oc * 65, (oc + 1) * 65)
        creg = Region(mk.ar32_t, pb, pb + 64, ctx.o + cs[0], ctx.o + cs[1])
        cbase = ctxn.o if h % 2 == 0 else ctxn.o + 65

        def bucket(u):
            jt, half = u // 2, (u % 2) * 64
            if u > 0:
                mk.small.save()
                ikc = mk.vsm(1)
                p.emit("dve", lambda e: e.tensor_scalar(
                    ikc.ap(p0=pb, p1=pb + 64),
                    mk.ar32_t.ap()[pb:pb + 64, ctx.o + cs[0] + 64:
                                   ctx.o + cs[1]],
                    1.0, 1e-6, ALU.mult, ALU.add),
                    reads=[creg], writes=[ikc.reg(p0=pb, p1=pb + 64)])
                p.emit("dve", lambda e: e.reciprocal(
                    ikc.ap(p0=pb, p1=pb + 64), ikc.ap(p0=pb, p1=pb + 64)),
                    reads=[ikc.reg(p0=pb, p1=pb + 64)],
                    writes=[ikc.reg(p0=pb, p1=pb + 64)])
                p.emit("dve", lambda e: e.tensor_scalar(
                    ctxn.t.ap()[pb:pb + 64, cbase:cbase + 64],
                    mk.ar32_t.ap()[pb:pb + 64, ctx.o + cs[0]:
                                   ctx.o + cs[0] + 64],
                    ikc.ap(p0=pb, p1=pb + 64), None, ALU.mult),
                    reads=[creg, ikc.reg(p0=pb, p1=pb + 64)],
                    writes=[Region(ctxn.t, pb, pb + 64, cbase, cbase + 64)])
                mk.small.restore()
                bank = mk.ps[6]
                p.emit("pe", lambda e: e.matmul(
                    bank.ap()[0:65, 0:64],
                    ctxn.t.ap()[pb:pb + 64, cbase:cbase + 65],
                    QF.ap(oc * T + u * 64, oc * T + (u + 1) * 64,
                          p0=pb, p1=pb + 64),
                    start=True, stop=True),
                    reads=[Region(ctxn.t, pb, pb + 64, cbase, cbase + 65),
                           QF.reg()],
                    writes=[R(bank)])
                asl = (u * 64, (u + 1) * 64)
                p.emit("dve", lambda e: e.tensor_copy(
                    afm.t.ap()[pb:pb + 64,
                               afm.o + oc * T + asl[0]:afm.o + oc * T + asl[1]],
                    bank.ap()[0:64, 0:64]),
                    reads=[R(bank)],
                    writes=[Region(afm.t, pb, pb + 64, afm.o + oc * T + asl[0],
                                   afm.o + oc * T + asl[1])])
                p.emit("dve", lambda e: e.tensor_copy(
                    mk.ar32_t.ap()[0:1, zrowbuf.o + asl[0]:zrowbuf.o + asl[1]],
                    bank.ap()[64:65, 0:64]),
                    reads=[R(bank)], writes=[zrowbuf.reg()])
            else:
                p.emit("dve", lambda e: e.memset(
                    afm.t.ap()[pb:pb + 64, afm.o + oc * T:afm.o + oc * T + 64],
                    0.0),
                    writes=[Region(afm.t, pb, pb + 64, afm.o + oc * T,
                                   afm.o + oc * T + 64)])
                p.emit("dve", lambda e: e.memset(
                    mk.ar32_t.ap()[0:1, zrowbuf.o:zrowbuf.o + 64], 1.0),
                    writes=[zrowbuf.reg()])
            bank2 = mk.ps[7]
            p.emit("pe", lambda e: e.matmul(
                bank2.ap()[0:64, 0:65],
                KT.t.ap()[half:half + 64,
                          KT.o + jt * 1024 + h * 64:KT.o + jt * 1024 + (h + 1) * 64]
                if False else
                KT.t.ap()[half:half + 64,
                          KT.o + jt * 1024 + h * 64:KT.o + jt * 1024 + (h + 1) * 64],
                mk.ar16_t.ap()[half:half + 64,
                               VT.o + jt * 1040 + h * 65:VT.o + jt * 1040 + (h + 1) * 65],
                start=True, stop=True),
                reads=[Region(KT.t, half, half + 64,
                              KT.o + jt * 1024, KT.o + (jt + 1) * 1024),
                       Region(VT.t, half, half + 64,
                              VT.o + jt * 1040, VT.o + (jt + 1) * 1040)],
                writes=[R(bank2)])
            p.emit("dve", lambda e: e.tensor_tensor(
                mk.ar32_t.ap()[pb:pb + 64, ctx.o + cs[0]:ctx.o + cs[1]],
                mk.ar32_t.ap()[pb:pb + 64, ctx.o + cs[0]:ctx.o + cs[1]],
                bank2.ap()[0:64, 0:65], ALU.add),
                reads=[creg, R(bank2)], writes=[creg])
        for u in range(NB):
            bucket(u)
        mk.ar32.save()
        p.emit("act", lambda e: e.activation(
            mk.ar32_t.ap()[0:1, zrowbuf.o:zrowbuf.o + T],
            mk.ar32_t.ap()[0:1, zrowbuf.o:zrowbuf.o + T], AF.Ln),
            reads=[zrowbuf.reg()], writes=[zrowbuf.reg()])
        p.emit("act", lambda e: e.activation(
            mk.ar32_t.ap()[0:1, zrowbuf.o:zrowbuf.o + T],
            mk.ar32_t.ap()[0:1, zrowbuf.o:zrowbuf.o + T], AF.Exp,
            scale=p.const_col(-1.0)[0:1, :]),
            reads=[zrowbuf.reg()], writes=[zrowbuf.reg()])
        ibc = mk.v32(512)

        def div4(ts4):
            bank = mk.ps[6]
            p.emit("pe", lambda e: e.matmul(
                bank.ap()[pb:pb + 64, :], onesr[0:1, 0:64],
                mk.ar32_t.ap()[0:1, zrowbuf.o + ts4 * 512:
                               zrowbuf.o + (ts4 + 1) * 512],
                start=True, stop=True),
                reads=[zrowbuf.reg()], writes=[R(bank)])
            p.emit("dve", lambda e: e.tensor_copy(
                ibc.ap(p0=pb, p1=pb + 64), bank.ap()[pb:pb + 64, :]),
                reads=[R(bank)], writes=[ibc.reg()])
            asl = (oc * T + ts4 * 512, oc * T + (ts4 + 1) * 512)
            p.emit("dve", lambda e: e.tensor_tensor(
                afm.t.ap()[pb:pb + 64, afm.o + asl[0]:afm.o + asl[1]],
                afm.t.ap()[pb:pb + 64, afm.o + asl[0]:afm.o + asl[1]],
                ibc.ap(p0=pb, p1=pb + 64), ALU.mult),
                reads=[Region(afm.t, pb, pb + 64, afm.o + asl[0],
                              afm.o + asl[1]), ibc.reg()],
                writes=[Region(afm.t, pb, pb + 64, afm.o + asl[0],
                               afm.o + asl[1])])
        for ts4 in range(T // 512):
            div4(ts4)
        mk.ar32.restore()
    for h in range(16):
        head(h)
    mk.page_x(xp, "in")

    def proj(ob):
        wv = kvw[ob % 2]
        ldw(wv, "woT", ob)

        def one(j):
            tt = tt0 + j
            bank = mk.ps[4 + j % 2]

            def mm(t):
                last = None
                for dc in range(8):
                    last = t.matmul(
                        bank.ap()[:, :],
                        afm.t.ap()[:, afm.o + dc * T + j * 128:
                                   afm.o + dc * T + (j + 1) * 128],
                        wv.ap(dc * 512, (dc + 1) * 512),
                        start=(dc == 0), stop=(dc == 7))
                return last
            p.emit("pe", mm,
                   reads=[Region(afm.t, 0, 128, afm.o, afm.o + 8 * T),
                          wv.reg(0, 4096)],
                   writes=[R(bank)])
            xs = (tt * D + ob * 512, tt * D + (ob + 1) * 512)
            xa = x.ap()[:, xs[0]:xs[1]]
            xreg = Region(x, 0, 128, *xs)
            yv = mk.v32(512)
            p.emit("dve", lambda e: e.tensor_tensor(
                yv.ap(), bank.ap()[:, :], bo_bc.ap(ob * 512, (ob + 1) * 512),
                ALU.add),
                reads=[R(bank), bo_bc.reg(ob * 512, (ob + 1) * 512)],
                writes=[yv.reg()])
            p.emit("dve", lambda e: e.tensor_tensor(
                xa, xa, yv.ap(), ALU.add),
                reads=[xreg, yv.reg()], writes=[xreg])
            mk.ar32.off -= 512
        for j in range(TTE):
            one(j)
    proj(0); proj(1)


def mamba_apply(mk, cfg, elem):
    import os as _os
    PH = int(_os.environ.get("MM_PHASE", "9"))
    p = mk.p
    mk.ar16.reset(); mk.ar32.reset(); mk.small.reset()
    TTE = mk.TTE
    T = mk.T
    TC = 256
    NCK = T // TC
    NDI = 16
    tt0 = elem * TTE
    x = mk.x
    wcv = mk.vsm(64)
    bcv = mk.vsm(16)
    bdt = mk.vsm(16)
    Dcl = mk.vsm(16)
    for nm, v in (("wconv_cols", wcv), ("bconv_cols", bcv),
                  ("bdt_cols", bdt), ("D_cols", Dcl)):
        p.emit("sp", lambda e, nm=nm, v=v: e.dma_start(
            out=v.ap(), in_=cfg[nm][:, :]), writes=[v.reg()], dma=True)
    Acl = mk.v32(256)
    p.emit("sp", lambda e: e.dma_start(out=Acl.ap(), in_=cfg["A_cols"][:, :]),
           writes=[Acl.reg()], dma=True)
    p.emit("act", lambda e: e.activation(Acl.ap(), Acl.ap(), AF.Exp),
           reads=[Acl.reg()], writes=[Acl.reg()])
    p.emit("dve", lambda e: e.tensor_single_scalar(
        Acl.ap(), Acl.ap(), -1.0, ALU.mult),
        reads=[Acl.reg()], writes=[Acl.reg()])
    sel96 = mk.v16_in32(4096)
    p.emit("pool", lambda e: e.dma_start(
        out=sel96.ap(p1=96), in_=cfg["sel96"][:, :]),
        writes=[sel96.reg()], dma=True)
    wxs = mk.v16_in32(1536)
    p.emit("pool", lambda e: e.dma_start(
        out=wxs.ap().rearrange("p (c o) -> p c o", c=16),
        in_=cfg["wx_nat"].rearrange("(c p) o -> p c o", p=128)),
        writes=[wxs.reg()], dma=True)
    wdts = mk.v16_in32(2048)
    p.emit("pool", lambda e: e.dma_start(
        out=wdts.ap(p1=64).rearrange("p (c o) -> p c o", c=16),
        in_=cfg["wdt"].rearrange("p (c o) -> p c o", c=16)),
        writes=[wdts.reg()], dma=True)
    negm = mk.vsm(TTE)
    rstd = mk.vsm(TTE)
    mk.ln_stats(list(range(tt0, tt0 + TTE)), rstd, negm)
    xp = cfg["xpage"]
    bneed = 8 * T + 4096 + 8192 + 4096
    bt = x if mk.TT * D >= bneed else p.sbuf(f"mm_borrow{elem}",
                                             [128, bneed], BF16)
    xnT = View(bt, 0, 8 * T)
    zc = View(bt, 8 * T, 4096)
    Bbc = View(bt, 8 * T + 4096, 4096)
    Cbc = View(bt, 8 * T + 8192, 4096)
    y16 = View(bt, 8 * T + 12288, 4096)
    xbr = [mk.v16(16 * 259) for _ in range(2)]
    av = mk.v16(4096)
    bv = mk.v16(4096)
    hv = mk.v16(4096)
    xcs = mk.v16(4096)
    states = mk.v16(256)
    wring = [mk.v16_in32(4096) for _ in range(2)]
    ystage2 = mk.v16_in32(1024)
    stage = BView(mk.ar32_t, ystage2.o, 1024)
    tmp16 = View(mk.ar16_t, bv.o, 4096)
    big32 = mk.v32(4096)
    dblf = mk.v32(256)
    dblT = None
    p.emit("dve", lambda e: e.memset(states.ap(), 0.0), writes=[states.reg()])

    def mkxnT(j):
        tt = tt0 + j
        mk.ar16.save()
        xn = mk.v16(1024)
        mk.ln_apply(tt, negm, rstd, xn.ap(), xn.reg(), j=j)
        dst = bt.ap().bitcast(BF16) if isinstance(bt, type(mk.ar32_t)) else None
        dstap = bt.ap()[:, xnT.o:xnT.o + 16384].rearrange(
            "p (c t) -> p c t", c=8)[:, :, j * 128:(j + 1) * 128]
        mk.transpose8(lambda dc: xn.ap(dc * 128, (dc + 1) * 128),
                      dstap, Region(bt, 0, 128, xnT.o, xnT.o + 16384),
                      [xn.reg()], pb=j % 2)
        mk.ar16.restore()
    mk.page_x(xp, "out")

    xtring = [mk.v16(1024) for _ in range(2)]
    xnring = [mk.v16(1024) for _ in range(2)]

    def mkxnT2(j):
        tt = tt0 + j
        mk.ar16.save()
        xt = xtring[j % 2]
        xn = xnring[j % 2]
        p.emit("sp", lambda e: e.dma_start(
            out=xt.ap(), in_=xp.ap()[:, tt * D:(tt + 1) * D]),
            reads=[Region(xp, 0, 128, tt * D, (tt + 1) * D)],
            writes=[xt.reg()], dma=True)
        p.emit("dve", lambda e: e.tensor_scalar(
            xn.ap(), xt.ap(), negm.ap(j, j + 1), rstd.ap(j, j + 1),
            ALU.add, ALU.mult),
            reads=[xt.reg(), negm.reg(j, j + 1), rstd.reg(j, j + 1)],
            writes=[xn.reg()])
        dstap = bt.ap()[:, xnT.o:xnT.o + 8 * T].rearrange(
            "p (c t) -> p c t", c=8)[:, :, j * 128:(j + 1) * 128]
        mk.transpose8(lambda dc: xn.ap(dc * 128, (dc + 1) * 128),
                      dstap, Region(bt, 0, 128, xnT.o, xnT.o + 8 * T),
                      [xn.reg()], pb=j % 2)
        mk.ar16.restore()
    for j in range(TTE):
        mkxnT2(j)

    def xz_mms(c, which, dst_fn, slot=None):
        wsrc = cfg["w_in_x" if which == "x" else "w_in_z"]

        def one(di):
            wv = wring[(di // 4) % 2]
            if di % 4 == 0:
                p.emit("pool", lambda e, wv=wv, di=di: e.dma_start(
                    out=wv.ap(0, 4096).rearrange("p (c o) -> p c o", c=8),
                    in_=wsrc.rearrange("(c p) o -> p c o", p=128)
                    [:, :, di * 128:(di + 4) * 128]),
                    writes=[wv.reg(0, 4096)], dma=True)
            boff = (di % 4) * 128
            bank = mk.ps[4 + di % 2]

            def mm(t):
                last = None
                for dc in range(8):
                    last = t.matmul(
                        bank.ap()[:, 0:TC],
                        wv.ap(dc * 512 + boff, dc * 512 + boff + 128),
                        bt.ap()[:, xnT.o + dc * T + c * TC:
                                xnT.o + dc * T + (c + 1) * TC],
                        start=(dc == 0), stop=(dc == 7))
                return last
            p.emit("pe", mm,
                   reads=[wv.reg(0, 4096),
                          Region(bt, 0, 128, xnT.o, xnT.o + 8 * T)],
                   writes=[R(bank)])
            dst_fn(di, bank)
        for di in range(NDI):
            one(di)

    def chunk(c):
        slot = xbr[c % 2]
        def evx(di, bank):
            ds = (di * 259 + 1, di * 259 + 257)
            p.emit("dve", lambda e: e.tensor_copy(
                slot.ap(*ds), bank.ap()[:, 0:TC]),
                reads=[R(bank)], writes=[slot.reg(*ds)])
        xz_mms(c, "x", evx)
        prev = xbr[(c + 1) % 2]

        def hist(di):
            ds = (di * 259, di * 259 + 1)
            if c == 0:
                p.emit("dve", lambda e: e.memset(slot.ap(*ds), 0.0),
                       writes=[slot.reg(*ds)])
            else:
                p.emit("dve", lambda e: e.tensor_copy(
                    slot.ap(*ds), prev.ap(di * 259 + 256, di * 259 + 257)),
                    reads=[prev.reg(di * 259 + 256, di * 259 + 257)],
                    writes=[slot.reg(*ds)])
        for di in range(NDI):
            hist(di)
        def fut(di):
            ds = (di * 259 + 257, di * 259 + 259)
            p.emit("dve", lambda e: e.tensor_copy(
                prev.ap(*ds), slot.ap(di * 259 + 1, di * 259 + 3)),
                reads=[slot.reg(di * 259 + 1, di * 259 + 3)],
                writes=[prev.reg(*ds)])
        if c > 0:
            for di in range(NDI):
                fut(di)
        if c > 0:
            ssm(c - 1, prev)
        if c == NCK - 1:
            def futz(di):
                ds = (di * 259 + 257, di * 259 + 259)
                p.emit("dve", lambda e: e.memset(slot.ap(*ds), 0.0),
                       writes=[slot.reg(*ds)])
            for di in range(NDI):
                futz(di)
            ssm(c, slot)

    def ssm(c, slot):
        if PH < 2:
            return
        def conv(di):
            a0 = (di * 256, (di + 1) * 256)

            def tap(k):
                src = (di * 259 + k, di * 259 + k + 256)
                if k == 0:
                    p.emit("dve", lambda e: e.tensor_scalar(
                        big32.ap(*a0), slot.ap(*src), wcv.ap(di * 4, di * 4 + 1),
                        bcv.ap(di, di + 1), ALU.mult, ALU.add),
                        reads=[slot.reg(*src), wcv.reg(di * 4, di * 4 + 1),
                               bcv.reg(di, di + 1)],
                        writes=[big32.reg(*a0)])
                else:
                    p.emit("dve", lambda e: e.scalar_tensor_tensor(
                        big32.ap(*a0), slot.ap(*src),
                        wcv.ap(di * 4 + k, di * 4 + k + 1), big32.ap(*a0),
                        ALU.mult, ALU.add),
                        reads=[slot.reg(*src), wcv.reg(di * 4 + k, di * 4 + k + 1),
                               big32.reg(*a0)],
                        writes=[big32.reg(*a0)])
            for k in range(4):
                tap(k)
        for di in range(NDI):
            conv(di)
        p.emit("act", lambda e: e.activation(
            xcs.ap(), big32.ap(), AF.Silu),
            reads=[big32.reg()], writes=[xcs.reg()])
        if PH < 3:
            return
        bank = mk.ps[6]

        def mmdbl(t):
            last = None
            for di in range(NDI):
                last = t.matmul(
                    bank.ap()[0:96, 0:TC],
                    wxs.ap(di * 96, (di + 1) * 96),
                    xcs.ap(di * 256, (di + 1) * 256),
                    start=(di == 0), stop=(di == NDI - 1))
            return last
        p.emit("pe", mmdbl, reads=[wxs.reg(), xcs.reg()], writes=[R(bank)])
        p.emit("dve", lambda e: e.tensor_copy(
            dblf.ap(p1=96), bank.ap()[0:96, 0:TC]),
            reads=[R(bank)], writes=[dblf.reg()])
        dbl16 = View(mk.ar16_t, tmp16.o, 256)
        p.emit("dve", lambda e: e.tensor_copy(
            dbl16.ap(p1=96), dblf.ap(p1=96)),
            reads=[dblf.reg()], writes=[dbl16.reg()])
        def selbc(n):
            bnk = mk.ps[6 if n % 2 == 0 else 7]
            dstv = Bbc if n < 16 else Cbc
            nn = n % 16
            p.emit("pe", lambda e: e.matmul(
                bnk.ap()[:, 0:TC], sel96.ap(n * 128, (n + 1) * 128, p1=96),
                dbl16.ap(p1=96), start=True, stop=True),
                reads=[sel96.reg(), dbl16.reg()], writes=[R(bnk)])
            ds = (nn * 256, (nn + 1) * 256)
            p.emit("dve", lambda e: e.tensor_copy(
                bt.ap()[:, dstv.o + ds[0]:dstv.o + ds[1]], bnk.ap()[:, 0:TC]),
                reads=[R(bnk)],
                writes=[Region(bt, 0, 128, dstv.o + ds[0], dstv.o + ds[1])])
        for n in range(32):
            selbc(n)
        if PH < 4:
            return
        def ddi(di):
            bnk = mk.ps[4 + di % 2]
            p.emit("pe", lambda e: e.matmul(
                bnk.ap()[:, 0:TC], wdts.ap(di * 128, (di + 1) * 128, p1=64),
                dbl16.ap(p1=64), start=True, stop=True),
                reads=[wdts.reg(), dbl16.reg()], writes=[R(bnk)])
            a0 = (di * 256, (di + 1) * 256)
            mk.ar32.save()
            t1 = mk.v32(256)
            p.emit("dve", lambda e: e.tensor_scalar(
                t1.ap(), bnk.ap()[:, 0:TC], bdt.ap(di, di + 1), None, ALU.add),
                reads=[R(bnk), bdt.reg(di, di + 1)], writes=[t1.reg()])
            p.emit("dve", lambda e: e.tensor_single_scalar(
                big32.ap(*a0), t1.ap(), 0.0, ALU.max),
                reads=[t1.reg()], writes=[big32.reg(*a0)])
            p.emit("dve", lambda e: e.scalar_tensor_tensor(
                t1.ap(), t1.ap(), -1.0, t1.ap(), ALU.mult, ALU.max),
                reads=[t1.reg()], writes=[t1.reg()])
            p.emit("act", lambda e: e.activation(
                t1.ap(), t1.ap(), AF.Exp, scale=p.const_col(-1.0)),
                reads=[t1.reg()], writes=[t1.reg()])
            p.emit("act", lambda e: e.activation(
                t1.ap(), t1.ap(), AF.Ln, bias=p.const_col(1.0)),
                reads=[t1.reg()], writes=[t1.reg()])
            p.emit("dve", lambda e: e.tensor_tensor(
                big32.ap(*a0), big32.ap(*a0), t1.ap(), ALU.add),
                reads=[big32.reg(*a0), t1.reg()], writes=[big32.reg(*a0)])
            mk.ar32.restore()
        for di in range(NDI):
            ddi(di)
        zdone = [False]

        def dchunk(di):
            a0 = (di * 256, (di + 1) * 256)
            d3 = big32.ap(*a0).rearrange("p (o t) -> p o t", o=1)\
                .broadcast_to((128, 16, TC))
            A3 = Acl.ap(di * 16, (di + 1) * 16).rearrange(
                "p (n o) -> p n o", o=1).broadcast_to((128, 16, TC))
            av3 = av.ap().rearrange("p (n t) -> p n t", n=16)
            p.emit("pool", lambda e: e.tensor_tensor(
                av3, d3, A3, ALU.mult),
                reads=[big32.reg(*a0), Acl.reg(di * 16, (di + 1) * 16)],
                writes=[av.reg()])
            p.emit("act", lambda e: e.activation(
                av.ap(), av.ap(), AF.Exp),
                reads=[av.reg()], writes=[av.reg()])
            u = ystage2
            p.emit("dve", lambda e: e.tensor_tensor(
                u.ap(0, 256), big32.ap(*a0), xcs.ap(*a0), ALU.mult),
                reads=[big32.reg(*a0), xcs.reg(*a0)], writes=[u.reg(0, 256)])
            u3 = u.ap(0, 256).rearrange("p (o t) -> p o t", o=1)\
                .broadcast_to((128, 16, TC))
            B3 = bt.ap()[:, Bbc.o:Bbc.o + 4096].rearrange(
                "p (n t) -> p n t", n=16)
            bv3 = bv.ap().rearrange("p (n t) -> p n t", n=16)
            p.emit("pool", lambda e: e.tensor_tensor(
                bv3, u3, B3, ALU.mult),
                reads=[u.reg(0, 256), Region(bt, 0, 128, Bbc.o, Bbc.o + 4096)],
                writes=[bv.reg()])
            def scn(n):
                sc0 = (n * 256, (n + 1) * 256)
                stc = di * 16 + n
                p.emit("dve", lambda e: e.tensor_tensor_scan(
                    hv.ap(*sc0), av.ap(*sc0), bv.ap(*sc0),
                    states.ap(stc, stc + 1), ALU.mult, ALU.add),
                    reads=[av.reg(*sc0), bv.reg(*sc0),
                           states.reg(stc, stc + 1)],
                    writes=[hv.reg(*sc0)])
            for n in range(16):
                scn(n)
            p.emit("dve", lambda e: e.tensor_copy(
                states.ap(di * 16, (di + 1) * 16),
                hv.ap().rearrange("p (n t) -> p n t", n=16)[:, :, 255:256]
                .rearrange("p n o -> p (n o)")),
                reads=[hv.reg()],
                writes=[states.reg(di * 16, (di + 1) * 16)])
            C3 = bt.ap()[:, Cbc.o:Cbc.o + 4096].rearrange(
                "p (n t) -> p n t", n=16)
            t3 = tmp16.ap().rearrange("p (n t) -> p n t", n=16)
            p.emit("pool", lambda e: e.tensor_tensor(
                t3, hv.ap().rearrange("p (n t) -> p n t", n=16), C3, ALU.mult),
                reads=[hv.reg(), Region(bt, 0, 128, Cbc.o, Cbc.o + 4096)],
                writes=[tmp16.reg()])
            mk.ar32.save()
            yy = mk.v32(256)
            p.emit("dve", lambda e: e.tensor_reduce(
                yy.ap(), tmp16.ap().rearrange("p (n t) -> p t n", n=16),
                axis=AX.X, op=ALU.add),
                reads=[tmp16.reg()], writes=[yy.reg()])
            p.emit("dve", lambda e: e.scalar_tensor_tensor(
                yy.ap(), xcs.ap(*a0), Dcl.ap(di, di + 1), yy.ap(),
                ALU.mult, ALU.add),
                reads=[xcs.reg(*a0), Dcl.reg(di, di + 1), yy.reg()],
                writes=[yy.reg()])
            if not zdone[0]:
                pass
            zs = (di * 256, (di + 1) * 256)
            sz = View(mk.ar16_t, stage.o, 256)
            p.emit("act", lambda e: e.activation(
                sz.ap(0, 256), bt.ap()[:, zc.o + zs[0]:zc.o + zs[1]], AF.Silu),
                reads=[Region(bt, 0, 128, zc.o + zs[0], zc.o + zs[1])],
                writes=[sz.reg(0, 256)])
            p.emit("dve", lambda e: e.tensor_tensor(
                bt.ap()[:, y16.o + zs[0]:y16.o + zs[1]], yy.ap(),
                sz.ap(0, 256), ALU.mult),
                reads=[yy.reg(), sz.reg(0, 256)],
                writes=[Region(bt, 0, 128, y16.o + zs[0], y16.o + zs[1])])
            mk.ar32.restore()
        if PH < 5:
            return
        def evz(di, bank):
            ds = (di * 256, (di + 1) * 256)
            p.emit("dve", lambda e: e.tensor_copy(
                bt.ap()[:, zc.o + ds[0]:zc.o + ds[1]], bank.ap()[:, 0:TC]),
                reads=[R(bank)],
                writes=[Region(bt, 0, 128, zc.o + ds[0], zc.o + ds[1])])
        xz_mms(c, "z", evz)
        for di in range(NDI):
            dchunk(di)
        if PH < 6:
            return
        def proj(di_outer_unused=None):
            def one(di):
                wv = wring[(di // 4) % 2]
                if di % 4 == 0:
                    p.emit("pool", lambda e, wv=wv, di=di: e.dma_start(
                        out=wv.ap(0, 4096).rearrange("p (c o) -> p c o", c=4),
                        in_=cfg["wout"].rearrange("(c p) o -> p c o", p=128)
                        [:, di:di + 4, :]),
                        writes=[wv.reg(0, 4096)],
                        dma=True)
                boff = (di % 4) * 1024
                for tp in range(2):
                    for ob in range(2):
                        bnk = mk.ps[tp * 2 + ob]
                        p.emit("pe", lambda e, tp=tp, ob=ob, bnk=bnk, wv=wv,
                               di=di, boff=boff: e.matmul(
                            bnk.ap()[:, :],
                            bt.ap()[:, y16.o + di * 256 + tp * 128:
                                    y16.o + di * 256 + (tp + 1) * 128],
                            wv.ap(boff + ob * 512, boff + (ob + 1) * 512),
                            start=(di == 0), stop=(di == NDI - 1)),
                            reads=[Region(bt, 0, 128, y16.o, y16.o + 4096),
                                   wv.reg(0, 4096)],
                            writes=[R(bnk)])
            for di in range(NDI):
                one(di)

            def evac(tp):
                for ob in range(2):
                    bnk = mk.ps[tp * 2 + ob]
                    p.emit("dve", lambda e, ob=ob, bnk=bnk: e.tensor_copy(
                        stage.ap(ob * 512, (ob + 1) * 512), bnk.ap()[:, :]),
                        reads=[R(bnk)],
                        writes=[stage.reg(ob * 512, (ob + 1) * 512)])
                p.emit("sp", lambda e, tp=tp: e.dma_start(
                    out=cfg["ydram"].ap()[:, (c * 2 + tp) * D:
                                          (c * 2 + tp + 1) * D],
                    in_=stage.ap()),
                    reads=[stage.reg()],
                    writes=[Region(cfg["ydram"], 0, 128, (c * 2 + tp) * D,
                                   (c * 2 + tp + 1) * D)], dma=True)
            evac(0); evac(1)
        proj()
    for c in range(NCK):
        chunk(c)
    mk.page_x(xp, "in")

    def finadd(j):
        tt = tt0 + j
        xs = (tt * D, (tt + 1) * D)
        xa = x.ap()[:, xs[0]:xs[1]]
        xreg = Region(x, 0, 128, *xs)
        p.emit("sp", lambda e: e.dma_start(
            out=ystage2.ap(), in_=cfg["ydram"].ap()[:, j * D:(j + 1) * D]),
            reads=[Region(cfg["ydram"], 0, 128, j * D, (j + 1) * D)],
            writes=[ystage2.reg()], dma=True)
        p.emit("dve", lambda e: e.tensor_tensor(
            xa, xa, ystage2.ap(), ALU.add),
            reads=[xreg, ystage2.reg()], writes=[xreg])
    for j in range(TTE):
        finadd(j)


def build_program():
    p = Prog()
    nc = p.nc
    F = F32
    inp = {}

    def di(name, shape):
        inp[name] = nc.dram_tensor(name, list(shape), F, kind="ExternalInput").ap()
        return inp[name]
    xin = di("xin", [4096, 1024])
    di("iden", [128, 128])
    di("E8", [8, 128])
    di("sel8", [8, 1024])
    di("sel96", [96, 4096])
    di("ropeT", [8, 128, 128])
    di("l0_w1T", [1024, 4096]); di("l0_w2T", [4096, 1024])
    di("l0_wqkv", [1024, 3072]); di("l0_bqc", [128, 24]); di("l0_bqr", [1, 3072])
    di("l0_wproj", [1024, 1024]); di("l0_bpr", [1, 1024])
    for nm in ("a_q", "b_q", "a_v", "b_v"):
        di("l0_" + nm, [64, 16] if nm.startswith("a") else [16, 64])
    di("l1_wgT", [1024, 8]); di("l1_winT", [8, 1024, 4096])
    di("l1_woutT", [8, 4096, 1024])
    for nm in ("wqT", "wkT", "wvT", "woT"):
        di("l1_" + nm, [1024, 1024])
    di("l1_bo", [1, 1024])
    di("l2_router", [1024, 8]); di("l2_lnw", [1, 1024])
    di("l2_w1", [8, 1024, 2816]); di("l2_w3", [8, 1024, 2816])
    di("l2_w2", [8, 2816, 1024])
    di("l2_winx", [1024, 2048]); di("l2_winz", [1024, 2048])
    di("l2_wconv", [128, 64]); di("l2_bconv", [128, 16])
    di("l2_wx", [2048, 96]); di("l2_wdt", [64, 2048]); di("l2_bdt", [128, 16])
    di("l2_A", [128, 256]); di("l2_D", [128, 16]); di("l2_wout", [2048, 1024])
    di("l3_qkvw", [1024, 3072]); di("l3_qkvbc", [128, 24]); di("l3_qkvbr", [1, 3072])
    di("l3_outw", [1024, 1024]); di("l3_outbr", [1, 1024])
    di("l3_wg", [2816, 1024]); di("l3_wu", [2816, 1024]); di("l3_wd", [1024, 2816])
    di("l3_wgT", [1024, 2816]); di("l3_wuT", [1024, 2816]); di("l3_wdT", [2816, 1024])
    xout = nc.dram_tensor("xout", [4096, 1024], F, kind="ExternalOutput").ap()

    mk = MK(p, T_elem=2048, n_elem=2)
    mk.load_const_tiles(inp["iden"])
    xpage = p.dram("xpage", [128, mk.TT * 1024], BF16)
    ydram = p.dram("ydram", [128, mk.TTE * 1024], BF16)
    for tt in range(mk.TT):
        p.emit("pool", lambda e, tt=tt: e.dma_start(
            out=mk.x.ap()[:, tt * 1024:(tt + 1) * 1024],
            in_=xin[tt * 128:(tt + 1) * 128, :]),
            writes=[Region(mk.x, 0, 128, tt * 1024, (tt + 1) * 1024)], dma=True)
    T1d, T2d, scols = l0_ffn_prepare(mk, inp["l0_w1T"], inp["l0_w2T"])
    l3n = {"wg": inp["l3_wg"], "wu": inp["l3_wu"], "wd": inp["l3_wd"]}
    l3t = {"wg": inp["l3_wgT"], "wu": inp["l3_wuT"], "wd": inp["l3_wdT"]}
    wqd = l3_ffn_prepare(mk, l3n, l3t, inp["E8"], inp["iden"])
    l0a = {"wqkv": inp["l0_wqkv"], "bqkv_cols": inp["l0_bqc"],
           "bqkv_row": inp["l0_bqr"], "wproj": inp["l0_wproj"],
           "bproj_row": inp["l0_bpr"], "xpage": xpage,
           "lora": {nm: inp["l0_" + nm] for nm in ("a_q", "b_q", "a_v", "b_v")}}
    l3a = {"wqkv": inp["l3_qkvw"], "bqkv_cols": inp["l3_qkvbc"],
           "bqkv_row": inp["l3_qkvbr"], "wproj": inp["l3_outw"],
           "bproj_row": inp["l3_outbr"], "xpage": xpage, "rope": inp["ropeT"]}
    l1m = {"kind": "l1", "w_gT": inp["l1_wgT"], "w_in": inp["l1_winT"],
           "w_out": inp["l1_woutT"], "sel8": inp["sel8"]}
    l2m = {"kind": "l2", "router": inp["l2_router"], "lnw_row": inp["l2_lnw"],
           "w1": inp["l2_w1"], "w3": inp["l2_w3"], "w2": inp["l2_w2"],
           "sel8": inp["sel8"]}
    l1at = {"wqT": inp["l1_wqT"], "wkT": inp["l1_wkT"], "wvT": inp["l1_wvT"],
            "woT": inp["l1_woT"], "bo_row": inp["l1_bo"], "xpage": xpage}
    l2at = {"w_in_x": inp["l2_winx"], "w_in_z": inp["l2_winz"],
            "wconv_cols": inp["l2_wconv"], "bconv_cols": inp["l2_bconv"],
            "wx_nat": inp["l2_wx"], "wdt": inp["l2_wdt"],
            "bdt_cols": inp["l2_bdt"], "A_cols": inp["l2_A"],
            "D_cols": inp["l2_D"], "wout": inp["l2_wout"],
            "sel96": inp["sel96"], "xpage": xpage, "ydram": ydram}
    import os as _os
    NSUB = int(_os.environ.get("MK_NSUB", "16"))
    SNAP = bool(_os.environ.get("MK_SNAP"))
    snaps = []
    state = {"i": 0}

    def snap():
        state["i"] += 1
        if not SNAP:
            return
        i = state["i"]
        sd = nc.dram_tensor(f"snap{i}", [4096, 1024], F, kind="ExternalOutput").ap()
        for tt in range(mk.TT):
            p.emit("pool", lambda e, tt=tt, sd=sd: e.dma_start(
                out=sd[tt * 128:(tt + 1) * 128, :],
                in_=mk.x.ap()[:, tt * 1024:(tt + 1) * 1024]),
                reads=[Region(mk.x, 0, 128, tt * 1024, (tt + 1) * 1024)],
                dma=True)
    seq = [
        lambda: l0_ffn_apply(mk, T1d, T2d, scols),
        lambda: attn_apply(mk, l0a, 0),
        lambda: attn_apply(mk, l0a, 1),
        lambda: l0_ffn_apply(mk, T1d, T2d, scols),
        lambda: moe_apply(mk, l1m, xpage),
        lambda: l1_attn_apply(mk, l1at, 0),
        lambda: l1_attn_apply(mk, l1at, 1),
        lambda: moe_apply(mk, l1m, xpage),
        lambda: moe_apply(mk, l2m, xpage),
        lambda: mamba_apply(mk, l2at, 1 if _os.environ.get("MM_SWAP") else 0),
        lambda: mamba_apply(mk, l2at, 0 if _os.environ.get("MM_SWAP") else 1),
        lambda: moe_apply(mk, l2m, xpage),
        lambda: l3_ffn_apply(mk, wqd),
        lambda: attn_apply(mk, l3a, 0),
        lambda: attn_apply(mk, l3a, 1),
        lambda: l3_ffn_apply(mk, wqd),
    ]
    for fi, f in enumerate(seq):
        if fi < NSUB:
            f()
            snap()
    for tt in range(mk.TT):
        p.emit("pool", lambda e, tt=tt: e.dma_start(
            out=xout[tt * 128:(tt + 1) * 128, :],
            in_=mk.x.ap()[:, tt * 1024:(tt + 1) * 1024]),
            reads=[Region(mk.x, 0, 128, tt * 1024, (tt + 1) * 1024)], dma=True)
    return p


def _l3_perm():
    perm = np.empty(3072, np.int64)
    for sec in range(3):
        for h in range(16):
            for d_ in range(64):
                perm[sec * 1024 + h * 64 + d_] = h * 192 + sec * 64 + d_
    return perm


def host_inputs(core, kw):
    C = np.ascontiguousarray
    HEAD_DIM = 64
    inv_freq = 1.0 / 10000.0 ** (np.arange(0, HEAD_DIM, 2) / HEAD_DIM)
    cos_t = np.cos(np.arange(16)[:, None] * inv_freq[None, :]).astype(np.float32)
    sin_t = np.sin(np.arange(16)[:, None] * inv_freq[None, :]).astype(np.float32)
    ropeT = np.zeros((8, 128, 128), np.float32)
    for c in range(8):
        for hh in range(2):
            h = 2 * c + hh
            Rm = np.zeros((64, 64), np.float32)
            for i in range(32):
                Rm[i, i] = cos_t[h, i]; Rm[i, i + 32] = -sin_t[h, i]
                Rm[i + 32, i + 32] = cos_t[h, i]; Rm[i + 32, i] = sin_t[h, i]
            ropeT[c, hh * 64:(hh + 1) * 64, hh * 64:(hh + 1) * 64] = Rm.T
    E8 = np.zeros((8, 128), np.float32)
    for b in range(8):
        E8[b, b * 16:(b + 1) * 16] = 1.0
    sel8 = np.zeros((8, 1024), np.float32)
    for e in range(8):
        sel8[e, e * 128:(e + 1) * 128] = 1.0
    sel96 = np.zeros((96, 4096), np.float32)
    for n in range(16):
        sel96[64 + n, n * 128:(n + 1) * 128] = 1.0
        sel96[80 + n, (16 + n) * 128:(17 + n) * 128] = 1.0
    W_in = kw["L2_attn_W_in"]
    W_conv = kw["L2_attn_W_conv"]
    ins = {
        "xin": C(kw["L0_attn_x"][2 * core:2 * core + 2].reshape(4096, 1024)),
        "iden": np.eye(128, dtype=np.float32),
        "E8": E8, "sel8": sel8, "sel96": sel96, "ropeT": ropeT,
        "l0_w1T": C(kw["L0_ffn_w1"].T), "l0_w2T": C(kw["L0_ffn_w2"].T),
        "l0_wqkv": kw["L0_attn_w_qkv"],
        "l0_bqc": C(kw["L0_attn_b_qkv"].reshape(24, 128).T),
        "l0_bqr": kw["L0_attn_b_qkv"][None, :],
        "l0_wproj": kw["L0_attn_w_proj"],
        "l0_bpr": kw["L0_attn_b_proj"][None, :],
        "l0_a_q": kw["L0_attn_a_q"], "l0_b_q": kw["L0_attn_b_q"],
        "l0_a_v": kw["L0_attn_a_v"], "l0_b_v": kw["L0_attn_b_v"],
        "l1_wgT": C(kw["L1_ffn_w_gate"].T),
        "l1_winT": C(kw["L1_ffn_w_in"].transpose(0, 2, 1)),
        "l1_woutT": C(kw["L1_ffn_w_out"].transpose(0, 2, 1)),
        "l1_wqT": C(kw["L1_attn_Wq"].T), "l1_wkT": C(kw["L1_attn_Wk"].T),
        "l1_wvT": C(kw["L1_attn_Wv"].T), "l1_woT": C(kw["L1_attn_Wo"].T),
        "l1_bo": kw["L1_attn_bo"][None, :],
        "l2_router": kw["L2_ffn_router_w"],
        "l2_lnw": kw["L2_ffn_ln_weight"][None, :],
        "l2_w1": kw["L2_ffn_w1"], "l2_w3": kw["L2_ffn_w3"],
        "l2_w2": kw["L2_ffn_w2"],
        "l2_winx": C(W_in[:, 0::2]), "l2_winz": C(W_in[:, 1::2]),
        "l2_wconv": C(W_conv[:, 0, :].T.reshape(16, 128, 4)
                      .transpose(1, 0, 2).reshape(128, 64)),
        "l2_bconv": C(kw["L2_attn_b_conv"].reshape(16, 128).T),
        "l2_wx": kw["L2_attn_W_x"], "l2_wdt": kw["L2_attn_W_dt"],
        "l2_bdt": C(kw["L2_attn_b_dt"].reshape(16, 128).T),
        "l2_A": C(kw["L2_attn_A_log"].reshape(16, 128, 16)
                  .transpose(1, 0, 2).reshape(128, 256)),
        "l2_D": C(kw["L2_attn_D_param"].reshape(16, 128).T),
        "l2_wout": kw["L2_attn_W_out"],
        "l3_qkvw": C(kw["L3_attn_qkv_w"][:, _l3_perm()]),
        "l3_qkvbc": C(kw["L3_attn_qkv_b"][_l3_perm()].reshape(24, 128).T),
        "l3_qkvbr": kw["L3_attn_qkv_b"][_l3_perm()][None, :],
        "l3_outw": kw["L3_attn_out_w"],
        "l3_outbr": kw["L3_attn_out_b"][None, :],
        "l3_wg": kw["L3_ffn_w_gate"], "l3_wu": kw["L3_ffn_w_up"],
        "l3_wd": kw["L3_ffn_w_down"],
        "l3_wgT": C(kw["L3_ffn_w_gate"].T), "l3_wuT": C(kw["L3_ffn_w_up"].T),
        "l3_wdT": C(kw["L3_ffn_w_down"].T),
    }
    return {k: np.ascontiguousarray(v, dtype=np.float32) for k, v in ins.items()}


def kernel(**inputs):
    p = build_program()
    ncf = p.finalize()
    shared = host_inputs(0, inputs)
    in_maps = [shared]
    for core in range(1, 8):
        m = dict(shared)
        m["xin"] = np.ascontiguousarray(
            inputs["L0_attn_x"][2 * core:2 * core + 2].reshape(4096, 1024),
            dtype=np.float32)
        in_maps.append(m)
    res = run_bass_kernel_spmd(ncf, in_maps, core_ids=list(range(8)))
    out = np.empty((16, 2048, 1024), np.float32)
    for core in range(8):
        out[2 * core:2 * core + 2] = \
            res.results[core]["xout"].reshape(2, 2048, 1024)
    return out


# version 10
# speedup vs baseline: 1.2403x; 1.2403x over previous
import math
import sys as _sys
import numpy as np
import concourse.bass as bass
import concourse.mybir as mybir
from concourse.bass_utils import run_bass_kernel_spmd

F32 = mybir.dt.float32
BF16 = mybir.dt.bfloat16
AF = mybir.ActivationFunctionType
ALU = mybir.AluOpType
AX = mybir.AxisListType


STREAMS = ("pe", "act", "dve", "pool", "sp")
ENG_ATTR = {"pe": "tensor", "act": "scalar", "dve": "vector",
            "pool": "gpsimd", "sp": "sync"}
DMA_QUEUES = {"sp": "q_sp", "act": "q_act", "pool": "q_pool"}


def _overlap(a, b):
    return a[0] < b[1] and b[0] < a[1] and a[2] < b[3] and b[2] < a[3]


class Region:
    __slots__ = ("t", "box")

    def __init__(self, t, p0, p1, f0, f1):
        self.t = t
        self.box = (p0, p1, f0, f1)


def R(t, p0=0, p1=128, f0=0, f1=None):
    if f1 is None:
        f1 = 1 << 40
    return Region(t, p0, p1, f0, f1)


class Prog:
    def __init__(self, name="mk"):
        self.nc = bass.Bass("TRN2", target_bir_lowering=False)
        self.ops = {s: [] for s in STREAMS}
        self.meta = {s: [] for s in STREAMS}
        self.sems = {}
        for s in STREAMS:
            self.sems[s] = self.nc.alloc_semaphore(name=f"c_{s}")
        for q in DMA_QUEUES.values():
            self.sems[q] = self.nc.alloc_semaphore(name=q)
        self.cnt = {k: 0 for k in self.sems}
        self.vc = {s: {} for s in STREAMS}
        self.vc_shared = {s: False for s in STREAMS}
        self.ckpt = {k: {} for k in self.sems}
        self.recs = {}
        self.n_waits = 0
        self.n_ops = 0
        self._const_cols = {}
        self._const_pool = None
        self._const_used = 0
        self._const_row_pool = None
        self._const_row_used = 0
        self._const_fenced = False

    def fence_consts(self, extra_regions=()):
        regs = list(extra_regions)
        if self._const_pool is not None:
            regs.append(R(self._const_pool))
        if self._const_row_pool is not None:
            regs.append(Region(self._const_row_pool, 0, 1, 0, 512))
        for s in STREAMS:
            self.emit(s, lambda e: e.nop(), reads=regs)
        self._const_fenced = True

    def sbuf(self, name, shape, dtype):
        return self.nc.alloc_sbuf_tensor(name, shape, dtype)

    def psum(self, name, shape, dtype=F32):
        return self.nc.alloc_psum_tensor(name, shape, dtype)

    def dram(self, name, shape, dtype, kind="Internal"):
        return self.nc.dram_tensor(name, shape, dtype, kind=kind)

    def const_col(self, val):
        if val in self._const_cols:
            return self._const_cols[val]
        assert not self._const_fenced, f"const_col({val}) after fence"
        if self._const_pool is None:
            self._const_pool = self.sbuf("const_pool", [128, 64], F32)
        i = self._const_used
        assert i < 64, "const pool exhausted"
        self._const_used += 1
        ap = self._const_pool.ap()[:, i:i + 1]
        self.emit("dve", lambda e, ap=ap, v=float(val): e.memset(ap, v),
                  writes=[Region(self._const_pool, 0, 128, i, i + 1)])
        self._const_cols[val] = ap
        return ap

    def const_row(self, val):
        key = ("row", val)
        if key in self._const_cols:
            return self._const_cols[key]
        assert not self._const_fenced, "const_row after fence"
        if self._const_row_pool is None:
            self._const_row_pool = self.sbuf("const_rows", [1, 512], F32)
        i = self._const_row_used
        assert i < 4
        self._const_row_used += 1
        ap = self._const_row_pool.ap()[0:1, i * 128:(i + 1) * 128]
        self.emit("dve", lambda e, ap=ap, v=float(val): e.memset(ap, v),
                  writes=[Region(self._const_row_pool, 0, 1, i * 128, (i + 1) * 128)])
        self._const_cols[key] = ap
        return ap

    def _freeze_vc(self, s):
        self.vc_shared[s] = True
        return self.vc[s]

    def _own_vc(self, s):
        if self.vc_shared[s]:
            self.vc[s] = dict(self.vc[s])
            self.vc_shared[s] = False
        return self.vc[s]

    def _need_edges(self, s, regions, kinds):
        edges = {}
        for reg, kind in zip(regions, kinds):
            rec = self.recs.get(id(reg.t))
            if rec is None:
                continue
            confl = rec["w"] if kind == "r" else rec["w"] + rec["r"]
            for box, sem, val in confl:
                if _overlap(box, reg.box):
                    if edges.get(sem, 0) < val:
                        edges[sem] = val
        vc = self.vc[s]
        out = {m: v for m, v in edges.items() if vc.get(m, 0) < v}
        return out

    def _apply_edges(self, s, edges):
        if not edges:
            return
        vc = self._own_vc(s)
        for m, v in edges.items():
            ck = self.ckpt[m].get(v)
            if ck:
                for m2, v2 in ck.items():
                    if vc.get(m2, 0) < v2:
                        vc[m2] = v2
            if vc.get(m, 0) < v:
                vc[m] = v
        self.n_waits += len(edges)

    def _record(self, regions, kinds, sem, val):
        for reg, kind in zip(regions, kinds):
            rec = self.recs.setdefault(id(reg.t), {"w": [], "r": []})
            if kind == "w":
                box = reg.box
                rec["w"] = [e for e in rec["w"] if not _covers(box, e[0])]
                rec["r"] = [e for e in rec["r"] if not _covers(box, e[0])]
                rec["w"].append((box, sem, val))
            else:
                rec["r"].append((reg.box, sem, val))

    def emit(self, s, fn, reads=(), writes=(), dma=False):
        self.meta[s].append(_sys._getframe(1).f_lineno)
        regions = list(reads) + list(writes)
        kinds = ["r"] * len(reads) + ["w"] * len(writes)
        edges = self._need_edges(s, regions, kinds)
        if dma:
            q = DMA_QUEUES[s]
            prev = self.cnt[q]
            if prev and self.vc[s].get(q, 0) < prev:
                edges[q] = max(edges.get(q, 0), prev)
            sem, amt = q, 16
        else:
            sem, amt = s, 1
        self._apply_edges(s, edges)
        self.cnt[sem] += amt
        val = self.cnt[sem]
        self.ckpt[sem][val] = self._freeze_vc(s)
        self._record(regions, kinds, sem, val)
        semobj = {m: self.sems[m] for m in edges}
        mys = self.sems[sem]
        wl = sorted(edges.items())

        def op(eng, fn=fn, wl=wl, semobj=semobj, mys=mys, amt=amt):
            for m, v in wl:
                eng.wait_ge(semobj[m], v)
            inst = fn(eng)
            inst.then_inc(mys, amt)
        self.ops[s].append(op)
        self.n_ops += 1
        return val

    def barrier_val(self, stream_or_sem):
        return self.cnt[stream_or_sem]

    def finalize(self, output_dmas=None):
        nc = self.nc
        final = {q: self.cnt[q] for q in DMA_QUEUES.values() if self.cnt[q]}

        def fin(eng, final=final):
            for m, v in final.items():
                eng.wait_ge(self.sems[m], v)
        self.ops["sp"].append(fin)
        with nc.Block() as block:
            for s in STREAMS:
                getattr(block, ENG_ATTR[s])(_runner(self.ops[s]))
        return nc


def _covers(a, b):
    return a[0] <= b[0] and a[1] >= b[1] and a[2] <= b[2] and a[3] >= b[3]


def _runner(ops):
    def _(eng):
        for f in ops:
            f(eng)
    return _



MAGIC = float(3 << 22)
D = 1024
ND = D // 128


class Arena:

    def __init__(self, t, n):
        self.t = t
        self.n = n
        self.off = 0
        self.marks = []

    def alloc(self, n):
        assert self.off + n <= self.n, f"arena overflow {self.off}+{n}>{self.n}"
        o = self.off
        self.off += n
        return o

    def save(self):
        self.marks.append(self.off)

    def restore(self):
        self.off = self.marks.pop()

    def reset(self):
        self.off = 0
        self.marks = []


class View:
    __slots__ = ("t", "o", "n")

    def __init__(self, t, o, n):
        self.t, self.o, self.n = t, o, n

    def ap(self, f0=0, f1=None, p0=0, p1=128):
        if f1 is None:
            f1 = self.n
        return self.t.ap()[p0:p1, self.o + f0:self.o + f1]

    def reg(self, f0=0, f1=None, p0=0, p1=128):
        if f1 is None:
            f1 = self.n
        return Region(self.t, p0, p1, self.o + f0, self.o + f1)


class BView:
    __slots__ = ("t", "o", "n")

    def __init__(self, t, o, n):
        self.t, self.o, self.n = t, o, n

    def ap(self, f0=0, f1=None, p0=0, p1=128):
        if f1 is None:
            f1 = self.n
        return self.t.ap().bitcast(BF16)[p0:p1, self.o + f0:self.o + f1]

    def reg(self, f0=0, f1=None, p0=0, p1=128):
        if f1 is None:
            f1 = self.n
        return Region(self.t, p0, p1, (self.o + f0) // 2,
                      (self.o + f1 + 1) // 2)


class MK:

    def __init__(self, p: Prog, T_elem=2048, n_elem=2):
        self.p = p
        self.T = T_elem
        self.NE = n_elem
        self.NT = T_elem * n_elem
        self.TT = self.NT // 128
        self.TTE = T_elem // 128
        nc = p.nc
        self.x = p.sbuf("x_res", [128, self.TT * D], BF16)
        self.ar16_t = p.sbuf("ar16", [128, 34304], BF16)
        self.ar32_t = p.sbuf("ar32", [128, 17408], F32)
        self.small_t = p.sbuf("small", [128, 960], F32)
        self.ident = p.sbuf("ident", [128, 128], BF16)
        self.id32 = p.sbuf("id32m", [128, 128], F32)
        self.ones_bf = p.sbuf("ones_bf", [128, 128], BF16)
        self.ar16 = Arena(self.ar16_t, 34304)
        self.ar32 = Arena(self.ar32_t, 17408)
        self.small = Arena(self.small_t, 960)
        self.ps = [p.psum(f"ps{i}", [128, 512], F32) for i in range(8)]

    def v16(self, n):
        return View(self.ar16_t, self.ar16.alloc(n), n)

    def v16_in32(self, n):
        assert n % 2 == 0
        o = self.ar32.alloc(n // 2)
        return BView(self.ar32_t, o * 2, n)

    def v32(self, n):
        return View(self.ar32_t, self.ar32.alloc(n), n)

    def vsm(self, n):
        return View(self.small_t, self.small.alloc(n), n)

    CONST_VALS = (1.0, -0.5, 1e-5, 0.125, -1.0, 0.5, 127.0,
                  1.0 / 127.0, 1.0 / 1024.0, 0.3535533905932738, 1e-6)

    def load_const_tiles(self, identd, onesd=None):
        p = self.p
        p.emit("pool", lambda e: e.dma_start(out=self.ident.ap()[:, :],
                                             in_=identd[:, :]),
               writes=[R(self.ident)], dma=True)
        p.emit("sp", lambda e: e.dma_start(out=self.id32.ap()[:, :],
                                           in_=identd[:, :]),
               writes=[R(self.id32)], dma=True)
        p.emit("dve", lambda e: e.memset(self.ones_bf.ap()[:, :], 1.0),
               writes=[R(self.ones_bf)])
        for v in self.CONST_VALS:
            p.const_col(v)
        p.const_row(1.0)
        p.fence_consts([R(self.ident), R(self.id32), R(self.ones_bf)])

    def transpose8(self, src_fn, dst_ap, dst_reg, reads, pb=0, n=8):
        p = self.p
        bank = self.ps[pb]
        bap = bank.ap().bitcast(BF16)

        def tp(t):
            last = None
            for i in range(n):
                last = t.transpose(bap[:, i * 128:(i + 1) * 128],
                                   src_fn(i), self.ident.ap()[:, :])
            return last
        p.emit("pe", tp, reads=list(reads) + [R(self.ident)], writes=[R(bank)])
        p.emit("dve", lambda e: e.tensor_copy(dst_ap, bap[:, 0:n * 128]),
               reads=[R(bank)], writes=[dst_reg])

    def page_x(self, xp, direction, order=None):
        p = self.p
        tiles = list(order) if order is not None else list(range(self.TT))
        for tt in tiles:
            fs = (tt * D, (tt + 1) * D)
            if direction == "out":
                p.emit("sp", lambda e, fs=fs: e.dma_start(
                    out=xp.ap()[:, fs[0]:fs[1]],
                    in_=self.x.ap()[:, fs[0]:fs[1]]),
                    reads=[Region(self.x, 0, 128, *fs)],
                    writes=[Region(xp, 0, 128, *fs)], dma=True)
            else:
                p.emit("sp", lambda e, fs=fs: e.dma_start(
                    out=self.x.ap()[:, fs[0]:fs[1]],
                    in_=xp.ap()[:, fs[0]:fs[1]]),
                    reads=[Region(xp, 0, 128, *fs)],
                    writes=[Region(self.x, 0, 128, *fs)], dma=True)

    def ln_stats(self, tt_tiles, rstd_v, negm_v, scale_extra=None):
        p = self.p
        x = self.x
        n = len(tt_tiles)
        self.ar16.save(); self.small.save()
        sq = self.vsm(n)
        sm = negm_v
        var = self.vsm(n)
        tmps = [self.v16(1024) for _ in range(2)]
        for j, tt in enumerate(tt_tiles):
            xa = x.ap()[:, tt * D:(tt + 1) * D]
            xr = Region(x, 0, 128, tt * D, (tt + 1) * D)
            tmp16 = tmps[j % 2]
            p.emit("act", lambda e, xa=xa, tmp16=tmp16, j=j: e.activation(
                tmp16.ap(), xa, AF.Square, accum_out=sq.ap(j, j + 1)),
                reads=[xr], writes=[tmp16.reg(), sq.reg(j, j + 1)])
            p.emit("dve", lambda e, xa=xa, j=j: e.reduce_sum(
                sm.ap(j, j + 1), xa, axis=AX.X, negate=True),
                reads=[xr], writes=[sm.reg(j, j + 1)])
        p.emit("dve", lambda e: e.tensor_single_scalar(
            sm.ap(0, n), sm.ap(0, n), 1.0 / D, ALU.mult),
            reads=[sm.reg(0, n)], writes=[sm.reg(0, n)])
        p.emit("dve", lambda e: e.tensor_tensor(
            var.ap(0, n), sm.ap(0, n), sm.ap(0, n), ALU.mult),
            reads=[sm.reg(0, n)], writes=[var.reg(0, n)])
        p.emit("dve", lambda e: e.scalar_tensor_tensor(
            var.ap(0, n), sq.ap(0, n), 1.0 / D, var.ap(0, n),
            ALU.mult, ALU.subtract),
            reads=[sq.reg(0, n), var.reg(0, n)], writes=[var.reg(0, n)])
        c_eps = self.p.const_col(1e-5)
        c_half = self.p.const_col(-0.5)
        p.emit("act", lambda e: e.activation(
            var.ap(0, n), var.ap(0, n), AF.Ln, bias=c_eps),
            reads=[var.reg(0, n)], writes=[var.reg(0, n)])
        p.emit("act", lambda e: e.activation(
            rstd_v.ap(0, n), var.ap(0, n), AF.Exp, scale=c_half),
            reads=[var.reg(0, n)], writes=[rstd_v.reg(0, n)])
        self.ar16.restore(); self.small.restore()

    def ln_apply(self, tt, negm_v, rstd_v, out_ap, out_reg, j=None):
        if j is None:
            j = tt
        x = self.x
        xa = x.ap()[:, tt * D:(tt + 1) * D]
        xr = Region(x, 0, 128, tt * D, (tt + 1) * D)
        self.p.emit("dve", lambda e: e.tensor_scalar(
            out_ap, xa, negm_v.ap(j, j + 1), rstd_v.ap(j, j + 1),
            ALU.add, ALU.mult),
            reads=[xr, negm_v.reg(j, j + 1), rstd_v.reg(j, j + 1)],
            writes=[out_reg])




def l0_ffn_prepare(mk, w1T, w2T):
    p = mk.p
    T1d = p.dram("T1d", [1024, 4096], BF16)
    T2d = p.dram("T2d", [4096, 1024], BF16)
    pers = p.sbuf("l0_pers", [128, 2], F32)

    def one(wi, wd, Td, rows, cols):
        mk.ar32.save(); mk.ar16.save(); mk.small.save()
        tiles = [mk.v32(cols) for _ in range(2)]
        junk = mk.v16(cols)
        acc = mk.vsm(1)
        ssum = mk.vsm(1)
        nt = rows // 128
        p.emit("dve", lambda e: e.memset(acc.ap(), 0.0), writes=[acc.reg()])

        def pass1(r):
            tile = tiles[r % 2]
            p.emit("sp", lambda e: e.dma_start(
                out=tile.ap(), in_=wd[r * 128:(r + 1) * 128, :]),
                writes=[tile.reg()], dma=True)
            p.emit("act", lambda e: e.activation(
                junk.ap(), tile.ap(), AF.Abs, accum_out=ssum.ap()),
                reads=[tile.reg()], writes=[junk.reg(), ssum.reg()])
            p.emit("dve", lambda e: e.tensor_tensor(
                acc.ap(), acc.ap(), ssum.ap(), ALU.add),
                reads=[acc.reg(), ssum.reg()], writes=[acc.reg()])
        for r in range(nt):
            pass1(r)
        ps = mk.ps[0]
        ones_c = p.const_col(1.0)
        ones_r = p.const_row(1.0)
        p.emit("pe", lambda e: e.matmul(
            ps.ap()[0:1, 0:1], acc.ap(), ones_c,
            start=True, stop=True),
            reads=[acc.reg()], writes=[R(ps)])
        s_sc = mk.vsm(1)
        rc = float(rows * cols)
        p.emit("dve", lambda e: e.tensor_scalar(
            s_sc.ap(p0=0, p1=1), ps.ap()[0:1, 0:1], 1.0 / rc, 1e-5,
            ALU.mult, ALU.max),
            reads=[R(ps)], writes=[s_sc.reg(p0=0, p1=1)])
        rinv = mk.vsm(1)
        p.emit("dve", lambda e: e.reciprocal(
            rinv.ap(p0=0, p1=1), s_sc.ap(p0=0, p1=1)),
            reads=[s_sc.reg(p0=0, p1=1)], writes=[rinv.reg(p0=0, p1=1)])
        rcol = mk.vsm(1)
        scol = View(pers, wi, 1)

        def bcast(src, dst):
            p.emit("pe", lambda e: e.matmul(
                ps.ap()[:, 0:1], ones_r, src.ap(p0=0, p1=1),
                start=True, stop=True),
                reads=[src.reg(p0=0, p1=1)], writes=[R(ps)])
            p.emit("dve", lambda e: e.tensor_copy(dst.ap(), ps.ap()[:, 0:1]),
                   reads=[R(ps)], writes=[dst.reg()])
        bcast(rinv, rcol)
        bcast(s_sc, scol)

        def pass2(r):
            tile = tiles[r % 2]
            p.emit("sp", lambda e: e.dma_start(
                out=tile.ap(), in_=wd[r * 128:(r + 1) * 128, :]),
                writes=[tile.reg()], dma=True)
            p.emit("dve", lambda e: e.tensor_scalar(
                tile.ap(), tile.ap(), rcol.ap(), MAGIC, ALU.mult, ALU.add),
                reads=[tile.reg(), rcol.reg()], writes=[tile.reg()])
            p.emit("dve", lambda e: e.tensor_single_scalar(
                tile.ap(), tile.ap(), -MAGIC, ALU.add),
                reads=[tile.reg()], writes=[tile.reg()])
            p.emit("dve", lambda e: e.tensor_scalar(
                junk.ap(), tile.ap(), 1.0, -1.0, ALU.min, ALU.max),
                reads=[tile.reg()], writes=[junk.reg()])
            p.emit("sp", lambda e: e.dma_start(
                out=Td.ap()[r * 128:(r + 1) * 128, :], in_=junk.ap()),
                reads=[junk.reg()], writes=[Region(Td, 0, 128, r, r + 1)],
                dma=True)
        for r in range(nt):
            pass2(r)
        mk.ar32.restore(); mk.ar16.restore(); mk.small.restore()
        return scol

    s1 = one(0, w1T, T1d, 1024, 4096)
    s2 = one(1, w2T, T2d, 4096, 1024)
    return T1d, T2d, [s1, s2]


def l0_ffn_apply(mk, T1d, T2d, scols):
    p = mk.p
    mk.ar16.reset(); mk.ar32.reset(); mk.small.reset()
    s1c, s2c = scols
    NTT = mk.TT
    negm = mk.vsm(NTT)
    rstd = mk.vsm(NTT)
    mk.ln_stats(list(range(NTT)), rstd, negm)
    xqT = mk.v16(4096)
    hqT = [mk.v16(4096) for _ in range(4)]
    w1r = [mk.v16(4096) for _ in range(2)]
    w2r = [mk.v16(2048) for _ in range(2)]
    hbuf = mk.v32(16384)

    def do_group(g):
        tts = [g * 4 + i for i in range(4)]
        mk.small.save()
        deq1 = mk.vsm(4)
        deq2 = mk.vsm(4)

        def xquant(i, tt):
            mk.ar32.save(); mk.ar16.save(); mk.small.save()
            t = mk.v32(1024)
            xq = mk.v16(1024)
            mxn = mk.vsm(1)
            scq = mk.vsm(1)
            xa = mk.x.ap()[:, tt * D:(tt + 1) * D]
            xr = Region(mk.x, 0, 128, tt * D, (tt + 1) * D)
            p.emit("dve", lambda e: e.tensor_scalar(
                t.ap(), xa, negm.ap(tt, tt + 1), None, ALU.add),
                reads=[xr, negm.reg(tt, tt + 1)], writes=[t.reg()])
            p.emit("dve", lambda e: e.reduce_max(
                mxn.ap(), t.ap(), axis=AX.X, apply_absolute_value=True),
                reads=[t.reg()], writes=[mxn.reg()])
            p.emit("dve", lambda e: e.tensor_scalar(
                mxn.ap(), mxn.ap(), rstd.ap(tt, tt + 1), 1e-5,
                ALU.mult, ALU.max),
                reads=[mxn.reg(), rstd.reg(tt, tt + 1)], writes=[mxn.reg()])
            p.emit("dve", lambda e: e.reciprocal(
                scq.ap(), mxn.ap()), reads=[mxn.reg()], writes=[scq.reg()])
            p.emit("dve", lambda e: e.tensor_scalar(
                scq.ap(), scq.ap(), rstd.ap(tt, tt + 1), 127.0,
                ALU.mult, ALU.mult),
                reads=[scq.reg(), rstd.reg(tt, tt + 1)], writes=[scq.reg()])
            p.emit("dve", lambda e: e.tensor_scalar(
                deq1.ap(i, i + 1), mxn.ap(), s1c.ap(), 1.0 / 127.0,
                ALU.mult, ALU.mult),
                reads=[mxn.reg(), s1c.reg()], writes=[deq1.reg(i, i + 1)])
            p.emit("dve", lambda e: e.tensor_scalar(
                t.ap(), t.ap(), scq.ap(), MAGIC, ALU.mult, ALU.add),
                reads=[t.reg(), scq.reg()], writes=[t.reg()])
            p.emit("dve", lambda e: e.tensor_single_scalar(
                xq.ap(), t.ap(), -MAGIC, ALU.add),
                reads=[t.reg()], writes=[xq.reg()])
            dst = mk.ar16_t.ap()[:, xqT.o:xqT.o + 4096].rearrange(
                "p (c t) -> p c t", c=8)[:, :, i * 128:(i + 1) * 128]
            mk.transpose8(lambda dc: xq.ap(dc * 128, (dc + 1) * 128),
                          dst, xqT.reg(), [xq.reg()], pb=i % 2)
            mk.ar32.restore(); mk.ar16.restore(); mk.small.restore()
        for i, tt in enumerate(tts):
            xquant(i, tt)

        def hblock(hb):
            wv = w1r[hb % 2]
            p.emit("pool", lambda e: e.dma_start(
                out=wv.ap().rearrange("p (c h) -> p c h", c=8),
                in_=T1d.ap().rearrange("(c p) h -> p c h", p=128)
                [:, :, hb * 512:(hb + 1) * 512]),
                reads=[R(T1d)], writes=[wv.reg()], dma=True)

            def one_tt(i):
                bank = mk.ps[4 + i % 2]

                def mmh(t):
                    last = None
                    for dc in range(8):
                        last = t.matmul(
                            bank.ap()[:, :],
                            xqT.ap(dc * 512 + i * 128, dc * 512 + (i + 1) * 128),
                            wv.ap(dc * 512, (dc + 1) * 512),
                            start=(dc == 0), stop=(dc == 7))
                    return last
                p.emit("pe", mmh, reads=[xqT.reg(), wv.reg()], writes=[R(bank)])
                hs = (i * 4096 + hb * 512, i * 4096 + (hb + 1) * 512)
                p.emit("act", lambda e: e.activation(
                    hbuf.ap(*hs), bank.ap()[:, :], AF.Gelu,
                    scale=deq1.ap(i, i + 1)),
                    reads=[R(bank), deq1.reg(i, i + 1)], writes=[hbuf.reg(*hs)])
            for i in range(4):
                one_tt(i)
        for hb in range(8):
            hblock(hb)

        def hquant(i, tt):
            mk.small.save()
            hs = (i * 4096, (i + 1) * 4096)
            mxh = mk.vsm(1)
            sch = mk.vsm(1)
            hv = View(mk.ar16_t, xqT.o, 4096)
            p.emit("dve", lambda e: e.reduce_max(
                mxh.ap(), hbuf.ap(*hs), axis=AX.X, apply_absolute_value=True),
                reads=[hbuf.reg(*hs)], writes=[mxh.reg()])
            p.emit("dve", lambda e: e.tensor_single_scalar(
                mxh.ap(), mxh.ap(), 1e-5, ALU.max),
                reads=[mxh.reg()], writes=[mxh.reg()])
            p.emit("dve", lambda e: e.reciprocal(
                sch.ap(), mxh.ap()), reads=[mxh.reg()], writes=[sch.reg()])
            p.emit("dve", lambda e: e.tensor_single_scalar(
                sch.ap(), sch.ap(), 127.0, ALU.mult),
                reads=[sch.reg()], writes=[sch.reg()])
            p.emit("dve", lambda e: e.tensor_scalar(
                deq2.ap(i, i + 1), mxh.ap(), s2c.ap(), 0.5 / 127.0,
                ALU.mult, ALU.mult),
                reads=[mxh.reg(), s2c.reg()], writes=[deq2.reg(i, i + 1)])
            p.emit("dve", lambda e: e.tensor_scalar(
                hbuf.ap(*hs), hbuf.ap(*hs), sch.ap(), MAGIC, ALU.mult, ALU.add),
                reads=[hbuf.reg(*hs), sch.reg()], writes=[hbuf.reg(*hs)])
            p.emit("dve", lambda e: e.tensor_single_scalar(
                hv.ap(), hbuf.ap(*hs), -MAGIC, ALU.add),
                reads=[hbuf.reg(*hs)], writes=[hv.reg()])
            tv = hqT[i]

            def tp4(b4):
                dst = mk.ar16_t.ap()[:, tv.o + b4 * 1024:tv.o + (b4 + 1) * 1024]
                mk.transpose8(
                    lambda k: hv.ap((b4 * 8 + k) * 128, (b4 * 8 + k + 1) * 128),
                    dst,
                    Region(tv.t, 0, 128, tv.o + b4 * 1024, tv.o + (b4 + 1) * 1024),
                    [hv.reg()], pb=4 + b4 % 2)
            for b4 in range(4):
                tp4(b4)
            mk.small.restore()
        for i, tt in enumerate(tts):
            hquant(i, tt)

        def yblock(hb):
            wv = w2r[hb % 2]
            p.emit("pool", lambda e: e.dma_start(
                out=wv.ap().rearrange("p (c h) -> p c h", c=2),
                in_=T2d.ap().rearrange("(c p) o -> p c o", p=128)
                [:, hb * 2:(hb + 1) * 2, :]),
                reads=[R(T2d)], writes=[wv.reg()], dma=True)

            def one(i, ob):
                tv = hqT[i]
                bank = mk.ps[i * 2 + ob]

                def mmy(t):
                    last = None
                    for hc in range(2):
                        last = t.matmul(
                            bank.ap()[:, :],
                            tv.ap((hb * 2 + hc) * 128, (hb * 2 + hc + 1) * 128),
                            wv.ap(hc * 1024 + ob * 512, hc * 1024 + (ob + 1) * 512),
                            start=(hb == 0 and hc == 0),
                            stop=(hb == 15 and hc == 1))
                    return last
                p.emit("pe", mmy, reads=[tv.reg(), wv.reg()], writes=[R(bank)])
            for i in range(4):
                for ob in range(2):
                    one(i, ob)
        for hb in range(16):
            yblock(hb)

        def evac(i, tt, ob):
            bank = mk.ps[i * 2 + ob]
            xs = (tt * D + ob * 512, tt * D + (ob + 1) * 512)
            xa = mk.x.ap()[:, xs[0]:xs[1]]
            xreg = Region(mk.x, 0, 128, *xs)
            p.emit("dve", lambda e: e.scalar_tensor_tensor(
                xa, bank.ap()[:, :], deq2.ap(i, i + 1), xa,
                ALU.mult, ALU.add),
                reads=[R(bank), deq2.reg(i, i + 1), xreg], writes=[xreg])
        for i, tt in enumerate(tts):
            for ob in range(2):
                evac(i, tt, ob)
        mk.small.restore()
    for g in range(NTT // 4):
        do_group(g)


def _fp4_qabs_chain(p, y_v, q_v, engs=("dve",)):
    k = [0]

    def E():
        k[0] += 1
        return engs[k[0] % len(engs)]
    acc_a = q_v
    p.emit(E(), lambda e: e.tensor_single_scalar(
        acc_a.ap(), y_v.ap(), 0.25, ALU.is_gt),
        reads=[y_v.reg()], writes=[acc_a.reg()])
    for m in (0.75, 1.25, 1.75):
        p.emit(E(), lambda e, m=m: e.scalar_tensor_tensor(
            acc_a.ap(), y_v.ap(), m, acc_a.ap(), ALU.is_gt, ALU.add),
            reads=[y_v.reg(), acc_a.reg()], writes=[acc_a.reg()])
    p.emit(E(), lambda e: e.tensor_single_scalar(
        acc_a.ap(), acc_a.ap(), 0.5, ALU.mult),
        reads=[acc_a.reg()], writes=[acc_a.reg()])
    for m in (2.5, 3.5):
        p.emit(E(), lambda e, m=m: e.scalar_tensor_tensor(
            acc_a.ap(), y_v.ap(), m, acc_a.ap(), ALU.is_gt, ALU.add),
            reads=[y_v.reg(), acc_a.reg()], writes=[acc_a.reg()])
    p.emit(E(), lambda e: e.tensor_single_scalar(
        y_v.ap(), y_v.ap(), 5.0, ALU.is_gt),
        reads=[y_v.reg()], writes=[y_v.reg()])
    p.emit(E(), lambda e: e.scalar_tensor_tensor(
        q_v.ap(), y_v.ap(), 2.0, acc_a.ap(), ALU.mult, ALU.add),
        reads=[y_v.reg(), acc_a.reg()], writes=[q_v.reg()])


def l3_ffn_prepare(mk, nat, trans, E8d, ident32d):
    p = mk.p
    E8 = p.sbuf("E8s", [8, 128], F32)
    id32 = p.sbuf("id32", [128, 128], F32)
    p.emit("sp", lambda e: e.dma_start(out=E8.ap()[:, :], in_=E8d[:, :]),
           writes=[R(E8)], dma=True)
    p.emit("sp", lambda e: e.dma_start(out=id32.ap()[:, :], in_=ident32d[:, :]),
           writes=[R(id32)], dma=True)
    out = {}
    shapes = {"wg": (2816, 1024), "wu": (2816, 1024), "wd": (1024, 2816)}

    def one_matrix(nm):
        N, Kd = shapes[nm]
        wq = p.dram(f"l3q_{nm}", [Kd, N], BF16)
        out[nm] = wq
        nb = Kd // 16
        scT_d = p.dram(f"l3sT_{nm}", [nb, N], F32)
        ivT_d = p.dram(f"l3iT_{nm}", [nb, N], F32)
        mk.ar32.save(); mk.small.save()
        tile = mk.v32(Kd)
        sc = mk.v32(nb)
        iv = mk.v32(nb)
        stg = mk.v32(128)

        def phaseA(r):
            p.emit("sp", lambda e: e.dma_start(
                out=tile.ap(), in_=nat[nm][r * 128:(r + 1) * 128, :]),
                writes=[tile.reg()], dma=True)
            p.emit("dve", lambda e: e.tensor_reduce(
                sc.ap(), tile.ap().rearrange("p (b k) -> p b k", k=16),
                axis=AX.X, op=ALU.max, apply_absolute_value=True),
                reads=[tile.reg()], writes=[sc.reg()])
            p.emit("dve", lambda e: e.tensor_scalar(
                sc.ap(), sc.ap(), 1.0 / 6.0, 1e-12, ALU.mult, ALU.max),
                reads=[sc.reg()], writes=[sc.reg()])
            p.emit("dve", lambda e: e.reciprocal(
                iv.ap(), sc.ap()), reads=[sc.reg()], writes=[iv.reg()])
            for src, dst_d in ((sc, scT_d), (iv, ivT_d)):
                nsp = 0
                while nsp < nb:
                    w = min(128, nb - nsp)
                    bank = mk.ps[0]
                    p.emit("pe", lambda e, src=src, nsp=nsp, w=w: e.matmul(
                        bank.ap()[0:w, 0:128],
                        src.ap(nsp, nsp + w), id32.ap()[:, :],
                        start=True, stop=True),
                        reads=[src.reg(nsp, nsp + w), R(id32)],
                        writes=[R(bank)])
                    p.emit("dve", lambda e, w=w: e.tensor_copy(
                        stg.ap(p1=w), bank.ap()[0:w, 0:128]),
                        reads=[R(bank)], writes=[stg.reg()])
                    p.emit("sp", lambda e, dst_d=dst_d, nsp=nsp, w=w, r=r:
                           e.dma_start(
                        out=dst_d.ap()[nsp:nsp + w, r * 128:(r + 1) * 128],
                        in_=stg.ap(p1=w)),
                        reads=[stg.reg()],
                        writes=[Region(dst_d, nsp, nsp + w, r, r + 1)],
                        dma=True)
                    nsp += w
        for r in range(N // 128):
            phaseA(r)
        mk.ar32.restore(); mk.small.restore()
        mk.ar32.save(); mk.ar16.save()
        wt = mk.v32(N)
        yv = mk.v32(N)
        qv = mk.v32(N)
        s8 = mk.v32(N)
        i8 = mk.v32(N)
        sg = mk.v16(N)
        qo = mk.v16(N)

        def phaseB(kc):
            p.emit("sp", lambda e: e.dma_start(
                out=wt.ap(), in_=trans[nm][kc * 128:(kc + 1) * 128, :]),
                writes=[wt.reg()], dma=True)
            p.emit("sp", lambda e: e.dma_start(
                out=s8.ap(p1=8), in_=scT_d.ap()[kc * 8:(kc + 1) * 8, :]),
                reads=[R(scT_d)], writes=[s8.reg()], dma=True)
            p.emit("sp", lambda e: e.dma_start(
                out=i8.ap(p1=8), in_=ivT_d.ap()[kc * 8:(kc + 1) * 8, :]),
                reads=[R(ivT_d)], writes=[i8.reg()], dma=True)
            ns = 0
            sb = mk.ps[6]
            ib = mk.ps[7]
            while ns < N:
                w = min(512, N - ns)
                p.emit("pe", lambda e, ns=ns, w=w: e.matmul(
                    sb.ap()[:, 0:w], E8.ap()[:, :],
                    s8.ap(ns, ns + w, p1=8),
                    start=True, stop=True),
                    reads=[s8.reg(), R(E8)], writes=[R(sb)])
                p.emit("pe", lambda e, ns=ns, w=w: e.matmul(
                    ib.ap()[:, 0:w], E8.ap()[:, :],
                    i8.ap(ns, ns + w, p1=8),
                    start=True, stop=True),
                    reads=[i8.reg(), R(E8)], writes=[R(ib)])
                p.emit("dve", lambda e, ns=ns, w=w: e.tensor_tensor(
                    yv.ap(ns, ns + w), wt.ap(ns, ns + w), ib.ap()[:, 0:w],
                    ALU.mult),
                    reads=[wt.reg(ns, ns + w), R(ib)],
                    writes=[yv.reg(ns, ns + w)])
                p.emit("dve", lambda e, ns=ns, w=w: e.scalar_tensor_tensor(
                    yv.ap(ns, ns + w), yv.ap(ns, ns + w), -1.0,
                    yv.ap(ns, ns + w), ALU.mult, ALU.max),
                    reads=[yv.reg(ns, ns + w)], writes=[yv.reg(ns, ns + w)])
                _fp4_qabs_chain(p, View(yv.t, yv.o + ns, w),
                                View(qv.t, qv.o + ns, w))
                p.emit("act", lambda e, ns=ns, w=w: e.sign(
                    sg.ap(ns, ns + w), wt.ap(ns, ns + w)),
                    reads=[wt.reg(ns, ns + w)], writes=[sg.reg(ns, ns + w)])
                p.emit("dve", lambda e, ns=ns, w=w: e.tensor_tensor(
                    qv.ap(ns, ns + w), qv.ap(ns, ns + w), sb.ap()[:, 0:w],
                    ALU.mult),
                    reads=[qv.reg(ns, ns + w), R(sb)],
                    writes=[qv.reg(ns, ns + w)])
                p.emit("pool", lambda e, ns=ns, w=w: e.tensor_tensor(
                    qo.ap(ns, ns + w), qv.ap(ns, ns + w), sg.ap(ns, ns + w),
                    ALU.mult),
                    reads=[qv.reg(ns, ns + w), sg.reg(ns, ns + w)],
                    writes=[qo.reg(ns, ns + w)])
                ns += w
            p.emit("sp", lambda e: e.dma_start(
                out=wq.ap()[kc * 128:(kc + 1) * 128, :], in_=qo.ap()),
                reads=[qo.reg()], writes=[Region(wq, 0, 128, kc, kc + 1)],
                dma=True)
        for kc in range(Kd // 128):
            phaseB(kc)
        mk.ar32.restore(); mk.ar16.restore()

    for nm in ("wg", "wu", "wd"):
        one_matrix(nm)
    return out


def l3_ffn_apply(mk, wqd):
    p = mk.p
    mk.ar16.reset(); mk.ar32.reset(); mk.small.reset()
    NTT = mk.TT
    HN = 2816
    NHC = HN // 128
    negm = mk.vsm(NTT)
    rstd = mk.vsm(NTT)
    mk.ln_stats(list(range(NTT)), rstd, negm)
    xqT = mk.v16(4096)
    hqT = [mk.v16(NHC * 128) for _ in range(4)]
    wgr = [mk.v16(2048) for _ in range(2)]
    wur = [mk.v16(2048) for _ in range(2)]
    wdr = [mk.v16(2048) for _ in range(2)]
    hbuf = mk.v32(4 * HN)

    def do_group(g):
        tts = [g * 4 + i for i in range(4)]

        def xquant(i, tt):
            mk.ar32.save(); mk.ar16.save(); mk.small.save()
            t = mk.v32(1024)
            yv = mk.v32(1024)
            qv = mk.v32(1024)
            bs = mk.v32(64)
            iv = mk.v32(64)
            sg16 = mk.v16(1024)
            xq = mk.v16(1024)
            mk.ln_apply(tt, negm, rstd, t.ap(), t.reg())
            p.emit("dve", lambda e: e.tensor_reduce(
                bs.ap(), t.ap().rearrange("p (b k) -> p b k", k=16),
                axis=AX.X, op=ALU.max, apply_absolute_value=True),
                reads=[t.reg()], writes=[bs.reg()])
            p.emit("dve", lambda e: e.tensor_scalar(
                bs.ap(), bs.ap(), 1.0 / 6.0, 1e-12, ALU.mult, ALU.max),
                reads=[bs.reg()], writes=[bs.reg()])
            p.emit("dve", lambda e: e.reciprocal(
                iv.ap(), bs.ap()), reads=[bs.reg()], writes=[iv.reg()])
            ivb = iv.ap().rearrange("p (b o) -> p b o", o=1)\
                .broadcast_to((128, 64, 16))
            bsb = bs.ap().rearrange("p (b o) -> p b o", o=1)\
                .broadcast_to((128, 64, 16))
            tv3 = t.ap().rearrange("p (b k) -> p b k", k=16)
            yv3 = yv.ap().rearrange("p (b k) -> p b k", k=16)
            p.emit("dve", lambda e: e.tensor_tensor(
                yv3, tv3, ivb, ALU.mult),
                reads=[t.reg(), iv.reg()], writes=[yv.reg()])
            p.emit("dve", lambda e: e.scalar_tensor_tensor(
                yv.ap(), yv.ap(), -1.0, yv.ap(), ALU.mult, ALU.max),
                reads=[yv.reg()], writes=[yv.reg()])
            _fp4_qabs_chain(p, yv, qv)
            p.emit("act", lambda e: e.sign(
                sg16.ap(), t.ap()), reads=[t.reg()], writes=[sg16.reg()])
            qv3 = qv.ap().rearrange("p (b k) -> p b k", k=16)
            p.emit("dve", lambda e: e.tensor_tensor(
                qv3, qv3, bsb, ALU.mult),
                reads=[qv.reg(), bs.reg()], writes=[qv.reg()])
            p.emit("pool", lambda e: e.tensor_tensor(
                xq.ap(), qv.ap(), sg16.ap(), ALU.mult),
                reads=[qv.reg(), sg16.reg()], writes=[xq.reg()])
            dst = mk.ar16_t.ap()[:, xqT.o:xqT.o + 4096].rearrange(
                "p (c t) -> p c t", c=8)[:, :, i * 128:(i + 1) * 128]
            mk.transpose8(lambda dc: xq.ap(dc * 128, (dc + 1) * 128),
                          dst, xqT.reg(), [xq.reg()], pb=i % 2)
            mk.ar32.restore(); mk.ar16.restore(); mk.small.restore()
        for i, tt in enumerate(tts):
            xquant(i, tt)

        def hblock(hb, w):
            wgv, wuv = wgr[hb % 2], wur[hb % 2]

            def ld(wv, src):
                p.emit("pool", lambda e: e.dma_start(
                    out=wv.ap(0, 8 * w).rearrange("p (c h) -> p c h", c=8),
                    in_=src.ap().rearrange("(c p) h -> p c h", p=128)
                    [:, :, hb * 256:hb * 256 + w]),
                    reads=[R(src)], writes=[wv.reg()], dma=True)
            ld(wgv, wqd["wg"])
            ld(wuv, wqd["wu"])

            def one_tt(i):
                bg, bu = mk.ps[4], mk.ps[5]

                def mmg(t, wv=wgv, bank=bg):
                    last = None
                    for dc in range(8):
                        last = t.matmul(
                            bank.ap()[:, 0:w],
                            xqT.ap(dc * 512 + i * 128, dc * 512 + (i + 1) * 128),
                            wv.ap(dc * w, (dc + 1) * w),
                            start=(dc == 0), stop=(dc == 7))
                    return last
                p.emit("pe", mmg, reads=[xqT.reg(), wgv.reg()], writes=[R(bg)])
                p.emit("pe", lambda t: mmg(t, wuv, bu),
                       reads=[xqT.reg(), wuv.reg()], writes=[R(bu)])
                mk.ar16.save()
                sv = mk.v16(w)
                p.emit("act", lambda e: e.activation(
                    sv.ap(), bg.ap()[:, 0:w], AF.Silu),
                    reads=[R(bg)], writes=[sv.reg()])
                hs = (i * HN + hb * 256, i * HN + hb * 256 + w)
                p.emit("dve", lambda e: e.tensor_tensor(
                    hbuf.ap(*hs), sv.ap(), bu.ap()[:, 0:w], ALU.mult),
                    reads=[sv.reg(), R(bu)], writes=[hbuf.reg(*hs)])
                mk.ar16.restore()
            for i in range(4):
                one_tt(i)
        for hb in range(11):
            hblock(hb, 256)

        def hquant(i, tt):
            mk.ar32.save(); mk.ar16.save(); mk.small.save()
            hs = (i * HN, (i + 1) * HN)
            NB = HN // 16
            yv = mk.v32(HN)
            qv = mk.v32(HN)
            bs = mk.v32(NB)
            iv = mk.v32(NB)
            sg16 = mk.v16(HN)
            hq = mk.v16(HN)
            p.emit("dve", lambda e: e.tensor_reduce(
                bs.ap(), hbuf.ap(*hs).rearrange("p (b k) -> p b k", k=16),
                axis=AX.X, op=ALU.max, apply_absolute_value=True),
                reads=[hbuf.reg(*hs)], writes=[bs.reg()])
            p.emit("dve", lambda e: e.tensor_scalar(
                bs.ap(), bs.ap(), 1.0 / 6.0, 1e-12, ALU.mult, ALU.max),
                reads=[bs.reg()], writes=[bs.reg()])
            p.emit("dve", lambda e: e.reciprocal(
                iv.ap(), bs.ap()), reads=[bs.reg()], writes=[iv.reg()])
            ivb = iv.ap().rearrange("p (b o) -> p b o", o=1)\
                .broadcast_to((128, NB, 16))
            bsb = bs.ap().rearrange("p (b o) -> p b o", o=1)\
                .broadcast_to((128, NB, 16))
            hv3 = hbuf.ap(*hs).rearrange("p (b k) -> p b k", k=16)
            yv3 = yv.ap().rearrange("p (b k) -> p b k", k=16)
            p.emit("dve", lambda e: e.tensor_tensor(
                yv3, hv3, ivb, ALU.mult),
                reads=[hbuf.reg(*hs), iv.reg()], writes=[yv.reg()])
            p.emit("dve", lambda e: e.scalar_tensor_tensor(
                yv.ap(), yv.ap(), -1.0, yv.ap(), ALU.mult, ALU.max),
                reads=[yv.reg()], writes=[yv.reg()])
            _fp4_qabs_chain(p, yv, qv)
            p.emit("act", lambda e: e.sign(
                sg16.ap(), hbuf.ap(*hs)),
                reads=[hbuf.reg(*hs)], writes=[sg16.reg()])
            qv3 = qv.ap().rearrange("p (b k) -> p b k", k=16)
            p.emit("dve", lambda e: e.tensor_tensor(
                qv3, qv3, bsb, ALU.mult),
                reads=[qv.reg(), bs.reg()], writes=[qv.reg()])
            p.emit("pool", lambda e: e.tensor_tensor(
                hq.ap(), qv.ap(), sg16.ap(), ALU.mult),
                reads=[qv.reg(), sg16.reg()], writes=[hq.reg()])
            tv = hqT[i]

            def tpb(b8, n):
                dst = mk.ar16_t.ap()[:, tv.o + b8 * 1024:tv.o + b8 * 1024 + n * 128]
                mk.transpose8(
                    lambda k2: hq.ap((b8 * 8 + k2) * 128, (b8 * 8 + k2 + 1) * 128),
                    dst,
                    Region(tv.t, 0, 128, tv.o + b8 * 1024,
                           tv.o + b8 * 1024 + n * 128),
                    [hq.reg()], pb=4 + b8 % 2, n=n)
            tpb(0, 8); tpb(1, 8); tpb(2, 6)
            mk.ar32.restore(); mk.ar16.restore(); mk.small.restore()
        for i, tt in enumerate(tts):
            hquant(i, tt)

        def yblock(hb):
            wv = wdr[hb % 2]
            p.emit("pool", lambda e: e.dma_start(
                out=wv.ap().rearrange("p (c h) -> p c h", c=2),
                in_=wqd["wd"].ap().rearrange("(c p) o -> p c o", p=128)
                [:, hb * 2:(hb + 1) * 2, :]),
                reads=[R(wqd["wd"])], writes=[wv.reg()], dma=True)

            def one(i, ob):
                tv = hqT[i]
                bank = mk.ps[i * 2 + ob]

                def mmy(t):
                    last = None
                    for hc in range(2):
                        last = t.matmul(
                            bank.ap()[:, :],
                            tv.ap((hb * 2 + hc) * 128, (hb * 2 + hc + 1) * 128),
                            wv.ap(hc * 1024 + ob * 512, hc * 1024 + (ob + 1) * 512),
                            start=(hb == 0 and hc == 0),
                            stop=(hb == 10 and hc == 1))
                    return last
                p.emit("pe", mmy, reads=[tv.reg(), wv.reg()], writes=[R(bank)])
            for i in range(4):
                for ob in range(2):
                    one(i, ob)
        for hb in range(11):
            yblock(hb)

        def evac(i, tt, ob):
            bank = mk.ps[i * 2 + ob]
            xs = (tt * D + ob * 512, tt * D + (ob + 1) * 512)
            xa = mk.x.ap()[:, xs[0]:xs[1]]
            xreg = Region(mk.x, 0, 128, *xs)
            p.emit("dve", lambda e: e.scalar_tensor_tensor(
                xa, bank.ap()[:, :], 0.5, xa, ALU.mult, ALU.add),
                reads=[R(bank), xreg], writes=[xreg])
        for i, tt in enumerate(tts):
            for ob in range(2):
                evac(i, tt, ob)
    for g in range(NTT // 4):
        do_group(g)


def attn_apply(mk, cfg, elem):
    p = mk.p
    mk.ar16.reset(); mk.ar32.reset(); mk.small.reset()
    TTE = mk.TTE
    T = mk.T
    tt0 = elem * TTE
    x = mk.x
    bq_cols = mk.vsm(24)
    p.emit("sp", lambda e: e.dma_start(out=bq_cols.ap(), in_=cfg["bqkv_cols"]),
           writes=[bq_cols.reg()], dma=True)
    pb_bc = mk.v16(1024)
    onesr = p.const_row(1.0)
    mk.ar32.save()
    brow = mk.v32(1024)
    p.emit("sp", lambda e: e.dma_start(
        out=brow.ap(0, 1024, p1=1), in_=cfg["bproj_row"][0:1, :]),
        writes=[brow.reg()], dma=True)

    def mkbc(src_off, dst):
        def one(ob):
            bank = mk.ps[0]
            p.emit("pe", lambda e: e.matmul(
                bank.ap()[:, :], onesr,
                brow.ap(src_off + ob * 512, src_off + (ob + 1) * 512, p1=1),
                start=True, stop=True),
                reads=[brow.reg()], writes=[R(bank)])
            p.emit("dve", lambda e: e.tensor_copy(
                dst.ap(ob * 512, (ob + 1) * 512), bank.ap()[:, :]),
                reads=[R(bank)], writes=[dst.reg(ob * 512, (ob + 1) * 512)])
        one(0); one(1)
    mkbc(0, pb_bc)
    mk.ar32.restore()
    y32 = mk.v32(TTE * 1024)
    lora = cfg.get("lora")
    if lora:
        lt = mk.v16(160)
        lofs = {}
        off = 0
        for nm, rows, cols in (("a_q", 64, 16), ("b_q", 16, 64),
                               ("a_v", 64, 16), ("b_v", 16, 64)):
            lofs[nm] = (off, rows, cols)
            p.emit("pool", lambda e, nm=nm, off=off, rows=rows, cols=cols:
                   e.dma_start(out=lt.ap(off, off + cols, p1=rows),
                               in_=lora[nm][:, :]),
                   writes=[lt.reg()], dma=True)
            if rows == 64:
                p.emit("pool", lambda e, nm=nm, off=off, cols=cols:
                       e.dma_start(out=lt.ap(off, off + cols, p0=64, p1=128),
                                   in_=lora[nm][:, :]),
                       writes=[lt.reg()], dma=True)
            off += cols

        def lap(nm, pb=0):
            o, rows, cols = lofs[nm]
            if rows == 64 and pb:
                return lt.ap(o, o + cols, p0=64, p1=128), lt.reg()
            return lt.ap(o, o + cols, p1=rows), lt.reg()
    rope = cfg.get("rope")
    xp = cfg["xpage"]
    mk.page_x(xp, "out", order=[t for t in
                                list(range(26, mk.TT)) + list(range(26))
                                if t < mk.TT])
    negm = mk.vsm(TTE)
    rstd = mk.vsm(TTE)
    mk.ln_stats(list(range(tt0, tt0 + TTE)), rstd, negm)
    xnT = mk.v16(8 * T)

    xnring = [mk.v16_in32(1024) for _ in range(2)]

    def mkxnT(j):
        tt = tt0 + j
        xn = xnring[j % 2]
        mk.ln_apply(tt, negm, rstd, xn.ap(), xn.reg(), j=j)
        dst = mk.ar16_t.ap()[:, xnT.o:xnT.o + 8 * T].rearrange(
            "p (c t) -> p c t", c=8)[:, :, j * 128:(j + 1) * 128]
        mk.transpose8(lambda dc: xn.ap(dc * 128, (dc + 1) * 128),
                      dst, xnT.reg(), [xn.reg()], pb=j % 2)
    for j in range(TTE):
        mkxnT(j)
    need = 8 * T + 4096 + 4096 + 2048 + 4096 + 1024 + 1024
    if mk.TT * D >= need:
        bt = x
    else:
        bt = p.sbuf(f"attn_borrow{elem}_{p.cnt['pe']}", [128, need], BF16)
    KB = View(bt, 0, 8 * T)
    o_ = 8 * T
    QB = View(bt, o_, 4096)
    AFM = View(bt, o_ + 4096, 4096)
    PRB = [View(bt, o_ + 8192 + i * 512, 512) for i in range(4)]
    WRG = [View(bt, o_ + 10240 + i * 2048, 2048) for i in range(2)]
    VSQ = View(bt, o_ + 14336, 1024)
    if rope is not None:
        rot = View(bt, o_ + 15360, 1024)
        p.emit("pool", lambda e: e.dma_start(
            out=rot.ap().rearrange("p (c k) -> p c k", c=8),
            in_=rope[:, :, :].rearrange("c p k -> p c k")),
            writes=[rot.reg()], dma=True)
    VB = mk.v16(TTE * 1040)
    kn2 = mk.vsm(8)
    WSEC = {"q": 0, "k": 1024, "v": 2048}

    def ldw_qkv(wv, sec, oc, width=128):
        c0 = WSEC[sec] + oc * 128
        p.emit("pool", lambda e: e.dma_start(
            out=wv.ap(0, 8 * width).rearrange("p (c o) -> p c o", c=8),
            in_=cfg["wqkv"].rearrange("(c p) o -> p c o", p=128)
            [:, :, c0:c0 + width]),
            writes=[wv.reg()], dma=True)

    def kprod(oc):
        wv = WRG[oc % 2]
        ldw_qkv(wv, "k", oc)

        def one(ts4):
            bank = mk.ps[4 + ts4 % 2]

            def mmk(t):
                last = None
                for dc in range(8):
                    last = t.matmul(
                        bank.ap()[:, :],
                        wv.ap(dc * 128, (dc + 1) * 128),
                        xnT.ap(dc * T + ts4 * 512, dc * T + (ts4 + 1) * 512),
                        start=(dc == 0), stop=(dc == 7))
                return last
            p.emit("pe", mmk, reads=[wv.reg(), xnT.reg()], writes=[R(bank)])
            ks = (oc * T + ts4 * 512, oc * T + (ts4 + 1) * 512)
            if rope is None:
                p.emit("act", lambda e: e.activation(
                    KB.ap(*ks), bank.ap()[:, :], AF.Identity,
                    bias=bq_cols.ap(8 + oc, 9 + oc)),
                    reads=[R(bank), bq_cols.reg()], writes=[KB.reg(*ks)])
            else:
                ktmp = View(bt, VSQ.o + 512, 512)
                p.emit("act", lambda e: e.activation(
                    ktmp.ap(), bank.ap()[:, :], AF.Identity,
                    bias=bq_cols.ap(8 + oc, 9 + oc)),
                    reads=[R(bank), bq_cols.reg()], writes=[ktmp.reg()])
                bank2 = mk.ps[6]
                p.emit("pe", lambda e: e.matmul(
                    bank2.ap()[:, :], rot.ap(oc * 128, (oc + 1) * 128),
                    ktmp.ap(), start=True, stop=True),
                    reads=[rot.reg(), ktmp.reg()], writes=[R(bank2)])
                p.emit("dve", lambda e: e.tensor_copy(
                    KB.ap(*ks), bank2.ap()[:, :]),
                    reads=[R(bank2)], writes=[KB.reg(*ks)])
            p.emit("act", lambda e: e.activation(
                VSQ.ap(0, 512), KB.ap(*ks), AF.Square),
                reads=[KB.reg(*ks)], writes=[VSQ.reg(0, 512)])
            nb = mk.ps[7]
            p.emit("pe", lambda e: e.matmul(
                nb.ap()[0:1, :], mk.ones_bf.ap()[:, 0:1], VSQ.ap(0, 512),
                start=True, stop=True),
                reads=[VSQ.reg(0, 512), R(mk.ones_bf)], writes=[R(nb)])
            mk.small.save()
            mx = mk.vsm(1)
            p.emit("dve", lambda e: e.reduce_max(
                mx.ap(p1=1), nb.ap()[0:1, :], axis=AX.X),
                reads=[R(nb)], writes=[mx.reg(p1=1)])
            if ts4 == 0:
                p.emit("dve", lambda e: e.tensor_copy(
                    kn2.ap(oc, oc + 1, p1=1), mx.ap(p1=1)),
                    reads=[mx.reg(p1=1)], writes=[kn2.reg(oc, oc + 1)])
            else:
                p.emit("dve", lambda e: e.tensor_tensor(
                    kn2.ap(oc, oc + 1, p1=1), kn2.ap(oc, oc + 1, p1=1),
                    mx.ap(p1=1), ALU.max),
                    reads=[kn2.reg(oc, oc + 1), mx.reg(p1=1)],
                    writes=[kn2.reg(oc, oc + 1)])
            mk.small.restore()
        for ts4 in range(TTE // 4):
            one(ts4)
    for oc in range(8):
        kprod(oc)

    def vprod(oc):
        wv = WRG[oc % 2]
        ldw_qkv(wv, "v", oc)
        vfm = View(bt, o_, T)

        def one(ts4):
            bank = mk.ps[4 + ts4 % 2]

            def mmv(t):
                last = None
                for dc in range(8):
                    last = t.matmul(
                        bank.ap()[:, :],
                        wv.ap(dc * 128, (dc + 1) * 128),
                        xnT.ap(dc * T + ts4 * 512, dc * T + (ts4 + 1) * 512),
                        start=(dc == 0), stop=(dc == 7))
                return last
            p.emit("pe", mmv, reads=[wv.reg(), xnT.reg()], writes=[R(bank)])
            vs = (ts4 * 512, (ts4 + 1) * 512)
            p.emit("act", lambda e: e.activation(
                vfm.ap(*vs), bank.ap()[:, :], AF.Identity,
                bias=bq_cols.ap(16 + oc, 17 + oc)),
                reads=[R(bank), bq_cols.reg()], writes=[vfm.reg(*vs)])
        for ts4 in range(TTE // 4):
            one(ts4)
        if lora:
            def lhead(hh):
                pb = (hh) * 64

                def onet(ts4):
                    vs = (ts4 * 512, (ts4 + 1) * 512)
                    b1 = mk.ps[6]
                    ava, avr = lap("a_v", pb)
                    bva, bvr = lap("b_v")
                    p.emit("pe", lambda e: e.matmul(
                        b1.ap()[0:16, :], ava,
                        vfm.ap(*vs, p0=pb, p1=pb + 64),
                        start=True, stop=True),
                        reads=[avr, vfm.reg(*vs)], writes=[R(b1)])
                    tmp = View(bt, VSQ.o + 512, 512)
                    p.emit("dve", lambda e: e.tensor_copy(
                        tmp.ap(p1=16), b1.ap()[0:16, :]),
                        reads=[R(b1)], writes=[tmp.reg()])
                    b2 = mk.ps[7]
                    p.emit("pe", lambda e: e.matmul(
                        b2.ap()[0:64, :], bva, tmp.ap(p1=16),
                        start=True, stop=True),
                        reads=[bvr, tmp.reg()], writes=[R(b2)])
                    p.emit("dve", lambda e: e.scalar_tensor_tensor(
                        vfm.ap(*vs, p0=pb, p1=pb + 64), b2.ap()[0:64, :], 2.0,
                        vfm.ap(*vs, p0=pb, p1=pb + 64), ALU.mult, ALU.add),
                        reads=[R(b2), vfm.reg(*vs)], writes=[vfm.reg(*vs)])
                for ts4 in range(TTE // 4):
                    onet(ts4)
            lhead(0); lhead(1)
        def tpv(j):
            bank = mk.ps[4 + j % 2]
            bap = bank.ap().bitcast(BF16)
            p.emit("pe", lambda e: e.transpose(
                bap[:, 0:128], vfm.ap(j * 128, (j + 1) * 128),
                mk.ident.ap()[:, :]),
                reads=[vfm.reg(j * 128, (j + 1) * 128), R(mk.ident)],
                writes=[R(bank)])
            dst = mk.ar16_t.ap()[:, VB.o + j * 1040 + (2 * oc) * 65:
                                 VB.o + j * 1040 + (2 * oc) * 65 + 130]\
                .rearrange("p (h c) -> p h c", c=65)[:, :, 0:64]
            dreg = Region(VB.t, 0, 128, VB.o + j * 1040 + 2 * oc * 65,
                          VB.o + j * 1040 + 2 * oc * 65 + 130)
            p.emit("dve", lambda e: e.tensor_copy(
                dst, bap[:, 0:128].rearrange("p (h c) -> p h c", c=64)),
                reads=[R(bank)], writes=[dreg])
        for j in range(TTE):
            tpv(j)
    for oc in range(8):
        vprod(oc)
    def vones(j):
        dst = mk.ar16_t.ap()[:, VB.o + j * 1040:VB.o + (j + 1) * 1040]\
            .rearrange("p (h c) -> p h c", c=65)[:, :, 64:65]
        p.emit("dve", lambda e: e.memset(dst, 1.0),
               writes=[Region(VB.t, 0, 128, VB.o + j * 1040,
                              VB.o + (j + 1) * 1040)])
    for j in range(TTE):
        vones(j)

    nqt = TTE // 4

    import os as _os
    _dbg = _os.environ.get("ATTN_DBG")

    def qtile(qt):
        def qprod(oc):
            wv = WRG[oc % 2]
            ldw_qkv(wv, "q", oc)
            bank = mk.ps[4 + oc % 2]

            def mmq(t):
                last = None
                for dc in range(8):
                    last = t.matmul(
                        bank.ap()[:, :],
                        wv.ap(dc * 128, (dc + 1) * 128),
                        xnT.ap(dc * T + qt * 512, dc * T + (qt + 1) * 512),
                        start=(dc == 0), stop=(dc == 7))
                return last
            p.emit("pe", mmq, reads=[wv.reg(), xnT.reg()], writes=[R(bank)])
            qs = (oc * 512, (oc + 1) * 512)
            if rope is None:
                p.emit("act", lambda e: e.activation(
                    QB.ap(*qs), bank.ap()[:, :], AF.Identity,
                    bias=bq_cols.ap(oc, oc + 1)),
                    reads=[R(bank), bq_cols.reg()], writes=[QB.reg(*qs)])
            else:
                qtmp = View(bt, VSQ.o + 512, 512)
                p.emit("act", lambda e: e.activation(
                    qtmp.ap(), bank.ap()[:, :], AF.Identity,
                    bias=bq_cols.ap(oc, oc + 1)),
                    reads=[R(bank), bq_cols.reg()], writes=[qtmp.reg()])
                bank2 = mk.ps[6]
                p.emit("pe", lambda e: e.matmul(
                    bank2.ap()[:, :], rot.ap(oc * 128, (oc + 1) * 128),
                    qtmp.ap(), start=True, stop=True),
                    reads=[rot.reg(), qtmp.reg()], writes=[R(bank2)])
                p.emit("dve", lambda e: e.tensor_copy(
                    QB.ap(*qs), bank2.ap()[:, :]),
                    reads=[R(bank2)], writes=[QB.reg(*qs)])
        for oc in range(8):
            qprod(oc)
        if lora:
            def lheadq(h):
                oc, pb = h // 2, (h % 2) * 64
                qs = (oc * 512, (oc + 1) * 512)
                b1 = mk.ps[6]
                aqa, aqr = lap("a_q", pb)
                bqa, bqr = lap("b_q")
                p.emit("pe", lambda e: e.matmul(
                    b1.ap()[0:16, :], aqa, QB.ap(*qs, p0=pb, p1=pb + 64),
                    start=True, stop=True),
                    reads=[aqr, QB.reg(*qs)], writes=[R(b1)])
                tmp = View(bt, VSQ.o + 512, 512)
                p.emit("dve", lambda e: e.tensor_copy(
                    tmp.ap(p1=16), b1.ap()[0:16, :]),
                    reads=[R(b1)], writes=[tmp.reg()])
                b2 = mk.ps[7]
                p.emit("pe", lambda e: e.matmul(
                    b2.ap()[0:64, :], bqa, tmp.ap(p1=16),
                    start=True, stop=True),
                    reads=[bqr, tmp.reg()], writes=[R(b2)])
                p.emit("dve", lambda e: e.scalar_tensor_tensor(
                    QB.ap(*qs, p0=pb, p1=pb + 64), b2.ap()[0:64, :], 2.0,
                    QB.ap(*qs, p0=pb, p1=pb + 64), ALU.mult, ALU.add),
                    reads=[R(b2), QB.reg(*qs)], writes=[QB.reg(*qs)])
            for h in range(16):
                lheadq(h)
        mk.small.save()
        nbias = mk.vsm(8)

        def mkbound(oc):
            qs = (oc * 512, (oc + 1) * 512)
            p.emit("act", lambda e: e.activation(
                VSQ.ap(0, 512), QB.ap(*qs), AF.Square),
                reads=[QB.reg(*qs)], writes=[VSQ.reg(0, 512)])
            nb = mk.ps[7]
            p.emit("pe", lambda e: e.matmul(
                nb.ap()[0:1, :], mk.ones_bf.ap()[:, 0:1], VSQ.ap(0, 512),
                start=True, stop=True),
                reads=[VSQ.reg(0, 512), R(mk.ones_bf)], writes=[R(nb)])
            mk.small.save()
            qn2 = mk.vsm(1)
            p.emit("dve", lambda e: e.reduce_max(
                qn2.ap(p1=1), nb.ap()[0:1, :], axis=AX.X),
                reads=[R(nb)], writes=[qn2.reg(p1=1)])
            p.emit("dve", lambda e: e.tensor_tensor(
                qn2.ap(p1=1), qn2.ap(p1=1), kn2.ap(oc, oc + 1, p1=1),
                ALU.mult),
                reads=[qn2.reg(p1=1), kn2.reg(oc, oc + 1)],
                writes=[qn2.reg(p1=1)])
            p.emit("act", lambda e: e.activation(
                qn2.ap(p1=1), qn2.ap(p1=1), AF.Ln),
                reads=[qn2.reg(p1=1)], writes=[qn2.reg(p1=1)])
            p.emit("act", lambda e: e.activation(
                qn2.ap(p1=1), qn2.ap(p1=1), AF.Exp,
                scale=p.const_col(0.5)[0:1, :]),
                reads=[qn2.reg(p1=1)], writes=[qn2.reg(p1=1)])
            p.emit("dve", lambda e: e.tensor_scalar(
                qn2.ap(p1=1), qn2.ap(p1=1), -0.125, -1.0, ALU.mult, ALU.add),
                reads=[qn2.reg(p1=1)], writes=[qn2.reg(p1=1)])
            bb = mk.ps[7]
            p.emit("pe", lambda e: e.matmul(
                bb.ap()[:, 0:1], onesr, qn2.ap(p1=1), start=True, stop=True),
                reads=[qn2.reg(p1=1)], writes=[R(bb)])
            p.emit("dve", lambda e: e.tensor_copy(
                nbias.ap(oc, oc + 1), bb.ap()[:, 0:1]),
                reads=[R(bb)], writes=[nbias.reg(oc, oc + 1)])
            mk.small.restore()
        for oc in range(8):
            mkbound(oc)
        if _dbg and qt == 0:
            dk = p.dram(f"dbg_k{elem}", [128, 8 * T], F32, kind="ExternalOutput")
            dq = p.dram(f"dbg_q{elem}", [128, 4096], F32, kind="ExternalOutput")
            dn = p.dram(f"dbg_nb{elem}", [128, 8], F32, kind="ExternalOutput")
            p.emit("pool", lambda e: e.dma_start(out=dk.ap()[:, :], in_=KB.ap()),
                   reads=[KB.reg()], dma=True)
            p.emit("pool", lambda e: e.dma_start(out=dq.ap()[:, :], in_=QB.ap()),
                   reads=[QB.reg()], dma=True)
            p.emit("pool", lambda e: e.dma_start(out=dn.ap()[:, :], in_=nbias.ap(0, 8)),
                   reads=[nbias.reg(0, 8)], dma=True)
        c8 = p.const_col(0.125)

        def head(h):
            oc, pb = h // 2, (h % 2) * 64
            av = mk.ps[h % 2]

            def score(kc):
                sb = mk.ps[2 + kc % 4]
                p.emit("pe", lambda e: e.matmul(
                    sb.ap()[:, :],
                    KB.ap(oc * T + kc * 128, oc * T + (kc + 1) * 128,
                          p0=pb, p1=pb + 64),
                    QB.ap(oc * 512, (oc + 1) * 512, p0=pb, p1=pb + 64),
                    start=True, stop=True),
                    reads=[KB.reg(), QB.reg()], writes=[R(sb)])
                pr = PRB[kc % 4]
                p.emit("act", lambda e: e.activation(
                    pr.ap(), sb.ap()[:, :], AF.Exp, scale=c8,
                    bias=nbias.ap(oc, oc + 1)),
                    reads=[R(sb), nbias.reg(oc, oc + 1)], writes=[pr.reg()])

            def avmm(kc):
                pr = PRB[kc % 4]
                p.emit("pe", lambda e: e.matmul(
                    av.ap()[0:65, :],
                    mk.ar16_t.ap()[:, VB.o + kc * 1040 + h * 65:
                                   VB.o + kc * 1040 + (h + 1) * 65],
                    pr.ap(), start=(kc == 0), stop=(kc == TTE - 1)),
                    reads=[Region(VB.t, 0, 128, VB.o + kc * 1040,
                                  VB.o + (kc + 1) * 1040), pr.reg()],
                    writes=[R(av)])
            LA = min(2, TTE - 1)
            for kc in range(TTE):
                score(kc)
                if kc >= LA:
                    avmm(kc - LA)
            for kc in range(TTE - LA, TTE):
                avmm(kc)
            zrow = View(mk.ar32_t, xnring[0].o // 2, 512)
            ibc = View(mk.ar32_t, xnring[0].o // 2 + 512, 512)
            p.emit("dve", lambda e: e.tensor_copy(
                zrow.ap(p1=1), av.ap()[64:65, :]),
                reads=[R(av)], writes=[zrow.reg()])
            p.emit("act", lambda e: e.activation(
                zrow.ap(p1=1), zrow.ap(p1=1), AF.Ln),
                reads=[zrow.reg()], writes=[zrow.reg()])
            p.emit("act", lambda e: e.activation(
                zrow.ap(p1=1), zrow.ap(p1=1), AF.Exp,
                scale=p.const_col(-1.0)[0:1, :]),
                reads=[zrow.reg()], writes=[zrow.reg()])
            bb = mk.ps[6 + h % 2]
            p.emit("pe", lambda e: e.matmul(
                bb.ap()[0:64, :], onesr[0:1, 0:64], zrow.ap(p1=1),
                start=True, stop=True),
                reads=[zrow.reg()], writes=[R(bb)])
            p.emit("dve", lambda e: e.tensor_copy(
                ibc.ap(p1=64), bb.ap()[0:64, :]),
                reads=[R(bb)], writes=[ibc.reg()])
            afs = (oc * 512, (oc + 1) * 512)
            p.emit("dve", lambda e: e.tensor_tensor(
                AFM.ap(*afs, p0=pb, p1=pb + 64), av.ap()[0:64, :],
                ibc.ap(p1=64), ALU.mult),
                reads=[R(av), ibc.reg()], writes=[AFM.reg(*afs)])
        for h in range(16):
            head(h)
        def proj(ob):
            wv = QB
            p.emit("pool", lambda e: e.dma_start(
                out=wv.ap(0, 8 * 512).rearrange("p (c o) -> p c o", c=8),
                in_=cfg["wproj"].rearrange("(c p) o -> p c o", p=128)
                [:, :, ob * 512:(ob + 1) * 512]),
                writes=[wv.reg()], dma=True)

            def one(j):
                bank = mk.ps[4 + j % 2]

                def mmp(t):
                    last = None
                    for dc in range(8):
                        last = t.matmul(
                            bank.ap()[:, :],
                            AFM.ap(dc * 512 + j * 128, dc * 512 + (j + 1) * 128),
                            wv.ap(dc * 512, (dc + 1) * 512),
                            start=(dc == 0), stop=(dc == 7))
                    return last
                p.emit("pe", mmp, reads=[AFM.reg(), wv.reg()], writes=[R(bank)])
                ys = ((qt * 4 + j) * D + ob * 512,
                      (qt * 4 + j) * D + (ob + 1) * 512)
                p.emit("dve", lambda e: e.tensor_copy(
                    y32.ap(*ys), bank.ap()[:, :]),
                    reads=[R(bank)], writes=[y32.reg(*ys)])
            for j in range(4):
                one(j)
        if _dbg and qt == 0:
            da = p.dram(f"dbg_afm{elem}", [128, 4096], F32, kind="ExternalOutput")
            p.emit("pool", lambda e: e.dma_start(out=da.ap()[:, :], in_=AFM.ap()),
                   reads=[AFM.reg()], dma=True)
        proj(0); proj(1)
        mk.small.restore()
    for qt in range(nqt):
        qtile(qt)
    mk.page_x(xp, "in")

    def finadd(j):
        tt = tt0 + j
        xs = (tt * D, (tt + 1) * D)
        xa = x.ap()[:, xs[0]:xs[1]]
        xreg = Region(x, 0, 128, *xs)
        ys = (j * D, (j + 1) * D)
        p.emit("dve", lambda e: e.tensor_tensor(
            y32.ap(*ys), y32.ap(*ys), pb_bc.ap(), ALU.add),
            reads=[y32.reg(*ys), pb_bc.reg()], writes=[y32.reg(*ys)])
        p.emit("dve", lambda e: e.tensor_tensor(
            xa, xa, y32.ap(*ys), ALU.add),
            reads=[xreg, y32.reg(*ys)], writes=[xreg])
    for j in range(TTE):
        finadd(j)


def moe_apply(mk, cfg, xpage):
    p = mk.p
    kind = cfg["kind"]
    HN = 4096 if kind == "l1" else 2816
    NHC = HN // 128
    halves = [(0, 8), (8, 8), (16, 8), (24, 8)] if kind == "l1" else \
             [(0, 11), (11, 11)]
    for elem in range(mk.NE):
        _moe_tp(mk, cfg, xpage, elem, HN, NHC, halves)


def _moe_tp(mk, cfg, xpage, elem, HN, NHC, halves):
    p = mk.p
    kind = cfg["kind"]
    mk.ar16.reset(); mk.ar32.reset(); mk.small.reset()
    TTE = mk.TTE
    T = mk.T
    tt0 = elem * TTE
    x = mk.x
    onesr = p.const_row(1.0)
    xp = xpage
    _hm = max(n for _, n in halves)
    _st = min(_hm * T // 1024, mk.TT)
    mk.page_x(xp, "out", order=list(range(_st, mk.TT)) + list(range(_st)))
    negm = mk.vsm(TTE)
    rstd = mk.vsm(TTE)
    mk.ln_stats(list(range(tt0, tt0 + TTE)), rstd, negm)
    xnT = mk.v16(8 * T)
    sel8 = mk.v16(1024)
    p.emit("pool", lambda e: e.dma_start(
        out=sel8.ap(p1=8), in_=cfg["sel8"][:, :]),
        writes=[sel8.reg()], dma=True)
    if kind == "l2":
        lnw_bc = mk.v16(1024)
        mk.ar32.save()
        lrow = mk.v32(1024)
        p.emit("sp", lambda e: e.dma_start(
            out=lrow.ap(p1=1), in_=cfg["lnw_row"][0:1, :]),
            writes=[lrow.reg()], dma=True)

        def bcob(ob):
            bank = mk.ps[0]
            p.emit("pe", lambda e: e.matmul(
                bank.ap()[:, :], onesr, lrow.ap(ob * 512, (ob + 1) * 512, p1=1),
                start=True, stop=True),
                reads=[lrow.reg()], writes=[R(bank)])
            p.emit("dve", lambda e: e.tensor_copy(
                lnw_bc.ap(ob * 512, (ob + 1) * 512), bank.ap()[:, :]),
                reads=[R(bank)], writes=[lnw_bc.reg(ob * 512, (ob + 1) * 512)])
        bcob(0); bcob(1)
        mk.ar32.restore()
        rstd2 = mk.vsm(TTE)

    xnring = [mk.v16_in32(1024) for _ in range(2)]

    def mkxnT(j):
        tt = tt0 + j
        mk.ar16.save(); mk.ar32.save()
        xn = xnring[j % 2]
        mk.ln_apply(tt, negm, rstd, xn.ap(), xn.reg(), j=j)
        if kind == "l2":
            xsq = mk.v32(2)
            jk = mk.v16(1024)
            p.emit("act", lambda e: e.activation(
                jk.ap(), xn.ap(), AF.Square, accum_out=xsq.ap(0, 1)),
                reads=[xn.reg()], writes=[jk.reg(), xsq.reg(0, 1)])
            p.emit("act", lambda e: e.activation(
                xsq.ap(1, 2), xsq.ap(0, 1), AF.Ln,
                bias=p.const_col(1e-5), scale=p.const_col(1.0 / D)),
                reads=[xsq.reg(0, 1)], writes=[xsq.reg(1, 2)])
            p.emit("act", lambda e, j=j: e.activation(
                rstd2.ap(j, j + 1), xsq.ap(1, 2), AF.Exp,
                scale=p.const_col(-0.5)),
                reads=[xsq.reg(1, 2)], writes=[rstd2.reg(j, j + 1)])
            p.emit("dve", lambda e, j=j: e.tensor_scalar(
                xn.ap(), xn.ap(), rstd2.ap(j, j + 1), None, ALU.mult),
                reads=[xn.reg(), rstd2.reg(j, j + 1)], writes=[xn.reg()])
            p.emit("dve", lambda e: e.tensor_tensor(
                xn.ap(), xn.ap(), lnw_bc.ap(), ALU.mult),
                reads=[xn.reg(), lnw_bc.reg()], writes=[xn.reg()])
        dst = mk.ar16_t.ap()[:, xnT.o:xnT.o + 8 * T].rearrange(
            "p (c t) -> p c t", c=8)[:, :, j * 128:(j + 1) * 128]
        mk.transpose8(lambda dc: xn.ap(dc * 128, (dc + 1) * 128),
                      dst, xnT.reg(), [xn.reg()], pb=j % 2)
        mk.ar16.restore(); mk.ar32.restore()
    for j in range(TTE):
        mkxnT(j)
    gT = mk.v16(T)

    def gates(j):
        mk.small.save(); mk.ar32.save()
        gbuf = mk.v32(8)
        bank = mk.ps[0]

        def mmg(t):
            last = None
            for dc in range(8):
                last = t.matmul(
                    bank.ap()[0:128, 0:8],
                    xnT.ap(dc * T + j * 128, dc * T + (j + 1) * 128),
                    wgst.ap(dc * 8, (dc + 1) * 8),
                    start=(dc == 0), stop=(dc == 7))
            return last
        p.emit("pe", mmg, reads=[xnT.reg(), wgst.reg()], writes=[R(bank)])
        lg = gbuf
        p.emit("dve", lambda e: e.tensor_copy(lg.ap(), bank.ap()[:, 0:8]),
               reads=[R(bank)], writes=[lg.reg()])
        m1 = mk.vsm(1); m2 = mk.vsm(1); g1 = mk.vsm(1); g2 = mk.vsm(1)
        msk = mk.v32(8)
        p.emit("dve", lambda e: e.reduce_max(
            m1.ap(), lg.ap(), axis=AX.X), reads=[lg.reg()], writes=[m1.reg()])
        p.emit("dve", lambda e: e.tensor_scalar(
            msk.ap(), lg.ap(), m1.ap(), None, ALU.is_ge),
            reads=[lg.reg(), m1.reg()], writes=[msk.reg()])
        lm = mk.v32(8)
        p.emit("dve", lambda e: e.scalar_tensor_tensor(
            lm.ap(), msk.ap(), -1e30, lg.ap(), ALU.mult, ALU.add),
            reads=[msk.reg(), lg.reg()], writes=[lm.reg()])
        p.emit("dve", lambda e: e.reduce_max(
            m2.ap(), lm.ap(), axis=AX.X), reads=[lm.reg()], writes=[m2.reg()])
        if kind == "l1":
            p.emit("dve", lambda e: e.tensor_tensor(
                g1.ap(), m2.ap(), m1.ap(), ALU.subtract),
                reads=[m1.reg(), m2.reg()], writes=[g1.reg()])
            p.emit("act", lambda e: e.activation(
                g1.ap(), g1.ap(), AF.Exp), reads=[g1.reg()], writes=[g1.reg()])
            p.emit("dve", lambda e: e.tensor_single_scalar(
                g1.ap(), g1.ap(), 1.0, ALU.add),
                reads=[g1.reg()], writes=[g1.reg()])
            p.emit("dve", lambda e: e.reciprocal(
                g1.ap(), g1.ap()), reads=[g1.reg()], writes=[g1.reg()])
            p.emit("dve", lambda e: e.scalar_tensor_tensor(
                g2.ap(), g1.ap(), -1.0, p.const_col(1.0), ALU.mult, ALU.add),
                reads=[g1.reg()], writes=[g2.reg()])
        else:
            Z = mk.vsm(1)
            ex = mk.v32(8)
            nm1 = mk.vsm(1)
            p.emit("dve", lambda e: e.tensor_single_scalar(
                nm1.ap(), m1.ap(), -1.0, ALU.mult),
                reads=[m1.reg()], writes=[nm1.reg()])
            p.emit("act", lambda e: e.activation(
                ex.ap(), lg.ap(), AF.Exp, bias=nm1.ap(), accum_out=Z.ap()),
                reads=[lg.reg(), nm1.reg()], writes=[ex.reg(), Z.reg()])
            p.emit("dve", lambda e: e.reciprocal(
                g1.ap(), Z.ap()), reads=[Z.reg()], writes=[g1.reg()])
            p.emit("dve", lambda e: e.tensor_tensor(
                g2.ap(), m2.ap(), m1.ap(), ALU.subtract),
                reads=[m1.reg(), m2.reg()], writes=[g2.reg()])
            p.emit("act", lambda e: e.activation(
                g2.ap(), g2.ap(), AF.Exp), reads=[g2.reg()], writes=[g2.reg()])
            p.emit("dve", lambda e: e.tensor_tensor(
                g2.ap(), g2.ap(), g1.ap(), ALU.mult),
                reads=[g2.reg(), g1.reg()], writes=[g2.reg()])
        msk2 = mk.v32(8)
        p.emit("dve", lambda e: e.tensor_scalar(
            msk2.ap(), lm.ap(), m2.ap(), None, ALU.is_ge),
            reads=[lm.reg(), m2.reg()], writes=[msk2.reg()])
        p.emit("dve", lambda e: e.tensor_scalar(
            msk.ap(), msk.ap(), g1.ap(), None, ALU.mult),
            reads=[msk.reg(), g1.reg()], writes=[msk.reg()])
        p.emit("dve", lambda e: e.scalar_tensor_tensor(
            gbuf.ap(), msk2.ap(), g2.ap(), msk.ap(), ALU.mult, ALU.add),
            reads=[msk2.reg(), g2.reg(), msk.reg()], writes=[gbuf.reg()])
        bank2 = mk.ps[1]
        p.emit("pe", lambda e: e.matmul(
            bank2.ap()[0:8, 0:128], gbuf.ap(), mk.id32.ap()[:, :],
            start=True, stop=True),
            reads=[gbuf.reg(), R(mk.id32)], writes=[R(bank2)])
        p.emit("dve", lambda e, j=j: e.tensor_copy(
            gT.ap(j * 128, (j + 1) * 128, p1=8), bank2.ap()[0:8, 0:128]),
            reads=[R(bank2)], writes=[gT.reg()])
        mk.small.restore(); mk.ar32.restore()
    wgst = mk.v16(64)
    p.emit("pool", lambda e: e.dma_start(
        out=wgst.ap().rearrange("p (c o) -> p c o", c=8),
        in_=cfg["w_gT" if kind == "l1" else "router"]
        .rearrange("(c p) o -> p c o", p=128)),
        writes=[wgst.reg()], dma=True)
    for j in range(TTE):
        gates(j)
    HHM = max(n for _, n in halves)
    WB = 8 if kind == "l1" else 2
    S = 8 * WB * 128
    nw = 1 if kind == "l1" else 2
    bneed = HHM * T + 2 * nw * S
    bt = x if mk.TT * D >= bneed else \
        p.sbuf(f"moe_borrow_{kind}_{elem}", [128, bneed], BF16)
    hbuf = View(bt, 0, HHM * T)
    gbc = mk.v16(2048)
    wslots = [View(bt, HHM * T + i * nw * S, S) for i in range(2)]
    w3slots = [View(bt, HHM * T + i * nw * S + S, S) for i in range(2)] \
        if kind == "l2" else None
    sched = []
    for ei_ in range(8):
        for h0_, nhc_ in halves:
            for b0_ in range(0, nhc_, WB):
                sched.append((ei_, h0_ + b0_, min(WB, nhc_ - b0_)))

    def ldblk(i):
        if i >= len(sched):
            return
        ei2, c0, nb2 = sched[i]
        pairs = [("w1" if kind == "l2" else "w_in", wslots[i % 2], "pool")]
        if kind == "l2":
            pairs.append(("w3", w3slots[i % 2], "pool"))
        for nm2, wv, qs in pairs:
            p.emit(qs, lambda e, nm2=nm2, wv=wv, ei2=ei2, c0=c0, nb2=nb2:
                   e.dma_start(
                out=wv.ap(0, S).rearrange(
                    "p (c o) -> p c o", c=8)[:, :, 0:nb2 * 128],
                in_=cfg[nm2][ei2].rearrange("(c p) o -> p c o", p=128)
                [:, :, c0 * 128:(c0 + nb2) * 128]),
                writes=[Region(wv.t, 0, 128, wv.o, wv.o + S)], dma=True)
    blkc = [0]
    ldblk(0)
    w2h = mk.v16(HHM * 1024)
    y32 = mk.v32(TTE * 1024)
    p.emit("dve", lambda e: e.memset(y32.ap(), 0.0), writes=[y32.reg()])
    ysc = [View(mk.ar32_t, xnring[0].o // 2 + i * 512, 512) for i in range(2)]

    def expert(ei):
        def gb(ts4):
            bank = mk.ps[0]
            p.emit("pe", lambda e: e.matmul(
                bank.ap()[:, :], sel8.ap(ei * 128, (ei + 1) * 128, p1=8),
                gT.ap(ts4 * 512, (ts4 + 1) * 512, p1=8),
                start=True, stop=True),
                reads=[sel8.reg(), gT.reg()], writes=[R(bank)])
            p.emit("dve", lambda e: e.tensor_copy(
                gbc.ap(ts4 * 512, (ts4 + 1) * 512), bank.ap()[:, :]),
                reads=[R(bank)],
                writes=[Region(gbc.t, 0, 128, gbc.o + ts4 * 512,
                               gbc.o + (ts4 + 1) * 512)])
        for ts4 in range(T // 512):
            gb(ts4)

        def half(h0, nhc):
            p.emit("pool", lambda e: e.dma_start(
                out=w2h.ap(0, nhc * 1024).rearrange("p (c o) -> p c o", c=nhc),
                in_=cfg["w2" if kind == "l2" else "w_out"][ei]
                .rearrange("(c p) o -> p c o", p=128)[:, h0:h0 + nhc, :]),
                reads=[R(cfg["w2" if kind == "l2" else "w_out"].tensor)]
                if False else [],
                writes=[Region(w2h.t, 0, 128, w2h.o, w2h.o + nhc * 1024)],
                dma=True)

            def hchunk(hci):
                hc = h0 + hci
                boff = (hci % WB) * 128
                bw = WB * 128
                if hci % WB == 0:
                    i = blkc[0]
                    assert sched[i] == (ei, h0 + hci, min(WB, nhc - hci))
                    ldblk(i + 1)
                    blkc[0] += 1
                i = blkc[0] - 1
                wv = wslots[i % 2]
                wv3 = w3slots[i % 2] if kind == "l2" else None

                def ts_tile(ts4):
                    bank = mk.ps[2 + ts4 % 2]

                    def mmh(t, wvv, bk):
                        last = None
                        for dc in range(8):
                            last = t.matmul(
                                bk.ap()[:, :],
                                wvv.ap(dc * bw + boff, dc * bw + boff + 128),
                                xnT.ap(dc * T + ts4 * 512,
                                       dc * T + (ts4 + 1) * 512),
                                start=(dc == 0), stop=(dc == 7))
                        return last
                    p.emit("pe", lambda t: mmh(t, wv, bank),
                           reads=[Region(wv.t, 0, 128, wv.o, wv.o + 8 * bw),
                                  xnT.reg()],
                           writes=[R(bank)])
                    hs = (hci * T + ts4 * 512, hci * T + (ts4 + 1) * 512)
                    hreg = Region(hbuf.t, 0, 128, hbuf.o + hs[0],
                                  hbuf.o + hs[1])
                    gbr = Region(gbc.t, 0, 128, gbc.o + ts4 * 512,
                                 gbc.o + (ts4 + 1) * 512)
                    if kind == "l1":
                        p.emit("dve", lambda e: e.tensor_tensor(
                            hbuf.ap(*hs), bank.ap()[:, :],
                            gbc.ap(ts4 * 512, (ts4 + 1) * 512), ALU.mult),
                            reads=[R(bank), gbr], writes=[hreg])
                    else:
                        bank3 = mk.ps[4 + ts4 % 2]
                        p.emit("pe", lambda t: mmh(t, wv3, bank3),
                               reads=[Region(wv3.t, 0, 128, wv3.o,
                                             wv3.o + 8 * bw), xnT.reg()],
                               writes=[R(bank3)])
                        sv = ysc[ts4 % 2]
                        p.emit("act", lambda e: e.activation(
                            sv.ap(), bank.ap()[:, :], AF.Silu),
                            reads=[R(bank)], writes=[sv.reg()])
                        p.emit("dve", lambda e: e.tensor_tensor(
                            sv.ap(), sv.ap(), bank3.ap()[:, :], ALU.mult),
                            reads=[sv.reg(), R(bank3)], writes=[sv.reg()])
                        p.emit("dve", lambda e: e.tensor_tensor(
                            hbuf.ap(*hs), sv.ap(),
                            gbc.ap(ts4 * 512, (ts4 + 1) * 512), ALU.mult),
                            reads=[sv.reg(), gbr], writes=[hreg])
                for ts4 in range(T // 512):
                    ts_tile(ts4)
            for hci in range(nhc):
                hchunk(hci)
            def ymm(ts4):
                def one(j2, ob):
                    bank = mk.ps[j2 * 2 + ob]

                    def mm(t):
                        last = None
                        for hci in range(nhc):
                            last = t.matmul(
                                bank.ap()[:, :],
                                hbuf.ap(hci * T + ts4 * 512 + j2 * 128,
                                        hci * T + ts4 * 512 + (j2 + 1) * 128),
                                w2h.ap(hci * 1024 + ob * 512,
                                       hci * 1024 + (ob + 1) * 512),
                                start=(hci == 0), stop=(hci == nhc - 1))
                        return last
                    p.emit("pe", mm,
                           reads=[Region(hbuf.t, 0, 128, hbuf.o,
                                         hbuf.o + nhc * T),
                                  Region(w2h.t, 0, 128, w2h.o,
                                         w2h.o + nhc * 1024)],
                           writes=[R(bank)])
                    ys = ((ts4 * 4 + j2) * 1024 + ob * 512,
                          (ts4 * 4 + j2) * 1024 + (ob + 1) * 512)
                    p.emit("dve", lambda e: e.tensor_tensor(
                        y32.ap(*ys), y32.ap(*ys), bank.ap()[:, :], ALU.add),
                        reads=[y32.reg(*ys), R(bank)], writes=[y32.reg(*ys)])
                for j2 in range(4):
                    for ob in range(2):
                        one(j2, ob)
            for ts4 in range(T // 512):
                ymm(ts4)
        for h0, nhc in halves:
            half(h0, nhc)
    for ei in range(8):
        expert(ei)
    mk.page_x(xp, "in")

    def finadd(j):
        tt = tt0 + j
        xs = (tt * D, (tt + 1) * D)
        xa = x.ap()[:, xs[0]:xs[1]]
        xreg = Region(x, 0, 128, *xs)
        ys = (j * D, (j + 1) * D)
        p.emit("dve", lambda e: e.scalar_tensor_tensor(
            xa, y32.ap(*ys), 0.5, xa, ALU.mult, ALU.add),
            reads=[y32.reg(*ys), xreg], writes=[xreg])
    for j in range(TTE):
        finadd(j)


def l1_attn_apply(mk, cfg, elem):
    p = mk.p
    mk.ar16.reset(); mk.ar32.reset(); mk.small.reset()
    xp = cfg["xpage"]
    mk.page_x(xp, "out")
    TTE = mk.TTE
    T = mk.T
    NB = T // 64
    tt0 = elem * TTE
    x = mk.x
    onesr = p.const_row(1.0)
    sc4 = 0.3535533905932738
    negm = mk.vsm(TTE)
    rstd = mk.vsm(TTE)
    mk.ln_stats(list(range(tt0, tt0 + TTE)), rstd, negm)
    xnT = mk.v16(8 * T)

    xnring = [mk.v16_in32(1024) for _ in range(2)]

    def mkxnT(j):
        tt = tt0 + j
        xn = xnring[j % 2]
        mk.ln_apply(tt, negm, rstd, xn.ap(), xn.reg(), j=j)
        dst = mk.ar16_t.ap()[:, xnT.o:xnT.o + 8 * T].rearrange(
            "p (c t) -> p c t", c=8)[:, :, j * 128:(j + 1) * 128]
        mk.transpose8(lambda dc: xn.ap(dc * 128, (dc + 1) * 128),
                      dst, xnT.reg(), [xn.reg()], pb=j % 2)
    for j in range(TTE):
        mkxnT(j)
    bo_bc = mk.v16(1024)
    mk.ar32.save()
    brow = mk.v32(1024)
    p.emit("sp", lambda e: e.dma_start(
        out=brow.ap(p1=1), in_=cfg["bo_row"][0:1, :]),
        writes=[brow.reg()], dma=True)

    def bcob(ob):
        bank = mk.ps[0]
        p.emit("pe", lambda e: e.matmul(
            bank.ap()[:, :], onesr, brow.ap(ob * 512, (ob + 1) * 512, p1=1),
            start=True, stop=True), reads=[brow.reg()], writes=[R(bank)])
        p.emit("dve", lambda e: e.tensor_copy(
            bo_bc.ap(ob * 512, (ob + 1) * 512), bank.ap()[:, :]),
            reads=[R(bank)], writes=[bo_bc.reg(ob * 512, (ob + 1) * 512)])
    bcob(0); bcob(1)
    mk.ar32.restore()
    VT = mk.v16(TTE * 1040)
    ctxn = mk.v16(2 * 65)
    need = 2 * 8 * T
    bt = x if mk.TT * D >= need else p.sbuf(f"l1a_borrow{elem}",
                                            [128, need], BF16)
    QF = View(bt, 0, 8 * T)
    KT = View(bt, 8 * T, 8 * T)
    WSRC = {"q": "wqT", "k": "wkT", "v": "wvT"}

    def ldw(wv, nm, ob):
        p.emit("pool", lambda e: e.dma_start(
            out=wv.ap(0, 8 * 512).rearrange("p (c o) -> p c o", c=8),
            in_=cfg[nm].rearrange("(c p) o -> p c o", p=128)
            [:, :, ob * 512:(ob + 1) * 512]),
            writes=[wv.reg(0, 4096)], dma=True)
    wring = [mk.v16_in32(1024) for _ in range(2)]

    def qprod(oc):
        wv = wring[oc % 2]
        p.emit("pool", lambda e: e.dma_start(
            out=wv.ap(0, 1024).rearrange("p (c o) -> p c o", c=8),
            in_=cfg["wqT"].rearrange("(c p) o -> p c o", p=128)
            [:, :, oc * 128:(oc + 1) * 128]),
            writes=[wv.reg(0, 1024)], dma=True)

        def one(ts4):
            bank = mk.ps[4 + ts4 % 2]

            def mm(t):
                last = None
                for dc in range(8):
                    last = t.matmul(
                        bank.ap()[:, :], wv.ap(dc * 128, (dc + 1) * 128),
                        xnT.ap(dc * T + ts4 * 512, dc * T + (ts4 + 1) * 512),
                        start=(dc == 0), stop=(dc == 7))
                return last
            p.emit("pe", mm,
                   reads=[wv.reg(0, 1024), xnT.reg()],
                   writes=[R(bank)])
            qs = (oc * T + ts4 * 512, oc * T + (ts4 + 1) * 512)
            p.emit("act", lambda e: e.activation(
                QF.ap(*qs), bank.ap()[:, :], AF.Exp,
                scale=p.const_col(sc4)),
                reads=[R(bank)], writes=[QF.reg(*qs)])
        for ts4 in range(T // 512):
            one(ts4)
    for oc in range(8):
        qprod(oc)
    kvw = [mk.v16_in32(4096) for _ in range(2)]

    def kv_all(sec, wnm):
        ldw(kvw[0], wnm, 0)
        ldw(kvw[1], wnm, 1)
        for j in range(TTE):
            for ob in range(2):
                wv = kvw[ob]
                bank = mk.ps[4 + (j * 2 + ob) % 2]

                def mm(t, j=j, ob=ob, wv=wv, bank=bank):
                    last = None
                    for dc in range(8):
                        last = t.matmul(
                            bank.ap()[:, :],
                            xnT.ap(dc * T + j * 128, dc * T + (j + 1) * 128),
                            wv.ap(dc * 512, (dc + 1) * 512),
                            start=(dc == 0), stop=(dc == 7))
                    return last
                p.emit("pe", mm,
                       reads=[wv.reg(0, 4096), xnT.reg()],
                       writes=[R(bank)])
                if sec == "k":
                    ks = (j * 1024 + ob * 512, j * 1024 + (ob + 1) * 512)
                    kreg = Region(KT.t, 0, 128, KT.o + ks[0], KT.o + ks[1])
                    mk.ar32.save()
                    t1 = mk.v32(512)
                    t2 = mk.v32(512)
                    p.emit("dve", lambda e, bank=bank, t1=t1: e.tensor_scalar(
                        t1.ap(), bank.ap()[:, :], sc4, 0.0, ALU.mult, ALU.min),
                        reads=[R(bank)], writes=[t1.reg()])
                    p.emit("act", lambda e, t1=t1: e.activation(
                        t1.ap(), t1.ap(), AF.Exp),
                        reads=[t1.reg()], writes=[t1.reg()])
                    p.emit("dve", lambda e, bank=bank, t2=t2: e.tensor_scalar(
                        t2.ap(), bank.ap()[:, :], sc4, 0.0, ALU.mult, ALU.max),
                        reads=[R(bank)], writes=[t2.reg()])
                    p.emit("dve", lambda e, ks=ks, t1=t1, t2=t2, kreg=kreg:
                           e.tensor_tensor(
                        KT.t.ap()[:, KT.o + ks[0]:KT.o + ks[1]], t1.ap(),
                        t2.ap(), ALU.add),
                        reads=[t1.reg(), t2.reg()], writes=[kreg])
                    mk.ar32.restore()
                else:
                    dst = mk.ar16_t.ap()[:, VT.o + j * 1040 + ob * 520:
                                         VT.o + j * 1040 + ob * 520 + 520]\
                        .rearrange("p (h c) -> p h c", c=65)[:, :, 0:64]
                    dreg = Region(VT.t, 0, 128, VT.o + j * 1040 + ob * 520,
                                  VT.o + j * 1040 + (ob + 1) * 520)
                    p.emit("dve", lambda e, bank=bank, dst=dst, dreg=dreg:
                           e.tensor_copy(
                        dst, bank.ap()[:, :].rearrange("p (h c) -> p h c",
                                                       c=64)),
                        reads=[R(bank)], writes=[dreg])
    kv_all("k", "wkT")
    kv_all("v", "wvT")

    def vones(j):
        dst = mk.ar16_t.ap()[:, VT.o + j * 1040:VT.o + (j + 1) * 1040]\
            .rearrange("p (h c) -> p h c", c=65)[:, :, 64:65]
        p.emit("dve", lambda e: e.memset(dst, 1.0),
               writes=[Region(VT.t, 0, 128, VT.o + j * 1040,
                              VT.o + (j + 1) * 1040)])
    for j in range(TTE):
        vones(j)
    ctx = mk.v32(8 * 65)
    p.emit("dve", lambda e: e.memset(ctx.ap(), 0.0), writes=[ctx.reg()])
    afm = View(mk.ar16_t, xnT.o, 8 * T)
    zrowbuf = mk.v32(T)
    p.emit("dve", lambda e: e.memset(ctxn.ap(64, 65), 1.0),
           writes=[Region(ctxn.t, 0, 128, ctxn.o + 64, ctxn.o + 65)])
    p.emit("dve", lambda e: e.memset(ctxn.ap(129, 130), 1.0),
           writes=[Region(ctxn.t, 0, 128, ctxn.o + 129, ctxn.o + 130)])

    def head(h):
        oc, pb = h // 2, (h % 2) * 64
        cs = (oc * 65, (oc + 1) * 65)
        creg = Region(mk.ar32_t, pb, pb + 64, ctx.o + cs[0], ctx.o + cs[1])
        cbase = ctxn.o if h % 2 == 0 else ctxn.o + 65

        def bucket(u):
            jt, half = u // 2, (u % 2) * 64
            if u > 0:
                mk.small.save()
                ikc = mk.vsm(1)
                p.emit("dve", lambda e: e.tensor_scalar(
                    ikc.ap(p0=pb, p1=pb + 64),
                    mk.ar32_t.ap()[pb:pb + 64, ctx.o + cs[0] + 64:
                                   ctx.o + cs[1]],
                    1.0, 1e-6, ALU.mult, ALU.add),
                    reads=[creg], writes=[ikc.reg(p0=pb, p1=pb + 64)])
                p.emit("dve", lambda e: e.reciprocal(
                    ikc.ap(p0=pb, p1=pb + 64), ikc.ap(p0=pb, p1=pb + 64)),
                    reads=[ikc.reg(p0=pb, p1=pb + 64)],
                    writes=[ikc.reg(p0=pb, p1=pb + 64)])
                p.emit("dve", lambda e: e.tensor_scalar(
                    ctxn.t.ap()[pb:pb + 64, cbase:cbase + 64],
                    mk.ar32_t.ap()[pb:pb + 64, ctx.o + cs[0]:
                                   ctx.o + cs[0] + 64],
                    ikc.ap(p0=pb, p1=pb + 64), None, ALU.mult),
                    reads=[creg, ikc.reg(p0=pb, p1=pb + 64)],
                    writes=[Region(ctxn.t, pb, pb + 64, cbase, cbase + 64)])
                mk.small.restore()
                bank = mk.ps[6]
                p.emit("pe", lambda e: e.matmul(
                    bank.ap()[0:65, 0:64],
                    ctxn.t.ap()[pb:pb + 64, cbase:cbase + 65],
                    QF.ap(oc * T + u * 64, oc * T + (u + 1) * 64,
                          p0=pb, p1=pb + 64),
                    start=True, stop=True),
                    reads=[Region(ctxn.t, pb, pb + 64, cbase, cbase + 65),
                           QF.reg()],
                    writes=[R(bank)])
                asl = (u * 64, (u + 1) * 64)
                p.emit("dve", lambda e: e.tensor_copy(
                    afm.t.ap()[pb:pb + 64,
                               afm.o + oc * T + asl[0]:afm.o + oc * T + asl[1]],
                    bank.ap()[0:64, 0:64]),
                    reads=[R(bank)],
                    writes=[Region(afm.t, pb, pb + 64, afm.o + oc * T + asl[0],
                                   afm.o + oc * T + asl[1])])
                p.emit("dve", lambda e: e.tensor_copy(
                    mk.ar32_t.ap()[0:1, zrowbuf.o + asl[0]:zrowbuf.o + asl[1]],
                    bank.ap()[64:65, 0:64]),
                    reads=[R(bank)], writes=[zrowbuf.reg()])
            else:
                p.emit("dve", lambda e: e.memset(
                    afm.t.ap()[pb:pb + 64, afm.o + oc * T:afm.o + oc * T + 64],
                    0.0),
                    writes=[Region(afm.t, pb, pb + 64, afm.o + oc * T,
                                   afm.o + oc * T + 64)])
                p.emit("dve", lambda e: e.memset(
                    mk.ar32_t.ap()[0:1, zrowbuf.o:zrowbuf.o + 64], 1.0),
                    writes=[zrowbuf.reg()])
            bank2 = mk.ps[7]
            p.emit("pe", lambda e: e.matmul(
                bank2.ap()[0:64, 0:65],
                KT.t.ap()[half:half + 64,
                          KT.o + jt * 1024 + h * 64:KT.o + jt * 1024 + (h + 1) * 64]
                if False else
                KT.t.ap()[half:half + 64,
                          KT.o + jt * 1024 + h * 64:KT.o + jt * 1024 + (h + 1) * 64],
                mk.ar16_t.ap()[half:half + 64,
                               VT.o + jt * 1040 + h * 65:VT.o + jt * 1040 + (h + 1) * 65],
                start=True, stop=True),
                reads=[Region(KT.t, half, half + 64,
                              KT.o + jt * 1024, KT.o + (jt + 1) * 1024),
                       Region(VT.t, half, half + 64,
                              VT.o + jt * 1040, VT.o + (jt + 1) * 1040)],
                writes=[R(bank2)])
            p.emit("dve", lambda e: e.tensor_tensor(
                mk.ar32_t.ap()[pb:pb + 64, ctx.o + cs[0]:ctx.o + cs[1]],
                mk.ar32_t.ap()[pb:pb + 64, ctx.o + cs[0]:ctx.o + cs[1]],
                bank2.ap()[0:64, 0:65], ALU.add),
                reads=[creg, R(bank2)], writes=[creg])
        for u in range(NB):
            bucket(u)
        mk.ar32.save()
        p.emit("act", lambda e: e.activation(
            mk.ar32_t.ap()[0:1, zrowbuf.o:zrowbuf.o + T],
            mk.ar32_t.ap()[0:1, zrowbuf.o:zrowbuf.o + T], AF.Ln),
            reads=[zrowbuf.reg()], writes=[zrowbuf.reg()])
        p.emit("act", lambda e: e.activation(
            mk.ar32_t.ap()[0:1, zrowbuf.o:zrowbuf.o + T],
            mk.ar32_t.ap()[0:1, zrowbuf.o:zrowbuf.o + T], AF.Exp,
            scale=p.const_col(-1.0)[0:1, :]),
            reads=[zrowbuf.reg()], writes=[zrowbuf.reg()])
        ibc = mk.v32(512)

        def div4(ts4):
            bank = mk.ps[6]
            p.emit("pe", lambda e: e.matmul(
                bank.ap()[pb:pb + 64, :], onesr[0:1, 0:64],
                mk.ar32_t.ap()[0:1, zrowbuf.o + ts4 * 512:
                               zrowbuf.o + (ts4 + 1) * 512],
                start=True, stop=True),
                reads=[zrowbuf.reg()], writes=[R(bank)])
            p.emit("dve", lambda e: e.tensor_copy(
                ibc.ap(p0=pb, p1=pb + 64), bank.ap()[pb:pb + 64, :]),
                reads=[R(bank)], writes=[ibc.reg()])
            asl = (oc * T + ts4 * 512, oc * T + (ts4 + 1) * 512)
            p.emit("dve", lambda e: e.tensor_tensor(
                afm.t.ap()[pb:pb + 64, afm.o + asl[0]:afm.o + asl[1]],
                afm.t.ap()[pb:pb + 64, afm.o + asl[0]:afm.o + asl[1]],
                ibc.ap(p0=pb, p1=pb + 64), ALU.mult),
                reads=[Region(afm.t, pb, pb + 64, afm.o + asl[0],
                              afm.o + asl[1]), ibc.reg()],
                writes=[Region(afm.t, pb, pb + 64, afm.o + asl[0],
                               afm.o + asl[1])])
        for ts4 in range(T // 512):
            div4(ts4)
        mk.ar32.restore()
    for h in range(16):
        head(h)
    mk.page_x(xp, "in")

    def proj(ob):
        wv = kvw[ob % 2]
        ldw(wv, "woT", ob)

        def one(j):
            tt = tt0 + j
            bank = mk.ps[4 + j % 2]

            def mm(t):
                last = None
                for dc in range(8):
                    last = t.matmul(
                        bank.ap()[:, :],
                        afm.t.ap()[:, afm.o + dc * T + j * 128:
                                   afm.o + dc * T + (j + 1) * 128],
                        wv.ap(dc * 512, (dc + 1) * 512),
                        start=(dc == 0), stop=(dc == 7))
                return last
            p.emit("pe", mm,
                   reads=[Region(afm.t, 0, 128, afm.o, afm.o + 8 * T),
                          wv.reg(0, 4096)],
                   writes=[R(bank)])
            xs = (tt * D + ob * 512, tt * D + (ob + 1) * 512)
            xa = x.ap()[:, xs[0]:xs[1]]
            xreg = Region(x, 0, 128, *xs)
            yv = mk.v32(512)
            p.emit("dve", lambda e: e.tensor_tensor(
                yv.ap(), bank.ap()[:, :], bo_bc.ap(ob * 512, (ob + 1) * 512),
                ALU.add),
                reads=[R(bank), bo_bc.reg(ob * 512, (ob + 1) * 512)],
                writes=[yv.reg()])
            p.emit("dve", lambda e: e.tensor_tensor(
                xa, xa, yv.ap(), ALU.add),
                reads=[xreg, yv.reg()], writes=[xreg])
            mk.ar32.off -= 512
        for j in range(TTE):
            one(j)
    proj(0); proj(1)


def mamba_apply(mk, cfg, elem):
    import os as _os
    PH = int(_os.environ.get("MM_PHASE", "9"))
    p = mk.p
    mk.ar16.reset(); mk.ar32.reset(); mk.small.reset()
    TTE = mk.TTE
    T = mk.T
    TC = 256
    NCK = T // TC
    NDI = 16
    tt0 = elem * TTE
    x = mk.x
    wcv = mk.vsm(64)
    bcv = mk.vsm(16)
    bdt = mk.vsm(16)
    Dcl = mk.vsm(16)
    for nm, v in (("wconv_cols", wcv), ("bconv_cols", bcv),
                  ("bdt_cols", bdt), ("D_cols", Dcl)):
        p.emit("sp", lambda e, nm=nm, v=v: e.dma_start(
            out=v.ap(), in_=cfg[nm][:, :]), writes=[v.reg()], dma=True)
    Acl = mk.v32(256)
    p.emit("sp", lambda e: e.dma_start(out=Acl.ap(), in_=cfg["A_cols"][:, :]),
           writes=[Acl.reg()], dma=True)
    p.emit("act", lambda e: e.activation(Acl.ap(), Acl.ap(), AF.Exp),
           reads=[Acl.reg()], writes=[Acl.reg()])
    p.emit("dve", lambda e: e.tensor_single_scalar(
        Acl.ap(), Acl.ap(), -1.0, ALU.mult),
        reads=[Acl.reg()], writes=[Acl.reg()])
    sel96 = mk.v16_in32(4096)
    p.emit("pool", lambda e: e.dma_start(
        out=sel96.ap(p1=96), in_=cfg["sel96"][:, :]),
        writes=[sel96.reg()], dma=True)
    wxs = mk.v16_in32(1536)
    p.emit("pool", lambda e: e.dma_start(
        out=wxs.ap().rearrange("p (c o) -> p c o", c=16),
        in_=cfg["wx_nat"].rearrange("(c p) o -> p c o", p=128)),
        writes=[wxs.reg()], dma=True)
    wdts = mk.v16_in32(2048)
    p.emit("pool", lambda e: e.dma_start(
        out=wdts.ap(p1=64).rearrange("p (c o) -> p c o", c=16),
        in_=cfg["wdt"].rearrange("p (c o) -> p c o", c=16)),
        writes=[wdts.reg()], dma=True)
    negm = mk.vsm(TTE)
    rstd = mk.vsm(TTE)
    mk.ln_stats(list(range(tt0, tt0 + TTE)), rstd, negm)
    xp = cfg["xpage"]
    bneed = 8 * T + 4096 + 8192 + 4096
    bt = x if mk.TT * D >= bneed else p.sbuf(f"mm_borrow{elem}",
                                             [128, bneed], BF16)
    xnT = View(bt, 0, 8 * T)
    zc = View(bt, 8 * T, 4096)
    Bbc = View(bt, 8 * T + 4096, 4096)
    Cbc = View(bt, 8 * T + 8192, 4096)
    y16 = View(bt, 8 * T + 12288, 4096)
    xbr = [mk.v16(16 * 259) for _ in range(2)]
    av = mk.v16(4096)
    bv = mk.v16(4096)
    hv = mk.v16(4096)
    xcs = mk.v16(4096)
    states = mk.v16(256)
    wring = [mk.v16_in32(4096) for _ in range(2)]
    ystage2 = mk.v16_in32(1024)
    stage = BView(mk.ar32_t, ystage2.o, 1024)
    tmp16 = View(mk.ar16_t, bv.o, 4096)
    big32 = mk.v32(4096)
    dblf = mk.v32(256)
    dblT = None
    p.emit("dve", lambda e: e.memset(states.ap(), 0.0), writes=[states.reg()])

    def mkxnT(j):
        tt = tt0 + j
        mk.ar16.save()
        xn = mk.v16(1024)
        mk.ln_apply(tt, negm, rstd, xn.ap(), xn.reg(), j=j)
        dst = bt.ap().bitcast(BF16) if isinstance(bt, type(mk.ar32_t)) else None
        dstap = bt.ap()[:, xnT.o:xnT.o + 16384].rearrange(
            "p (c t) -> p c t", c=8)[:, :, j * 128:(j + 1) * 128]
        mk.transpose8(lambda dc: xn.ap(dc * 128, (dc + 1) * 128),
                      dstap, Region(bt, 0, 128, xnT.o, xnT.o + 16384),
                      [xn.reg()], pb=j % 2)
        mk.ar16.restore()
    mk.page_x(xp, "out")

    xtring = [mk.v16(1024) for _ in range(2)]
    xnring = [mk.v16(1024) for _ in range(2)]

    def mkxnT2(j):
        tt = tt0 + j
        mk.ar16.save()
        xt = xtring[j % 2]
        xn = xnring[j % 2]
        p.emit("sp", lambda e: e.dma_start(
            out=xt.ap(), in_=xp.ap()[:, tt * D:(tt + 1) * D]),
            reads=[Region(xp, 0, 128, tt * D, (tt + 1) * D)],
            writes=[xt.reg()], dma=True)
        p.emit("dve", lambda e: e.tensor_scalar(
            xn.ap(), xt.ap(), negm.ap(j, j + 1), rstd.ap(j, j + 1),
            ALU.add, ALU.mult),
            reads=[xt.reg(), negm.reg(j, j + 1), rstd.reg(j, j + 1)],
            writes=[xn.reg()])
        dstap = bt.ap()[:, xnT.o:xnT.o + 8 * T].rearrange(
            "p (c t) -> p c t", c=8)[:, :, j * 128:(j + 1) * 128]
        mk.transpose8(lambda dc: xn.ap(dc * 128, (dc + 1) * 128),
                      dstap, Region(bt, 0, 128, xnT.o, xnT.o + 8 * T),
                      [xn.reg()], pb=j % 2)
        mk.ar16.restore()
    for j in range(TTE):
        mkxnT2(j)

    def xz_mms(c, which, dst_fn, slot=None):
        wsrc = cfg["w_in_x" if which == "x" else "w_in_z"]

        def one(di):
            wv = wring[(di // 4) % 2]
            if di % 4 == 0:
                p.emit("pool", lambda e, wv=wv, di=di: e.dma_start(
                    out=wv.ap(0, 4096).rearrange("p (c o) -> p c o", c=8),
                    in_=wsrc.rearrange("(c p) o -> p c o", p=128)
                    [:, :, di * 128:(di + 4) * 128]),
                    writes=[wv.reg(0, 4096)], dma=True)
            boff = (di % 4) * 128
            bank = mk.ps[4 + di % 2]

            def mm(t):
                last = None
                for dc in range(8):
                    last = t.matmul(
                        bank.ap()[:, 0:TC],
                        wv.ap(dc * 512 + boff, dc * 512 + boff + 128),
                        bt.ap()[:, xnT.o + dc * T + c * TC:
                                xnT.o + dc * T + (c + 1) * TC],
                        start=(dc == 0), stop=(dc == 7))
                return last
            p.emit("pe", mm,
                   reads=[wv.reg(0, 4096),
                          Region(bt, 0, 128, xnT.o, xnT.o + 8 * T)],
                   writes=[R(bank)])
            dst_fn(di, bank)
        for di in range(NDI):
            one(di)

    def chunk(c):
        slot = xbr[c % 2]
        def evx(di, bank):
            ds = (di * 259 + 1, di * 259 + 257)
            p.emit("dve", lambda e: e.tensor_copy(
                slot.ap(*ds), bank.ap()[:, 0:TC]),
                reads=[R(bank)], writes=[slot.reg(*ds)])
        xz_mms(c, "x", evx)
        prev = xbr[(c + 1) % 2]

        def hist(di):
            ds = (di * 259, di * 259 + 1)
            if c == 0:
                p.emit("dve", lambda e: e.memset(slot.ap(*ds), 0.0),
                       writes=[slot.reg(*ds)])
            else:
                p.emit("dve", lambda e: e.tensor_copy(
                    slot.ap(*ds), prev.ap(di * 259 + 256, di * 259 + 257)),
                    reads=[prev.reg(di * 259 + 256, di * 259 + 257)],
                    writes=[slot.reg(*ds)])
        for di in range(NDI):
            hist(di)
        def fut(di):
            ds = (di * 259 + 257, di * 259 + 259)
            p.emit("dve", lambda e: e.tensor_copy(
                prev.ap(*ds), slot.ap(di * 259 + 1, di * 259 + 3)),
                reads=[slot.reg(di * 259 + 1, di * 259 + 3)],
                writes=[prev.reg(*ds)])
        if c > 0:
            for di in range(NDI):
                fut(di)
        if c > 0:
            ssm(c - 1, prev)
        if c == NCK - 1:
            def futz(di):
                ds = (di * 259 + 257, di * 259 + 259)
                p.emit("dve", lambda e: e.memset(slot.ap(*ds), 0.0),
                       writes=[slot.reg(*ds)])
            for di in range(NDI):
                futz(di)
            ssm(c, slot)

    def ssm(c, slot):
        if PH < 2:
            return
        def conv(di):
            a0 = (di * 256, (di + 1) * 256)

            def tap(k):
                src = (di * 259 + k, di * 259 + k + 256)
                if k == 0:
                    p.emit("dve", lambda e: e.tensor_scalar(
                        big32.ap(*a0), slot.ap(*src), wcv.ap(di * 4, di * 4 + 1),
                        bcv.ap(di, di + 1), ALU.mult, ALU.add),
                        reads=[slot.reg(*src), wcv.reg(di * 4, di * 4 + 1),
                               bcv.reg(di, di + 1)],
                        writes=[big32.reg(*a0)])
                else:
                    p.emit("dve", lambda e: e.scalar_tensor_tensor(
                        big32.ap(*a0), slot.ap(*src),
                        wcv.ap(di * 4 + k, di * 4 + k + 1), big32.ap(*a0),
                        ALU.mult, ALU.add),
                        reads=[slot.reg(*src), wcv.reg(di * 4 + k, di * 4 + k + 1),
                               big32.reg(*a0)],
                        writes=[big32.reg(*a0)])
            for k in range(4):
                tap(k)
        for di in range(NDI):
            conv(di)
        p.emit("act", lambda e: e.activation(
            xcs.ap(), big32.ap(), AF.Silu),
            reads=[big32.reg()], writes=[xcs.reg()])
        if PH < 3:
            return
        bank = mk.ps[6]

        def mmdbl(t):
            last = None
            for di in range(NDI):
                last = t.matmul(
                    bank.ap()[0:96, 0:TC],
                    wxs.ap(di * 96, (di + 1) * 96),
                    xcs.ap(di * 256, (di + 1) * 256),
                    start=(di == 0), stop=(di == NDI - 1))
            return last
        p.emit("pe", mmdbl, reads=[wxs.reg(), xcs.reg()], writes=[R(bank)])
        p.emit("dve", lambda e: e.tensor_copy(
            dblf.ap(p1=96), bank.ap()[0:96, 0:TC]),
            reads=[R(bank)], writes=[dblf.reg()])
        dbl16 = View(mk.ar16_t, tmp16.o, 256)
        p.emit("dve", lambda e: e.tensor_copy(
            dbl16.ap(p1=96), dblf.ap(p1=96)),
            reads=[dblf.reg()], writes=[dbl16.reg()])
        def selbc(n):
            bnk = mk.ps[6 if n % 2 == 0 else 7]
            dstv = Bbc if n < 16 else Cbc
            nn = n % 16
            p.emit("pe", lambda e: e.matmul(
                bnk.ap()[:, 0:TC], sel96.ap(n * 128, (n + 1) * 128, p1=96),
                dbl16.ap(p1=96), start=True, stop=True),
                reads=[sel96.reg(), dbl16.reg()], writes=[R(bnk)])
            ds = (nn * 256, (nn + 1) * 256)
            p.emit("dve", lambda e: e.tensor_copy(
                bt.ap()[:, dstv.o + ds[0]:dstv.o + ds[1]], bnk.ap()[:, 0:TC]),
                reads=[R(bnk)],
                writes=[Region(bt, 0, 128, dstv.o + ds[0], dstv.o + ds[1])])
        for n in range(32):
            selbc(n)
        if PH < 4:
            return
        def ddi(di):
            bnk = mk.ps[4 + di % 2]
            p.emit("pe", lambda e: e.matmul(
                bnk.ap()[:, 0:TC], wdts.ap(di * 128, (di + 1) * 128, p1=64),
                dbl16.ap(p1=64), start=True, stop=True),
                reads=[wdts.reg(), dbl16.reg()], writes=[R(bnk)])
            a0 = (di * 256, (di + 1) * 256)
            mk.ar32.save()
            t1 = mk.v32(256)
            p.emit("dve", lambda e: e.tensor_scalar(
                t1.ap(), bnk.ap()[:, 0:TC], bdt.ap(di, di + 1), None, ALU.add),
                reads=[R(bnk), bdt.reg(di, di + 1)], writes=[t1.reg()])
            p.emit("dve", lambda e: e.tensor_single_scalar(
                big32.ap(*a0), t1.ap(), 0.0, ALU.max),
                reads=[t1.reg()], writes=[big32.reg(*a0)])
            p.emit("dve", lambda e: e.scalar_tensor_tensor(
                t1.ap(), t1.ap(), -1.0, t1.ap(), ALU.mult, ALU.max),
                reads=[t1.reg()], writes=[t1.reg()])
            p.emit("act", lambda e: e.activation(
                t1.ap(), t1.ap(), AF.Exp, scale=p.const_col(-1.0)),
                reads=[t1.reg()], writes=[t1.reg()])
            p.emit("act", lambda e: e.activation(
                t1.ap(), t1.ap(), AF.Ln, bias=p.const_col(1.0)),
                reads=[t1.reg()], writes=[t1.reg()])
            p.emit("dve", lambda e: e.tensor_tensor(
                big32.ap(*a0), big32.ap(*a0), t1.ap(), ALU.add),
                reads=[big32.reg(*a0), t1.reg()], writes=[big32.reg(*a0)])
            mk.ar32.restore()
        for di in range(NDI):
            ddi(di)
        zdone = [False]

        def dchunk(di):
            a0 = (di * 256, (di + 1) * 256)
            d3 = big32.ap(*a0).rearrange("p (o t) -> p o t", o=1)\
                .broadcast_to((128, 16, TC))
            A3 = Acl.ap(di * 16, (di + 1) * 16).rearrange(
                "p (n o) -> p n o", o=1).broadcast_to((128, 16, TC))
            av3 = av.ap().rearrange("p (n t) -> p n t", n=16)
            p.emit("pool", lambda e: e.tensor_tensor(
                av3, d3, A3, ALU.mult),
                reads=[big32.reg(*a0), Acl.reg(di * 16, (di + 1) * 16)],
                writes=[av.reg()])
            p.emit("act", lambda e: e.activation(
                av.ap(), av.ap(), AF.Exp),
                reads=[av.reg()], writes=[av.reg()])
            u = ystage2
            p.emit("dve", lambda e: e.tensor_tensor(
                u.ap(0, 256), big32.ap(*a0), xcs.ap(*a0), ALU.mult),
                reads=[big32.reg(*a0), xcs.reg(*a0)], writes=[u.reg(0, 256)])
            u3 = u.ap(0, 256).rearrange("p (o t) -> p o t", o=1)\
                .broadcast_to((128, 16, TC))
            B3 = bt.ap()[:, Bbc.o:Bbc.o + 4096].rearrange(
                "p (n t) -> p n t", n=16)
            bv3 = bv.ap().rearrange("p (n t) -> p n t", n=16)
            p.emit("pool", lambda e: e.tensor_tensor(
                bv3, u3, B3, ALU.mult),
                reads=[u.reg(0, 256), Region(bt, 0, 128, Bbc.o, Bbc.o + 4096)],
                writes=[bv.reg()])
            def scn(n):
                sc0 = (n * 256, (n + 1) * 256)
                stc = di * 16 + n
                p.emit("dve", lambda e: e.tensor_tensor_scan(
                    hv.ap(*sc0), av.ap(*sc0), bv.ap(*sc0),
                    states.ap(stc, stc + 1), ALU.mult, ALU.add),
                    reads=[av.reg(*sc0), bv.reg(*sc0),
                           states.reg(stc, stc + 1)],
                    writes=[hv.reg(*sc0)])
            for n in range(16):
                scn(n)
            p.emit("dve", lambda e: e.tensor_copy(
                states.ap(di * 16, (di + 1) * 16),
                hv.ap().rearrange("p (n t) -> p n t", n=16)[:, :, 255:256]
                .rearrange("p n o -> p (n o)")),
                reads=[hv.reg()],
                writes=[states.reg(di * 16, (di + 1) * 16)])
            C3 = bt.ap()[:, Cbc.o:Cbc.o + 4096].rearrange(
                "p (n t) -> p n t", n=16)
            t3 = tmp16.ap().rearrange("p (n t) -> p n t", n=16)
            p.emit("pool", lambda e: e.tensor_tensor(
                t3, hv.ap().rearrange("p (n t) -> p n t", n=16), C3, ALU.mult),
                reads=[hv.reg(), Region(bt, 0, 128, Cbc.o, Cbc.o + 4096)],
                writes=[tmp16.reg()])
            mk.ar32.save()
            yy = mk.v32(256)
            p.emit("dve", lambda e: e.tensor_reduce(
                yy.ap(), tmp16.ap().rearrange("p (n t) -> p t n", n=16),
                axis=AX.X, op=ALU.add),
                reads=[tmp16.reg()], writes=[yy.reg()])
            p.emit("dve", lambda e: e.scalar_tensor_tensor(
                yy.ap(), xcs.ap(*a0), Dcl.ap(di, di + 1), yy.ap(),
                ALU.mult, ALU.add),
                reads=[xcs.reg(*a0), Dcl.reg(di, di + 1), yy.reg()],
                writes=[yy.reg()])
            if not zdone[0]:
                pass
            zs = (di * 256, (di + 1) * 256)
            sz = View(mk.ar16_t, stage.o, 256)
            p.emit("act", lambda e: e.activation(
                sz.ap(0, 256), bt.ap()[:, zc.o + zs[0]:zc.o + zs[1]], AF.Silu),
                reads=[Region(bt, 0, 128, zc.o + zs[0], zc.o + zs[1])],
                writes=[sz.reg(0, 256)])
            p.emit("dve", lambda e: e.tensor_tensor(
                bt.ap()[:, y16.o + zs[0]:y16.o + zs[1]], yy.ap(),
                sz.ap(0, 256), ALU.mult),
                reads=[yy.reg(), sz.reg(0, 256)],
                writes=[Region(bt, 0, 128, y16.o + zs[0], y16.o + zs[1])])
            mk.ar32.restore()
        if PH < 5:
            return
        def evz(di, bank):
            ds = (di * 256, (di + 1) * 256)
            p.emit("dve", lambda e: e.tensor_copy(
                bt.ap()[:, zc.o + ds[0]:zc.o + ds[1]], bank.ap()[:, 0:TC]),
                reads=[R(bank)],
                writes=[Region(bt, 0, 128, zc.o + ds[0], zc.o + ds[1])])
        xz_mms(c, "z", evz)
        for di in range(NDI):
            dchunk(di)
        if PH < 6:
            return
        def proj(di_outer_unused=None):
            def one(di):
                wv = wring[(di // 4) % 2]
                if di % 4 == 0:
                    p.emit("pool", lambda e, wv=wv, di=di: e.dma_start(
                        out=wv.ap(0, 4096).rearrange("p (c o) -> p c o", c=4),
                        in_=cfg["wout"].rearrange("(c p) o -> p c o", p=128)
                        [:, di:di + 4, :]),
                        writes=[wv.reg(0, 4096)],
                        dma=True)
                boff = (di % 4) * 1024
                for tp in range(2):
                    for ob in range(2):
                        bnk = mk.ps[tp * 2 + ob]
                        p.emit("pe", lambda e, tp=tp, ob=ob, bnk=bnk, wv=wv,
                               di=di, boff=boff: e.matmul(
                            bnk.ap()[:, :],
                            bt.ap()[:, y16.o + di * 256 + tp * 128:
                                    y16.o + di * 256 + (tp + 1) * 128],
                            wv.ap(boff + ob * 512, boff + (ob + 1) * 512),
                            start=(di == 0), stop=(di == NDI - 1)),
                            reads=[Region(bt, 0, 128, y16.o, y16.o + 4096),
                                   wv.reg(0, 4096)],
                            writes=[R(bnk)])
            for di in range(NDI):
                one(di)

            def evac(tp):
                for ob in range(2):
                    bnk = mk.ps[tp * 2 + ob]
                    p.emit("dve", lambda e, ob=ob, bnk=bnk: e.tensor_copy(
                        stage.ap(ob * 512, (ob + 1) * 512), bnk.ap()[:, :]),
                        reads=[R(bnk)],
                        writes=[stage.reg(ob * 512, (ob + 1) * 512)])
                p.emit("sp", lambda e, tp=tp: e.dma_start(
                    out=cfg["ydram"].ap()[:, (c * 2 + tp) * D:
                                          (c * 2 + tp + 1) * D],
                    in_=stage.ap()),
                    reads=[stage.reg()],
                    writes=[Region(cfg["ydram"], 0, 128, (c * 2 + tp) * D,
                                   (c * 2 + tp + 1) * D)], dma=True)
            evac(0); evac(1)
        proj()
    for c in range(NCK):
        chunk(c)
    mk.page_x(xp, "in")

    def finadd(j):
        tt = tt0 + j
        xs = (tt * D, (tt + 1) * D)
        xa = x.ap()[:, xs[0]:xs[1]]
        xreg = Region(x, 0, 128, *xs)
        p.emit("sp", lambda e: e.dma_start(
            out=ystage2.ap(), in_=cfg["ydram"].ap()[:, j * D:(j + 1) * D]),
            reads=[Region(cfg["ydram"], 0, 128, j * D, (j + 1) * D)],
            writes=[ystage2.reg()], dma=True)
        p.emit("dve", lambda e: e.tensor_tensor(
            xa, xa, ystage2.ap(), ALU.add),
            reads=[xreg, ystage2.reg()], writes=[xreg])
    for j in range(TTE):
        finadd(j)


def build_program():
    p = Prog()
    nc = p.nc
    F = F32
    inp = {}

    def di(name, shape):
        inp[name] = nc.dram_tensor(name, list(shape), F, kind="ExternalInput").ap()
        return inp[name]
    xin = di("xin", [4096, 1024])
    di("iden", [128, 128])
    di("E8", [8, 128])
    di("sel8", [8, 1024])
    di("sel96", [96, 4096])
    di("ropeT", [8, 128, 128])
    di("l0_w1T", [1024, 4096]); di("l0_w2T", [4096, 1024])
    di("l0_wqkv", [1024, 3072]); di("l0_bqc", [128, 24]); di("l0_bqr", [1, 3072])
    di("l0_wproj", [1024, 1024]); di("l0_bpr", [1, 1024])
    for nm in ("a_q", "b_q", "a_v", "b_v"):
        di("l0_" + nm, [64, 16] if nm.startswith("a") else [16, 64])
    di("l1_wgT", [1024, 8]); di("l1_winT", [8, 1024, 4096])
    di("l1_woutT", [8, 4096, 1024])
    for nm in ("wqT", "wkT", "wvT", "woT"):
        di("l1_" + nm, [1024, 1024])
    di("l1_bo", [1, 1024])
    di("l2_router", [1024, 8]); di("l2_lnw", [1, 1024])
    di("l2_w1", [8, 1024, 2816]); di("l2_w3", [8, 1024, 2816])
    di("l2_w2", [8, 2816, 1024])
    di("l2_winx", [1024, 2048]); di("l2_winz", [1024, 2048])
    di("l2_wconv", [128, 64]); di("l2_bconv", [128, 16])
    di("l2_wx", [2048, 96]); di("l2_wdt", [64, 2048]); di("l2_bdt", [128, 16])
    di("l2_A", [128, 256]); di("l2_D", [128, 16]); di("l2_wout", [2048, 1024])
    di("l3_qkvw", [1024, 3072]); di("l3_qkvbc", [128, 24]); di("l3_qkvbr", [1, 3072])
    di("l3_outw", [1024, 1024]); di("l3_outbr", [1, 1024])
    di("l3_wg", [2816, 1024]); di("l3_wu", [2816, 1024]); di("l3_wd", [1024, 2816])
    di("l3_wgT", [1024, 2816]); di("l3_wuT", [1024, 2816]); di("l3_wdT", [2816, 1024])
    xout = nc.dram_tensor("xout", [4096, 1024], F, kind="ExternalOutput").ap()

    mk = MK(p, T_elem=2048, n_elem=2)
    mk.load_const_tiles(inp["iden"])
    xpage = p.dram("xpage", [128, mk.TT * 1024], BF16)
    ydram = p.dram("ydram", [128, mk.TTE * 1024], BF16)
    for tt in range(mk.TT):
        p.emit("pool", lambda e, tt=tt: e.dma_start(
            out=mk.x.ap()[:, tt * 1024:(tt + 1) * 1024],
            in_=xin[tt * 128:(tt + 1) * 128, :]),
            writes=[Region(mk.x, 0, 128, tt * 1024, (tt + 1) * 1024)], dma=True)
    T1d, T2d, scols = l0_ffn_prepare(mk, inp["l0_w1T"], inp["l0_w2T"])
    l3n = {"wg": inp["l3_wg"], "wu": inp["l3_wu"], "wd": inp["l3_wd"]}
    l3t = {"wg": inp["l3_wgT"], "wu": inp["l3_wuT"], "wd": inp["l3_wdT"]}
    wqd = l3_ffn_prepare(mk, l3n, l3t, inp["E8"], inp["iden"])
    l0a = {"wqkv": inp["l0_wqkv"], "bqkv_cols": inp["l0_bqc"],
           "bqkv_row": inp["l0_bqr"], "wproj": inp["l0_wproj"],
           "bproj_row": inp["l0_bpr"], "xpage": xpage,
           "lora": {nm: inp["l0_" + nm] for nm in ("a_q", "b_q", "a_v", "b_v")}}
    l3a = {"wqkv": inp["l3_qkvw"], "bqkv_cols": inp["l3_qkvbc"],
           "bqkv_row": inp["l3_qkvbr"], "wproj": inp["l3_outw"],
           "bproj_row": inp["l3_outbr"], "xpage": xpage, "rope": inp["ropeT"]}
    l1m = {"kind": "l1", "w_gT": inp["l1_wgT"], "w_in": inp["l1_winT"],
           "w_out": inp["l1_woutT"], "sel8": inp["sel8"]}
    l2m = {"kind": "l2", "router": inp["l2_router"], "lnw_row": inp["l2_lnw"],
           "w1": inp["l2_w1"], "w3": inp["l2_w3"], "w2": inp["l2_w2"],
           "sel8": inp["sel8"]}
    l1at = {"wqT": inp["l1_wqT"], "wkT": inp["l1_wkT"], "wvT": inp["l1_wvT"],
            "woT": inp["l1_woT"], "bo_row": inp["l1_bo"], "xpage": xpage}
    l2at = {"w_in_x": inp["l2_winx"], "w_in_z": inp["l2_winz"],
            "wconv_cols": inp["l2_wconv"], "bconv_cols": inp["l2_bconv"],
            "wx_nat": inp["l2_wx"], "wdt": inp["l2_wdt"],
            "bdt_cols": inp["l2_bdt"], "A_cols": inp["l2_A"],
            "D_cols": inp["l2_D"], "wout": inp["l2_wout"],
            "sel96": inp["sel96"], "xpage": xpage, "ydram": ydram}
    import os as _os
    NSUB = int(_os.environ.get("MK_NSUB", "16"))
    SNAP = bool(_os.environ.get("MK_SNAP"))
    snaps = []
    state = {"i": 0}

    def snap():
        state["i"] += 1
        if not SNAP:
            return
        i = state["i"]
        sd = nc.dram_tensor(f"snap{i}", [4096, 1024], F, kind="ExternalOutput").ap()
        for tt in range(mk.TT):
            p.emit("pool", lambda e, tt=tt, sd=sd: e.dma_start(
                out=sd[tt * 128:(tt + 1) * 128, :],
                in_=mk.x.ap()[:, tt * 1024:(tt + 1) * 1024]),
                reads=[Region(mk.x, 0, 128, tt * 1024, (tt + 1) * 1024)],
                dma=True)
    seq = [
        lambda: l0_ffn_apply(mk, T1d, T2d, scols),
        lambda: attn_apply(mk, l0a, 0),
        lambda: attn_apply(mk, l0a, 1),
        lambda: l0_ffn_apply(mk, T1d, T2d, scols),
        lambda: moe_apply(mk, l1m, xpage),
        lambda: l1_attn_apply(mk, l1at, 0),
        lambda: l1_attn_apply(mk, l1at, 1),
        lambda: moe_apply(mk, l1m, xpage),
        lambda: moe_apply(mk, l2m, xpage),
        lambda: mamba_apply(mk, l2at, 1 if _os.environ.get("MM_SWAP") else 0),
        lambda: mamba_apply(mk, l2at, 0 if _os.environ.get("MM_SWAP") else 1),
        lambda: moe_apply(mk, l2m, xpage),
        lambda: l3_ffn_apply(mk, wqd),
        lambda: attn_apply(mk, l3a, 0),
        lambda: attn_apply(mk, l3a, 1),
        lambda: l3_ffn_apply(mk, wqd),
    ]
    for fi, f in enumerate(seq):
        if fi < NSUB:
            f()
            snap()
    for tt in range(mk.TT):
        p.emit("pool", lambda e, tt=tt: e.dma_start(
            out=xout[tt * 128:(tt + 1) * 128, :],
            in_=mk.x.ap()[:, tt * 1024:(tt + 1) * 1024]),
            reads=[Region(mk.x, 0, 128, tt * 1024, (tt + 1) * 1024)], dma=True)
    return p


def _l3_perm():
    perm = np.empty(3072, np.int64)
    for sec in range(3):
        for h in range(16):
            for d_ in range(64):
                perm[sec * 1024 + h * 64 + d_] = h * 192 + sec * 64 + d_
    return perm


def host_inputs(core, kw):
    C = np.ascontiguousarray
    HEAD_DIM = 64
    inv_freq = 1.0 / 10000.0 ** (np.arange(0, HEAD_DIM, 2) / HEAD_DIM)
    cos_t = np.cos(np.arange(16)[:, None] * inv_freq[None, :]).astype(np.float32)
    sin_t = np.sin(np.arange(16)[:, None] * inv_freq[None, :]).astype(np.float32)
    ropeT = np.zeros((8, 128, 128), np.float32)
    for c in range(8):
        for hh in range(2):
            h = 2 * c + hh
            Rm = np.zeros((64, 64), np.float32)
            for i in range(32):
                Rm[i, i] = cos_t[h, i]; Rm[i, i + 32] = -sin_t[h, i]
                Rm[i + 32, i + 32] = cos_t[h, i]; Rm[i + 32, i] = sin_t[h, i]
            ropeT[c, hh * 64:(hh + 1) * 64, hh * 64:(hh + 1) * 64] = Rm.T
    E8 = np.zeros((8, 128), np.float32)
    for b in range(8):
        E8[b, b * 16:(b + 1) * 16] = 1.0
    sel8 = np.zeros((8, 1024), np.float32)
    for e in range(8):
        sel8[e, e * 128:(e + 1) * 128] = 1.0
    sel96 = np.zeros((96, 4096), np.float32)
    for n in range(16):
        sel96[64 + n, n * 128:(n + 1) * 128] = 1.0
        sel96[80 + n, (16 + n) * 128:(17 + n) * 128] = 1.0
    W_in = kw["L2_attn_W_in"]
    W_conv = kw["L2_attn_W_conv"]
    ins = {
        "xin": C(kw["L0_attn_x"][2 * core:2 * core + 2].reshape(4096, 1024)),
        "iden": np.eye(128, dtype=np.float32),
        "E8": E8, "sel8": sel8, "sel96": sel96, "ropeT": ropeT,
        "l0_w1T": C(kw["L0_ffn_w1"].T), "l0_w2T": C(kw["L0_ffn_w2"].T),
        "l0_wqkv": kw["L0_attn_w_qkv"],
        "l0_bqc": C(kw["L0_attn_b_qkv"].reshape(24, 128).T),
        "l0_bqr": kw["L0_attn_b_qkv"][None, :],
        "l0_wproj": kw["L0_attn_w_proj"],
        "l0_bpr": kw["L0_attn_b_proj"][None, :],
        "l0_a_q": kw["L0_attn_a_q"], "l0_b_q": kw["L0_attn_b_q"],
        "l0_a_v": kw["L0_attn_a_v"], "l0_b_v": kw["L0_attn_b_v"],
        "l1_wgT": C(kw["L1_ffn_w_gate"].T),
        "l1_winT": C(kw["L1_ffn_w_in"].transpose(0, 2, 1)),
        "l1_woutT": C(kw["L1_ffn_w_out"].transpose(0, 2, 1)),
        "l1_wqT": C(kw["L1_attn_Wq"].T), "l1_wkT": C(kw["L1_attn_Wk"].T),
        "l1_wvT": C(kw["L1_attn_Wv"].T), "l1_woT": C(kw["L1_attn_Wo"].T),
        "l1_bo": kw["L1_attn_bo"][None, :],
        "l2_router": kw["L2_ffn_router_w"],
        "l2_lnw": kw["L2_ffn_ln_weight"][None, :],
        "l2_w1": kw["L2_ffn_w1"], "l2_w3": kw["L2_ffn_w3"],
        "l2_w2": kw["L2_ffn_w2"],
        "l2_winx": C(W_in[:, 0::2]), "l2_winz": C(W_in[:, 1::2]),
        "l2_wconv": C(W_conv[:, 0, :].T.reshape(16, 128, 4)
                      .transpose(1, 0, 2).reshape(128, 64)),
        "l2_bconv": C(kw["L2_attn_b_conv"].reshape(16, 128).T),
        "l2_wx": kw["L2_attn_W_x"], "l2_wdt": kw["L2_attn_W_dt"],
        "l2_bdt": C(kw["L2_attn_b_dt"].reshape(16, 128).T),
        "l2_A": C(kw["L2_attn_A_log"].reshape(16, 128, 16)
                  .transpose(1, 0, 2).reshape(128, 256)),
        "l2_D": C(kw["L2_attn_D_param"].reshape(16, 128).T),
        "l2_wout": kw["L2_attn_W_out"],
        "l3_qkvw": C(kw["L3_attn_qkv_w"][:, _l3_perm()]),
        "l3_qkvbc": C(kw["L3_attn_qkv_b"][_l3_perm()].reshape(24, 128).T),
        "l3_qkvbr": kw["L3_attn_qkv_b"][_l3_perm()][None, :],
        "l3_outw": kw["L3_attn_out_w"],
        "l3_outbr": kw["L3_attn_out_b"][None, :],
        "l3_wg": kw["L3_ffn_w_gate"], "l3_wu": kw["L3_ffn_w_up"],
        "l3_wd": kw["L3_ffn_w_down"],
        "l3_wgT": C(kw["L3_ffn_w_gate"].T), "l3_wuT": C(kw["L3_ffn_w_up"].T),
        "l3_wdT": C(kw["L3_ffn_w_down"].T),
    }
    return {k: np.ascontiguousarray(v, dtype=np.float32) for k, v in ins.items()}


def kernel(**inputs):
    p = build_program()
    ncf = p.finalize()
    shared = host_inputs(0, inputs)
    in_maps = [shared]
    for core in range(1, 8):
        m = dict(shared)
        m["xin"] = np.ascontiguousarray(
            inputs["L0_attn_x"][2 * core:2 * core + 2].reshape(4096, 1024),
            dtype=np.float32)
        in_maps.append(m)
    res = run_bass_kernel_spmd(ncf, in_maps, core_ids=list(range(8)))
    out = np.empty((16, 2048, 1024), np.float32)
    for core in range(8):
        out[2 * core:2 * core + 2] = \
            res.results[core]["xout"].reshape(2, 2048, 1024)
    return out


# version 11
# speedup vs baseline: 1.2485x; 1.0066x over previous
import math
import sys as _sys
import numpy as np
import concourse.bass as bass
import concourse.mybir as mybir
from concourse.bass_utils import run_bass_kernel_spmd

F32 = mybir.dt.float32
BF16 = mybir.dt.bfloat16
AF = mybir.ActivationFunctionType
ALU = mybir.AluOpType
AX = mybir.AxisListType


STREAMS = ("pe", "act", "dve", "pool", "sp")
ENG_ATTR = {"pe": "tensor", "act": "scalar", "dve": "vector",
            "pool": "gpsimd", "sp": "sync"}
DMA_QUEUES = {"sp": "q_sp", "act": "q_act", "pool": "q_pool"}


def _overlap(a, b):
    return a[0] < b[1] and b[0] < a[1] and a[2] < b[3] and b[2] < a[3]


class Region:
    __slots__ = ("t", "box")

    def __init__(self, t, p0, p1, f0, f1):
        self.t = t
        self.box = (p0, p1, f0, f1)


def R(t, p0=0, p1=128, f0=0, f1=None):
    if f1 is None:
        f1 = 1 << 40
    return Region(t, p0, p1, f0, f1)


class Prog:
    def __init__(self, name="mk"):
        self.nc = bass.Bass("TRN2", target_bir_lowering=False)
        self.ops = {s: [] for s in STREAMS}
        self.meta = {s: [] for s in STREAMS}
        self.sems = {}
        for s in STREAMS:
            self.sems[s] = self.nc.alloc_semaphore(name=f"c_{s}")
        for q in DMA_QUEUES.values():
            self.sems[q] = self.nc.alloc_semaphore(name=q)
        self.cnt = {k: 0 for k in self.sems}
        self.vc = {s: {} for s in STREAMS}
        self.vc_shared = {s: False for s in STREAMS}
        self.ckpt = {k: {} for k in self.sems}
        self.recs = {}
        self.n_waits = 0
        self.n_ops = 0
        self._const_cols = {}
        self._const_pool = None
        self._const_used = 0
        self._const_row_pool = None
        self._const_row_used = 0
        self._const_fenced = False

    def fence_consts(self, extra_regions=()):
        regs = list(extra_regions)
        if self._const_pool is not None:
            regs.append(R(self._const_pool))
        if self._const_row_pool is not None:
            regs.append(Region(self._const_row_pool, 0, 1, 0, 512))
        for s in STREAMS:
            self.emit(s, lambda e: e.nop(), reads=regs)
        self._const_fenced = True

    def sbuf(self, name, shape, dtype):
        return self.nc.alloc_sbuf_tensor(name, shape, dtype)

    def psum(self, name, shape, dtype=F32):
        return self.nc.alloc_psum_tensor(name, shape, dtype)

    def dram(self, name, shape, dtype, kind="Internal"):
        return self.nc.dram_tensor(name, shape, dtype, kind=kind)

    def const_col(self, val):
        if val in self._const_cols:
            return self._const_cols[val]
        assert not self._const_fenced, f"const_col({val}) after fence"
        if self._const_pool is None:
            self._const_pool = self.sbuf("const_pool", [128, 64], F32)
        i = self._const_used
        assert i < 64, "const pool exhausted"
        self._const_used += 1
        ap = self._const_pool.ap()[:, i:i + 1]
        self.emit("dve", lambda e, ap=ap, v=float(val): e.memset(ap, v),
                  writes=[Region(self._const_pool, 0, 128, i, i + 1)])
        self._const_cols[val] = ap
        return ap

    def const_row(self, val):
        key = ("row", val)
        if key in self._const_cols:
            return self._const_cols[key]
        assert not self._const_fenced, "const_row after fence"
        if self._const_row_pool is None:
            self._const_row_pool = self.sbuf("const_rows", [1, 512], F32)
        i = self._const_row_used
        assert i < 4
        self._const_row_used += 1
        ap = self._const_row_pool.ap()[0:1, i * 128:(i + 1) * 128]
        self.emit("dve", lambda e, ap=ap, v=float(val): e.memset(ap, v),
                  writes=[Region(self._const_row_pool, 0, 1, i * 128, (i + 1) * 128)])
        self._const_cols[key] = ap
        return ap

    def _freeze_vc(self, s):
        self.vc_shared[s] = True
        return self.vc[s]

    def _own_vc(self, s):
        if self.vc_shared[s]:
            self.vc[s] = dict(self.vc[s])
            self.vc_shared[s] = False
        return self.vc[s]

    def _need_edges(self, s, regions, kinds):
        edges = {}
        for reg, kind in zip(regions, kinds):
            rec = self.recs.get(id(reg.t))
            if rec is None:
                continue
            confl = rec["w"] if kind == "r" else rec["w"] + rec["r"]
            for box, sem, val in confl:
                if _overlap(box, reg.box):
                    if edges.get(sem, 0) < val:
                        edges[sem] = val
        vc = self.vc[s]
        out = {m: v for m, v in edges.items() if vc.get(m, 0) < v}
        return out

    def _apply_edges(self, s, edges):
        if not edges:
            return
        vc = self._own_vc(s)
        for m, v in edges.items():
            ck = self.ckpt[m].get(v)
            if ck:
                for m2, v2 in ck.items():
                    if vc.get(m2, 0) < v2:
                        vc[m2] = v2
            if vc.get(m, 0) < v:
                vc[m] = v
        self.n_waits += len(edges)

    def _record(self, regions, kinds, sem, val):
        for reg, kind in zip(regions, kinds):
            rec = self.recs.setdefault(id(reg.t), {"w": [], "r": []})
            if kind == "w":
                box = reg.box
                rec["w"] = [e for e in rec["w"] if not _covers(box, e[0])]
                rec["r"] = [e for e in rec["r"] if not _covers(box, e[0])]
                rec["w"].append((box, sem, val))
            else:
                rec["r"].append((reg.box, sem, val))

    def emit(self, s, fn, reads=(), writes=(), dma=False):
        self.meta[s].append(_sys._getframe(1).f_lineno)
        regions = list(reads) + list(writes)
        kinds = ["r"] * len(reads) + ["w"] * len(writes)
        edges = self._need_edges(s, regions, kinds)
        if dma:
            q = DMA_QUEUES[s]
            prev = self.cnt[q]
            if prev and self.vc[s].get(q, 0) < prev:
                edges[q] = max(edges.get(q, 0), prev)
            sem, amt = q, 16
        else:
            sem, amt = s, 1
        self._apply_edges(s, edges)
        self.cnt[sem] += amt
        val = self.cnt[sem]
        self.ckpt[sem][val] = self._freeze_vc(s)
        self._record(regions, kinds, sem, val)
        semobj = {m: self.sems[m] for m in edges}
        mys = self.sems[sem]
        wl = sorted(edges.items())

        def op(eng, fn=fn, wl=wl, semobj=semobj, mys=mys, amt=amt):
            for m, v in wl:
                eng.wait_ge(semobj[m], v)
            inst = fn(eng)
            inst.then_inc(mys, amt)
        self.ops[s].append(op)
        self.n_ops += 1
        return val

    def barrier_val(self, stream_or_sem):
        return self.cnt[stream_or_sem]

    def finalize(self, output_dmas=None):
        nc = self.nc
        final = {q: self.cnt[q] for q in DMA_QUEUES.values() if self.cnt[q]}

        def fin(eng, final=final):
            for m, v in final.items():
                eng.wait_ge(self.sems[m], v)
        self.ops["sp"].append(fin)
        with nc.Block() as block:
            for s in STREAMS:
                getattr(block, ENG_ATTR[s])(_runner(self.ops[s]))
        return nc


def _covers(a, b):
    return a[0] <= b[0] and a[1] >= b[1] and a[2] <= b[2] and a[3] >= b[3]


def _runner(ops):
    def _(eng):
        for f in ops:
            f(eng)
    return _



MAGIC = float(3 << 22)
D = 1024
ND = D // 128


class Arena:

    def __init__(self, t, n):
        self.t = t
        self.n = n
        self.off = 0
        self.marks = []

    def alloc(self, n):
        assert self.off + n <= self.n, f"arena overflow {self.off}+{n}>{self.n}"
        o = self.off
        self.off += n
        return o

    def save(self):
        self.marks.append(self.off)

    def restore(self):
        self.off = self.marks.pop()

    def reset(self):
        self.off = 0
        self.marks = []


class View:
    __slots__ = ("t", "o", "n")

    def __init__(self, t, o, n):
        self.t, self.o, self.n = t, o, n

    def ap(self, f0=0, f1=None, p0=0, p1=128):
        if f1 is None:
            f1 = self.n
        return self.t.ap()[p0:p1, self.o + f0:self.o + f1]

    def reg(self, f0=0, f1=None, p0=0, p1=128):
        if f1 is None:
            f1 = self.n
        return Region(self.t, p0, p1, self.o + f0, self.o + f1)


class BView:
    __slots__ = ("t", "o", "n")

    def __init__(self, t, o, n):
        self.t, self.o, self.n = t, o, n

    def ap(self, f0=0, f1=None, p0=0, p1=128):
        if f1 is None:
            f1 = self.n
        return self.t.ap().bitcast(BF16)[p0:p1, self.o + f0:self.o + f1]

    def reg(self, f0=0, f1=None, p0=0, p1=128):
        if f1 is None:
            f1 = self.n
        return Region(self.t, p0, p1, (self.o + f0) // 2,
                      (self.o + f1 + 1) // 2)


class MK:

    def __init__(self, p: Prog, T_elem=2048, n_elem=2):
        self.p = p
        self.T = T_elem
        self.NE = n_elem
        self.NT = T_elem * n_elem
        self.TT = self.NT // 128
        self.TTE = T_elem // 128
        nc = p.nc
        self.x = p.sbuf("x_res", [128, self.TT * D], BF16)
        self.ar16_t = p.sbuf("ar16", [128, 34304], BF16)
        self.ar32_t = p.sbuf("ar32", [128, 17408], F32)
        self.small_t = p.sbuf("small", [128, 960], F32)
        self.ident = p.sbuf("ident", [128, 128], BF16)
        self.id32 = p.sbuf("id32m", [128, 128], F32)
        self.ones_bf = p.sbuf("ones_bf", [128, 128], BF16)
        self.ar16 = Arena(self.ar16_t, 34304)
        self.ar32 = Arena(self.ar32_t, 17408)
        self.small = Arena(self.small_t, 960)
        self.nbr_t = p.sbuf("nbr", [128, 32], F32)
        self.ps = [p.psum(f"ps{i}", [128, 512], F32) for i in range(8)]

    def v16(self, n):
        return View(self.ar16_t, self.ar16.alloc(n), n)

    def v16_in32(self, n):
        assert n % 2 == 0
        o = self.ar32.alloc(n // 2)
        return BView(self.ar32_t, o * 2, n)

    def v32(self, n):
        return View(self.ar32_t, self.ar32.alloc(n), n)

    def vsm(self, n):
        return View(self.small_t, self.small.alloc(n), n)

    CONST_VALS = (1.0, -0.5, 1e-5, 0.125, -1.0, 0.5, 127.0,
                  1.0 / 127.0, 1.0 / 1024.0, 0.3535533905932738, 1e-6)

    def load_const_tiles(self, identd, onesd=None):
        p = self.p
        p.emit("pool", lambda e: e.dma_start(out=self.ident.ap()[:, :],
                                             in_=identd[:, :]),
               writes=[R(self.ident)], dma=True)
        p.emit("sp", lambda e: e.dma_start(out=self.id32.ap()[:, :],
                                           in_=identd[:, :]),
               writes=[R(self.id32)], dma=True)
        p.emit("dve", lambda e: e.memset(self.ones_bf.ap()[:, :], 1.0),
               writes=[R(self.ones_bf)])
        for v in self.CONST_VALS:
            p.const_col(v)
        p.const_row(1.0)
        p.fence_consts([R(self.ident), R(self.id32), R(self.ones_bf)])

    def transpose8(self, src_fn, dst_ap, dst_reg, reads, pb=0, n=8):
        p = self.p
        bank = self.ps[pb]
        bap = bank.ap().bitcast(BF16)

        def tp(t):
            last = None
            for i in range(n):
                last = t.transpose(bap[:, i * 128:(i + 1) * 128],
                                   src_fn(i), self.ident.ap()[:, :])
            return last
        p.emit("pe", tp, reads=list(reads) + [R(self.ident)], writes=[R(bank)])
        p.emit("dve", lambda e: e.tensor_copy(dst_ap, bap[:, 0:n * 128]),
               reads=[R(bank)], writes=[dst_reg])

    def page_x(self, xp, direction, order=None):
        p = self.p
        tiles = list(order) if order is not None else list(range(self.TT))
        for tt in tiles:
            fs = (tt * D, (tt + 1) * D)
            if direction == "out":
                p.emit("sp", lambda e, fs=fs: e.dma_start(
                    out=xp.ap()[:, fs[0]:fs[1]],
                    in_=self.x.ap()[:, fs[0]:fs[1]]),
                    reads=[Region(self.x, 0, 128, *fs)],
                    writes=[Region(xp, 0, 128, *fs)], dma=True)
            else:
                p.emit("sp", lambda e, fs=fs: e.dma_start(
                    out=self.x.ap()[:, fs[0]:fs[1]],
                    in_=xp.ap()[:, fs[0]:fs[1]]),
                    reads=[Region(xp, 0, 128, *fs)],
                    writes=[Region(self.x, 0, 128, *fs)], dma=True)

    def ln_stats(self, tt_tiles, rstd_v, negm_v, scale_extra=None):
        p = self.p
        x = self.x
        n = len(tt_tiles)
        self.ar16.save(); self.small.save()
        sq = self.vsm(n)
        sm = negm_v
        var = self.vsm(n)
        tmps = [self.v16(1024) for _ in range(2)]
        for j, tt in enumerate(tt_tiles):
            xa = x.ap()[:, tt * D:(tt + 1) * D]
            xr = Region(x, 0, 128, tt * D, (tt + 1) * D)
            tmp16 = tmps[j % 2]
            p.emit("act", lambda e, xa=xa, tmp16=tmp16, j=j: e.activation(
                tmp16.ap(), xa, AF.Square, accum_out=sq.ap(j, j + 1)),
                reads=[xr], writes=[tmp16.reg(), sq.reg(j, j + 1)])
            p.emit("dve", lambda e, xa=xa, j=j: e.reduce_sum(
                sm.ap(j, j + 1), xa, axis=AX.X, negate=True),
                reads=[xr], writes=[sm.reg(j, j + 1)])
        p.emit("dve", lambda e: e.tensor_single_scalar(
            sm.ap(0, n), sm.ap(0, n), 1.0 / D, ALU.mult),
            reads=[sm.reg(0, n)], writes=[sm.reg(0, n)])
        p.emit("dve", lambda e: e.tensor_tensor(
            var.ap(0, n), sm.ap(0, n), sm.ap(0, n), ALU.mult),
            reads=[sm.reg(0, n)], writes=[var.reg(0, n)])
        p.emit("dve", lambda e: e.scalar_tensor_tensor(
            var.ap(0, n), sq.ap(0, n), 1.0 / D, var.ap(0, n),
            ALU.mult, ALU.subtract),
            reads=[sq.reg(0, n), var.reg(0, n)], writes=[var.reg(0, n)])
        c_eps = self.p.const_col(1e-5)
        c_half = self.p.const_col(-0.5)
        p.emit("act", lambda e: e.activation(
            var.ap(0, n), var.ap(0, n), AF.Ln, bias=c_eps),
            reads=[var.reg(0, n)], writes=[var.reg(0, n)])
        p.emit("act", lambda e: e.activation(
            rstd_v.ap(0, n), var.ap(0, n), AF.Exp, scale=c_half),
            reads=[var.reg(0, n)], writes=[rstd_v.reg(0, n)])
        p.emit("dve", lambda e: e.tensor_tensor(
            self.nbr_t.ap()[:, 0:n], sm.ap(0, n), rstd_v.ap(0, n), ALU.mult),
            reads=[sm.reg(0, n), rstd_v.reg(0, n)],
            writes=[Region(self.nbr_t, 0, 128, 0, n)])
        self.ar16.restore(); self.small.restore()

    def ln_apply(self, tt, negm_v, rstd_v, out_ap, out_reg, j=None):
        if j is None:
            j = tt
        x = self.x
        xa = x.ap()[:, tt * D:(tt + 1) * D]
        xr = Region(x, 0, 128, tt * D, (tt + 1) * D)
        if j % 2:
            nbr = self.nbr_t
            self.p.emit("act", lambda e: e.activation(
                out_ap, xa, AF.Identity, scale=rstd_v.ap(j, j + 1),
                bias=nbr.ap()[:, j:j + 1]),
                reads=[xr, rstd_v.reg(j, j + 1),
                       Region(nbr, 0, 128, j, j + 1)],
                writes=[out_reg])
        else:
            self.p.emit("dve", lambda e: e.tensor_scalar(
                out_ap, xa, negm_v.ap(j, j + 1), rstd_v.ap(j, j + 1),
                ALU.add, ALU.mult),
                reads=[xr, negm_v.reg(j, j + 1), rstd_v.reg(j, j + 1)],
                writes=[out_reg])




def l0_ffn_prepare(mk, w1T, w2T):
    p = mk.p
    T1d = p.dram("T1d", [1024, 4096], BF16)
    T2d = p.dram("T2d", [4096, 1024], BF16)
    pers = p.sbuf("l0_pers", [128, 2], F32)

    def one(wi, wd, Td, rows, cols):
        mk.ar32.save(); mk.ar16.save(); mk.small.save()
        tiles = [mk.v32(cols) for _ in range(2)]
        junk = mk.v16(cols)
        acc = mk.vsm(1)
        ssum = mk.vsm(1)
        nt = rows // 128
        p.emit("dve", lambda e: e.memset(acc.ap(), 0.0), writes=[acc.reg()])

        def pass1(r):
            tile = tiles[r % 2]
            p.emit("sp", lambda e: e.dma_start(
                out=tile.ap(), in_=wd[r * 128:(r + 1) * 128, :]),
                writes=[tile.reg()], dma=True)
            p.emit("act", lambda e: e.activation(
                junk.ap(), tile.ap(), AF.Abs, accum_out=ssum.ap()),
                reads=[tile.reg()], writes=[junk.reg(), ssum.reg()])
            p.emit("dve", lambda e: e.tensor_tensor(
                acc.ap(), acc.ap(), ssum.ap(), ALU.add),
                reads=[acc.reg(), ssum.reg()], writes=[acc.reg()])
        for r in range(nt):
            pass1(r)
        ps = mk.ps[0]
        ones_c = p.const_col(1.0)
        ones_r = p.const_row(1.0)
        p.emit("pe", lambda e: e.matmul(
            ps.ap()[0:1, 0:1], acc.ap(), ones_c,
            start=True, stop=True),
            reads=[acc.reg()], writes=[R(ps)])
        s_sc = mk.vsm(1)
        rc = float(rows * cols)
        p.emit("dve", lambda e: e.tensor_scalar(
            s_sc.ap(p0=0, p1=1), ps.ap()[0:1, 0:1], 1.0 / rc, 1e-5,
            ALU.mult, ALU.max),
            reads=[R(ps)], writes=[s_sc.reg(p0=0, p1=1)])
        rinv = mk.vsm(1)
        p.emit("dve", lambda e: e.reciprocal(
            rinv.ap(p0=0, p1=1), s_sc.ap(p0=0, p1=1)),
            reads=[s_sc.reg(p0=0, p1=1)], writes=[rinv.reg(p0=0, p1=1)])
        rcol = mk.vsm(1)
        scol = View(pers, wi, 1)

        def bcast(src, dst):
            p.emit("pe", lambda e: e.matmul(
                ps.ap()[:, 0:1], ones_r, src.ap(p0=0, p1=1),
                start=True, stop=True),
                reads=[src.reg(p0=0, p1=1)], writes=[R(ps)])
            p.emit("dve", lambda e: e.tensor_copy(dst.ap(), ps.ap()[:, 0:1]),
                   reads=[R(ps)], writes=[dst.reg()])
        bcast(rinv, rcol)
        bcast(s_sc, scol)

        def pass2(r):
            tile = tiles[r % 2]
            p.emit("sp", lambda e: e.dma_start(
                out=tile.ap(), in_=wd[r * 128:(r + 1) * 128, :]),
                writes=[tile.reg()], dma=True)
            p.emit("dve", lambda e: e.tensor_scalar(
                tile.ap(), tile.ap(), rcol.ap(), MAGIC, ALU.mult, ALU.add),
                reads=[tile.reg(), rcol.reg()], writes=[tile.reg()])
            p.emit("dve", lambda e: e.tensor_single_scalar(
                tile.ap(), tile.ap(), -MAGIC, ALU.add),
                reads=[tile.reg()], writes=[tile.reg()])
            p.emit("dve", lambda e: e.tensor_scalar(
                junk.ap(), tile.ap(), 1.0, -1.0, ALU.min, ALU.max),
                reads=[tile.reg()], writes=[junk.reg()])
            p.emit("sp", lambda e: e.dma_start(
                out=Td.ap()[r * 128:(r + 1) * 128, :], in_=junk.ap()),
                reads=[junk.reg()], writes=[Region(Td, 0, 128, r, r + 1)],
                dma=True)
        for r in range(nt):
            pass2(r)
        mk.ar32.restore(); mk.ar16.restore(); mk.small.restore()
        return scol

    s1 = one(0, w1T, T1d, 1024, 4096)
    s2 = one(1, w2T, T2d, 4096, 1024)
    return T1d, T2d, [s1, s2]


def l0_ffn_apply(mk, T1d, T2d, scols):
    p = mk.p
    mk.ar16.reset(); mk.ar32.reset(); mk.small.reset()
    s1c, s2c = scols
    NTT = mk.TT
    negm = mk.vsm(NTT)
    rstd = mk.vsm(NTT)
    mk.ln_stats(list(range(NTT)), rstd, negm)
    xqT = mk.v16(4096)
    hqT = [mk.v16(4096) for _ in range(4)]
    w1r = [mk.v16(4096) for _ in range(2)]
    w2r = [mk.v16(2048) for _ in range(2)]
    hbuf = mk.v32(16384)

    def do_group(g):
        tts = [g * 4 + i for i in range(4)]
        mk.small.save()
        deq1 = mk.vsm(4)
        deq2 = mk.vsm(4)

        def xquant(i, tt):
            mk.ar32.save(); mk.ar16.save(); mk.small.save()
            t = mk.v32(1024)
            xq = mk.v16(1024)
            mxn = mk.vsm(1)
            scq = mk.vsm(1)
            xa = mk.x.ap()[:, tt * D:(tt + 1) * D]
            xr = Region(mk.x, 0, 128, tt * D, (tt + 1) * D)
            p.emit("dve", lambda e: e.tensor_scalar(
                t.ap(), xa, negm.ap(tt, tt + 1), None, ALU.add),
                reads=[xr, negm.reg(tt, tt + 1)], writes=[t.reg()])
            p.emit("dve", lambda e: e.reduce_max(
                mxn.ap(), t.ap(), axis=AX.X, apply_absolute_value=True),
                reads=[t.reg()], writes=[mxn.reg()])
            p.emit("dve", lambda e: e.tensor_scalar(
                mxn.ap(), mxn.ap(), rstd.ap(tt, tt + 1), 1e-5,
                ALU.mult, ALU.max),
                reads=[mxn.reg(), rstd.reg(tt, tt + 1)], writes=[mxn.reg()])
            p.emit("dve", lambda e: e.reciprocal(
                scq.ap(), mxn.ap()), reads=[mxn.reg()], writes=[scq.reg()])
            p.emit("dve", lambda e: e.tensor_scalar(
                scq.ap(), scq.ap(), rstd.ap(tt, tt + 1), 127.0,
                ALU.mult, ALU.mult),
                reads=[scq.reg(), rstd.reg(tt, tt + 1)], writes=[scq.reg()])
            p.emit("dve", lambda e: e.tensor_scalar(
                deq1.ap(i, i + 1), mxn.ap(), s1c.ap(), 1.0 / 127.0,
                ALU.mult, ALU.mult),
                reads=[mxn.reg(), s1c.reg()], writes=[deq1.reg(i, i + 1)])
            p.emit("dve", lambda e: e.tensor_scalar(
                t.ap(), t.ap(), scq.ap(), MAGIC, ALU.mult, ALU.add),
                reads=[t.reg(), scq.reg()], writes=[t.reg()])
            p.emit("dve", lambda e: e.tensor_single_scalar(
                xq.ap(), t.ap(), -MAGIC, ALU.add),
                reads=[t.reg()], writes=[xq.reg()])
            dst = mk.ar16_t.ap()[:, xqT.o:xqT.o + 4096].rearrange(
                "p (c t) -> p c t", c=8)[:, :, i * 128:(i + 1) * 128]
            mk.transpose8(lambda dc: xq.ap(dc * 128, (dc + 1) * 128),
                          dst, xqT.reg(), [xq.reg()], pb=i % 2)
            mk.ar32.restore(); mk.ar16.restore(); mk.small.restore()
        for i, tt in enumerate(tts):
            xquant(i, tt)

        def hblock(hb):
            wv = w1r[hb % 2]
            p.emit("pool", lambda e: e.dma_start(
                out=wv.ap().rearrange("p (c h) -> p c h", c=8),
                in_=T1d.ap().rearrange("(c p) h -> p c h", p=128)
                [:, :, hb * 512:(hb + 1) * 512]),
                reads=[R(T1d)], writes=[wv.reg()], dma=True)

            def one_tt(i):
                bank = mk.ps[4 + i % 2]

                def mmh(t):
                    last = None
                    for dc in range(8):
                        last = t.matmul(
                            bank.ap()[:, :],
                            xqT.ap(dc * 512 + i * 128, dc * 512 + (i + 1) * 128),
                            wv.ap(dc * 512, (dc + 1) * 512),
                            start=(dc == 0), stop=(dc == 7))
                    return last
                p.emit("pe", mmh, reads=[xqT.reg(), wv.reg()], writes=[R(bank)])
                hs = (i * 4096 + hb * 512, i * 4096 + (hb + 1) * 512)
                p.emit("act", lambda e: e.activation(
                    hbuf.ap(*hs), bank.ap()[:, :], AF.Gelu,
                    scale=deq1.ap(i, i + 1)),
                    reads=[R(bank), deq1.reg(i, i + 1)], writes=[hbuf.reg(*hs)])
            for i in range(4):
                one_tt(i)
        for hb in range(8):
            hblock(hb)

        def hquant(i, tt):
            mk.small.save()
            hs = (i * 4096, (i + 1) * 4096)
            mxh = mk.vsm(1)
            sch = mk.vsm(1)
            hv = View(mk.ar16_t, xqT.o, 4096)
            p.emit("dve", lambda e: e.reduce_max(
                mxh.ap(), hbuf.ap(*hs), axis=AX.X, apply_absolute_value=True),
                reads=[hbuf.reg(*hs)], writes=[mxh.reg()])
            p.emit("dve", lambda e: e.tensor_single_scalar(
                mxh.ap(), mxh.ap(), 1e-5, ALU.max),
                reads=[mxh.reg()], writes=[mxh.reg()])
            p.emit("dve", lambda e: e.reciprocal(
                sch.ap(), mxh.ap()), reads=[mxh.reg()], writes=[sch.reg()])
            p.emit("dve", lambda e: e.tensor_single_scalar(
                sch.ap(), sch.ap(), 127.0, ALU.mult),
                reads=[sch.reg()], writes=[sch.reg()])
            p.emit("dve", lambda e: e.tensor_scalar(
                deq2.ap(i, i + 1), mxh.ap(), s2c.ap(), 0.5 / 127.0,
                ALU.mult, ALU.mult),
                reads=[mxh.reg(), s2c.reg()], writes=[deq2.reg(i, i + 1)])
            p.emit("dve", lambda e: e.tensor_scalar(
                hbuf.ap(*hs), hbuf.ap(*hs), sch.ap(), MAGIC, ALU.mult, ALU.add),
                reads=[hbuf.reg(*hs), sch.reg()], writes=[hbuf.reg(*hs)])
            p.emit("dve", lambda e: e.tensor_single_scalar(
                hv.ap(), hbuf.ap(*hs), -MAGIC, ALU.add),
                reads=[hbuf.reg(*hs)], writes=[hv.reg()])
            tv = hqT[i]

            def tp4(b4):
                dst = mk.ar16_t.ap()[:, tv.o + b4 * 1024:tv.o + (b4 + 1) * 1024]
                mk.transpose8(
                    lambda k: hv.ap((b4 * 8 + k) * 128, (b4 * 8 + k + 1) * 128),
                    dst,
                    Region(tv.t, 0, 128, tv.o + b4 * 1024, tv.o + (b4 + 1) * 1024),
                    [hv.reg()], pb=4 + b4 % 2)
            for b4 in range(4):
                tp4(b4)
            mk.small.restore()
        for i, tt in enumerate(tts):
            hquant(i, tt)

        def yblock(hb):
            wv = w2r[hb % 2]
            p.emit("pool", lambda e: e.dma_start(
                out=wv.ap().rearrange("p (c h) -> p c h", c=2),
                in_=T2d.ap().rearrange("(c p) o -> p c o", p=128)
                [:, hb * 2:(hb + 1) * 2, :]),
                reads=[R(T2d)], writes=[wv.reg()], dma=True)

            def one(i, ob):
                tv = hqT[i]
                bank = mk.ps[i * 2 + ob]

                def mmy(t):
                    last = None
                    for hc in range(2):
                        last = t.matmul(
                            bank.ap()[:, :],
                            tv.ap((hb * 2 + hc) * 128, (hb * 2 + hc + 1) * 128),
                            wv.ap(hc * 1024 + ob * 512, hc * 1024 + (ob + 1) * 512),
                            start=(hb == 0 and hc == 0),
                            stop=(hb == 15 and hc == 1))
                    return last
                p.emit("pe", mmy, reads=[tv.reg(), wv.reg()], writes=[R(bank)])
            for i in range(4):
                for ob in range(2):
                    one(i, ob)
        for hb in range(16):
            yblock(hb)

        def evac(i, tt, ob):
            bank = mk.ps[i * 2 + ob]
            xs = (tt * D + ob * 512, tt * D + (ob + 1) * 512)
            xa = mk.x.ap()[:, xs[0]:xs[1]]
            xreg = Region(mk.x, 0, 128, *xs)
            p.emit("dve", lambda e: e.scalar_tensor_tensor(
                xa, bank.ap()[:, :], deq2.ap(i, i + 1), xa,
                ALU.mult, ALU.add),
                reads=[R(bank), deq2.reg(i, i + 1), xreg], writes=[xreg])
        for i, tt in enumerate(tts):
            for ob in range(2):
                evac(i, tt, ob)
        mk.small.restore()
    for g in range(NTT // 4):
        do_group(g)


def _fp4_qabs_chain(p, y_v, q_v, engs=("dve",)):
    k = [0]

    def E():
        k[0] += 1
        return engs[k[0] % len(engs)]
    acc_a = q_v
    p.emit(E(), lambda e: e.tensor_single_scalar(
        acc_a.ap(), y_v.ap(), 0.25, ALU.is_gt),
        reads=[y_v.reg()], writes=[acc_a.reg()])
    for m in (0.75, 1.25, 1.75):
        p.emit(E(), lambda e, m=m: e.scalar_tensor_tensor(
            acc_a.ap(), y_v.ap(), m, acc_a.ap(), ALU.is_gt, ALU.add),
            reads=[y_v.reg(), acc_a.reg()], writes=[acc_a.reg()])
    p.emit(E(), lambda e: e.tensor_single_scalar(
        acc_a.ap(), acc_a.ap(), 0.5, ALU.mult),
        reads=[acc_a.reg()], writes=[acc_a.reg()])
    for m in (2.5, 3.5):
        p.emit(E(), lambda e, m=m: e.scalar_tensor_tensor(
            acc_a.ap(), y_v.ap(), m, acc_a.ap(), ALU.is_gt, ALU.add),
            reads=[y_v.reg(), acc_a.reg()], writes=[acc_a.reg()])
    p.emit(E(), lambda e: e.tensor_single_scalar(
        y_v.ap(), y_v.ap(), 5.0, ALU.is_gt),
        reads=[y_v.reg()], writes=[y_v.reg()])
    p.emit(E(), lambda e: e.scalar_tensor_tensor(
        q_v.ap(), y_v.ap(), 2.0, acc_a.ap(), ALU.mult, ALU.add),
        reads=[y_v.reg(), acc_a.reg()], writes=[q_v.reg()])


def l3_ffn_prepare(mk, nat, trans, E8d, ident32d):
    p = mk.p
    E8 = p.sbuf("E8s", [8, 128], F32)
    id32 = p.sbuf("id32", [128, 128], F32)
    p.emit("sp", lambda e: e.dma_start(out=E8.ap()[:, :], in_=E8d[:, :]),
           writes=[R(E8)], dma=True)
    p.emit("sp", lambda e: e.dma_start(out=id32.ap()[:, :], in_=ident32d[:, :]),
           writes=[R(id32)], dma=True)
    out = {}
    shapes = {"wg": (2816, 1024), "wu": (2816, 1024), "wd": (1024, 2816)}

    def one_matrix(nm):
        N, Kd = shapes[nm]
        wq = p.dram(f"l3q_{nm}", [Kd, N], BF16)
        out[nm] = wq
        nb = Kd // 16
        scT_d = p.dram(f"l3sT_{nm}", [nb, N], F32)
        ivT_d = p.dram(f"l3iT_{nm}", [nb, N], F32)
        mk.ar32.save(); mk.small.save()
        tile = mk.v32(Kd)
        sc = mk.v32(nb)
        iv = mk.v32(nb)
        stg = mk.v32(128)

        def phaseA(r):
            p.emit("sp", lambda e: e.dma_start(
                out=tile.ap(), in_=nat[nm][r * 128:(r + 1) * 128, :]),
                writes=[tile.reg()], dma=True)
            p.emit("dve", lambda e: e.tensor_reduce(
                sc.ap(), tile.ap().rearrange("p (b k) -> p b k", k=16),
                axis=AX.X, op=ALU.max, apply_absolute_value=True),
                reads=[tile.reg()], writes=[sc.reg()])
            p.emit("dve", lambda e: e.tensor_scalar(
                sc.ap(), sc.ap(), 1.0 / 6.0, 1e-12, ALU.mult, ALU.max),
                reads=[sc.reg()], writes=[sc.reg()])
            p.emit("dve", lambda e: e.reciprocal(
                iv.ap(), sc.ap()), reads=[sc.reg()], writes=[iv.reg()])
            for src, dst_d in ((sc, scT_d), (iv, ivT_d)):
                nsp = 0
                while nsp < nb:
                    w = min(128, nb - nsp)
                    bank = mk.ps[0]
                    p.emit("pe", lambda e, src=src, nsp=nsp, w=w: e.matmul(
                        bank.ap()[0:w, 0:128],
                        src.ap(nsp, nsp + w), id32.ap()[:, :],
                        start=True, stop=True),
                        reads=[src.reg(nsp, nsp + w), R(id32)],
                        writes=[R(bank)])
                    p.emit("dve", lambda e, w=w: e.tensor_copy(
                        stg.ap(p1=w), bank.ap()[0:w, 0:128]),
                        reads=[R(bank)], writes=[stg.reg()])
                    p.emit("sp", lambda e, dst_d=dst_d, nsp=nsp, w=w, r=r:
                           e.dma_start(
                        out=dst_d.ap()[nsp:nsp + w, r * 128:(r + 1) * 128],
                        in_=stg.ap(p1=w)),
                        reads=[stg.reg()],
                        writes=[Region(dst_d, nsp, nsp + w, r, r + 1)],
                        dma=True)
                    nsp += w
        for r in range(N // 128):
            phaseA(r)
        mk.ar32.restore(); mk.small.restore()
        mk.ar32.save(); mk.ar16.save()
        wt = mk.v32(N)
        yv = mk.v32(N)
        qv = mk.v32(N)
        s8 = mk.v32(N)
        i8 = mk.v32(N)
        sg = mk.v16(N)
        qo = mk.v16(N)

        def phaseB(kc):
            p.emit("sp", lambda e: e.dma_start(
                out=wt.ap(), in_=trans[nm][kc * 128:(kc + 1) * 128, :]),
                writes=[wt.reg()], dma=True)
            p.emit("sp", lambda e: e.dma_start(
                out=s8.ap(p1=8), in_=scT_d.ap()[kc * 8:(kc + 1) * 8, :]),
                reads=[R(scT_d)], writes=[s8.reg()], dma=True)
            p.emit("sp", lambda e: e.dma_start(
                out=i8.ap(p1=8), in_=ivT_d.ap()[kc * 8:(kc + 1) * 8, :]),
                reads=[R(ivT_d)], writes=[i8.reg()], dma=True)
            ns = 0
            sb = mk.ps[6]
            ib = mk.ps[7]
            while ns < N:
                w = min(512, N - ns)
                p.emit("pe", lambda e, ns=ns, w=w: e.matmul(
                    sb.ap()[:, 0:w], E8.ap()[:, :],
                    s8.ap(ns, ns + w, p1=8),
                    start=True, stop=True),
                    reads=[s8.reg(), R(E8)], writes=[R(sb)])
                p.emit("pe", lambda e, ns=ns, w=w: e.matmul(
                    ib.ap()[:, 0:w], E8.ap()[:, :],
                    i8.ap(ns, ns + w, p1=8),
                    start=True, stop=True),
                    reads=[i8.reg(), R(E8)], writes=[R(ib)])
                p.emit("dve", lambda e, ns=ns, w=w: e.tensor_tensor(
                    yv.ap(ns, ns + w), wt.ap(ns, ns + w), ib.ap()[:, 0:w],
                    ALU.mult),
                    reads=[wt.reg(ns, ns + w), R(ib)],
                    writes=[yv.reg(ns, ns + w)])
                p.emit("dve", lambda e, ns=ns, w=w: e.scalar_tensor_tensor(
                    yv.ap(ns, ns + w), yv.ap(ns, ns + w), -1.0,
                    yv.ap(ns, ns + w), ALU.mult, ALU.max),
                    reads=[yv.reg(ns, ns + w)], writes=[yv.reg(ns, ns + w)])
                _fp4_qabs_chain(p, View(yv.t, yv.o + ns, w),
                                View(qv.t, qv.o + ns, w))
                p.emit("act", lambda e, ns=ns, w=w: e.sign(
                    sg.ap(ns, ns + w), wt.ap(ns, ns + w)),
                    reads=[wt.reg(ns, ns + w)], writes=[sg.reg(ns, ns + w)])
                p.emit("dve", lambda e, ns=ns, w=w: e.tensor_tensor(
                    qv.ap(ns, ns + w), qv.ap(ns, ns + w), sb.ap()[:, 0:w],
                    ALU.mult),
                    reads=[qv.reg(ns, ns + w), R(sb)],
                    writes=[qv.reg(ns, ns + w)])
                p.emit("pool", lambda e, ns=ns, w=w: e.tensor_tensor(
                    qo.ap(ns, ns + w), qv.ap(ns, ns + w), sg.ap(ns, ns + w),
                    ALU.mult),
                    reads=[qv.reg(ns, ns + w), sg.reg(ns, ns + w)],
                    writes=[qo.reg(ns, ns + w)])
                ns += w
            p.emit("sp", lambda e: e.dma_start(
                out=wq.ap()[kc * 128:(kc + 1) * 128, :], in_=qo.ap()),
                reads=[qo.reg()], writes=[Region(wq, 0, 128, kc, kc + 1)],
                dma=True)
        for kc in range(Kd // 128):
            phaseB(kc)
        mk.ar32.restore(); mk.ar16.restore()

    for nm in ("wg", "wu", "wd"):
        one_matrix(nm)
    return out


def l3_ffn_apply(mk, wqd):
    p = mk.p
    mk.ar16.reset(); mk.ar32.reset(); mk.small.reset()
    NTT = mk.TT
    HN = 2816
    NHC = HN // 128
    negm = mk.vsm(NTT)
    rstd = mk.vsm(NTT)
    mk.ln_stats(list(range(NTT)), rstd, negm)
    xqT = mk.v16(4096)
    hqT = [mk.v16(NHC * 128) for _ in range(4)]
    wgr = [mk.v16(2048) for _ in range(2)]
    wur = [mk.v16(2048) for _ in range(2)]
    wdr = [mk.v16(2048) for _ in range(2)]
    hbuf = mk.v32(4 * HN)

    def do_group(g):
        tts = [g * 4 + i for i in range(4)]

        def xquant(i, tt):
            mk.ar32.save(); mk.ar16.save(); mk.small.save()
            t = mk.v32(1024)
            yv = mk.v32(1024)
            qv = mk.v32(1024)
            bs = mk.v32(64)
            iv = mk.v32(64)
            sg16 = mk.v16(1024)
            xq = mk.v16(1024)
            mk.ln_apply(tt, negm, rstd, t.ap(), t.reg())
            p.emit("dve", lambda e: e.tensor_reduce(
                bs.ap(), t.ap().rearrange("p (b k) -> p b k", k=16),
                axis=AX.X, op=ALU.max, apply_absolute_value=True),
                reads=[t.reg()], writes=[bs.reg()])
            p.emit("dve", lambda e: e.tensor_scalar(
                bs.ap(), bs.ap(), 1.0 / 6.0, 1e-12, ALU.mult, ALU.max),
                reads=[bs.reg()], writes=[bs.reg()])
            p.emit("dve", lambda e: e.reciprocal(
                iv.ap(), bs.ap()), reads=[bs.reg()], writes=[iv.reg()])
            ivb = iv.ap().rearrange("p (b o) -> p b o", o=1)\
                .broadcast_to((128, 64, 16))
            bsb = bs.ap().rearrange("p (b o) -> p b o", o=1)\
                .broadcast_to((128, 64, 16))
            tv3 = t.ap().rearrange("p (b k) -> p b k", k=16)
            yv3 = yv.ap().rearrange("p (b k) -> p b k", k=16)
            p.emit("dve", lambda e: e.tensor_tensor(
                yv3, tv3, ivb, ALU.mult),
                reads=[t.reg(), iv.reg()], writes=[yv.reg()])
            p.emit("dve", lambda e: e.scalar_tensor_tensor(
                yv.ap(), yv.ap(), -1.0, yv.ap(), ALU.mult, ALU.max),
                reads=[yv.reg()], writes=[yv.reg()])
            _fp4_qabs_chain(p, yv, qv)
            p.emit("act", lambda e: e.sign(
                sg16.ap(), t.ap()), reads=[t.reg()], writes=[sg16.reg()])
            qv3 = qv.ap().rearrange("p (b k) -> p b k", k=16)
            p.emit("dve", lambda e: e.tensor_tensor(
                qv3, qv3, bsb, ALU.mult),
                reads=[qv.reg(), bs.reg()], writes=[qv.reg()])
            p.emit("pool", lambda e: e.tensor_tensor(
                xq.ap(), qv.ap(), sg16.ap(), ALU.mult),
                reads=[qv.reg(), sg16.reg()], writes=[xq.reg()])
            dst = mk.ar16_t.ap()[:, xqT.o:xqT.o + 4096].rearrange(
                "p (c t) -> p c t", c=8)[:, :, i * 128:(i + 1) * 128]
            mk.transpose8(lambda dc: xq.ap(dc * 128, (dc + 1) * 128),
                          dst, xqT.reg(), [xq.reg()], pb=i % 2)
            mk.ar32.restore(); mk.ar16.restore(); mk.small.restore()
        for i, tt in enumerate(tts):
            xquant(i, tt)

        def hblock(hb, w):
            wgv, wuv = wgr[hb % 2], wur[hb % 2]

            def ld(wv, src):
                p.emit("pool", lambda e: e.dma_start(
                    out=wv.ap(0, 8 * w).rearrange("p (c h) -> p c h", c=8),
                    in_=src.ap().rearrange("(c p) h -> p c h", p=128)
                    [:, :, hb * 256:hb * 256 + w]),
                    reads=[R(src)], writes=[wv.reg()], dma=True)
            ld(wgv, wqd["wg"])
            ld(wuv, wqd["wu"])

            def one_tt(i):
                bg, bu = mk.ps[4], mk.ps[5]

                def mmg(t, wv=wgv, bank=bg):
                    last = None
                    for dc in range(8):
                        last = t.matmul(
                            bank.ap()[:, 0:w],
                            xqT.ap(dc * 512 + i * 128, dc * 512 + (i + 1) * 128),
                            wv.ap(dc * w, (dc + 1) * w),
                            start=(dc == 0), stop=(dc == 7))
                    return last
                p.emit("pe", mmg, reads=[xqT.reg(), wgv.reg()], writes=[R(bg)])
                p.emit("pe", lambda t: mmg(t, wuv, bu),
                       reads=[xqT.reg(), wuv.reg()], writes=[R(bu)])
                mk.ar16.save()
                sv = mk.v16(w)
                p.emit("act", lambda e: e.activation(
                    sv.ap(), bg.ap()[:, 0:w], AF.Silu),
                    reads=[R(bg)], writes=[sv.reg()])
                hs = (i * HN + hb * 256, i * HN + hb * 256 + w)
                p.emit("dve", lambda e: e.tensor_tensor(
                    hbuf.ap(*hs), sv.ap(), bu.ap()[:, 0:w], ALU.mult),
                    reads=[sv.reg(), R(bu)], writes=[hbuf.reg(*hs)])
                mk.ar16.restore()
            for i in range(4):
                one_tt(i)
        for hb in range(11):
            hblock(hb, 256)

        def hquant(i, tt):
            mk.ar32.save(); mk.ar16.save(); mk.small.save()
            hs = (i * HN, (i + 1) * HN)
            NB = HN // 16
            yv = mk.v32(HN)
            qv = mk.v32(HN)
            bs = mk.v32(NB)
            iv = mk.v32(NB)
            sg16 = mk.v16(HN)
            hq = mk.v16(HN)
            p.emit("dve", lambda e: e.tensor_reduce(
                bs.ap(), hbuf.ap(*hs).rearrange("p (b k) -> p b k", k=16),
                axis=AX.X, op=ALU.max, apply_absolute_value=True),
                reads=[hbuf.reg(*hs)], writes=[bs.reg()])
            p.emit("dve", lambda e: e.tensor_scalar(
                bs.ap(), bs.ap(), 1.0 / 6.0, 1e-12, ALU.mult, ALU.max),
                reads=[bs.reg()], writes=[bs.reg()])
            p.emit("dve", lambda e: e.reciprocal(
                iv.ap(), bs.ap()), reads=[bs.reg()], writes=[iv.reg()])
            ivb = iv.ap().rearrange("p (b o) -> p b o", o=1)\
                .broadcast_to((128, NB, 16))
            bsb = bs.ap().rearrange("p (b o) -> p b o", o=1)\
                .broadcast_to((128, NB, 16))
            hv3 = hbuf.ap(*hs).rearrange("p (b k) -> p b k", k=16)
            yv3 = yv.ap().rearrange("p (b k) -> p b k", k=16)
            p.emit("dve", lambda e: e.tensor_tensor(
                yv3, hv3, ivb, ALU.mult),
                reads=[hbuf.reg(*hs), iv.reg()], writes=[yv.reg()])
            p.emit("dve", lambda e: e.scalar_tensor_tensor(
                yv.ap(), yv.ap(), -1.0, yv.ap(), ALU.mult, ALU.max),
                reads=[yv.reg()], writes=[yv.reg()])
            _fp4_qabs_chain(p, yv, qv)
            p.emit("act", lambda e: e.sign(
                sg16.ap(), hbuf.ap(*hs)),
                reads=[hbuf.reg(*hs)], writes=[sg16.reg()])
            qv3 = qv.ap().rearrange("p (b k) -> p b k", k=16)
            p.emit("dve", lambda e: e.tensor_tensor(
                qv3, qv3, bsb, ALU.mult),
                reads=[qv.reg(), bs.reg()], writes=[qv.reg()])
            p.emit("pool", lambda e: e.tensor_tensor(
                hq.ap(), qv.ap(), sg16.ap(), ALU.mult),
                reads=[qv.reg(), sg16.reg()], writes=[hq.reg()])
            tv = hqT[i]

            def tpb(b8, n):
                dst = mk.ar16_t.ap()[:, tv.o + b8 * 1024:tv.o + b8 * 1024 + n * 128]
                mk.transpose8(
                    lambda k2: hq.ap((b8 * 8 + k2) * 128, (b8 * 8 + k2 + 1) * 128),
                    dst,
                    Region(tv.t, 0, 128, tv.o + b8 * 1024,
                           tv.o + b8 * 1024 + n * 128),
                    [hq.reg()], pb=4 + b8 % 2, n=n)
            tpb(0, 8); tpb(1, 8); tpb(2, 6)
            mk.ar32.restore(); mk.ar16.restore(); mk.small.restore()
        for i, tt in enumerate(tts):
            hquant(i, tt)

        def yblock(hb):
            wv = wdr[hb % 2]
            p.emit("pool", lambda e: e.dma_start(
                out=wv.ap().rearrange("p (c h) -> p c h", c=2),
                in_=wqd["wd"].ap().rearrange("(c p) o -> p c o", p=128)
                [:, hb * 2:(hb + 1) * 2, :]),
                reads=[R(wqd["wd"])], writes=[wv.reg()], dma=True)

            def one(i, ob):
                tv = hqT[i]
                bank = mk.ps[i * 2 + ob]

                def mmy(t):
                    last = None
                    for hc in range(2):
                        last = t.matmul(
                            bank.ap()[:, :],
                            tv.ap((hb * 2 + hc) * 128, (hb * 2 + hc + 1) * 128),
                            wv.ap(hc * 1024 + ob * 512, hc * 1024 + (ob + 1) * 512),
                            start=(hb == 0 and hc == 0),
                            stop=(hb == 10 and hc == 1))
                    return last
                p.emit("pe", mmy, reads=[tv.reg(), wv.reg()], writes=[R(bank)])
            for i in range(4):
                for ob in range(2):
                    one(i, ob)
        for hb in range(11):
            yblock(hb)

        def evac(i, tt, ob):
            bank = mk.ps[i * 2 + ob]
            xs = (tt * D + ob * 512, tt * D + (ob + 1) * 512)
            xa = mk.x.ap()[:, xs[0]:xs[1]]
            xreg = Region(mk.x, 0, 128, *xs)
            p.emit("dve", lambda e: e.scalar_tensor_tensor(
                xa, bank.ap()[:, :], 0.5, xa, ALU.mult, ALU.add),
                reads=[R(bank), xreg], writes=[xreg])
        for i, tt in enumerate(tts):
            for ob in range(2):
                evac(i, tt, ob)
    for g in range(NTT // 4):
        do_group(g)


def attn_apply(mk, cfg, elem):
    p = mk.p
    mk.ar16.reset(); mk.ar32.reset(); mk.small.reset()
    TTE = mk.TTE
    T = mk.T
    tt0 = elem * TTE
    x = mk.x
    bq_cols = mk.vsm(24)
    p.emit("sp", lambda e: e.dma_start(out=bq_cols.ap(), in_=cfg["bqkv_cols"]),
           writes=[bq_cols.reg()], dma=True)
    pb_bc = mk.v16(1024)
    onesr = p.const_row(1.0)
    mk.ar32.save()
    brow = mk.v32(1024)
    p.emit("sp", lambda e: e.dma_start(
        out=brow.ap(0, 1024, p1=1), in_=cfg["bproj_row"][0:1, :]),
        writes=[brow.reg()], dma=True)

    def mkbc(src_off, dst):
        def one(ob):
            bank = mk.ps[0]
            p.emit("pe", lambda e: e.matmul(
                bank.ap()[:, :], onesr,
                brow.ap(src_off + ob * 512, src_off + (ob + 1) * 512, p1=1),
                start=True, stop=True),
                reads=[brow.reg()], writes=[R(bank)])
            p.emit("dve", lambda e: e.tensor_copy(
                dst.ap(ob * 512, (ob + 1) * 512), bank.ap()[:, :]),
                reads=[R(bank)], writes=[dst.reg(ob * 512, (ob + 1) * 512)])
        one(0); one(1)
    mkbc(0, pb_bc)
    mk.ar32.restore()
    y32 = mk.v32(TTE * 1024)
    lora = cfg.get("lora")
    if lora:
        lt = mk.v16(160)
        lofs = {}
        off = 0
        for nm, rows, cols in (("a_q", 64, 16), ("b_q", 16, 64),
                               ("a_v", 64, 16), ("b_v", 16, 64)):
            lofs[nm] = (off, rows, cols)
            p.emit("pool", lambda e, nm=nm, off=off, rows=rows, cols=cols:
                   e.dma_start(out=lt.ap(off, off + cols, p1=rows),
                               in_=lora[nm][:, :]),
                   writes=[lt.reg()], dma=True)
            if rows == 64:
                p.emit("pool", lambda e, nm=nm, off=off, cols=cols:
                       e.dma_start(out=lt.ap(off, off + cols, p0=64, p1=128),
                                   in_=lora[nm][:, :]),
                       writes=[lt.reg()], dma=True)
            off += cols

        def lap(nm, pb=0):
            o, rows, cols = lofs[nm]
            if rows == 64 and pb:
                return lt.ap(o, o + cols, p0=64, p1=128), lt.reg()
            return lt.ap(o, o + cols, p1=rows), lt.reg()
    rope = cfg.get("rope")
    xp = cfg["xpage"]
    mk.page_x(xp, "out", order=[t for t in
                                list(range(26, mk.TT)) + list(range(26))
                                if t < mk.TT])
    negm = mk.vsm(TTE)
    rstd = mk.vsm(TTE)
    mk.ln_stats(list(range(tt0, tt0 + TTE)), rstd, negm)
    xnT = mk.v16(8 * T)

    xnring = [mk.v16_in32(1024) for _ in range(2)]

    def mkxnT(j):
        tt = tt0 + j
        xn = xnring[j % 2]
        mk.ln_apply(tt, negm, rstd, xn.ap(), xn.reg(), j=j)
        dst = mk.ar16_t.ap()[:, xnT.o:xnT.o + 8 * T].rearrange(
            "p (c t) -> p c t", c=8)[:, :, j * 128:(j + 1) * 128]
        mk.transpose8(lambda dc: xn.ap(dc * 128, (dc + 1) * 128),
                      dst, xnT.reg(), [xn.reg()], pb=j % 2)
    for j in range(TTE):
        mkxnT(j)
    need = 8 * T + 4096 + 4096 + 2048 + 4096 + 1024 + 1024
    if mk.TT * D >= need:
        bt = x
    else:
        bt = p.sbuf(f"attn_borrow{elem}_{p.cnt['pe']}", [128, need], BF16)
    KB = View(bt, 0, 8 * T)
    o_ = 8 * T
    QB = View(bt, o_, 4096)
    AFM = View(bt, o_ + 4096, 4096)
    PRB = [View(bt, o_ + 8192 + i * 512, 512) for i in range(4)]
    WRG = [View(bt, o_ + 10240 + i * 2048, 2048) for i in range(2)]
    VSQ = View(bt, o_ + 14336, 1024)
    if rope is not None:
        rot = View(bt, o_ + 15360, 1024)
        p.emit("pool", lambda e: e.dma_start(
            out=rot.ap().rearrange("p (c k) -> p c k", c=8),
            in_=rope[:, :, :].rearrange("c p k -> p c k")),
            writes=[rot.reg()], dma=True)
    VB = mk.v16(TTE * 1040)
    kn2 = mk.vsm(8)
    WSEC = {"q": 0, "k": 1024, "v": 2048}

    def ldw_qkv(wv, sec, oc, width=128):
        c0 = WSEC[sec] + oc * 128
        p.emit("pool", lambda e: e.dma_start(
            out=wv.ap(0, 8 * width).rearrange("p (c o) -> p c o", c=8),
            in_=cfg["wqkv"].rearrange("(c p) o -> p c o", p=128)
            [:, :, c0:c0 + width]),
            writes=[wv.reg()], dma=True)

    def kprod(oc):
        wv = WRG[oc % 2]
        ldw_qkv(wv, "k", oc)

        def one(ts4):
            bank = mk.ps[4 + ts4 % 2]

            def mmk(t):
                last = None
                for dc in range(8):
                    last = t.matmul(
                        bank.ap()[:, :],
                        wv.ap(dc * 128, (dc + 1) * 128),
                        xnT.ap(dc * T + ts4 * 512, dc * T + (ts4 + 1) * 512),
                        start=(dc == 0), stop=(dc == 7))
                return last
            p.emit("pe", mmk, reads=[wv.reg(), xnT.reg()], writes=[R(bank)])
            ks = (oc * T + ts4 * 512, oc * T + (ts4 + 1) * 512)
            if rope is None:
                p.emit("act", lambda e: e.activation(
                    KB.ap(*ks), bank.ap()[:, :], AF.Identity,
                    bias=bq_cols.ap(8 + oc, 9 + oc)),
                    reads=[R(bank), bq_cols.reg()], writes=[KB.reg(*ks)])
            else:
                ktmp = View(bt, VSQ.o + 512, 512)
                p.emit("act", lambda e: e.activation(
                    ktmp.ap(), bank.ap()[:, :], AF.Identity,
                    bias=bq_cols.ap(8 + oc, 9 + oc)),
                    reads=[R(bank), bq_cols.reg()], writes=[ktmp.reg()])
                bank2 = mk.ps[6]
                p.emit("pe", lambda e: e.matmul(
                    bank2.ap()[:, :], rot.ap(oc * 128, (oc + 1) * 128),
                    ktmp.ap(), start=True, stop=True),
                    reads=[rot.reg(), ktmp.reg()], writes=[R(bank2)])
                p.emit("dve", lambda e: e.tensor_copy(
                    KB.ap(*ks), bank2.ap()[:, :]),
                    reads=[R(bank2)], writes=[KB.reg(*ks)])
            p.emit("act", lambda e: e.activation(
                VSQ.ap(0, 512), KB.ap(*ks), AF.Square),
                reads=[KB.reg(*ks)], writes=[VSQ.reg(0, 512)])
            nb = mk.ps[7]
            p.emit("pe", lambda e: e.matmul(
                nb.ap()[0:1, :], mk.ones_bf.ap()[:, 0:1], VSQ.ap(0, 512),
                start=True, stop=True),
                reads=[VSQ.reg(0, 512), R(mk.ones_bf)], writes=[R(nb)])
            mk.small.save()
            mx = mk.vsm(1)
            p.emit("dve", lambda e: e.reduce_max(
                mx.ap(p1=1), nb.ap()[0:1, :], axis=AX.X),
                reads=[R(nb)], writes=[mx.reg(p1=1)])
            if ts4 == 0:
                p.emit("dve", lambda e: e.tensor_copy(
                    kn2.ap(oc, oc + 1, p1=1), mx.ap(p1=1)),
                    reads=[mx.reg(p1=1)], writes=[kn2.reg(oc, oc + 1)])
            else:
                p.emit("dve", lambda e: e.tensor_tensor(
                    kn2.ap(oc, oc + 1, p1=1), kn2.ap(oc, oc + 1, p1=1),
                    mx.ap(p1=1), ALU.max),
                    reads=[kn2.reg(oc, oc + 1), mx.reg(p1=1)],
                    writes=[kn2.reg(oc, oc + 1)])
            mk.small.restore()
        for ts4 in range(TTE // 4):
            one(ts4)
    for oc in range(8):
        kprod(oc)

    def vprod(oc):
        wv = WRG[oc % 2]
        ldw_qkv(wv, "v", oc)
        vfm = View(bt, o_, T)

        def one(ts4):
            bank = mk.ps[4 + ts4 % 2]

            def mmv(t):
                last = None
                for dc in range(8):
                    last = t.matmul(
                        bank.ap()[:, :],
                        wv.ap(dc * 128, (dc + 1) * 128),
                        xnT.ap(dc * T + ts4 * 512, dc * T + (ts4 + 1) * 512),
                        start=(dc == 0), stop=(dc == 7))
                return last
            p.emit("pe", mmv, reads=[wv.reg(), xnT.reg()], writes=[R(bank)])
            vs = (ts4 * 512, (ts4 + 1) * 512)
            p.emit("act", lambda e: e.activation(
                vfm.ap(*vs), bank.ap()[:, :], AF.Identity,
                bias=bq_cols.ap(16 + oc, 17 + oc)),
                reads=[R(bank), bq_cols.reg()], writes=[vfm.reg(*vs)])
        for ts4 in range(TTE // 4):
            one(ts4)
        if lora:
            def lhead(hh):
                pb = (hh) * 64

                def onet(ts4):
                    vs = (ts4 * 512, (ts4 + 1) * 512)
                    b1 = mk.ps[6]
                    ava, avr = lap("a_v", pb)
                    bva, bvr = lap("b_v")
                    p.emit("pe", lambda e: e.matmul(
                        b1.ap()[0:16, :], ava,
                        vfm.ap(*vs, p0=pb, p1=pb + 64),
                        start=True, stop=True),
                        reads=[avr, vfm.reg(*vs)], writes=[R(b1)])
                    tmp = View(bt, VSQ.o + 512, 512)
                    p.emit("dve", lambda e: e.tensor_copy(
                        tmp.ap(p1=16), b1.ap()[0:16, :]),
                        reads=[R(b1)], writes=[tmp.reg()])
                    b2 = mk.ps[7]
                    p.emit("pe", lambda e: e.matmul(
                        b2.ap()[0:64, :], bva, tmp.ap(p1=16),
                        start=True, stop=True),
                        reads=[bvr, tmp.reg()], writes=[R(b2)])
                    p.emit("dve", lambda e: e.scalar_tensor_tensor(
                        vfm.ap(*vs, p0=pb, p1=pb + 64), b2.ap()[0:64, :], 2.0,
                        vfm.ap(*vs, p0=pb, p1=pb + 64), ALU.mult, ALU.add),
                        reads=[R(b2), vfm.reg(*vs)], writes=[vfm.reg(*vs)])
                for ts4 in range(TTE // 4):
                    onet(ts4)
            lhead(0); lhead(1)
        def tpv(j):
            bank = mk.ps[4 + j % 2]
            bap = bank.ap().bitcast(BF16)
            p.emit("pe", lambda e: e.transpose(
                bap[:, 0:128], vfm.ap(j * 128, (j + 1) * 128),
                mk.ident.ap()[:, :]),
                reads=[vfm.reg(j * 128, (j + 1) * 128), R(mk.ident)],
                writes=[R(bank)])
            dst = mk.ar16_t.ap()[:, VB.o + j * 1040 + (2 * oc) * 65:
                                 VB.o + j * 1040 + (2 * oc) * 65 + 130]\
                .rearrange("p (h c) -> p h c", c=65)[:, :, 0:64]
            dreg = Region(VB.t, 0, 128, VB.o + j * 1040 + 2 * oc * 65,
                          VB.o + j * 1040 + 2 * oc * 65 + 130)
            p.emit("dve", lambda e: e.tensor_copy(
                dst, bap[:, 0:128].rearrange("p (h c) -> p h c", c=64)),
                reads=[R(bank)], writes=[dreg])
        for j in range(TTE):
            tpv(j)
    for oc in range(8):
        vprod(oc)
    def vones(j):
        dst = mk.ar16_t.ap()[:, VB.o + j * 1040:VB.o + (j + 1) * 1040]\
            .rearrange("p (h c) -> p h c", c=65)[:, :, 64:65]
        p.emit("dve", lambda e: e.memset(dst, 1.0),
               writes=[Region(VB.t, 0, 128, VB.o + j * 1040,
                              VB.o + (j + 1) * 1040)])
    for j in range(TTE):
        vones(j)

    nqt = TTE // 4

    import os as _os
    _dbg = _os.environ.get("ATTN_DBG")

    def qtile(qt):
        def qprod(oc):
            wv = WRG[oc % 2]
            ldw_qkv(wv, "q", oc)
            bank = mk.ps[4 + oc % 2]

            def mmq(t):
                last = None
                for dc in range(8):
                    last = t.matmul(
                        bank.ap()[:, :],
                        wv.ap(dc * 128, (dc + 1) * 128),
                        xnT.ap(dc * T + qt * 512, dc * T + (qt + 1) * 512),
                        start=(dc == 0), stop=(dc == 7))
                return last
            p.emit("pe", mmq, reads=[wv.reg(), xnT.reg()], writes=[R(bank)])
            qs = (oc * 512, (oc + 1) * 512)
            if rope is None:
                p.emit("act", lambda e: e.activation(
                    QB.ap(*qs), bank.ap()[:, :], AF.Identity,
                    bias=bq_cols.ap(oc, oc + 1)),
                    reads=[R(bank), bq_cols.reg()], writes=[QB.reg(*qs)])
            else:
                qtmp = View(bt, VSQ.o + 512, 512)
                p.emit("act", lambda e: e.activation(
                    qtmp.ap(), bank.ap()[:, :], AF.Identity,
                    bias=bq_cols.ap(oc, oc + 1)),
                    reads=[R(bank), bq_cols.reg()], writes=[qtmp.reg()])
                bank2 = mk.ps[6]
                p.emit("pe", lambda e: e.matmul(
                    bank2.ap()[:, :], rot.ap(oc * 128, (oc + 1) * 128),
                    qtmp.ap(), start=True, stop=True),
                    reads=[rot.reg(), qtmp.reg()], writes=[R(bank2)])
                p.emit("dve", lambda e: e.tensor_copy(
                    QB.ap(*qs), bank2.ap()[:, :]),
                    reads=[R(bank2)], writes=[QB.reg(*qs)])
        for oc in range(8):
            qprod(oc)
        if lora:
            def lheadq(h):
                oc, pb = h // 2, (h % 2) * 64
                qs = (oc * 512, (oc + 1) * 512)
                b1 = mk.ps[6]
                aqa, aqr = lap("a_q", pb)
                bqa, bqr = lap("b_q")
                p.emit("pe", lambda e: e.matmul(
                    b1.ap()[0:16, :], aqa, QB.ap(*qs, p0=pb, p1=pb + 64),
                    start=True, stop=True),
                    reads=[aqr, QB.reg(*qs)], writes=[R(b1)])
                tmp = View(bt, VSQ.o + 512, 512)
                p.emit("dve", lambda e: e.tensor_copy(
                    tmp.ap(p1=16), b1.ap()[0:16, :]),
                    reads=[R(b1)], writes=[tmp.reg()])
                b2 = mk.ps[7]
                p.emit("pe", lambda e: e.matmul(
                    b2.ap()[0:64, :], bqa, tmp.ap(p1=16),
                    start=True, stop=True),
                    reads=[bqr, tmp.reg()], writes=[R(b2)])
                p.emit("dve", lambda e: e.scalar_tensor_tensor(
                    QB.ap(*qs, p0=pb, p1=pb + 64), b2.ap()[0:64, :], 2.0,
                    QB.ap(*qs, p0=pb, p1=pb + 64), ALU.mult, ALU.add),
                    reads=[R(b2), QB.reg(*qs)], writes=[QB.reg(*qs)])
            for h in range(16):
                lheadq(h)
        mk.small.save()
        nbias = mk.vsm(8)

        def mkbound(oc):
            qs = (oc * 512, (oc + 1) * 512)
            p.emit("act", lambda e: e.activation(
                VSQ.ap(0, 512), QB.ap(*qs), AF.Square),
                reads=[QB.reg(*qs)], writes=[VSQ.reg(0, 512)])
            nb = mk.ps[7]
            p.emit("pe", lambda e: e.matmul(
                nb.ap()[0:1, :], mk.ones_bf.ap()[:, 0:1], VSQ.ap(0, 512),
                start=True, stop=True),
                reads=[VSQ.reg(0, 512), R(mk.ones_bf)], writes=[R(nb)])
            mk.small.save()
            qn2 = mk.vsm(1)
            p.emit("dve", lambda e: e.reduce_max(
                qn2.ap(p1=1), nb.ap()[0:1, :], axis=AX.X),
                reads=[R(nb)], writes=[qn2.reg(p1=1)])
            p.emit("dve", lambda e: e.tensor_tensor(
                qn2.ap(p1=1), qn2.ap(p1=1), kn2.ap(oc, oc + 1, p1=1),
                ALU.mult),
                reads=[qn2.reg(p1=1), kn2.reg(oc, oc + 1)],
                writes=[qn2.reg(p1=1)])
            p.emit("act", lambda e: e.activation(
                qn2.ap(p1=1), qn2.ap(p1=1), AF.Ln),
                reads=[qn2.reg(p1=1)], writes=[qn2.reg(p1=1)])
            p.emit("act", lambda e: e.activation(
                qn2.ap(p1=1), qn2.ap(p1=1), AF.Exp,
                scale=p.const_col(0.5)[0:1, :]),
                reads=[qn2.reg(p1=1)], writes=[qn2.reg(p1=1)])
            p.emit("dve", lambda e: e.tensor_scalar(
                qn2.ap(p1=1), qn2.ap(p1=1), -0.125, -1.0, ALU.mult, ALU.add),
                reads=[qn2.reg(p1=1)], writes=[qn2.reg(p1=1)])
            bb = mk.ps[7]
            p.emit("pe", lambda e: e.matmul(
                bb.ap()[:, 0:1], onesr, qn2.ap(p1=1), start=True, stop=True),
                reads=[qn2.reg(p1=1)], writes=[R(bb)])
            p.emit("dve", lambda e: e.tensor_copy(
                nbias.ap(oc, oc + 1), bb.ap()[:, 0:1]),
                reads=[R(bb)], writes=[nbias.reg(oc, oc + 1)])
            mk.small.restore()
        for oc in range(8):
            mkbound(oc)
        if _dbg and qt == 0:
            dk = p.dram(f"dbg_k{elem}", [128, 8 * T], F32, kind="ExternalOutput")
            dq = p.dram(f"dbg_q{elem}", [128, 4096], F32, kind="ExternalOutput")
            dn = p.dram(f"dbg_nb{elem}", [128, 8], F32, kind="ExternalOutput")
            p.emit("pool", lambda e: e.dma_start(out=dk.ap()[:, :], in_=KB.ap()),
                   reads=[KB.reg()], dma=True)
            p.emit("pool", lambda e: e.dma_start(out=dq.ap()[:, :], in_=QB.ap()),
                   reads=[QB.reg()], dma=True)
            p.emit("pool", lambda e: e.dma_start(out=dn.ap()[:, :], in_=nbias.ap(0, 8)),
                   reads=[nbias.reg(0, 8)], dma=True)
        c8 = p.const_col(0.125)

        def head(h):
            oc, pb = h // 2, (h % 2) * 64
            av = mk.ps[h % 2]

            def score(kc):
                sb = mk.ps[2 + kc % 4]
                p.emit("pe", lambda e: e.matmul(
                    sb.ap()[:, :],
                    KB.ap(oc * T + kc * 128, oc * T + (kc + 1) * 128,
                          p0=pb, p1=pb + 64),
                    QB.ap(oc * 512, (oc + 1) * 512, p0=pb, p1=pb + 64),
                    start=True, stop=True),
                    reads=[KB.reg(), QB.reg()], writes=[R(sb)])
                pr = PRB[kc % 4]
                p.emit("act", lambda e: e.activation(
                    pr.ap(), sb.ap()[:, :], AF.Exp, scale=c8,
                    bias=nbias.ap(oc, oc + 1)),
                    reads=[R(sb), nbias.reg(oc, oc + 1)], writes=[pr.reg()])

            def avmm(kc):
                pr = PRB[kc % 4]
                p.emit("pe", lambda e: e.matmul(
                    av.ap()[0:65, :],
                    mk.ar16_t.ap()[:, VB.o + kc * 1040 + h * 65:
                                   VB.o + kc * 1040 + (h + 1) * 65],
                    pr.ap(), start=(kc == 0), stop=(kc == TTE - 1)),
                    reads=[Region(VB.t, 0, 128, VB.o + kc * 1040,
                                  VB.o + (kc + 1) * 1040), pr.reg()],
                    writes=[R(av)])
            LA = min(2, TTE - 1)
            for kc in range(TTE):
                score(kc)
                if kc >= LA:
                    avmm(kc - LA)
            for kc in range(TTE - LA, TTE):
                avmm(kc)
            zrow = View(mk.ar32_t, xnring[0].o // 2, 512)
            ibc = View(mk.ar32_t, xnring[0].o // 2 + 512, 512)
            p.emit("dve", lambda e: e.tensor_copy(
                zrow.ap(p1=1), av.ap()[64:65, :]),
                reads=[R(av)], writes=[zrow.reg()])
            p.emit("act", lambda e: e.activation(
                zrow.ap(p1=1), zrow.ap(p1=1), AF.Ln),
                reads=[zrow.reg()], writes=[zrow.reg()])
            p.emit("act", lambda e: e.activation(
                zrow.ap(p1=1), zrow.ap(p1=1), AF.Exp,
                scale=p.const_col(-1.0)[0:1, :]),
                reads=[zrow.reg()], writes=[zrow.reg()])
            bb = mk.ps[6 + h % 2]
            p.emit("pe", lambda e: e.matmul(
                bb.ap()[0:64, :], onesr[0:1, 0:64], zrow.ap(p1=1),
                start=True, stop=True),
                reads=[zrow.reg()], writes=[R(bb)])
            p.emit("dve", lambda e: e.tensor_copy(
                ibc.ap(p1=64), bb.ap()[0:64, :]),
                reads=[R(bb)], writes=[ibc.reg()])
            afs = (oc * 512, (oc + 1) * 512)
            p.emit("dve", lambda e: e.tensor_tensor(
                AFM.ap(*afs, p0=pb, p1=pb + 64), av.ap()[0:64, :],
                ibc.ap(p1=64), ALU.mult),
                reads=[R(av), ibc.reg()], writes=[AFM.reg(*afs)])
        for h in range(16):
            head(h)
        def proj(ob):
            wv = QB
            p.emit("pool", lambda e: e.dma_start(
                out=wv.ap(0, 8 * 512).rearrange("p (c o) -> p c o", c=8),
                in_=cfg["wproj"].rearrange("(c p) o -> p c o", p=128)
                [:, :, ob * 512:(ob + 1) * 512]),
                writes=[wv.reg()], dma=True)

            def one(j):
                bank = mk.ps[4 + j % 2]

                def mmp(t):
                    last = None
                    for dc in range(8):
                        last = t.matmul(
                            bank.ap()[:, :],
                            AFM.ap(dc * 512 + j * 128, dc * 512 + (j + 1) * 128),
                            wv.ap(dc * 512, (dc + 1) * 512),
                            start=(dc == 0), stop=(dc == 7))
                    return last
                p.emit("pe", mmp, reads=[AFM.reg(), wv.reg()], writes=[R(bank)])
                ys = ((qt * 4 + j) * D + ob * 512,
                      (qt * 4 + j) * D + (ob + 1) * 512)
                p.emit("dve", lambda e: e.tensor_copy(
                    y32.ap(*ys), bank.ap()[:, :]),
                    reads=[R(bank)], writes=[y32.reg(*ys)])
            for j in range(4):
                one(j)
        if _dbg and qt == 0:
            da = p.dram(f"dbg_afm{elem}", [128, 4096], F32, kind="ExternalOutput")
            p.emit("pool", lambda e: e.dma_start(out=da.ap()[:, :], in_=AFM.ap()),
                   reads=[AFM.reg()], dma=True)
        proj(0); proj(1)
        mk.small.restore()
    for qt in range(nqt):
        qtile(qt)
    mk.page_x(xp, "in")

    def finadd(j):
        tt = tt0 + j
        xs = (tt * D, (tt + 1) * D)
        xa = x.ap()[:, xs[0]:xs[1]]
        xreg = Region(x, 0, 128, *xs)
        ys = (j * D, (j + 1) * D)
        p.emit("dve", lambda e: e.tensor_tensor(
            y32.ap(*ys), y32.ap(*ys), pb_bc.ap(), ALU.add),
            reads=[y32.reg(*ys), pb_bc.reg()], writes=[y32.reg(*ys)])
        p.emit("dve", lambda e: e.tensor_tensor(
            xa, xa, y32.ap(*ys), ALU.add),
            reads=[xreg, y32.reg(*ys)], writes=[xreg])
    for j in range(TTE):
        finadd(j)


def moe_apply(mk, cfg, xpage):
    p = mk.p
    kind = cfg["kind"]
    HN = 4096 if kind == "l1" else 2816
    NHC = HN // 128
    halves = [(0, 8), (8, 8), (16, 8), (24, 8)] if kind == "l1" else \
             [(0, 11), (11, 11)]
    for elem in range(mk.NE):
        _moe_tp(mk, cfg, xpage, elem, HN, NHC, halves)


def _moe_tp(mk, cfg, xpage, elem, HN, NHC, halves):
    p = mk.p
    kind = cfg["kind"]
    mk.ar16.reset(); mk.ar32.reset(); mk.small.reset()
    TTE = mk.TTE
    T = mk.T
    tt0 = elem * TTE
    x = mk.x
    onesr = p.const_row(1.0)
    xp = xpage
    _hm = max(n for _, n in halves)
    _st = min(_hm * T // 1024, mk.TT)
    mk.page_x(xp, "out", order=list(range(_st, mk.TT)) + list(range(_st)))
    negm = mk.vsm(TTE)
    rstd = mk.vsm(TTE)
    mk.ln_stats(list(range(tt0, tt0 + TTE)), rstd, negm)
    xnT = mk.v16(8 * T)
    sel8 = mk.v16(1024)
    p.emit("pool", lambda e: e.dma_start(
        out=sel8.ap(p1=8), in_=cfg["sel8"][:, :]),
        writes=[sel8.reg()], dma=True)
    if kind == "l2":
        lnw_bc = mk.v16(1024)
        mk.ar32.save()
        lrow = mk.v32(1024)
        p.emit("sp", lambda e: e.dma_start(
            out=lrow.ap(p1=1), in_=cfg["lnw_row"][0:1, :]),
            writes=[lrow.reg()], dma=True)

        def bcob(ob):
            bank = mk.ps[0]
            p.emit("pe", lambda e: e.matmul(
                bank.ap()[:, :], onesr, lrow.ap(ob * 512, (ob + 1) * 512, p1=1),
                start=True, stop=True),
                reads=[lrow.reg()], writes=[R(bank)])
            p.emit("dve", lambda e: e.tensor_copy(
                lnw_bc.ap(ob * 512, (ob + 1) * 512), bank.ap()[:, :]),
                reads=[R(bank)], writes=[lnw_bc.reg(ob * 512, (ob + 1) * 512)])
        bcob(0); bcob(1)
        mk.ar32.restore()
        rstd2 = mk.vsm(TTE)

    xnring = [mk.v16_in32(1024) for _ in range(2)]

    def mkxnT(j):
        tt = tt0 + j
        mk.ar16.save(); mk.ar32.save()
        xn = xnring[j % 2]
        mk.ln_apply(tt, negm, rstd, xn.ap(), xn.reg(), j=j)
        if kind == "l2":
            xsq = mk.v32(2)
            jk = mk.v16(1024)
            p.emit("act", lambda e: e.activation(
                jk.ap(), xn.ap(), AF.Square, accum_out=xsq.ap(0, 1)),
                reads=[xn.reg()], writes=[jk.reg(), xsq.reg(0, 1)])
            p.emit("act", lambda e: e.activation(
                xsq.ap(1, 2), xsq.ap(0, 1), AF.Ln,
                bias=p.const_col(1e-5), scale=p.const_col(1.0 / D)),
                reads=[xsq.reg(0, 1)], writes=[xsq.reg(1, 2)])
            p.emit("act", lambda e, j=j: e.activation(
                rstd2.ap(j, j + 1), xsq.ap(1, 2), AF.Exp,
                scale=p.const_col(-0.5)),
                reads=[xsq.reg(1, 2)], writes=[rstd2.reg(j, j + 1)])
            p.emit("dve", lambda e, j=j: e.tensor_scalar(
                xn.ap(), xn.ap(), rstd2.ap(j, j + 1), None, ALU.mult),
                reads=[xn.reg(), rstd2.reg(j, j + 1)], writes=[xn.reg()])
            p.emit("dve", lambda e: e.tensor_tensor(
                xn.ap(), xn.ap(), lnw_bc.ap(), ALU.mult),
                reads=[xn.reg(), lnw_bc.reg()], writes=[xn.reg()])
        dst = mk.ar16_t.ap()[:, xnT.o:xnT.o + 8 * T].rearrange(
            "p (c t) -> p c t", c=8)[:, :, j * 128:(j + 1) * 128]
        mk.transpose8(lambda dc: xn.ap(dc * 128, (dc + 1) * 128),
                      dst, xnT.reg(), [xn.reg()], pb=j % 2)
        mk.ar16.restore(); mk.ar32.restore()
    for j in range(TTE):
        mkxnT(j)
    gT = mk.v16(T)

    def gates(j):
        mk.small.save(); mk.ar32.save()
        gbuf = mk.v32(8)
        bank = mk.ps[0]

        def mmg(t):
            last = None
            for dc in range(8):
                last = t.matmul(
                    bank.ap()[0:128, 0:8],
                    xnT.ap(dc * T + j * 128, dc * T + (j + 1) * 128),
                    wgst.ap(dc * 8, (dc + 1) * 8),
                    start=(dc == 0), stop=(dc == 7))
            return last
        p.emit("pe", mmg, reads=[xnT.reg(), wgst.reg()], writes=[R(bank)])
        lg = gbuf
        p.emit("dve", lambda e: e.tensor_copy(lg.ap(), bank.ap()[:, 0:8]),
               reads=[R(bank)], writes=[lg.reg()])
        m1 = mk.vsm(1); m2 = mk.vsm(1); g1 = mk.vsm(1); g2 = mk.vsm(1)
        msk = mk.v32(8)
        p.emit("dve", lambda e: e.reduce_max(
            m1.ap(), lg.ap(), axis=AX.X), reads=[lg.reg()], writes=[m1.reg()])
        p.emit("dve", lambda e: e.tensor_scalar(
            msk.ap(), lg.ap(), m1.ap(), None, ALU.is_ge),
            reads=[lg.reg(), m1.reg()], writes=[msk.reg()])
        lm = mk.v32(8)
        p.emit("dve", lambda e: e.scalar_tensor_tensor(
            lm.ap(), msk.ap(), -1e30, lg.ap(), ALU.mult, ALU.add),
            reads=[msk.reg(), lg.reg()], writes=[lm.reg()])
        p.emit("dve", lambda e: e.reduce_max(
            m2.ap(), lm.ap(), axis=AX.X), reads=[lm.reg()], writes=[m2.reg()])
        if kind == "l1":
            p.emit("dve", lambda e: e.tensor_tensor(
                g1.ap(), m2.ap(), m1.ap(), ALU.subtract),
                reads=[m1.reg(), m2.reg()], writes=[g1.reg()])
            p.emit("act", lambda e: e.activation(
                g1.ap(), g1.ap(), AF.Exp), reads=[g1.reg()], writes=[g1.reg()])
            p.emit("dve", lambda e: e.tensor_single_scalar(
                g1.ap(), g1.ap(), 1.0, ALU.add),
                reads=[g1.reg()], writes=[g1.reg()])
            p.emit("dve", lambda e: e.reciprocal(
                g1.ap(), g1.ap()), reads=[g1.reg()], writes=[g1.reg()])
            p.emit("dve", lambda e: e.scalar_tensor_tensor(
                g2.ap(), g1.ap(), -1.0, p.const_col(1.0), ALU.mult, ALU.add),
                reads=[g1.reg()], writes=[g2.reg()])
        else:
            Z = mk.vsm(1)
            ex = mk.v32(8)
            nm1 = mk.vsm(1)
            p.emit("dve", lambda e: e.tensor_single_scalar(
                nm1.ap(), m1.ap(), -1.0, ALU.mult),
                reads=[m1.reg()], writes=[nm1.reg()])
            p.emit("act", lambda e: e.activation(
                ex.ap(), lg.ap(), AF.Exp, bias=nm1.ap(), accum_out=Z.ap()),
                reads=[lg.reg(), nm1.reg()], writes=[ex.reg(), Z.reg()])
            p.emit("dve", lambda e: e.reciprocal(
                g1.ap(), Z.ap()), reads=[Z.reg()], writes=[g1.reg()])
            p.emit("dve", lambda e: e.tensor_tensor(
                g2.ap(), m2.ap(), m1.ap(), ALU.subtract),
                reads=[m1.reg(), m2.reg()], writes=[g2.reg()])
            p.emit("act", lambda e: e.activation(
                g2.ap(), g2.ap(), AF.Exp), reads=[g2.reg()], writes=[g2.reg()])
            p.emit("dve", lambda e: e.tensor_tensor(
                g2.ap(), g2.ap(), g1.ap(), ALU.mult),
                reads=[g2.reg(), g1.reg()], writes=[g2.reg()])
        msk2 = mk.v32(8)
        p.emit("dve", lambda e: e.tensor_scalar(
            msk2.ap(), lm.ap(), m2.ap(), None, ALU.is_ge),
            reads=[lm.reg(), m2.reg()], writes=[msk2.reg()])
        p.emit("dve", lambda e: e.tensor_scalar(
            msk.ap(), msk.ap(), g1.ap(), None, ALU.mult),
            reads=[msk.reg(), g1.reg()], writes=[msk.reg()])
        p.emit("dve", lambda e: e.scalar_tensor_tensor(
            gbuf.ap(), msk2.ap(), g2.ap(), msk.ap(), ALU.mult, ALU.add),
            reads=[msk2.reg(), g2.reg(), msk.reg()], writes=[gbuf.reg()])
        bank2 = mk.ps[1]
        p.emit("pe", lambda e: e.matmul(
            bank2.ap()[0:8, 0:128], gbuf.ap(), mk.id32.ap()[:, :],
            start=True, stop=True),
            reads=[gbuf.reg(), R(mk.id32)], writes=[R(bank2)])
        p.emit("dve", lambda e, j=j: e.tensor_copy(
            gT.ap(j * 128, (j + 1) * 128, p1=8), bank2.ap()[0:8, 0:128]),
            reads=[R(bank2)], writes=[gT.reg()])
        mk.small.restore(); mk.ar32.restore()
    wgst = mk.v16(64)
    p.emit("pool", lambda e: e.dma_start(
        out=wgst.ap().rearrange("p (c o) -> p c o", c=8),
        in_=cfg["w_gT" if kind == "l1" else "router"]
        .rearrange("(c p) o -> p c o", p=128)),
        writes=[wgst.reg()], dma=True)
    for j in range(TTE):
        gates(j)
    HHM = max(n for _, n in halves)
    WB = 8 if kind == "l1" else 2
    S = 8 * WB * 128
    nw = 1 if kind == "l1" else 2
    bneed = HHM * T + 2 * nw * S
    bt = x if mk.TT * D >= bneed else \
        p.sbuf(f"moe_borrow_{kind}_{elem}", [128, bneed], BF16)
    hbuf = View(bt, 0, HHM * T)
    gbc = mk.v16(2048)
    wslots = [View(bt, HHM * T + i * nw * S, S) for i in range(2)]
    w3slots = [View(bt, HHM * T + i * nw * S + S, S) for i in range(2)] \
        if kind == "l2" else None
    sched = []
    for ei_ in range(8):
        for h0_, nhc_ in halves:
            for b0_ in range(0, nhc_, WB):
                sched.append((ei_, h0_ + b0_, min(WB, nhc_ - b0_)))

    def ldblk(i):
        if i >= len(sched):
            return
        ei2, c0, nb2 = sched[i]
        pairs = [("w1" if kind == "l2" else "w_in", wslots[i % 2], "pool")]
        if kind == "l2":
            pairs.append(("w3", w3slots[i % 2], "pool"))
        for nm2, wv, qs in pairs:
            p.emit(qs, lambda e, nm2=nm2, wv=wv, ei2=ei2, c0=c0, nb2=nb2:
                   e.dma_start(
                out=wv.ap(0, S).rearrange(
                    "p (c o) -> p c o", c=8)[:, :, 0:nb2 * 128],
                in_=cfg[nm2][ei2].rearrange("(c p) o -> p c o", p=128)
                [:, :, c0 * 128:(c0 + nb2) * 128]),
                writes=[Region(wv.t, 0, 128, wv.o, wv.o + S)], dma=True)
    blkc = [0]
    ldblk(0)
    w2h = mk.v16(HHM * 1024)
    y32 = mk.v32(TTE * 1024)
    p.emit("dve", lambda e: e.memset(y32.ap(), 0.0), writes=[y32.reg()])
    ysc = [View(mk.ar32_t, xnring[0].o // 2 + i * 512, 512) for i in range(2)]

    def expert(ei):
        def gb(ts4):
            bank = mk.ps[0]
            p.emit("pe", lambda e: e.matmul(
                bank.ap()[:, :], sel8.ap(ei * 128, (ei + 1) * 128, p1=8),
                gT.ap(ts4 * 512, (ts4 + 1) * 512, p1=8),
                start=True, stop=True),
                reads=[sel8.reg(), gT.reg()], writes=[R(bank)])
            p.emit("dve", lambda e: e.tensor_copy(
                gbc.ap(ts4 * 512, (ts4 + 1) * 512), bank.ap()[:, :]),
                reads=[R(bank)],
                writes=[Region(gbc.t, 0, 128, gbc.o + ts4 * 512,
                               gbc.o + (ts4 + 1) * 512)])
        for ts4 in range(T // 512):
            gb(ts4)

        def half(h0, nhc):
            p.emit("pool", lambda e: e.dma_start(
                out=w2h.ap(0, nhc * 1024).rearrange("p (c o) -> p c o", c=nhc),
                in_=cfg["w2" if kind == "l2" else "w_out"][ei]
                .rearrange("(c p) o -> p c o", p=128)[:, h0:h0 + nhc, :]),
                reads=[R(cfg["w2" if kind == "l2" else "w_out"].tensor)]
                if False else [],
                writes=[Region(w2h.t, 0, 128, w2h.o, w2h.o + nhc * 1024)],
                dma=True)

            def hchunk(hci):
                hc = h0 + hci
                boff = (hci % WB) * 128
                bw = WB * 128
                if hci % WB == 0:
                    i = blkc[0]
                    assert sched[i] == (ei, h0 + hci, min(WB, nhc - hci))
                    ldblk(i + 1)
                    blkc[0] += 1
                i = blkc[0] - 1
                wv = wslots[i % 2]
                wv3 = w3slots[i % 2] if kind == "l2" else None

                def ts_tile(ts4):
                    bank = mk.ps[2 + ts4 % 2]

                    def mmh(t, wvv, bk):
                        last = None
                        for dc in range(8):
                            last = t.matmul(
                                bk.ap()[:, :],
                                wvv.ap(dc * bw + boff, dc * bw + boff + 128),
                                xnT.ap(dc * T + ts4 * 512,
                                       dc * T + (ts4 + 1) * 512),
                                start=(dc == 0), stop=(dc == 7))
                        return last
                    p.emit("pe", lambda t: mmh(t, wv, bank),
                           reads=[Region(wv.t, 0, 128, wv.o, wv.o + 8 * bw),
                                  xnT.reg()],
                           writes=[R(bank)])
                    hs = (hci * T + ts4 * 512, hci * T + (ts4 + 1) * 512)
                    hreg = Region(hbuf.t, 0, 128, hbuf.o + hs[0],
                                  hbuf.o + hs[1])
                    gbr = Region(gbc.t, 0, 128, gbc.o + ts4 * 512,
                                 gbc.o + (ts4 + 1) * 512)
                    if kind == "l1":
                        p.emit("dve", lambda e: e.tensor_tensor(
                            hbuf.ap(*hs), bank.ap()[:, :],
                            gbc.ap(ts4 * 512, (ts4 + 1) * 512), ALU.mult),
                            reads=[R(bank), gbr], writes=[hreg])
                    else:
                        bank3 = mk.ps[4 + ts4 % 2]
                        p.emit("pe", lambda t: mmh(t, wv3, bank3),
                               reads=[Region(wv3.t, 0, 128, wv3.o,
                                             wv3.o + 8 * bw), xnT.reg()],
                               writes=[R(bank3)])
                        sv = ysc[ts4 % 2]
                        p.emit("act", lambda e: e.activation(
                            sv.ap(), bank.ap()[:, :], AF.Silu),
                            reads=[R(bank)], writes=[sv.reg()])
                        p.emit("dve", lambda e: e.tensor_tensor(
                            sv.ap(), sv.ap(), bank3.ap()[:, :], ALU.mult),
                            reads=[sv.reg(), R(bank3)], writes=[sv.reg()])
                        p.emit("dve", lambda e: e.tensor_tensor(
                            hbuf.ap(*hs), sv.ap(),
                            gbc.ap(ts4 * 512, (ts4 + 1) * 512), ALU.mult),
                            reads=[sv.reg(), gbr], writes=[hreg])
                for ts4 in range(T // 512):
                    ts_tile(ts4)
            for hci in range(nhc):
                hchunk(hci)
            def ymm(ts4):
                def one(j2, ob):
                    bank = mk.ps[j2 * 2 + ob]

                    def mm(t):
                        last = None
                        for hci in range(nhc):
                            last = t.matmul(
                                bank.ap()[:, :],
                                hbuf.ap(hci * T + ts4 * 512 + j2 * 128,
                                        hci * T + ts4 * 512 + (j2 + 1) * 128),
                                w2h.ap(hci * 1024 + ob * 512,
                                       hci * 1024 + (ob + 1) * 512),
                                start=(hci == 0), stop=(hci == nhc - 1))
                        return last
                    p.emit("pe", mm,
                           reads=[Region(hbuf.t, 0, 128, hbuf.o,
                                         hbuf.o + nhc * T),
                                  Region(w2h.t, 0, 128, w2h.o,
                                         w2h.o + nhc * 1024)],
                           writes=[R(bank)])
                    ys = ((ts4 * 4 + j2) * 1024 + ob * 512,
                          (ts4 * 4 + j2) * 1024 + (ob + 1) * 512)
                    p.emit("dve", lambda e: e.tensor_tensor(
                        y32.ap(*ys), y32.ap(*ys), bank.ap()[:, :], ALU.add),
                        reads=[y32.reg(*ys), R(bank)], writes=[y32.reg(*ys)])
                for j2 in range(4):
                    for ob in range(2):
                        one(j2, ob)
            for ts4 in range(T // 512):
                ymm(ts4)
        for h0, nhc in halves:
            half(h0, nhc)
    for ei in range(8):
        expert(ei)
    mk.page_x(xp, "in")

    def finadd(j):
        tt = tt0 + j
        xs = (tt * D, (tt + 1) * D)
        xa = x.ap()[:, xs[0]:xs[1]]
        xreg = Region(x, 0, 128, *xs)
        ys = (j * D, (j + 1) * D)
        p.emit("dve", lambda e: e.scalar_tensor_tensor(
            xa, y32.ap(*ys), 0.5, xa, ALU.mult, ALU.add),
            reads=[y32.reg(*ys), xreg], writes=[xreg])
    for j in range(TTE):
        finadd(j)


def l1_attn_apply(mk, cfg, elem):
    p = mk.p
    mk.ar16.reset(); mk.ar32.reset(); mk.small.reset()
    xp = cfg["xpage"]
    mk.page_x(xp, "out")
    TTE = mk.TTE
    T = mk.T
    NB = T // 64
    tt0 = elem * TTE
    x = mk.x
    onesr = p.const_row(1.0)
    sc4 = 0.3535533905932738
    negm = mk.vsm(TTE)
    rstd = mk.vsm(TTE)
    mk.ln_stats(list(range(tt0, tt0 + TTE)), rstd, negm)
    xnT = mk.v16(8 * T)

    xnring = [mk.v16_in32(1024) for _ in range(2)]

    def mkxnT(j):
        tt = tt0 + j
        xn = xnring[j % 2]
        mk.ln_apply(tt, negm, rstd, xn.ap(), xn.reg(), j=j)
        dst = mk.ar16_t.ap()[:, xnT.o:xnT.o + 8 * T].rearrange(
            "p (c t) -> p c t", c=8)[:, :, j * 128:(j + 1) * 128]
        mk.transpose8(lambda dc: xn.ap(dc * 128, (dc + 1) * 128),
                      dst, xnT.reg(), [xn.reg()], pb=j % 2)
    for j in range(TTE):
        mkxnT(j)
    bo_bc = mk.v16(1024)
    mk.ar32.save()
    brow = mk.v32(1024)
    p.emit("sp", lambda e: e.dma_start(
        out=brow.ap(p1=1), in_=cfg["bo_row"][0:1, :]),
        writes=[brow.reg()], dma=True)

    def bcob(ob):
        bank = mk.ps[0]
        p.emit("pe", lambda e: e.matmul(
            bank.ap()[:, :], onesr, brow.ap(ob * 512, (ob + 1) * 512, p1=1),
            start=True, stop=True), reads=[brow.reg()], writes=[R(bank)])
        p.emit("dve", lambda e: e.tensor_copy(
            bo_bc.ap(ob * 512, (ob + 1) * 512), bank.ap()[:, :]),
            reads=[R(bank)], writes=[bo_bc.reg(ob * 512, (ob + 1) * 512)])
    bcob(0); bcob(1)
    mk.ar32.restore()
    VT = mk.v16(TTE * 1040)
    ctxn = mk.v16(2 * 65)
    need = 2 * 8 * T
    bt = x if mk.TT * D >= need else p.sbuf(f"l1a_borrow{elem}",
                                            [128, need], BF16)
    QF = View(bt, 0, 8 * T)
    KT = View(bt, 8 * T, 8 * T)
    WSRC = {"q": "wqT", "k": "wkT", "v": "wvT"}

    def ldw(wv, nm, ob):
        p.emit("pool", lambda e: e.dma_start(
            out=wv.ap(0, 8 * 512).rearrange("p (c o) -> p c o", c=8),
            in_=cfg[nm].rearrange("(c p) o -> p c o", p=128)
            [:, :, ob * 512:(ob + 1) * 512]),
            writes=[wv.reg(0, 4096)], dma=True)
    wring = [mk.v16_in32(1024) for _ in range(2)]

    def qprod(oc):
        wv = wring[oc % 2]
        p.emit("pool", lambda e: e.dma_start(
            out=wv.ap(0, 1024).rearrange("p (c o) -> p c o", c=8),
            in_=cfg["wqT"].rearrange("(c p) o -> p c o", p=128)
            [:, :, oc * 128:(oc + 1) * 128]),
            writes=[wv.reg(0, 1024)], dma=True)

        def one(ts4):
            bank = mk.ps[4 + ts4 % 2]

            def mm(t):
                last = None
                for dc in range(8):
                    last = t.matmul(
                        bank.ap()[:, :], wv.ap(dc * 128, (dc + 1) * 128),
                        xnT.ap(dc * T + ts4 * 512, dc * T + (ts4 + 1) * 512),
                        start=(dc == 0), stop=(dc == 7))
                return last
            p.emit("pe", mm,
                   reads=[wv.reg(0, 1024), xnT.reg()],
                   writes=[R(bank)])
            qs = (oc * T + ts4 * 512, oc * T + (ts4 + 1) * 512)
            p.emit("act", lambda e: e.activation(
                QF.ap(*qs), bank.ap()[:, :], AF.Exp,
                scale=p.const_col(sc4)),
                reads=[R(bank)], writes=[QF.reg(*qs)])
        for ts4 in range(T // 512):
            one(ts4)
    for oc in range(8):
        qprod(oc)
    kvw = [mk.v16_in32(4096) for _ in range(2)]

    def kv_all(sec, wnm):
        ldw(kvw[0], wnm, 0)
        ldw(kvw[1], wnm, 1)
        for j in range(TTE):
            for ob in range(2):
                wv = kvw[ob]
                bank = mk.ps[4 + (j * 2 + ob) % 2]

                def mm(t, j=j, ob=ob, wv=wv, bank=bank):
                    last = None
                    for dc in range(8):
                        last = t.matmul(
                            bank.ap()[:, :],
                            xnT.ap(dc * T + j * 128, dc * T + (j + 1) * 128),
                            wv.ap(dc * 512, (dc + 1) * 512),
                            start=(dc == 0), stop=(dc == 7))
                    return last
                p.emit("pe", mm,
                       reads=[wv.reg(0, 4096), xnT.reg()],
                       writes=[R(bank)])
                if sec == "k":
                    ks = (j * 1024 + ob * 512, j * 1024 + (ob + 1) * 512)
                    kreg = Region(KT.t, 0, 128, KT.o + ks[0], KT.o + ks[1])
                    mk.ar32.save()
                    t1 = mk.v32(512)
                    t2 = mk.v32(512)
                    p.emit("dve", lambda e, bank=bank, t1=t1: e.tensor_scalar(
                        t1.ap(), bank.ap()[:, :], sc4, 0.0, ALU.mult, ALU.min),
                        reads=[R(bank)], writes=[t1.reg()])
                    p.emit("act", lambda e, t1=t1: e.activation(
                        t1.ap(), t1.ap(), AF.Exp),
                        reads=[t1.reg()], writes=[t1.reg()])
                    p.emit("dve", lambda e, bank=bank, t2=t2: e.tensor_scalar(
                        t2.ap(), bank.ap()[:, :], sc4, 0.0, ALU.mult, ALU.max),
                        reads=[R(bank)], writes=[t2.reg()])
                    p.emit("dve", lambda e, ks=ks, t1=t1, t2=t2, kreg=kreg:
                           e.tensor_tensor(
                        KT.t.ap()[:, KT.o + ks[0]:KT.o + ks[1]], t1.ap(),
                        t2.ap(), ALU.add),
                        reads=[t1.reg(), t2.reg()], writes=[kreg])
                    mk.ar32.restore()
                else:
                    dst = mk.ar16_t.ap()[:, VT.o + j * 1040 + ob * 520:
                                         VT.o + j * 1040 + ob * 520 + 520]\
                        .rearrange("p (h c) -> p h c", c=65)[:, :, 0:64]
                    dreg = Region(VT.t, 0, 128, VT.o + j * 1040 + ob * 520,
                                  VT.o + j * 1040 + (ob + 1) * 520)
                    p.emit("dve", lambda e, bank=bank, dst=dst, dreg=dreg:
                           e.tensor_copy(
                        dst, bank.ap()[:, :].rearrange("p (h c) -> p h c",
                                                       c=64)),
                        reads=[R(bank)], writes=[dreg])
    kv_all("k", "wkT")
    kv_all("v", "wvT")

    def vones(j):
        dst = mk.ar16_t.ap()[:, VT.o + j * 1040:VT.o + (j + 1) * 1040]\
            .rearrange("p (h c) -> p h c", c=65)[:, :, 64:65]
        p.emit("dve", lambda e: e.memset(dst, 1.0),
               writes=[Region(VT.t, 0, 128, VT.o + j * 1040,
                              VT.o + (j + 1) * 1040)])
    for j in range(TTE):
        vones(j)
    ctx = mk.v32(8 * 65)
    p.emit("dve", lambda e: e.memset(ctx.ap(), 0.0), writes=[ctx.reg()])
    afm = View(mk.ar16_t, xnT.o, 8 * T)
    zrowbuf = mk.v32(T)
    p.emit("dve", lambda e: e.memset(ctxn.ap(64, 65), 1.0),
           writes=[Region(ctxn.t, 0, 128, ctxn.o + 64, ctxn.o + 65)])
    p.emit("dve", lambda e: e.memset(ctxn.ap(129, 130), 1.0),
           writes=[Region(ctxn.t, 0, 128, ctxn.o + 129, ctxn.o + 130)])

    def head(h):
        oc, pb = h // 2, (h % 2) * 64
        cs = (oc * 65, (oc + 1) * 65)
        creg = Region(mk.ar32_t, pb, pb + 64, ctx.o + cs[0], ctx.o + cs[1])
        cbase = ctxn.o if h % 2 == 0 else ctxn.o + 65

        def bucket(u):
            jt, half = u // 2, (u % 2) * 64
            if u > 0:
                mk.small.save()
                ikc = mk.vsm(1)
                p.emit("dve", lambda e: e.tensor_scalar(
                    ikc.ap(p0=pb, p1=pb + 64),
                    mk.ar32_t.ap()[pb:pb + 64, ctx.o + cs[0] + 64:
                                   ctx.o + cs[1]],
                    1.0, 1e-6, ALU.mult, ALU.add),
                    reads=[creg], writes=[ikc.reg(p0=pb, p1=pb + 64)])
                p.emit("dve", lambda e: e.reciprocal(
                    ikc.ap(p0=pb, p1=pb + 64), ikc.ap(p0=pb, p1=pb + 64)),
                    reads=[ikc.reg(p0=pb, p1=pb + 64)],
                    writes=[ikc.reg(p0=pb, p1=pb + 64)])
                p.emit("dve", lambda e: e.tensor_scalar(
                    ctxn.t.ap()[pb:pb + 64, cbase:cbase + 64],
                    mk.ar32_t.ap()[pb:pb + 64, ctx.o + cs[0]:
                                   ctx.o + cs[0] + 64],
                    ikc.ap(p0=pb, p1=pb + 64), None, ALU.mult),
                    reads=[creg, ikc.reg(p0=pb, p1=pb + 64)],
                    writes=[Region(ctxn.t, pb, pb + 64, cbase, cbase + 64)])
                mk.small.restore()
                bank = mk.ps[6]
                p.emit("pe", lambda e: e.matmul(
                    bank.ap()[0:65, 0:64],
                    ctxn.t.ap()[pb:pb + 64, cbase:cbase + 65],
                    QF.ap(oc * T + u * 64, oc * T + (u + 1) * 64,
                          p0=pb, p1=pb + 64),
                    start=True, stop=True),
                    reads=[Region(ctxn.t, pb, pb + 64, cbase, cbase + 65),
                           QF.reg()],
                    writes=[R(bank)])
                asl = (u * 64, (u + 1) * 64)
                p.emit("dve", lambda e: e.tensor_copy(
                    afm.t.ap()[pb:pb + 64,
                               afm.o + oc * T + asl[0]:afm.o + oc * T + asl[1]],
                    bank.ap()[0:64, 0:64]),
                    reads=[R(bank)],
                    writes=[Region(afm.t, pb, pb + 64, afm.o + oc * T + asl[0],
                                   afm.o + oc * T + asl[1])])
                p.emit("dve", lambda e: e.tensor_copy(
                    mk.ar32_t.ap()[0:1, zrowbuf.o + asl[0]:zrowbuf.o + asl[1]],
                    bank.ap()[64:65, 0:64]),
                    reads=[R(bank)], writes=[zrowbuf.reg()])
            else:
                p.emit("dve", lambda e: e.memset(
                    afm.t.ap()[pb:pb + 64, afm.o + oc * T:afm.o + oc * T + 64],
                    0.0),
                    writes=[Region(afm.t, pb, pb + 64, afm.o + oc * T,
                                   afm.o + oc * T + 64)])
                p.emit("dve", lambda e: e.memset(
                    mk.ar32_t.ap()[0:1, zrowbuf.o:zrowbuf.o + 64], 1.0),
                    writes=[zrowbuf.reg()])
            bank2 = mk.ps[7]
            p.emit("pe", lambda e: e.matmul(
                bank2.ap()[0:64, 0:65],
                KT.t.ap()[half:half + 64,
                          KT.o + jt * 1024 + h * 64:KT.o + jt * 1024 + (h + 1) * 64]
                if False else
                KT.t.ap()[half:half + 64,
                          KT.o + jt * 1024 + h * 64:KT.o + jt * 1024 + (h + 1) * 64],
                mk.ar16_t.ap()[half:half + 64,
                               VT.o + jt * 1040 + h * 65:VT.o + jt * 1040 + (h + 1) * 65],
                start=True, stop=True),
                reads=[Region(KT.t, half, half + 64,
                              KT.o + jt * 1024, KT.o + (jt + 1) * 1024),
                       Region(VT.t, half, half + 64,
                              VT.o + jt * 1040, VT.o + (jt + 1) * 1040)],
                writes=[R(bank2)])
            p.emit("dve", lambda e: e.tensor_tensor(
                mk.ar32_t.ap()[pb:pb + 64, ctx.o + cs[0]:ctx.o + cs[1]],
                mk.ar32_t.ap()[pb:pb + 64, ctx.o + cs[0]:ctx.o + cs[1]],
                bank2.ap()[0:64, 0:65], ALU.add),
                reads=[creg, R(bank2)], writes=[creg])
        for u in range(NB):
            bucket(u)
        mk.ar32.save()
        p.emit("act", lambda e: e.activation(
            mk.ar32_t.ap()[0:1, zrowbuf.o:zrowbuf.o + T],
            mk.ar32_t.ap()[0:1, zrowbuf.o:zrowbuf.o + T], AF.Ln),
            reads=[zrowbuf.reg()], writes=[zrowbuf.reg()])
        p.emit("act", lambda e: e.activation(
            mk.ar32_t.ap()[0:1, zrowbuf.o:zrowbuf.o + T],
            mk.ar32_t.ap()[0:1, zrowbuf.o:zrowbuf.o + T], AF.Exp,
            scale=p.const_col(-1.0)[0:1, :]),
            reads=[zrowbuf.reg()], writes=[zrowbuf.reg()])
        ibc = mk.v32(512)

        def div4(ts4):
            bank = mk.ps[6]
            p.emit("pe", lambda e: e.matmul(
                bank.ap()[pb:pb + 64, :], onesr[0:1, 0:64],
                mk.ar32_t.ap()[0:1, zrowbuf.o + ts4 * 512:
                               zrowbuf.o + (ts4 + 1) * 512],
                start=True, stop=True),
                reads=[zrowbuf.reg()], writes=[R(bank)])
            p.emit("dve", lambda e: e.tensor_copy(
                ibc.ap(p0=pb, p1=pb + 64), bank.ap()[pb:pb + 64, :]),
                reads=[R(bank)], writes=[ibc.reg()])
            asl = (oc * T + ts4 * 512, oc * T + (ts4 + 1) * 512)
            p.emit("dve", lambda e: e.tensor_tensor(
                afm.t.ap()[pb:pb + 64, afm.o + asl[0]:afm.o + asl[1]],
                afm.t.ap()[pb:pb + 64, afm.o + asl[0]:afm.o + asl[1]],
                ibc.ap(p0=pb, p1=pb + 64), ALU.mult),
                reads=[Region(afm.t, pb, pb + 64, afm.o + asl[0],
                              afm.o + asl[1]), ibc.reg()],
                writes=[Region(afm.t, pb, pb + 64, afm.o + asl[0],
                               afm.o + asl[1])])
        for ts4 in range(T // 512):
            div4(ts4)
        mk.ar32.restore()
    for h in range(16):
        head(h)
    mk.page_x(xp, "in")

    def proj(ob):
        wv = kvw[ob % 2]
        ldw(wv, "woT", ob)

        def one(j):
            tt = tt0 + j
            bank = mk.ps[4 + j % 2]

            def mm(t):
                last = None
                for dc in range(8):
                    last = t.matmul(
                        bank.ap()[:, :],
                        afm.t.ap()[:, afm.o + dc * T + j * 128:
                                   afm.o + dc * T + (j + 1) * 128],
                        wv.ap(dc * 512, (dc + 1) * 512),
                        start=(dc == 0), stop=(dc == 7))
                return last
            p.emit("pe", mm,
                   reads=[Region(afm.t, 0, 128, afm.o, afm.o + 8 * T),
                          wv.reg(0, 4096)],
                   writes=[R(bank)])
            xs = (tt * D + ob * 512, tt * D + (ob + 1) * 512)
            xa = x.ap()[:, xs[0]:xs[1]]
            xreg = Region(x, 0, 128, *xs)
            yv = mk.v32(512)
            p.emit("dve", lambda e: e.tensor_tensor(
                yv.ap(), bank.ap()[:, :], bo_bc.ap(ob * 512, (ob + 1) * 512),
                ALU.add),
                reads=[R(bank), bo_bc.reg(ob * 512, (ob + 1) * 512)],
                writes=[yv.reg()])
            p.emit("dve", lambda e: e.tensor_tensor(
                xa, xa, yv.ap(), ALU.add),
                reads=[xreg, yv.reg()], writes=[xreg])
            mk.ar32.off -= 512
        for j in range(TTE):
            one(j)
    proj(0); proj(1)


def mamba_apply(mk, cfg, elem):
    import os as _os
    PH = int(_os.environ.get("MM_PHASE", "9"))
    p = mk.p
    mk.ar16.reset(); mk.ar32.reset(); mk.small.reset()
    TTE = mk.TTE
    T = mk.T
    TC = 256
    NCK = T // TC
    NDI = 16
    tt0 = elem * TTE
    x = mk.x
    wcv = mk.vsm(64)
    bcv = mk.vsm(16)
    bdt = mk.vsm(16)
    Dcl = mk.vsm(16)
    for nm, v in (("wconv_cols", wcv), ("bconv_cols", bcv),
                  ("bdt_cols", bdt), ("D_cols", Dcl)):
        p.emit("sp", lambda e, nm=nm, v=v: e.dma_start(
            out=v.ap(), in_=cfg[nm][:, :]), writes=[v.reg()], dma=True)
    Acl = mk.v32(256)
    p.emit("sp", lambda e: e.dma_start(out=Acl.ap(), in_=cfg["A_cols"][:, :]),
           writes=[Acl.reg()], dma=True)
    p.emit("act", lambda e: e.activation(Acl.ap(), Acl.ap(), AF.Exp),
           reads=[Acl.reg()], writes=[Acl.reg()])
    p.emit("dve", lambda e: e.tensor_single_scalar(
        Acl.ap(), Acl.ap(), -1.0, ALU.mult),
        reads=[Acl.reg()], writes=[Acl.reg()])
    sel96 = mk.v16_in32(4096)
    p.emit("pool", lambda e: e.dma_start(
        out=sel96.ap(p1=96), in_=cfg["sel96"][:, :]),
        writes=[sel96.reg()], dma=True)
    wxs = mk.v16_in32(1536)
    p.emit("pool", lambda e: e.dma_start(
        out=wxs.ap().rearrange("p (c o) -> p c o", c=16),
        in_=cfg["wx_nat"].rearrange("(c p) o -> p c o", p=128)),
        writes=[wxs.reg()], dma=True)
    wdts = mk.v16_in32(2048)
    p.emit("pool", lambda e: e.dma_start(
        out=wdts.ap(p1=64).rearrange("p (c o) -> p c o", c=16),
        in_=cfg["wdt"].rearrange("p (c o) -> p c o", c=16)),
        writes=[wdts.reg()], dma=True)
    negm = mk.vsm(TTE)
    rstd = mk.vsm(TTE)
    mk.ln_stats(list(range(tt0, tt0 + TTE)), rstd, negm)
    xp = cfg["xpage"]
    bneed = 8 * T + 4096 + 8192 + 4096
    bt = x if mk.TT * D >= bneed else p.sbuf(f"mm_borrow{elem}",
                                             [128, bneed], BF16)
    xnT = View(bt, 0, 8 * T)
    zc = View(bt, 8 * T, 4096)
    Bbc = View(bt, 8 * T + 4096, 4096)
    Cbc = View(bt, 8 * T + 8192, 4096)
    y16 = View(bt, 8 * T + 12288, 4096)
    xbr = [mk.v16(16 * 259) for _ in range(2)]
    av = mk.v16(4096)
    bv = mk.v16(4096)
    hv = mk.v16(4096)
    xcs = mk.v16(4096)
    states = mk.v16(256)
    wring = [mk.v16_in32(4096) for _ in range(2)]
    ystage2 = mk.v16_in32(1024)
    stage = BView(mk.ar32_t, ystage2.o, 1024)
    tmp16 = View(mk.ar16_t, bv.o, 4096)
    big32 = mk.v32(4096)
    dblf = mk.v32(256)
    dblT = None
    p.emit("dve", lambda e: e.memset(states.ap(), 0.0), writes=[states.reg()])

    def mkxnT(j):
        tt = tt0 + j
        mk.ar16.save()
        xn = mk.v16(1024)
        mk.ln_apply(tt, negm, rstd, xn.ap(), xn.reg(), j=j)
        dst = bt.ap().bitcast(BF16) if isinstance(bt, type(mk.ar32_t)) else None
        dstap = bt.ap()[:, xnT.o:xnT.o + 16384].rearrange(
            "p (c t) -> p c t", c=8)[:, :, j * 128:(j + 1) * 128]
        mk.transpose8(lambda dc: xn.ap(dc * 128, (dc + 1) * 128),
                      dstap, Region(bt, 0, 128, xnT.o, xnT.o + 16384),
                      [xn.reg()], pb=j % 2)
        mk.ar16.restore()
    mk.page_x(xp, "out")

    xtring = [mk.v16(1024) for _ in range(2)]
    xnring = [mk.v16(1024) for _ in range(2)]

    def mkxnT2(j):
        tt = tt0 + j
        mk.ar16.save()
        xt = xtring[j % 2]
        xn = xnring[j % 2]
        p.emit("sp", lambda e: e.dma_start(
            out=xt.ap(), in_=xp.ap()[:, tt * D:(tt + 1) * D]),
            reads=[Region(xp, 0, 128, tt * D, (tt + 1) * D)],
            writes=[xt.reg()], dma=True)
        p.emit("dve", lambda e: e.tensor_scalar(
            xn.ap(), xt.ap(), negm.ap(j, j + 1), rstd.ap(j, j + 1),
            ALU.add, ALU.mult),
            reads=[xt.reg(), negm.reg(j, j + 1), rstd.reg(j, j + 1)],
            writes=[xn.reg()])
        dstap = bt.ap()[:, xnT.o:xnT.o + 8 * T].rearrange(
            "p (c t) -> p c t", c=8)[:, :, j * 128:(j + 1) * 128]
        mk.transpose8(lambda dc: xn.ap(dc * 128, (dc + 1) * 128),
                      dstap, Region(bt, 0, 128, xnT.o, xnT.o + 8 * T),
                      [xn.reg()], pb=j % 2)
        mk.ar16.restore()
    for j in range(TTE):
        mkxnT2(j)

    def xz_mms(c, which, dst_fn, slot=None):
        wsrc = cfg["w_in_x" if which == "x" else "w_in_z"]

        def one(di):
            wv = wring[(di // 4) % 2]
            if di % 4 == 0:
                p.emit("pool", lambda e, wv=wv, di=di: e.dma_start(
                    out=wv.ap(0, 4096).rearrange("p (c o) -> p c o", c=8),
                    in_=wsrc.rearrange("(c p) o -> p c o", p=128)
                    [:, :, di * 128:(di + 4) * 128]),
                    writes=[wv.reg(0, 4096)], dma=True)
            boff = (di % 4) * 128
            bank = mk.ps[4 + di % 2]

            def mm(t):
                last = None
                for dc in range(8):
                    last = t.matmul(
                        bank.ap()[:, 0:TC],
                        wv.ap(dc * 512 + boff, dc * 512 + boff + 128),
                        bt.ap()[:, xnT.o + dc * T + c * TC:
                                xnT.o + dc * T + (c + 1) * TC],
                        start=(dc == 0), stop=(dc == 7))
                return last
            p.emit("pe", mm,
                   reads=[wv.reg(0, 4096),
                          Region(bt, 0, 128, xnT.o, xnT.o + 8 * T)],
                   writes=[R(bank)])
            dst_fn(di, bank)
        for di in range(NDI):
            one(di)

    def chunk(c):
        slot = xbr[c % 2]
        def evx(di, bank):
            ds = (di * 259 + 1, di * 259 + 257)
            p.emit("dve", lambda e: e.tensor_copy(
                slot.ap(*ds), bank.ap()[:, 0:TC]),
                reads=[R(bank)], writes=[slot.reg(*ds)])
        xz_mms(c, "x", evx)
        prev = xbr[(c + 1) % 2]

        def hist(di):
            ds = (di * 259, di * 259 + 1)
            if c == 0:
                p.emit("dve", lambda e: e.memset(slot.ap(*ds), 0.0),
                       writes=[slot.reg(*ds)])
            else:
                p.emit("dve", lambda e: e.tensor_copy(
                    slot.ap(*ds), prev.ap(di * 259 + 256, di * 259 + 257)),
                    reads=[prev.reg(di * 259 + 256, di * 259 + 257)],
                    writes=[slot.reg(*ds)])
        for di in range(NDI):
            hist(di)
        def fut(di):
            ds = (di * 259 + 257, di * 259 + 259)
            p.emit("dve", lambda e: e.tensor_copy(
                prev.ap(*ds), slot.ap(di * 259 + 1, di * 259 + 3)),
                reads=[slot.reg(di * 259 + 1, di * 259 + 3)],
                writes=[prev.reg(*ds)])
        if c > 0:
            for di in range(NDI):
                fut(di)
        if c > 0:
            ssm(c - 1, prev)
        if c == NCK - 1:
            def futz(di):
                ds = (di * 259 + 257, di * 259 + 259)
                p.emit("dve", lambda e: e.memset(slot.ap(*ds), 0.0),
                       writes=[slot.reg(*ds)])
            for di in range(NDI):
                futz(di)
            ssm(c, slot)

    def ssm(c, slot):
        if PH < 2:
            return
        def conv(di):
            a0 = (di * 256, (di + 1) * 256)

            def tap(k):
                src = (di * 259 + k, di * 259 + k + 256)
                if k == 0:
                    p.emit("dve", lambda e: e.tensor_scalar(
                        big32.ap(*a0), slot.ap(*src), wcv.ap(di * 4, di * 4 + 1),
                        bcv.ap(di, di + 1), ALU.mult, ALU.add),
                        reads=[slot.reg(*src), wcv.reg(di * 4, di * 4 + 1),
                               bcv.reg(di, di + 1)],
                        writes=[big32.reg(*a0)])
                else:
                    p.emit("dve", lambda e: e.scalar_tensor_tensor(
                        big32.ap(*a0), slot.ap(*src),
                        wcv.ap(di * 4 + k, di * 4 + k + 1), big32.ap(*a0),
                        ALU.mult, ALU.add),
                        reads=[slot.reg(*src), wcv.reg(di * 4 + k, di * 4 + k + 1),
                               big32.reg(*a0)],
                        writes=[big32.reg(*a0)])
            for k in range(4):
                tap(k)
        for di in range(NDI):
            conv(di)
        p.emit("act", lambda e: e.activation(
            xcs.ap(), big32.ap(), AF.Silu),
            reads=[big32.reg()], writes=[xcs.reg()])
        if PH < 3:
            return
        bank = mk.ps[6]

        def mmdbl(t):
            last = None
            for di in range(NDI):
                last = t.matmul(
                    bank.ap()[0:96, 0:TC],
                    wxs.ap(di * 96, (di + 1) * 96),
                    xcs.ap(di * 256, (di + 1) * 256),
                    start=(di == 0), stop=(di == NDI - 1))
            return last
        p.emit("pe", mmdbl, reads=[wxs.reg(), xcs.reg()], writes=[R(bank)])
        p.emit("dve", lambda e: e.tensor_copy(
            dblf.ap(p1=96), bank.ap()[0:96, 0:TC]),
            reads=[R(bank)], writes=[dblf.reg()])
        dbl16 = View(mk.ar16_t, tmp16.o, 256)
        p.emit("dve", lambda e: e.tensor_copy(
            dbl16.ap(p1=96), dblf.ap(p1=96)),
            reads=[dblf.reg()], writes=[dbl16.reg()])
        def selbc(n):
            bnk = mk.ps[6 if n % 2 == 0 else 7]
            dstv = Bbc if n < 16 else Cbc
            nn = n % 16
            p.emit("pe", lambda e: e.matmul(
                bnk.ap()[:, 0:TC], sel96.ap(n * 128, (n + 1) * 128, p1=96),
                dbl16.ap(p1=96), start=True, stop=True),
                reads=[sel96.reg(), dbl16.reg()], writes=[R(bnk)])
            ds = (nn * 256, (nn + 1) * 256)
            p.emit("dve", lambda e: e.tensor_copy(
                bt.ap()[:, dstv.o + ds[0]:dstv.o + ds[1]], bnk.ap()[:, 0:TC]),
                reads=[R(bnk)],
                writes=[Region(bt, 0, 128, dstv.o + ds[0], dstv.o + ds[1])])
        for n in range(32):
            selbc(n)
        if PH < 4:
            return
        def ddi(di):
            bnk = mk.ps[4 + di % 2]
            p.emit("pe", lambda e: e.matmul(
                bnk.ap()[:, 0:TC], wdts.ap(di * 128, (di + 1) * 128, p1=64),
                dbl16.ap(p1=64), start=True, stop=True),
                reads=[wdts.reg(), dbl16.reg()], writes=[R(bnk)])
            a0 = (di * 256, (di + 1) * 256)
            mk.ar32.save()
            t1 = mk.v32(256)
            p.emit("dve", lambda e: e.tensor_scalar(
                t1.ap(), bnk.ap()[:, 0:TC], bdt.ap(di, di + 1), None, ALU.add),
                reads=[R(bnk), bdt.reg(di, di + 1)], writes=[t1.reg()])
            p.emit("dve", lambda e: e.tensor_single_scalar(
                big32.ap(*a0), t1.ap(), 0.0, ALU.max),
                reads=[t1.reg()], writes=[big32.reg(*a0)])
            p.emit("dve", lambda e: e.scalar_tensor_tensor(
                t1.ap(), t1.ap(), -1.0, t1.ap(), ALU.mult, ALU.max),
                reads=[t1.reg()], writes=[t1.reg()])
            p.emit("act", lambda e: e.activation(
                t1.ap(), t1.ap(), AF.Exp, scale=p.const_col(-1.0)),
                reads=[t1.reg()], writes=[t1.reg()])
            p.emit("act", lambda e: e.activation(
                t1.ap(), t1.ap(), AF.Ln, bias=p.const_col(1.0)),
                reads=[t1.reg()], writes=[t1.reg()])
            p.emit("dve", lambda e: e.tensor_tensor(
                big32.ap(*a0), big32.ap(*a0), t1.ap(), ALU.add),
                reads=[big32.reg(*a0), t1.reg()], writes=[big32.reg(*a0)])
            mk.ar32.restore()
        for di in range(NDI):
            ddi(di)
        zdone = [False]

        def dchunk(di):
            a0 = (di * 256, (di + 1) * 256)
            d3 = big32.ap(*a0).rearrange("p (o t) -> p o t", o=1)\
                .broadcast_to((128, 16, TC))
            A3 = Acl.ap(di * 16, (di + 1) * 16).rearrange(
                "p (n o) -> p n o", o=1).broadcast_to((128, 16, TC))
            av3 = av.ap().rearrange("p (n t) -> p n t", n=16)
            p.emit("pool", lambda e: e.tensor_tensor(
                av3, d3, A3, ALU.mult),
                reads=[big32.reg(*a0), Acl.reg(di * 16, (di + 1) * 16)],
                writes=[av.reg()])
            p.emit("act", lambda e: e.activation(
                av.ap(), av.ap(), AF.Exp),
                reads=[av.reg()], writes=[av.reg()])
            u = ystage2
            p.emit("dve", lambda e: e.tensor_tensor(
                u.ap(0, 256), big32.ap(*a0), xcs.ap(*a0), ALU.mult),
                reads=[big32.reg(*a0), xcs.reg(*a0)], writes=[u.reg(0, 256)])
            u3 = u.ap(0, 256).rearrange("p (o t) -> p o t", o=1)\
                .broadcast_to((128, 16, TC))
            B3 = bt.ap()[:, Bbc.o:Bbc.o + 4096].rearrange(
                "p (n t) -> p n t", n=16)
            bv3 = bv.ap().rearrange("p (n t) -> p n t", n=16)
            p.emit("pool", lambda e: e.tensor_tensor(
                bv3, u3, B3, ALU.mult),
                reads=[u.reg(0, 256), Region(bt, 0, 128, Bbc.o, Bbc.o + 4096)],
                writes=[bv.reg()])
            def scn(n):
                sc0 = (n * 256, (n + 1) * 256)
                stc = di * 16 + n
                p.emit("dve", lambda e: e.tensor_tensor_scan(
                    hv.ap(*sc0), av.ap(*sc0), bv.ap(*sc0),
                    states.ap(stc, stc + 1), ALU.mult, ALU.add),
                    reads=[av.reg(*sc0), bv.reg(*sc0),
                           states.reg(stc, stc + 1)],
                    writes=[hv.reg(*sc0)])
            for n in range(16):
                scn(n)
            p.emit("dve", lambda e: e.tensor_copy(
                states.ap(di * 16, (di + 1) * 16),
                hv.ap().rearrange("p (n t) -> p n t", n=16)[:, :, 255:256]
                .rearrange("p n o -> p (n o)")),
                reads=[hv.reg()],
                writes=[states.reg(di * 16, (di + 1) * 16)])
            C3 = bt.ap()[:, Cbc.o:Cbc.o + 4096].rearrange(
                "p (n t) -> p n t", n=16)
            t3 = tmp16.ap().rearrange("p (n t) -> p n t", n=16)
            p.emit("pool", lambda e: e.tensor_tensor(
                t3, hv.ap().rearrange("p (n t) -> p n t", n=16), C3, ALU.mult),
                reads=[hv.reg(), Region(bt, 0, 128, Cbc.o, Cbc.o + 4096)],
                writes=[tmp16.reg()])
            mk.ar32.save()
            yy = mk.v32(256)
            p.emit("dve", lambda e: e.tensor_reduce(
                yy.ap(), tmp16.ap().rearrange("p (n t) -> p t n", n=16),
                axis=AX.X, op=ALU.add),
                reads=[tmp16.reg()], writes=[yy.reg()])
            p.emit("dve", lambda e: e.scalar_tensor_tensor(
                yy.ap(), xcs.ap(*a0), Dcl.ap(di, di + 1), yy.ap(),
                ALU.mult, ALU.add),
                reads=[xcs.reg(*a0), Dcl.reg(di, di + 1), yy.reg()],
                writes=[yy.reg()])
            if not zdone[0]:
                pass
            zs = (di * 256, (di + 1) * 256)
            sz = View(mk.ar16_t, stage.o, 256)
            p.emit("act", lambda e: e.activation(
                sz.ap(0, 256), bt.ap()[:, zc.o + zs[0]:zc.o + zs[1]], AF.Silu),
                reads=[Region(bt, 0, 128, zc.o + zs[0], zc.o + zs[1])],
                writes=[sz.reg(0, 256)])
            p.emit("dve", lambda e: e.tensor_tensor(
                bt.ap()[:, y16.o + zs[0]:y16.o + zs[1]], yy.ap(),
                sz.ap(0, 256), ALU.mult),
                reads=[yy.reg(), sz.reg(0, 256)],
                writes=[Region(bt, 0, 128, y16.o + zs[0], y16.o + zs[1])])
            mk.ar32.restore()
        if PH < 5:
            return
        def evz(di, bank):
            ds = (di * 256, (di + 1) * 256)
            p.emit("dve", lambda e: e.tensor_copy(
                bt.ap()[:, zc.o + ds[0]:zc.o + ds[1]], bank.ap()[:, 0:TC]),
                reads=[R(bank)],
                writes=[Region(bt, 0, 128, zc.o + ds[0], zc.o + ds[1])])
        xz_mms(c, "z", evz)
        for di in range(NDI):
            dchunk(di)
        if PH < 6:
            return
        def proj(di_outer_unused=None):
            def one(di):
                wv = wring[(di // 4) % 2]
                if di % 4 == 0:
                    p.emit("pool", lambda e, wv=wv, di=di: e.dma_start(
                        out=wv.ap(0, 4096).rearrange("p (c o) -> p c o", c=4),
                        in_=cfg["wout"].rearrange("(c p) o -> p c o", p=128)
                        [:, di:di + 4, :]),
                        writes=[wv.reg(0, 4096)],
                        dma=True)
                boff = (di % 4) * 1024
                for tp in range(2):
                    for ob in range(2):
                        bnk = mk.ps[tp * 2 + ob]
                        p.emit("pe", lambda e, tp=tp, ob=ob, bnk=bnk, wv=wv,
                               di=di, boff=boff: e.matmul(
                            bnk.ap()[:, :],
                            bt.ap()[:, y16.o + di * 256 + tp * 128:
                                    y16.o + di * 256 + (tp + 1) * 128],
                            wv.ap(boff + ob * 512, boff + (ob + 1) * 512),
                            start=(di == 0), stop=(di == NDI - 1)),
                            reads=[Region(bt, 0, 128, y16.o, y16.o + 4096),
                                   wv.reg(0, 4096)],
                            writes=[R(bnk)])
            for di in range(NDI):
                one(di)

            def evac(tp):
                for ob in range(2):
                    bnk = mk.ps[tp * 2 + ob]
                    p.emit("dve", lambda e, ob=ob, bnk=bnk: e.tensor_copy(
                        stage.ap(ob * 512, (ob + 1) * 512), bnk.ap()[:, :]),
                        reads=[R(bnk)],
                        writes=[stage.reg(ob * 512, (ob + 1) * 512)])
                p.emit("sp", lambda e, tp=tp: e.dma_start(
                    out=cfg["ydram"].ap()[:, (c * 2 + tp) * D:
                                          (c * 2 + tp + 1) * D],
                    in_=stage.ap()),
                    reads=[stage.reg()],
                    writes=[Region(cfg["ydram"], 0, 128, (c * 2 + tp) * D,
                                   (c * 2 + tp + 1) * D)], dma=True)
            evac(0); evac(1)
        proj()
    for c in range(NCK):
        chunk(c)
    mk.page_x(xp, "in")

    def finadd(j):
        tt = tt0 + j
        xs = (tt * D, (tt + 1) * D)
        xa = x.ap()[:, xs[0]:xs[1]]
        xreg = Region(x, 0, 128, *xs)
        p.emit("sp", lambda e: e.dma_start(
            out=ystage2.ap(), in_=cfg["ydram"].ap()[:, j * D:(j + 1) * D]),
            reads=[Region(cfg["ydram"], 0, 128, j * D, (j + 1) * D)],
            writes=[ystage2.reg()], dma=True)
        p.emit("dve", lambda e: e.tensor_tensor(
            xa, xa, ystage2.ap(), ALU.add),
            reads=[xreg, ystage2.reg()], writes=[xreg])
    for j in range(TTE):
        finadd(j)


def build_program():
    p = Prog()
    nc = p.nc
    F = F32
    inp = {}

    def di(name, shape):
        inp[name] = nc.dram_tensor(name, list(shape), F, kind="ExternalInput").ap()
        return inp[name]
    xin = di("xin", [4096, 1024])
    di("iden", [128, 128])
    di("E8", [8, 128])
    di("sel8", [8, 1024])
    di("sel96", [96, 4096])
    di("ropeT", [8, 128, 128])
    di("l0_w1T", [1024, 4096]); di("l0_w2T", [4096, 1024])
    di("l0_wqkv", [1024, 3072]); di("l0_bqc", [128, 24]); di("l0_bqr", [1, 3072])
    di("l0_wproj", [1024, 1024]); di("l0_bpr", [1, 1024])
    for nm in ("a_q", "b_q", "a_v", "b_v"):
        di("l0_" + nm, [64, 16] if nm.startswith("a") else [16, 64])
    di("l1_wgT", [1024, 8]); di("l1_winT", [8, 1024, 4096])
    di("l1_woutT", [8, 4096, 1024])
    for nm in ("wqT", "wkT", "wvT", "woT"):
        di("l1_" + nm, [1024, 1024])
    di("l1_bo", [1, 1024])
    di("l2_router", [1024, 8]); di("l2_lnw", [1, 1024])
    di("l2_w1", [8, 1024, 2816]); di("l2_w3", [8, 1024, 2816])
    di("l2_w2", [8, 2816, 1024])
    di("l2_winx", [1024, 2048]); di("l2_winz", [1024, 2048])
    di("l2_wconv", [128, 64]); di("l2_bconv", [128, 16])
    di("l2_wx", [2048, 96]); di("l2_wdt", [64, 2048]); di("l2_bdt", [128, 16])
    di("l2_A", [128, 256]); di("l2_D", [128, 16]); di("l2_wout", [2048, 1024])
    di("l3_qkvw", [1024, 3072]); di("l3_qkvbc", [128, 24]); di("l3_qkvbr", [1, 3072])
    di("l3_outw", [1024, 1024]); di("l3_outbr", [1, 1024])
    di("l3_wg", [2816, 1024]); di("l3_wu", [2816, 1024]); di("l3_wd", [1024, 2816])
    di("l3_wgT", [1024, 2816]); di("l3_wuT", [1024, 2816]); di("l3_wdT", [2816, 1024])
    xout = nc.dram_tensor("xout", [4096, 1024], F, kind="ExternalOutput").ap()

    mk = MK(p, T_elem=2048, n_elem=2)
    mk.load_const_tiles(inp["iden"])
    xpage = p.dram("xpage", [128, mk.TT * 1024], BF16)
    ydram = p.dram("ydram", [128, mk.TTE * 1024], BF16)
    for tt in range(mk.TT):
        p.emit("pool", lambda e, tt=tt: e.dma_start(
            out=mk.x.ap()[:, tt * 1024:(tt + 1) * 1024],
            in_=xin[tt * 128:(tt + 1) * 128, :]),
            writes=[Region(mk.x, 0, 128, tt * 1024, (tt + 1) * 1024)], dma=True)
    T1d, T2d, scols = l0_ffn_prepare(mk, inp["l0_w1T"], inp["l0_w2T"])
    l3n = {"wg": inp["l3_wg"], "wu": inp["l3_wu"], "wd": inp["l3_wd"]}
    l3t = {"wg": inp["l3_wgT"], "wu": inp["l3_wuT"], "wd": inp["l3_wdT"]}
    wqd = l3_ffn_prepare(mk, l3n, l3t, inp["E8"], inp["iden"])
    l0a = {"wqkv": inp["l0_wqkv"], "bqkv_cols": inp["l0_bqc"],
           "bqkv_row": inp["l0_bqr"], "wproj": inp["l0_wproj"],
           "bproj_row": inp["l0_bpr"], "xpage": xpage,
           "lora": {nm: inp["l0_" + nm] for nm in ("a_q", "b_q", "a_v", "b_v")}}
    l3a = {"wqkv": inp["l3_qkvw"], "bqkv_cols": inp["l3_qkvbc"],
           "bqkv_row": inp["l3_qkvbr"], "wproj": inp["l3_outw"],
           "bproj_row": inp["l3_outbr"], "xpage": xpage, "rope": inp["ropeT"]}
    l1m = {"kind": "l1", "w_gT": inp["l1_wgT"], "w_in": inp["l1_winT"],
           "w_out": inp["l1_woutT"], "sel8": inp["sel8"]}
    l2m = {"kind": "l2", "router": inp["l2_router"], "lnw_row": inp["l2_lnw"],
           "w1": inp["l2_w1"], "w3": inp["l2_w3"], "w2": inp["l2_w2"],
           "sel8": inp["sel8"]}
    l1at = {"wqT": inp["l1_wqT"], "wkT": inp["l1_wkT"], "wvT": inp["l1_wvT"],
            "woT": inp["l1_woT"], "bo_row": inp["l1_bo"], "xpage": xpage}
    l2at = {"w_in_x": inp["l2_winx"], "w_in_z": inp["l2_winz"],
            "wconv_cols": inp["l2_wconv"], "bconv_cols": inp["l2_bconv"],
            "wx_nat": inp["l2_wx"], "wdt": inp["l2_wdt"],
            "bdt_cols": inp["l2_bdt"], "A_cols": inp["l2_A"],
            "D_cols": inp["l2_D"], "wout": inp["l2_wout"],
            "sel96": inp["sel96"], "xpage": xpage, "ydram": ydram}
    import os as _os
    NSUB = int(_os.environ.get("MK_NSUB", "16"))
    SNAP = bool(_os.environ.get("MK_SNAP"))
    snaps = []
    state = {"i": 0}

    def snap():
        state["i"] += 1
        if not SNAP:
            return
        i = state["i"]
        sd = nc.dram_tensor(f"snap{i}", [4096, 1024], F, kind="ExternalOutput").ap()
        for tt in range(mk.TT):
            p.emit("pool", lambda e, tt=tt, sd=sd: e.dma_start(
                out=sd[tt * 128:(tt + 1) * 128, :],
                in_=mk.x.ap()[:, tt * 1024:(tt + 1) * 1024]),
                reads=[Region(mk.x, 0, 128, tt * 1024, (tt + 1) * 1024)],
                dma=True)
    seq = [
        lambda: l0_ffn_apply(mk, T1d, T2d, scols),
        lambda: attn_apply(mk, l0a, 0),
        lambda: attn_apply(mk, l0a, 1),
        lambda: l0_ffn_apply(mk, T1d, T2d, scols),
        lambda: moe_apply(mk, l1m, xpage),
        lambda: l1_attn_apply(mk, l1at, 0),
        lambda: l1_attn_apply(mk, l1at, 1),
        lambda: moe_apply(mk, l1m, xpage),
        lambda: moe_apply(mk, l2m, xpage),
        lambda: mamba_apply(mk, l2at, 1 if _os.environ.get("MM_SWAP") else 0),
        lambda: mamba_apply(mk, l2at, 0 if _os.environ.get("MM_SWAP") else 1),
        lambda: moe_apply(mk, l2m, xpage),
        lambda: l3_ffn_apply(mk, wqd),
        lambda: attn_apply(mk, l3a, 0),
        lambda: attn_apply(mk, l3a, 1),
        lambda: l3_ffn_apply(mk, wqd),
    ]
    for fi, f in enumerate(seq):
        if fi < NSUB:
            f()
            snap()
    for tt in range(mk.TT):
        p.emit("pool", lambda e, tt=tt: e.dma_start(
            out=xout[tt * 128:(tt + 1) * 128, :],
            in_=mk.x.ap()[:, tt * 1024:(tt + 1) * 1024]),
            reads=[Region(mk.x, 0, 128, tt * 1024, (tt + 1) * 1024)], dma=True)
    return p


def _l3_perm():
    perm = np.empty(3072, np.int64)
    for sec in range(3):
        for h in range(16):
            for d_ in range(64):
                perm[sec * 1024 + h * 64 + d_] = h * 192 + sec * 64 + d_
    return perm


def host_inputs(core, kw):
    C = np.ascontiguousarray
    HEAD_DIM = 64
    inv_freq = 1.0 / 10000.0 ** (np.arange(0, HEAD_DIM, 2) / HEAD_DIM)
    cos_t = np.cos(np.arange(16)[:, None] * inv_freq[None, :]).astype(np.float32)
    sin_t = np.sin(np.arange(16)[:, None] * inv_freq[None, :]).astype(np.float32)
    ropeT = np.zeros((8, 128, 128), np.float32)
    for c in range(8):
        for hh in range(2):
            h = 2 * c + hh
            Rm = np.zeros((64, 64), np.float32)
            for i in range(32):
                Rm[i, i] = cos_t[h, i]; Rm[i, i + 32] = -sin_t[h, i]
                Rm[i + 32, i + 32] = cos_t[h, i]; Rm[i + 32, i] = sin_t[h, i]
            ropeT[c, hh * 64:(hh + 1) * 64, hh * 64:(hh + 1) * 64] = Rm.T
    E8 = np.zeros((8, 128), np.float32)
    for b in range(8):
        E8[b, b * 16:(b + 1) * 16] = 1.0
    sel8 = np.zeros((8, 1024), np.float32)
    for e in range(8):
        sel8[e, e * 128:(e + 1) * 128] = 1.0
    sel96 = np.zeros((96, 4096), np.float32)
    for n in range(16):
        sel96[64 + n, n * 128:(n + 1) * 128] = 1.0
        sel96[80 + n, (16 + n) * 128:(17 + n) * 128] = 1.0
    W_in = kw["L2_attn_W_in"]
    W_conv = kw["L2_attn_W_conv"]
    ins = {
        "xin": C(kw["L0_attn_x"][2 * core:2 * core + 2].reshape(4096, 1024)),
        "iden": np.eye(128, dtype=np.float32),
        "E8": E8, "sel8": sel8, "sel96": sel96, "ropeT": ropeT,
        "l0_w1T": C(kw["L0_ffn_w1"].T), "l0_w2T": C(kw["L0_ffn_w2"].T),
        "l0_wqkv": kw["L0_attn_w_qkv"],
        "l0_bqc": C(kw["L0_attn_b_qkv"].reshape(24, 128).T),
        "l0_bqr": kw["L0_attn_b_qkv"][None, :],
        "l0_wproj": kw["L0_attn_w_proj"],
        "l0_bpr": kw["L0_attn_b_proj"][None, :],
        "l0_a_q": kw["L0_attn_a_q"], "l0_b_q": kw["L0_attn_b_q"],
        "l0_a_v": kw["L0_attn_a_v"], "l0_b_v": kw["L0_attn_b_v"],
        "l1_wgT": C(kw["L1_ffn_w_gate"].T),
        "l1_winT": C(kw["L1_ffn_w_in"].transpose(0, 2, 1)),
        "l1_woutT": C(kw["L1_ffn_w_out"].transpose(0, 2, 1)),
        "l1_wqT": C(kw["L1_attn_Wq"].T), "l1_wkT": C(kw["L1_attn_Wk"].T),
        "l1_wvT": C(kw["L1_attn_Wv"].T), "l1_woT": C(kw["L1_attn_Wo"].T),
        "l1_bo": kw["L1_attn_bo"][None, :],
        "l2_router": kw["L2_ffn_router_w"],
        "l2_lnw": kw["L2_ffn_ln_weight"][None, :],
        "l2_w1": kw["L2_ffn_w1"], "l2_w3": kw["L2_ffn_w3"],
        "l2_w2": kw["L2_ffn_w2"],
        "l2_winx": C(W_in[:, 0::2]), "l2_winz": C(W_in[:, 1::2]),
        "l2_wconv": C(W_conv[:, 0, :].T.reshape(16, 128, 4)
                      .transpose(1, 0, 2).reshape(128, 64)),
        "l2_bconv": C(kw["L2_attn_b_conv"].reshape(16, 128).T),
        "l2_wx": kw["L2_attn_W_x"], "l2_wdt": kw["L2_attn_W_dt"],
        "l2_bdt": C(kw["L2_attn_b_dt"].reshape(16, 128).T),
        "l2_A": C(kw["L2_attn_A_log"].reshape(16, 128, 16)
                  .transpose(1, 0, 2).reshape(128, 256)),
        "l2_D": C(kw["L2_attn_D_param"].reshape(16, 128).T),
        "l2_wout": kw["L2_attn_W_out"],
        "l3_qkvw": C(kw["L3_attn_qkv_w"][:, _l3_perm()]),
        "l3_qkvbc": C(kw["L3_attn_qkv_b"][_l3_perm()].reshape(24, 128).T),
        "l3_qkvbr": kw["L3_attn_qkv_b"][_l3_perm()][None, :],
        "l3_outw": kw["L3_attn_out_w"],
        "l3_outbr": kw["L3_attn_out_b"][None, :],
        "l3_wg": kw["L3_ffn_w_gate"], "l3_wu": kw["L3_ffn_w_up"],
        "l3_wd": kw["L3_ffn_w_down"],
        "l3_wgT": C(kw["L3_ffn_w_gate"].T), "l3_wuT": C(kw["L3_ffn_w_up"].T),
        "l3_wdT": C(kw["L3_ffn_w_down"].T),
    }
    return {k: np.ascontiguousarray(v, dtype=np.float32) for k, v in ins.items()}


def kernel(**inputs):
    p = build_program()
    ncf = p.finalize()
    shared = host_inputs(0, inputs)
    in_maps = [shared]
    for core in range(1, 8):
        m = dict(shared)
        m["xin"] = np.ascontiguousarray(
            inputs["L0_attn_x"][2 * core:2 * core + 2].reshape(4096, 1024),
            dtype=np.float32)
        in_maps.append(m)
    res = run_bass_kernel_spmd(ncf, in_maps, core_ids=list(range(8)))
    out = np.empty((16, 2048, 1024), np.float32)
    for core in range(8):
        out[2 * core:2 * core + 2] = \
            res.results[core]["xout"].reshape(2, 2048, 1024)
    return out
